# version 01
# baseline (speedup 1.0000x reference)
import math
import contextlib
import numpy as np
import ml_dtypes

import concourse.bass as bass
import concourse.mybir as mybir
from concourse.bass_utils import run_bass_kernel_spmd

F32 = mybir.dt.float32
BF16 = mybir.dt.bfloat16
AF = mybir.ActivationFunctionType
ALU = mybir.AluOpType
AX = mybir.AxisListType

NCORE = 8
T = 8192
D = 2048
TOWN = 1024
NF = 16
NI = 64
INTER = 8192
HPC = 2
DH = 128
ATT_SCALE = 1.0 / math.sqrt(128.0)
EPS = 1e-5
R2048 = 1.0 / math.sqrt(2048.0)
R8192 = 1.0 / math.sqrt(8192.0)
BF = ml_dtypes.bfloat16


def _hmat(n):
    a = np.arange(n)
    bits = a[:, None] & a[None, :]
    pc = np.zeros((n, n), dtype=np.int64)
    x = bits.copy()
    while x.any():
        pc += x & 1
        x >>= 1
    return np.where(pc % 2 == 0, 1.0, -1.0).astype(np.float32)


class Ctr:
    def __init__(self, es, nc, name):
        self.h = es.enter_context(nc.semaphore(name))
        self.v = 0


class Builder:

    def __init__(self, core_id, taps=(), upto=None):
        self.core_id = core_id
        self.taps = set(taps)
        self.upto = upto
        self.stopped = False
        self.nc = bass.Bass("TRN2", target_bir_lowering=False,
                    dynamic_dma_scratch_size=4096)
        self.es = contextlib.ExitStack()
        self.ops = {k: [] for k in ("tensor", "scalar", "vector", "gpsimd", "sync")}
        self.H0 = 2 * core_id

    def q(self, eng, fn):
        if not self.stopped:
            self.ops[eng].append(fn)

    def qw(self, eng, ctr, val=None):
        v = ctr.v if val is None else val
        if v <= 0:
            return
        h = ctr.h
        self.q(eng, lambda e: e.wait_ge(h, v))

    def phase_done(self, label):
        if self.upto is not None and label == self.upto:
            self.stopped = True

    def pe(self, fn, inc=False):
        if inc:
            h = self.sP.h
            self.sP.v += 1
            self.q("tensor", lambda e: fn(e).then_inc(h, 1))
        else:
            self.q("tensor", fn)
        return self.sP.v

    def ve(self, fn, inc=False):
        if inc:
            h = self.sV.h
            self.sV.v += 1
            self.q("vector", lambda e: fn(e).then_inc(h, 1))
        else:
            self.q("vector", fn)
        return self.sV.v

    def ae(self, fn, inc=False):
        if inc:
            h = self.sA.h
            self.sA.v += 1
            self.q("scalar", lambda e: fn(e).then_inc(h, 1))
        else:
            self.q("scalar", fn)
        return self.sA.v

    def dma(self, eng, out, in_, ctr=None, **kw):
        ctr = ctr if ctr is not None else (self.dG if eng == "gpsimd" else self.dS)
        h = ctr.h
        ctr.v += 16

        def _go(e, out=out, in_=in_, h=h, kw=kw):
            try:
                return e.dma_start(out=out, in_=in_, **kw).then_inc(h, 16)
            except Exception:
                import sys
                print("FAILING DMA out=", str(out)[:300], " in_=", str(in_)[:300],
                      file=sys.stderr)
                raise
        self.q(eng, _go)
        return ctr.v

    def tdma(self, out, in_):
        h = self.dS.h
        self.dS.v += 16
        self.q("sync", lambda e: e.dma_start_transpose(out, in_).then_inc(h, 16))
        return self.dS.v

    def barrier(self):
        pairs = [("tensor", self.sP), ("scalar", self.sA), ("vector", self.sV),
                 ("gpsimd", self.sG)]
        waits = []
        for eng, ctr in pairs:
            h = ctr.h
            pv = ctr.v
            ctr.v += 1
            self.q(eng, lambda e, h=h: e.drain(fusable=False).then_inc(h, 1))
            waits.append((h, ctr.v))
        waits += [(self.dS.h, self.dS.v), (self.dG.h, self.dG.v), (self.dW.h, self.dW.v)]
        for grp in (self.u2, self.d2, self.c2, self.g2):
            waits += [(s.h, s.v) for s in grp]
        for eng in ("tensor", "scalar", "vector", "gpsimd", "sync"):
            for h, v in waits:
                if v > 0:
                    self.q(eng, lambda e, h=h, v=v: e.wait_ge(h, v))

    def tap(self, name, src_dram):
        if name in self.dbg:
            self.barrier()
            self.dma("sync", self.dbg[name], src_dram)
            self.qw("sync", self.dS)

    def tap_sbuf(self, name, sb_ap):
        if name in self.dbg:
            self.barrier()
            self.dma("sync", self.dbg[name], sb_ap)
            self.qw("sync", self.dS)

    def declare_dram(self):
        nc = self.nc
        dt = nc.dram_tensor
        self.inp = {}
        for nm, shape in [
            ("x_own", [TOWN, D]),
            ("aWq_my", [HPC, D, DH]), ("aWk_my", [HPC, D, DH]), ("aWv_my", [HPC, D, DH]),
            ("abq_my", [HPC, DH]), ("abk_my", [HPC, DH]), ("abv_my", [HPC, DH]),
            ("aWo", [D, D]), ("abo", [D]),
            ("bWq", [D, D]), ("bWk_my", [D, HPC * DH]), ("bWv_my", [D, HPC * DH]),
            ("bWo", [D, D]),
            ("bbq", [D]), ("bbk_my", [HPC * DH]), ("bbv_my", [HPC * DH]), ("bbo", [D]),
            ("su_ug", [D]), ("su_dp", [INTER]), ("wsug", [2 * INTER]), ("wsdp", [D]),
            ("fWup", [INTER, D]), ("fWgate", [INTER, D]), ("fWdown", [D, INTER]),
        ]:
            self.inp[nm] = dt(nm, shape, F32, kind="ExternalInput").ap()
        self.y_own = dt("y_own", [TOWN, D], F32, kind="ExternalOutput").ap()

        self.xstream = dt("xstream", [128, NF * TOWN], F32, kind="Internal").ap()
        self.wup_bf = dt("wup_bf", [INTER, D], BF16, kind="Internal").ap()
        self.wgate_bf = dt("wgate_bf", [INTER, D], BF16, kind="Internal").ap()
        self.wdown_bf = dt("wdown_bf", [D, INTER], BF16, kind="Internal").ap()
        self.wupT = dt("wupT", [D, INTER], BF16, kind="Internal").ap()
        self.wgateT = dt("wgateT", [D, INTER], BF16, kind="Internal").ap()
        self.wdT = dt("wdT", [INTER, D], BF16, kind="Internal").ap()
        self.wdH = dt("wdH", [INTER, D], BF16, kind="Internal").ap()
        self.wd2 = dt("wd2", [NF, 128, NI, 128], BF16, kind="Internal").ap()
        self.ag_in = dt("ag_in", [D, TOWN], BF16, kind="Internal").ap()
        self.ag_out = dt("ag_out", [NCORE * D, TOWN], BF16, kind="Internal").ap()
        self.y_in = dt("y_in", [HPC * DH, T], BF16, kind="Internal").ap()
        self.y_out = dt("y_out", [NCORE * HPC * DH, T], BF16, kind="Internal").ap()

        self.dbg = {}
        def tapout(nm, shape, dtp):
            self.dbg[nm] = dt("dbg_" + nm, shape, dtp, kind="ExternalOutput").ap()
        if "stream" in self.taps: tapout("stream", [128, NF * TOWN], F32)
        if "xln" in self.taps: tapout("xln", [128, NF * TOWN], BF16)
        if "G" in self.taps: tapout("G", [NCORE * D, TOWN], BF16)
        if "kvq" in self.taps:
            tapout("K", [128, 2 * T], BF16)
            tapout("V", [128, NI * 256], BF16)
            tapout("Q", [128, 2 * T], BF16)
        if "yout" in self.taps: tapout("yout", [NCORE * HPC * DH, T], BF16)
        if "wd2" in self.taps: tapout("wd2", [NF, 128, NI, 128], BF16)
        if "x1" in self.taps: tapout("x1", [128, NF * TOWN], F32)
        if "h" in self.taps: tapout("h", [128, 4 * 16384], BF16)

    def declare_inline(self):
        nc = self.nc
        self.h128_t = nc.inline_tensor(_hmat(128).astype(BF), name="h128i")
        self.h64_t = nc.inline_tensor(_hmat(64).astype(BF), name="h64i")
        self.id128_t = nc.inline_tensor(np.eye(128, dtype=np.float32), name="id128i")
        self.onescol_t = nc.inline_tensor(np.ones((128, 1), dtype=np.float32).astype(BF),
                                          name="onescoli")
        self.onesrow_t = nc.inline_tensor(np.ones((1, 128), dtype=np.float32), name="onesrowi")
        m = np.zeros((128, 4, 512), dtype=np.float32)
        for k in range(4):
            for p in range(128):
                m[p, k, 128 * k + p:] = 1.0
        self.masks_t = nc.inline_tensor(m.reshape(128, 2048).astype(BF), name="masksi")

    def alloc_sbuf(self):
        nc = self.nc
        A = nc.alloc_sbuf_tensor_at
        BASE = (nc.sbuf_base + 31) & ~31
        assert BASE <= 4224, BASE
        self.B = [A(f"B{i}", [128, 16384], BF16, offset=BASE + 32768 * i) for i in range(6)]
        o = BASE + 32768 * 6

        def at(name, shape, dtype):
            nonlocal o
            t = A(name, shape, dtype, offset=o)
            o += int(np.prod(shape[1:])) * (4 if dtype == F32 else 2)
            o = (o + 31) & ~31
            return t

        self.io32 = at("io32", [128, 2048], F32)
        self.ptb = at("ptb", [128, 2048], BF16)
        self.maskb = None
        self.rows32 = at("rows32", [1, 2048], F32)
        self.silu_sb = at("silu_sb", [128, 1024], BF16)
        self.ysb = self.silu_sb
        self.h128_sb = at("h128_sb", [128, 128], BF16)
        self.h64_sb = at("h64_sb", [64, 64], BF16)
        self.id128_sb = at("id128_sb", [128, 128], F32)
        self.onescol_sb = at("onescol_sb", [128, 1], BF16)
        self.onesrow_sb = at("onesrow_sb", [1, 128], F32)
        self.cbias = at("cbias", [128, 32], F32)
        self.sucol = at("sucol", [128, 16], F32)
        self.wsgcol = at("wsgcol", [128, 64], F32)
        self.wsucol = at("wsucol", [128, 64], F32)
        self.wsdcol = at("wsdcol", [128, 16], F32)
        self.bcol = {}
        for nm in ("abo", "bbq", "bbo"):
            self.bcol[nm] = at(f"col_{nm}", [128, 16], F32)
        for nm in ("abq_my", "abk_my", "bbk_my"):
            self.bcol[nm] = at(f"col_{nm}", [128, 2], F32)
        self.bvb = at("bvb", [128, 256], BF16)
        self.r1 = at("r1", [1, 512], F32)
        self.r2 = at("r2", [1, 32], F32)
        self.qb1 = at("qb1", [1, 64], F32)
        self.sqs = self.ptb
        self.tmpcol = at("tmpcol", [128, 64], F32)
        assert o <= 229344, f"sbuf overflow {o}"
        self.sbuf_top = o

        self.rb = self.io32.ap()[:, 1536:2048]
        io_off = BASE + 32768 * 6
        self.maskb = A("maskb", [128, 2048], BF16, offset=io_off)
        self.B0f = A("B0f", [128, 8192], F32, offset=BASE)
        self.x1f = A("x1f", [128, 16384], F32, offset=BASE + 32768 * 4)
        self.gbuf = [A(f"gbuf{i}", [128, 1024], BF16, offset=io_off + 8192 + 2048 * i)
                     for i in range(2)]

        self.p01 = self.es.enter_context(nc.psum_tensor("p01", [128, 1024], F32))
        self.p23 = self.es.enter_context(nc.psum_tensor("p23", [128, 1024], F32))
        self.p45 = self.es.enter_context(nc.psum_tensor("p45", [128, 1024], F32))
        self.p67 = self.es.enter_context(nc.psum_tensor("p67", [128, 1024], F32))

    def make_sems(self):
        nc, es = self.nc, self.es
        self.dS = Ctr(es, nc, "dS")
        self.dG = Ctr(es, nc, "dG")
        self.dW = Ctr(es, nc, "dW")
        self.sP = Ctr(es, nc, "sP")
        self.sA = Ctr(es, nc, "sA")
        self.sV = Ctr(es, nc, "sV")
        self.sG = Ctr(es, nc, "sG")
        self.cc = Ctr(es, nc, "cc")
        self.u2 = [Ctr(es, nc, "u0"), Ctr(es, nc, "u1")]
        self.d2 = [Ctr(es, nc, "d0"), Ctr(es, nc, "d1")]
        self.c2 = [Ctr(es, nc, "c0"), Ctr(es, nc, "c1")]
        self.g2 = [Ctr(es, nc, "g0"), Ctr(es, nc, "g1")]

    def ph_consts(self):
        for dst, src in [
            (self.h128_sb, self.h128_t), (self.h64_sb, self.h64_t),
            (self.id128_sb, self.id128_t), (self.onescol_sb, self.onescol_t),
            (self.onesrow_sb, self.onesrow_t),
        ]:
            self.dma("sync", dst[:], src.ap())

        def vec2col(flat, nch, dst, mul=None):
            src = flat.rearrange("(c p) -> c p", p=128) if len(flat.shape) == 1 else flat
            self.qw("sync", self.sV)
            self.dma("sync", self.io32[0:nch, 0:128], src)
            self.qw("tensor", self.dS)
            self.qw("tensor", self.sV)
            self.pe(lambda e, nch=nch: e.transpose(self.p01[:, 0:nch], self.io32[0:nch, 0:128],
                                                   self.id128_sb[0:nch, 0:nch]), inc=True)
            self.qw("vector", self.sP)
            if mul is None:
                self.ve(lambda e, d=dst, nch=nch: e.tensor_copy(out=d[:, 0:nch],
                                                                in_=self.p01[:, 0:nch]), inc=True)
            else:
                self.ve(lambda e, d=dst, nch=nch, m=mul: e.tensor_scalar(
                    out=d[:, 0:nch], in0=self.p01[:, 0:nch], scalar1=float(m), scalar2=None,
                    op0=ALU.mult), inc=True)

        vec2col(self.inp["su_ug"], 16, self.sucol)
        vec2col(self.inp["wsdp"], 16, self.wsdcol, mul=R8192)
        for nm in ("abo", "bbq", "bbo"):
            vec2col(self.inp[nm], 16, self.bcol[nm])
        for nm in ("abq_my", "abk_my", "bbk_my"):
            vec2col(self.inp[nm], 2, self.bcol[nm])
        vec2col(self.inp["su_dp"], 64, self.tmpcol)
        vec2col(self.inp["wsug"][0:INTER], 64, self.wsucol, mul=R2048)
        self.qw("vector", self.sV)
        self.ve(lambda e: e.tensor_tensor(out=self.wsucol[:], in0=self.wsucol[:],
                                          in1=self.tmpcol[:], op=ALU.mult), inc=True)
        vec2col(self.inp["wsug"][INTER:2 * INTER], 64, self.wsgcol, mul=R2048)
        self.barrier()
        self.phase_done("consts")

    def _transpose_tiles_to(self, dst_is_output):
        xs_v = self.xstream.rearrange("p (c t) -> p c t", c=NF, t=TOWN)
        for tc in range(8):
            self.qw("sync", self.sV)
            if dst_is_output:
                self.dma("sync", self.io32.ap().rearrange("p (c t) -> p c t", c=NF, t=128),
                         xs_v[:, :, tc * 128:(tc + 1) * 128])
            else:
                self.dma("sync", self.io32[:], self.inp["x_own"][tc * 128:(tc + 1) * 128, :])
            wS = self.dS.v
            evs = {}
            pairs = [self.p01, self.p23, self.p45, self.p67]
            for fc in range(16):
                slot = fc % 8
                ps = pairs[slot % 4]
                col = 512 * (slot // 4)
                self.qw("tensor", self.dS, wS)
                if fc >= 8:
                    self.qw("tensor", self.sV, evs[fc - 8])
                self.pe(lambda e, fc=fc, ps=ps, col=col: e.transpose(
                    ps[:, col:col + 128], self.io32[:, fc * 128:(fc + 1) * 128],
                    self.id128_sb[:]), inc=True)
                self.qw("vector", self.sP)
                if fc == 0:
                    self.qw("vector", self.dS)
                evs[fc] = self.ve(lambda e, fc=fc, ps=ps, col=col: e.tensor_copy(
                    out=self.B0f[:, fc * 128:(fc + 1) * 128],
                    in_=ps[:, col:col + 128]), inc=True)
            self.qw("sync", self.sV)
            if dst_is_output:
                self.dma("sync", self.y_own[tc * 128:(tc + 1) * 128, :], self.B0f[:, 0:2048])
            else:
                src = self.B0f.ap()[:, 0:2048].rearrange("p (c t) -> p c t", c=NF, t=128)
                self.dma("sync", xs_v[:, :, tc * 128:(tc + 1) * 128], src)
        self.barrier()

    def ph_input(self):
        self._transpose_tiles_to(dst_is_output=False)
        self.tap("stream", self.xstream)
        self.phase_done("input")

    def ph_output(self):
        self._transpose_tiles_to(dst_is_output=True)
        for eng in ("tensor", "scalar", "vector", "gpsimd", "sync"):
            self.qw(eng, self.dS)

    def ph_prep(self):
        for dst, src in ((self.wup_bf, self.inp["fWup"]), (self.wgate_bf, self.inp["fWgate"]),
                         (self.wdown_bf, self.inp["fWdown"])):
            self.dma("gpsimd", dst, src)
        wG = self.dG.v
        self.qw("sync", self.dG, wG)

        stage = self.B[1]

        def transpose_matrix(src, dst, rows, cols):
            nslab = rows // 1024
            for fc in range(cols // 128):
                self.qw("sync", self.dS)
                for sl in range(nslab):
                    self.tdma(stage[:, sl * 1024:(sl + 1) * 1024],
                              src[sl * 1024:(sl + 1) * 1024, fc * 128:(fc + 1) * 128])
                self.qw("sync", self.dS)
                self.dma("sync", dst[fc * 128:(fc + 1) * 128, 0:rows], stage[:, 0:rows])

        transpose_matrix(self.wup_bf, self.wupT, INTER, D)
        transpose_matrix(self.wgate_bf, self.wgateT, INTER, D)
        transpose_matrix(self.wdown_bf, self.wdT, D, INTER)
        self.barrier()

        wdT_v = self.wdT.rearrange("(cc p) c -> cc p c", cc=64, p=128)
        wdH_v = self.wdH.rearrange("(cp p) c -> cp p c", cp=64, p=128)
        for p in range(128):
            self.qw("sync", self.sV)
            self.dma("sync", stage[0:64, 0:2048], wdT_v[:, p, :])
            wS = self.dS.v
            self.qw("tensor", self.dS, wS)
            self.qw("tensor", self.sV)
            for k in range(4):
                ps = (self.p01 if k < 2 else self.p23)[0:64, (k % 2) * 512:(k % 2) * 512 + 512]
                self.pe(lambda e, ps=ps, k=k: e.matmul(ps, self.h64_sb[:],
                        stage[0:64, k * 512:(k + 1) * 512], start=True, stop=True),
                        inc=(k == 3))
            self.qw("vector", self.sP)
            self.ve(lambda e: e.tensor_copy(out=stage[64:128, 0:1024], in_=self.p01[0:64, :]))
            self.ve(lambda e: e.tensor_copy(out=stage[64:128, 1024:2048], in_=self.p23[0:64, :]),
                    inc=True)
            self.qw("sync", self.sV)
            self.dma("sync", wdH_v[:, p, :], stage[64:128, 0:2048])
        self.barrier()

        for ic in range(NI):
            self.qw("sync", self.sV)
            self.dma("sync", stage[:, 0:2048], self.wdH[ic * 128:(ic + 1) * 128, :])
            wS = self.dS.v
            self.qw("tensor", self.dS, wS)
            self.qw("tensor", self.sV)
            for k in range(4):
                ps = (self.p01 if k < 2 else self.p23)[:, (k % 2) * 512:(k % 2) * 512 + 512]
                self.pe(lambda e, ps=ps, k=k: e.matmul(ps, self.h128_sb[:],
                        stage[:, k * 512:(k + 1) * 512], start=True, stop=True), inc=(k == 3))
            self.qw("vector", self.sP)
            self.ve(lambda e: e.tensor_copy(out=stage[:, 2048:3072], in_=self.p01[:]))
            self.ve(lambda e: e.tensor_copy(out=stage[:, 3072:4096], in_=self.p23[:]), inc=True)
            self.qw("sync", self.sV)
            for ct in range(16):
                self.dma("sync", self.wd2[ct, :, ic, :],
                         stage[:, 2048 + ct * 128:2048 + (ct + 1) * 128])
        self.barrier()
        self.tap("wd2", self.wd2)
        self.phase_done("prep")

    def ph_ln(self, with_su):
        xln = self.B[4]
        sums_lo, sums_hi = self.p67[0:1, 0:512], self.p67[0:1, 512:1024]
        sq_lo, sq_hi = self.p45[0:1, 0:512], self.p45[0:1, 512:1024]
        for c in range(NF):
            self.qw("sync", self.sV)
            self.dma("sync", self.io32[:, 0:1024], self.xstream[:, c * 1024:(c + 1) * 1024])
            self.qw("vector", self.dS)
            self.ve(lambda e, c=c: e.tensor_copy(out=xln[:, c * 1024:(c + 1) * 1024],
                                                 in_=self.io32[:, 0:1024]), inc=True)
            self.qw("scalar", self.sV)
            self.qw("scalar", self.sP)
            self.ae(lambda e, c=c: e.activation(out=self.sqs[:, 0:1024],
                    in_=xln[:, c * 1024:(c + 1) * 1024], func=AF.Square), inc=True)
            self.qw("tensor", self.sV)
            self.pe(lambda e, c=c: e.matmul(sums_lo, self.onescol_sb[:],
                    xln[:, c * 1024:c * 1024 + 512], start=(c == 0), stop=(c == NF - 1)))
            self.pe(lambda e, c=c: e.matmul(sums_hi, self.onescol_sb[:],
                    xln[:, c * 1024 + 512:(c + 1) * 1024], start=(c == 0), stop=(c == NF - 1)))
            self.qw("tensor", self.sA)
            self.pe(lambda e, c=c: e.matmul(sq_lo, self.onescol_sb[:], self.sqs[:, 0:512],
                    start=(c == 0), stop=(c == NF - 1)))
            self.pe(lambda e, c=c: e.matmul(sq_hi, self.onescol_sb[:], self.sqs[:, 512:1024],
                    start=(c == 0), stop=(c == NF - 1)), inc=True)
        mu = self.rows32[0:1, 0:1024]
        ex2 = self.rows32[0:1, 1024:2048]
        self.qw("vector", self.sP)
        self.ve(lambda e: e.tensor_scalar(out=mu, in0=self.p67[0:1, :], scalar1=1.0 / 2048.0,
                                          scalar2=None, op0=ALU.mult), inc=True)
        self.qw("vector", self.sV)
        self.ve(lambda e: e.tensor_scalar(out=ex2, in0=self.p45[0:1, :], scalar1=1.0 / 2048.0,
                                          scalar2=float(EPS), op0=ALU.mult, op1=ALU.add),
                inc=True)
        for k in range(2):
            self.qw("vector", self.sV)
            self.ve(lambda e, k=k: e.scalar_tensor_tensor(
                out=self.r1[0:1, 0:512], in0=mu[0:1, k * 512:(k + 1) * 512], scalar=-1.0,
                in1=mu[0:1, k * 512:(k + 1) * 512], op0=ALU.mult, op1=ALU.mult), inc=True)
            self.qw("vector", self.sV)
            self.ve(lambda e, k=k: e.tensor_tensor(
                out=ex2[0:1, k * 512:(k + 1) * 512], in0=ex2[0:1, k * 512:(k + 1) * 512],
                in1=self.r1[0:1, 0:512], op=ALU.add), inc=True)
        self.qw("scalar", self.sV)
        self.ae(lambda e: e.activation(out=ex2, in_=ex2, func=AF.Ln), inc=True)
        self.qw("scalar", self.sA)
        self.ae(lambda e: e.activation(out=ex2, in_=ex2, func=AF.Exp, scale=-0.5), inc=True)
        self.qw("tensor", self.sA)
        for k in range(2):
            self.pe(lambda e, k=k: e.matmul(self.p01[:, k * 512:(k + 1) * 512],
                    self.onesrow_sb[:], mu[0:1, k * 512:(k + 1) * 512], start=True, stop=True))
            self.pe(lambda e, k=k: e.matmul(self.p23[:, k * 512:(k + 1) * 512],
                    self.onesrow_sb[:], ex2[0:1, k * 512:(k + 1) * 512], start=True, stop=True),
                    inc=(k == 1))
        wbc = self.sP.v
        for c in range(NF):
            self.qw("vector", self.sP, wbc)
            self.qw("vector", self.sV)
            self.ve(lambda e, c=c: e.tensor_tensor(out=self.io32[:, 0:1024],
                    in0=xln[:, c * 1024:(c + 1) * 1024], in1=self.p01[:], op=ALU.subtract),
                    inc=True)
            self.qw("vector", self.sV)
            su = self.sucol[:, c:c + 1] if with_su else 1.0
            self.ve(lambda e, c=c, su=su: e.scalar_tensor_tensor(
                out=xln[:, c * 1024:(c + 1) * 1024], in0=self.io32[:, 0:1024],
                scalar=su, in1=self.p23[:], op0=ALU.mult, op1=ALU.mult), inc=True)
        self.barrier()

    def ph_ffn(self):
        self.ph_ln(with_su=True)
        self.tap_sbuf("xln", self.B[4][:])
        xln = self.B[4]
        mm_done = {}
        ev_done = {}

        def maybe_evac(cur_c):
            for k in range(NF - 1, -1, -1):
                if k in mm_done and k not in ev_done and 2 * k >= cur_c:
                    psk = mm_done[k][1]
                    self.qw("vector", self.sP, mm_done[k][0])
                    ev_done[k] = self.ve(lambda e, k=k, psk=psk: e.tensor_copy(
                        out=self.x1f[:, k * 1024:(k + 1) * 1024], in_=psk[:]), inc=True)

        for idx, c in enumerate(range(NF - 1, -1, -1)):
            ps = self.p01 if (idx % 2 == 0) else self.p23
            if idx >= 2:
                prev_k = NF - 1 - (idx - 2)
                self.qw("tensor", self.sV, ev_done[prev_k])
            self.pe(lambda e, c=c, ps=ps: e.matmul(ps[:, 0:512], self.h128_sb[:],
                    xln[:, c * 1024:c * 1024 + 512], start=True, stop=True))
            v = self.pe(lambda e, c=c, ps=ps: e.matmul(ps[:, 512:1024], self.h128_sb[:],
                    xln[:, c * 1024 + 512:(c + 1) * 1024], start=True, stop=True), inc=True)
            mm_done[c] = (v, ps)
            maybe_evac(c)
        maybe_evac(-100)
        for s in (1, 2, 4, 8):
            for a0 in range(NF):
                if a0 & s:
                    continue
                b0 = a0 + s
                self.qw("vector", self.sV)
                self.ve(lambda e, a0=a0: e.tensor_copy(out=self.io32[:, 0:1024],
                        in_=self.x1f[:, a0 * 1024:(a0 + 1) * 1024]), inc=True)
                self.qw("vector", self.sV)
                self.ve(lambda e, a0=a0, b0=b0: e.tensor_tensor(
                    out=self.x1f[:, a0 * 1024:(a0 + 1) * 1024],
                    in0=self.x1f[:, a0 * 1024:(a0 + 1) * 1024],
                    in1=self.x1f[:, b0 * 1024:(b0 + 1) * 1024], op=ALU.add), inc=True)
                self.qw("vector", self.sV)
                self.ve(lambda e, a0=a0, b0=b0: e.tensor_tensor(
                    out=self.x1f[:, b0 * 1024:(b0 + 1) * 1024], in0=self.io32[:, 0:1024],
                    in1=self.x1f[:, b0 * 1024:(b0 + 1) * 1024], op=ALU.subtract), inc=True)
        self.tap_sbuf("x1", self.x1f[:])
        x1b = self.B[4]
        self.qw("vector", self.sV)
        self.ve(lambda e: e.tensor_copy(out=self.ptb[:, 0:1024], in_=self.x1f[:, 0:1024]),
                inc=True)
        for c in range(1, NF):
            self.qw("vector", self.sV)
            self.ve(lambda e, c=c: e.tensor_copy(out=x1b[:, c * 1024:(c + 1) * 1024],
                    in_=self.x1f[:, c * 1024:(c + 1) * 1024]), inc=True)
        self.qw("vector", self.sV)
        self.ve(lambda e: e.tensor_copy(out=x1b[:, 0:1024], in_=self.ptb[:, 0:1024]), inc=True)
        self.barrier()

        upbuf = [self.B[5].ap()[:, 0:4096], self.B[5].ap()[:, 4096:8192]]
        gabuf = [self.B[5].ap()[:, 8192:12288], self.B[5].ap()[:, 12288:16384]]
        wupT_v = self.wupT.rearrange("(f p) i -> p f i", f=16, p=128)
        wgaT_v = self.wgateT.rearrange("(f p) i -> p f i", f=16, p=128)

        def h_ap(ic):
            return self.B[ic // 16][:, (ic % 16) * 1024:(ic % 16) * 1024 + 1024]

        nslab = NI // 2
        mm_after_slab = {}
        slab_w = {}
        hdone = {}
        for sl in range(nslab):
            if sl >= 2:
                self.qw("sync", self.sP, mm_after_slab[sl - 2])
            usem = self.u2[sl % 2]
            for half in range(2):
                self.dma("sync", upbuf[sl % 2][:, half * 2048:(half + 1) * 2048].rearrange(
                    "p (f i) -> p f i", f=8, i=256),
                    wupT_v[:, half * 8:(half + 1) * 8, sl * 256:(sl + 1) * 256], ctr=usem)
                self.dma("sync", gabuf[sl % 2][:, half * 2048:(half + 1) * 2048].rearrange(
                    "p (f i) -> p f i", f=8, i=256),
                    wgaT_v[:, half * 8:(half + 1) * 8, sl * 256:(sl + 1) * 256], ctr=usem)
            slab_w[sl] = usem.v
            for j in range(2):
                ic = sl * 2 + j
                pu = self.p01 if (ic % 2 == 0) else self.p45
                pg = self.p23 if (ic % 2 == 0) else self.p67
                self.qw("tensor", self.u2[sl % 2], slab_w[sl])
                if ic >= 2:
                    self.qw("tensor", self.sV, hdone[ic - 2][0])
                    self.qw("tensor", self.sA, hdone[ic - 2][1])
                for f in range(16):
                    off = (f // 8) * 2048 + (f % 8) * 512 + j * 256
                    lhs_u = upbuf[sl % 2][:, off:off + 128] if False else None
                    hh = f // 8
                    base = hh * 2048 + (f % 8) * 256 + j * 128
                    lu = upbuf[sl % 2][:, base:base + 128]
                    lg = gabuf[sl % 2][:, base:base + 128]
                    self.pe(lambda e, f=f, pu=pu, lu=lu: e.matmul(
                        pu[:, 0:512], lu, x1b[:, f * 1024:f * 1024 + 512],
                        start=(f == 0), stop=(f == 15)))
                    self.pe(lambda e, f=f, pu=pu, lu=lu: e.matmul(
                        pu[:, 512:1024], lu, x1b[:, f * 1024 + 512:(f + 1) * 1024],
                        start=(f == 0), stop=(f == 15)))
                    self.pe(lambda e, f=f, pg=pg, lg=lg: e.matmul(
                        pg[:, 0:512], lg, x1b[:, f * 1024:f * 1024 + 512],
                        start=(f == 0), stop=(f == 15)))
                    self.pe(lambda e, f=f, pg=pg, lg=lg: e.matmul(
                        pg[:, 512:1024], lg, x1b[:, f * 1024 + 512:(f + 1) * 1024],
                        start=(f == 0), stop=(f == 15)), inc=(f == 15))
                wmm = self.sP.v
                mm_after_slab[sl] = wmm
                self.qw("scalar", self.sP, wmm)
                if ic >= 1:
                    self.qw("scalar", self.sV, hdone[ic - 1][0])
                vA = self.ae(lambda e, ic=ic, pg=pg: e.activation(
                    out=self.silu_sb[:], in_=pg[:], func=AF.Silu,
                    scale=self.wsgcol[:, ic:ic + 1]), inc=True)
                self.qw("vector", self.sA, vA)
                vV = self.ve(lambda e, ic=ic, pu=pu: e.scalar_tensor_tensor(
                    out=h_ap(ic), in0=pu[:], scalar=self.wsucol[:, ic:ic + 1],
                    in1=self.silu_sb[:], op0=ALU.mult, op1=ALU.mult), inc=True)
                hdone[ic] = (vV, vA)
        self.barrier()
        if "h" in self.dbg:
            self.dma("sync", self.dbg["h"].rearrange("p (r n) -> r p n", r=4, n=16384)[0], self.B[0][:])
            self.dma("sync", self.dbg["h"].rearrange("p (r n) -> r p n", r=4, n=16384)[1], self.B[1][:])
            self.dma("sync", self.dbg["h"].rearrange("p (r n) -> r p n", r=4, n=16384)[2], self.B[2][:])
            self.dma("sync", self.dbg["h"].rearrange("p (r n) -> r p n", r=4, n=16384)[3], self.B[3][:])
            self.qw("sync", self.dS)

        wdb = [self.B[5].ap()[:, 0:8192], self.B[5].ap()[:, 8192:16384]]
        dn_mm = {}
        for ct in range(NF):
            if ct >= 2:
                self.qw("sync", self.sP, dn_mm[ct - 2])
            self.dma("sync", wdb[ct % 2], self.wd2[ct].rearrange("p ic c -> p (ic c)"),
                     ctr=self.d2[ct % 2])
            wW = self.d2[ct % 2].v
            ps = self.p01 if (ct % 2 == 0) else self.p23
            self.qw("tensor", self.d2[ct % 2], wW)
            self.qw("tensor", self.sV)
            for ic in range(NI):
                self.pe(lambda e, ct=ct, ic=ic, ps=ps: e.matmul(
                    ps[:, 0:512], wdb[ct % 2][:, ic * 128:(ic + 1) * 128],
                    h_ap(ic)[:, 0:512], start=(ic == 0), stop=(ic == NI - 1)))
                self.pe(lambda e, ct=ct, ic=ic, ps=ps: e.matmul(
                    ps[:, 512:1024], wdb[ct % 2][:, ic * 128:(ic + 1) * 128],
                    h_ap(ic)[:, 512:1024], start=(ic == 0), stop=(ic == NI - 1)),
                    inc=(ic == NI - 1))
            wmm = self.sP.v
            dn_mm[ct] = wmm
            self.qw("sync", self.sV)
            self.dma("sync", self.io32[:, 0:1024], self.xstream[:, ct * 1024:(ct + 1) * 1024])
            wS = self.dS.v
            self.qw("vector", self.sP, wmm)
            self.qw("vector", self.dS, wS)
            self.ve(lambda e, ct=ct, ps=ps: e.scalar_tensor_tensor(
                out=self.io32[:, 1024:2048], in0=ps[:], scalar=self.wsdcol[:, ct:ct + 1],
                in1=self.io32[:, 0:1024], op0=ALU.mult, op1=ALU.add), inc=True)
            self.qw("sync", self.sV)
            self.dma("sync", self.xstream[:, ct * 1024:(ct + 1) * 1024], self.io32[:, 1024:2048])
        self.barrier()
        self.tap("stream", self.xstream)

    def _colslab_proj(self, wsrc, bias_col, out_cb, rhs, nk=NF, wcols=None):
        wcb = [self.B[5].ap()[:, 0:2048], self.B[5].ap()[:, 2048:4096]]
        mmv = {}
        for ct in range(NF):
            if ct >= 2:
                self.qw("gpsimd", self.sP, mmv[ct - 2])
            src = wsrc.rearrange("(k p) (ct c) -> p ct k c", p=128, k=nk, ct=NF, c=128)
            self.dma("gpsimd", wcb[ct % 2].rearrange("p (k c) -> p k c", k=nk, c=128),
                     src[:, ct, :, :], ctr=self.c2[ct % 2])
            wW = self.c2[ct % 2].v
            ps = self.p01 if (ct % 2 == 0) else self.p23
            self.qw("tensor", self.c2[ct % 2], wW)
            self.qw("tensor", self.sV)
            for k in range(nk):
                self.pe(lambda e, k=k, ct=ct, ps=ps: e.matmul(
                    ps[:, 0:512], wcb[ct % 2][:, k * 128:(k + 1) * 128], rhs(k, 0),
                    start=(k == 0), stop=(k == nk - 1)))
                self.pe(lambda e, k=k, ct=ct, ps=ps: e.matmul(
                    ps[:, 512:1024], wcb[ct % 2][:, k * 128:(k + 1) * 128], rhs(k, 1),
                    start=(k == 0), stop=(k == nk - 1)), inc=(k == nk - 1))
            wmm = self.sP.v
            mmv[ct] = wmm
            out_cb(ct, ps, wmm)

    def ph_attn(self, variant):
        a = variant == "a"
        self.ph_ln(with_su=False)
        self.tap_sbuf("xln", self.B[4][:])
        xln = self.B[4]
        agin_v = self.ag_in.rearrange("(c p) t -> p c t", c=NF, p=128)
        self.dma("gpsimd", agin_v, xln.ap().rearrange("p (c t) -> p c t", c=NF, t=TOWN))
        self.qw("gpsimd", self.dG)
        hcc = self.cc.h
        self.cc.v += 1
        rg = [list(range(NCORE))]
        self.q("gpsimd", lambda e: e.collective_compute(
            "AllGather", ALU.bypass, replica_groups=rg,
            ins=[self.ag_in.opt()], outs=[self.ag_out.opt()]).then_inc(hcc, 1))
        wcc = self.cc.v

        KT, VT, QT, QFL = self.B[0], self.B[1], self.B[2], self.B[3]
        pfx = "a" if a else "b"

        if not a:
            def qfl_out(ct, ps, wmm):
                self.qw("vector", self.sP, wmm)
                self.ve(lambda e, ct=ct, ps=ps: e.tensor_scalar(
                    out=QFL[:, ct * 1024:(ct + 1) * 1024], in0=ps[:],
                    scalar1=self.bcol["bbq"][:, ct:ct + 1], scalar2=None, op0=ALU.add),
                    inc=True)

            self._colslab_proj(self.inp["bWq"], None, qfl_out,
                               lambda f, hh: xln[:, f * 1024 + hh * 512:f * 1024 + hh * 512 + 512])
            self.qw("vector", self.sV)
            qfl_v = QFL.ap().rearrange("p (j u) -> p j u", j=16, u=1024)
            for hl in range(2):
                src = qfl_v[:, :, hl * 512:(hl + 1) * 512].rearrange("p j v -> p v j")
                dst = QT.ap()[:, hl * 8192:(hl + 1) * 8192].rearrange(
                    "p (v j) -> p v j", v=512, j=16)
                self.ve(lambda e, src=src, dst=dst: e.tensor_copy(out=dst, in_=src), inc=True)

        wk_sl = self.B[5].ap()[:, 4096:8192]
        wv_sl = self.B[5].ap()[:, 8192:12288]
        wq_sl = self.B[5].ap()[:, 12288:16384]
        mats = [("k", wk_sl), ("v", wv_sl)] + ([("q", wq_sl)] if a else [])
        for nm, dstb in mats:
            for f in range(NF):
                for hl in range(2):
                    if a:
                        wsrc = self.inp["aW" + nm + "_my"][hl, f * 128:(f + 1) * 128, :]
                    else:
                        wsrc = self.inp["bW" + nm + "_my"][f * 128:(f + 1) * 128,
                                                           hl * 128:(hl + 1) * 128]
                    self.dma("gpsimd", dstb[:, f * 256 + hl * 128:f * 256 + hl * 128 + 128],
                             wsrc, ctr=self.dW)
        wslices = self.dW.v

        if a:
            bsrc = self.inp["abv_my"].rearrange("h d -> (h d)")
        else:
            bsrc = self.inp["bbv_my"]
        self.dma("sync", self.r1[0:1, 0:256], bsrc.rearrange("(o n) -> o n", o=1))
        wb = self.dS.v
        self.qw("tensor", self.dS, wb)
        self.qw("tensor", self.sV)
        self.pe(lambda e: e.matmul(self.p45[:, 0:256], self.onesrow_sb[:], self.r1[0:1, 0:256],
                                   start=True, stop=True), inc=True)
        self.qw("vector", self.sP)
        self.ve(lambda e: e.tensor_copy(out=self.bvb[:], in_=self.p45[:, 0:256]), inc=True)

        self.qw("sync", self.cc, wcc)
        self.qw("tensor", self.cc, wcc)
        self.qw("tensor", self.dW, wslices)

        G_v = self.ag_out.rearrange("(r c p) t -> r c p t", r=NCORE, c=NF, p=128)

        def gload(r, f, par):
            self.dma("sync", self.gbuf[par][:], G_v[r, f], ctr=self.g2[par])
            return self.g2[par].v

        bk = self.bcol["abk_my" if a else "bbk_my"]
        bq = self.bcol["abq_my"] if a else None

        def kq_pass(wsl, dstT, bias_col):
            gmm = []
            for r in range(NCORE):
                for f in range(NF):
                    par = f % 2
                    n = r * NF + f
                    if n >= 2:
                        self.qw("sync", self.sP, gmm[n - 2])
                    wS = gload(r, f, par)
                    self.qw("tensor", self.g2[par], wS)
                    if f == 0:
                        self.qw("tensor", self.sV)
                    for hl in range(2):
                        ps = self.p01 if hl == 0 else self.p23
                        for t2 in range(2):
                            self.pe(lambda e, f=f, hl=hl, t2=t2, ps=ps, par=par, wsl=wsl:
                                    e.matmul(ps[:, t2 * 512:(t2 + 1) * 512],
                                             wsl[:, f * 256 + hl * 128:f * 256 + hl * 128 + 128],
                                             self.gbuf[par][:, t2 * 512:(t2 + 1) * 512],
                                             start=(f == 0), stop=(f == NF - 1)),
                                    inc=(hl == 1 and t2 == 1))
                    gmm.append(self.sP.v)
                self.qw("vector", self.sP, gmm[-1])
                for hl in range(2):
                    ps = self.p01 if hl == 0 else self.p23
                    bcolv = bias_col[:, hl:hl + 1]
                    for t2 in range(2):
                        self.ve(lambda e, r=r, hl=hl, t2=t2, ps=ps, bcolv=bcolv: e.tensor_scalar(
                            out=dstT[:, hl * 8192 + r * 1024 + t2 * 512:
                                     hl * 8192 + r * 1024 + t2 * 512 + 512],
                            in0=ps[:, t2 * 512:(t2 + 1) * 512], scalar1=bcolv, scalar2=None,
                            op0=ALU.add), inc=(hl == 1 and t2 == 1))

        kq_pass(wk_sl, KT, bk)
        self.qw("sync", self.sP)
        if a:
            kq_pass(wq_sl, QT, bq)
            self.qw("sync", self.sP)

        gmmv = []
        for r in range(NCORE):
            vmm = 0
            for f in range(NF):
                par = f % 2
                n = r * NF + f
                if n >= 2:
                    self.qw("sync", self.sP, gmmv[n - 2])
                wS = gload(r, f, par)
                self.qw("tensor", self.g2[par], wS)
                if f == 0:
                    self.qw("tensor", self.sV)
                for tc in range(8):
                    ps = (self.p01 if tc < 4 else self.p23)
                    ofs = (tc % 4) * 256
                    self.pe(lambda e, f=f, tc=tc, ps=ps, ofs=ofs, par=par: e.matmul(
                        ps[:, ofs:ofs + 256], self.gbuf[par][:, tc * 128:(tc + 1) * 128],
                        wv_sl[:, f * 256:(f + 1) * 256],
                        start=(f == 0 and tc % 2 == 0),
                        stop=(f == NF - 1 and tc % 2 == 1)), inc=(tc == 7))
                vmm = self.sP.v
                gmmv.append(vmm)
            self.qw("vector", self.sP, vmm)
            for tc in range(8):
                ps = (self.p01 if tc < 4 else self.p23)
                ofs = (tc % 4) * 256
                self.ve(lambda e, r=r, tc=tc, ps=ps, ofs=ofs: e.tensor_tensor(
                    out=VT[:, (r * 8 + tc) * 256:(r * 8 + tc) * 256 + 256],
                    in0=ps[:, ofs:ofs + 256], in1=self.bvb[:], op=ALU.add), inc=(tc == 7))
        self.barrier()
        if "K" in self.dbg:
            self.dma("sync", self.dbg["K"], KT[:])
            self.dma("sync", self.dbg["V"], VT[:])
            self.dma("sync", self.dbg["Q"], QT[:])
            self.qw("sync", self.dS)
        if "G" in self.dbg:
            self.tap("G", self.ag_out)
        self.phase_done("proj_" + pfx)

        for hl in range(2):
            for blk in range(16):
                self.qw("scalar", self.sP)
                self.ae(lambda e, hl=hl, blk=blk: e.activation(
                    out=self.sqs[:, 0:512], in_=KT[:, hl * 8192 + blk * 512:hl * 8192 + blk * 512 + 512],
                    func=AF.Square), inc=True)
                self.qw("tensor", self.sA)
                self.qw("tensor", self.sV)
                self.pe(lambda e: e.matmul(self.p45[0:1, 0:512], self.onescol_sb[:],
                        self.sqs[:, 0:512], start=True, stop=True), inc=True)
                self.qw("vector", self.sP)
                self.ve(lambda e, blk=blk: e.tensor_reduce(
                    out=self.r2[0:1, blk:blk + 1], in_=self.p45[0:1, 0:512], axis=AX.X,
                    op=ALU.max), inc=True)
            self.qw("vector", self.sV)
            self.ve(lambda e: e.tensor_reduce(out=self.r2[0:1, 16:17], in_=self.r2[0:1, 0:16],
                                              axis=AX.X, op=ALU.max), inc=True)
            for blk in range(16):
                self.qw("scalar", self.sP)
                self.ae(lambda e, hl=hl, blk=blk: e.activation(
                    out=self.sqs[:, 0:512], in_=QT[:, hl * 8192 + blk * 512:hl * 8192 + blk * 512 + 512],
                    func=AF.Square), inc=True)
                self.qw("tensor", self.sA)
                self.qw("tensor", self.sV)
                self.pe(lambda e: e.matmul(self.p45[0:1, 0:512], self.onescol_sb[:],
                        self.sqs[:, 0:512], start=True, stop=True), inc=True)
                self.qw("vector", self.sP)
                self.ve(lambda e, hl=hl, blk=blk: e.tensor_reduce(
                    out=self.qb1[0:1, hl * 16 + blk:hl * 16 + blk + 1],
                    in_=self.p45[0:1, 0:512], axis=AX.X, op=ALU.max), inc=True)
            self.qw("vector", self.sV)
            self.ve(lambda e, hl=hl: e.tensor_scalar(
                out=self.r1[0:1, 0:16], in0=self.qb1[0:1, hl * 16:(hl + 1) * 16],
                scalar1=self.r2[0:1, 16:17], scalar2=float((ATT_SCALE * 1.002) ** 2),
                op0=ALU.mult, op1=ALU.mult), inc=True)
            self.qw("scalar", self.sV)
            self.ae(lambda e: e.activation(out=self.r1[0:1, 0:16], in_=self.r1[0:1, 0:16],
                                           func=AF.Ln), inc=True)
            self.qw("scalar", self.sA)
            self.ae(lambda e: e.activation(out=self.r1[0:1, 0:16], in_=self.r1[0:1, 0:16],
                    func=AF.Exp, scale=0.5), inc=True)
            self.qw("vector", self.sA)
            self.ve(lambda e: e.tensor_scalar(out=self.r1[0:1, 16:32], in0=self.r1[0:1, 0:16],
                    scalar1=-1.0, scalar2=None, op0=ALU.mult), inc=True)
            self.qw("tensor", self.sV)
            self.pe(lambda e: e.matmul(self.p45[:, 0:16], self.onesrow_sb[:],
                    self.r1[0:1, 16:32], start=True, stop=True), inc=True)
            self.qw("vector", self.sP)
            self.ve(lambda e, hl=hl: e.tensor_copy(out=self.cbias[:, hl * 16:(hl + 1) * 16],
                    in_=self.p45[:, 0:16]), inc=True)
        self.barrier()

        self.dma("sync", self.maskb[:], self.masks_t.ap())
        self.qw("sync", self.dS)
        for eng in ("tensor", "scalar", "vector"):
            self.qw(eng, self.dS)
        outT = self.p45[:, 0:512]
        sums = self.p45[0:1, 512:1024]
        for hl in range(2):
            for tb in range(NF):
                nsc = 4 * (tb + 1)
                npair = nsc // 2
                qrhs0 = QT[:, hl * 8192 + tb * 512:hl * 8192 + tb * 512 + 512]
                exp_done = {}
                cons_done = {}
                for pr in range(npair):
                    ps = self.p01 if (pr % 2 == 0) else self.p23
                    par = pr % 2
                    if pr >= 2:
                        self.qw("tensor", self.sA, exp_done[pr - 2])
                    else:
                        self.qw("tensor", self.sA)
                    for k in range(2):
                        sc = 2 * pr + k
                        self.pe(lambda e, sc=sc, k=k, ps=ps, hl=hl, qrhs0=qrhs0: e.matmul(
                            ps[:, k * 512:(k + 1) * 512],
                            KT[:, hl * 8192 + sc * 128:hl * 8192 + sc * 128 + 128],
                            qrhs0, start=True, stop=True), inc=(k == 1))
                    wS = self.sP.v
                    self.qw("scalar", self.sP, wS)
                    if pr >= 2:
                        self.qw("scalar", self.sP, cons_done[pr - 2])
                    exp_done[pr] = self.ae(lambda e, ps=ps, par=par, hl=hl, tb=tb: e.activation(
                        out=self.ptb[:, par * 1024:(par + 1) * 1024], in_=ps[:], func=AF.Exp,
                        scale=ATT_SCALE, bias=self.cbias[:, hl * 16 + tb:hl * 16 + tb + 1]),
                        inc=True)
                    wmask = None
                    for k in range(2):
                        sc = 2 * pr + k
                        if sc >= 4 * tb:
                            kk = sc - 4 * tb
                            self.qw("vector", self.sA, exp_done[pr])
                            wmask = self.ve(lambda e, k=k, par=par, kk=kk: e.tensor_tensor(
                                out=self.ptb[:, par * 1024 + k * 512:par * 1024 + (k + 1) * 512],
                                in0=self.ptb[:, par * 1024 + k * 512:par * 1024 + (k + 1) * 512],
                                in1=self.maskb[:, kk * 512:(kk + 1) * 512], op=ALU.mult),
                                inc=True)
                    if wmask is not None:
                        self.qw("tensor", self.sV, wmask)
                    else:
                        self.qw("tensor", self.sA, exp_done[pr])
                    if pr == 0:
                        self.qw("tensor", self.sA)
                        self.qw("tensor", self.sV)
                    for k in range(2):
                        sc = 2 * pr + k
                        st = (sc == 0)
                        sp = (sc == nsc - 1)
                        self.pe(lambda e, k=k, par=par, st=st, sp=sp: e.matmul(
                            sums, self.onescol_sb[:],
                            self.ptb[:, par * 1024 + k * 512:par * 1024 + (k + 1) * 512],
                            start=st, stop=sp))
                        self.pe(lambda e, k=k, par=par, sc=sc, hl=hl, st=st, sp=sp: e.matmul(
                            outT, VT[:, sc * 256 + hl * 128:sc * 256 + hl * 128 + 128],
                            self.ptb[:, par * 1024 + k * 512:par * 1024 + (k + 1) * 512],
                            start=st, stop=sp),
                            inc=(k == 1))
                    cons_done[pr] = self.sP.v
                wcons = cons_done[npair - 1]
                self.qw("scalar", self.sP, wcons)
                self.ae(lambda e: e.activation(out=self.r1[0:1, 0:512], in_=sums, func=AF.Ln),
                        inc=True)
                self.qw("scalar", self.sA)
                wrec = self.ae(lambda e: e.activation(out=self.r1[0:1, 0:512],
                               in_=self.r1[0:1, 0:512], func=AF.Exp, scale=-1.0), inc=True)
                self.qw("tensor", self.sA, wrec)
                self.qw("tensor", self.sV)
                wbc = self.pe(lambda e: e.matmul(self.p67[:, 0:512], self.onesrow_sb[:],
                              self.r1[0:1, 0:512], start=True, stop=True), inc=True)
                self.qw("vector", self.sP, wbc)
                wrb = self.ve(lambda e: e.tensor_copy(out=self.rb, in_=self.p67[:, 0:512]),
                              inc=True)
                ypar = (hl * NF + tb) % 2
                self.qw("vector", self.sV, wrb)
                self.qw("vector", self.dS)
                wye = self.ve(lambda e, ypar=ypar: e.scalar_tensor_tensor(
                    out=self.ysb[:, ypar * 512:(ypar + 1) * 512], in0=outT, scalar=1.0,
                    in1=self.rb, op0=ALU.mult, op1=ALU.mult), inc=True)
                self.qw("sync", self.sV, wye)
                self.dma("sync", self.y_in[hl * 128:(hl + 1) * 128, tb * 512:(tb + 1) * 512],
                         self.ysb[:, ypar * 512:(ypar + 1) * 512])
        self.barrier()
        self.phase_done("scores_" + pfx)

        hcc = self.cc.h
        self.cc.v += 1
        self.q("gpsimd", lambda e: e.collective_compute(
            "AllGather", ALU.bypass, replica_groups=rg,
            ins=[self.y_in.opt()], outs=[self.y_out.opt()]).then_inc(hcc, 1))
        wcc2 = self.cc.v
        self.qw("sync", self.cc, wcc2)
        if "yout" in self.dbg:
            self.dma("sync", self.dbg["yout"], self.y_out)
            self.qw("sync", self.dS)

        ysall = self.B[3]
        if not hasattr(self, "_cid"):
            self._cid = self.nc.partition_id(engines=[mybir.EngineType.SP])
        cid = self._cid
        for k in range(16):
            self.dma("sync", ysall[:, k * 1024:(k + 1) * 1024],
                     self.y_out[k * 128:(k + 1) * 128, bass.ts(cid, TOWN)])
        wY = self.dS.v
        self.qw("tensor", self.dS, wY)
        self.qw("gpsimd", self.dS, wY)
        self.qw("sync", self.dS, wY)
        bo = self.bcol["abo" if a else "bbo"]

        def wo_out(ct, ps, wmm):
            self.qw("sync", self.sV)
            self.dma("sync", self.io32[:, 0:1024], self.xstream[:, ct * 1024:(ct + 1) * 1024])
            wS = self.dS.v
            self.qw("vector", self.sP, wmm)
            self.qw("vector", self.dS, wS)
            self.ve(lambda e, ct=ct, ps=ps: e.scalar_tensor_tensor(
                out=self.io32[:, 1024:2048], in0=ps[:], scalar=bo[:, ct:ct + 1],
                in1=self.io32[:, 0:1024], op0=ALU.add, op1=ALU.add), inc=True)
            self.qw("sync", self.sV)
            self.dma("sync", self.xstream[:, ct * 1024:(ct + 1) * 1024],
                     self.io32[:, 1024:2048])

        self._colslab_proj(self.inp["aWo" if a else "bWo"], None, wo_out,
                           lambda k, hh: ysall[:, k * 1024 + hh * 512:k * 1024 + hh * 512 + 512])
        self.barrier()
        self.tap("stream", self.xstream)
        self.phase_done("attn_" + pfx)

    def build(self, phases):
        self.declare_dram()
        self.declare_inline()
        self.alloc_sbuf()
        self.make_sems()
        for ph in phases:
            if self.stopped:
                break
            ph()
        with self.es:
            block = self.es.enter_context(self.nc.Block())
            for eng in ("tensor", "scalar", "vector", "gpsimd", "sync"):
                getattr(block, eng)(lambda e, eng=eng: [fn(e) for fn in self.ops[eng]])
        return self.nc


def build_core(core_id, taps=(), upto=None, which="full"):
    b = Builder(core_id, taps=taps, upto=upto)
    if which == "ffn_only":
        phases = [b.ph_consts, b.ph_input, b.ph_prep, b.ph_ffn, b.ph_output]
    elif which == "attn_a_only":
        phases = [b.ph_consts, b.ph_input, b.ph_prep, lambda: b.ph_attn("a"), b.ph_output]
    elif which == "attn_b_only":
        phases = [b.ph_consts, b.ph_input, b.ph_prep, lambda: b.ph_attn("b"), b.ph_output]
    else:
        phases = [b.ph_consts, b.ph_input, b.ph_prep,
                  lambda: b.ph_attn("a"), b.ph_ffn,
                  lambda: b.ph_attn("b"), b.ph_ffn, b.ph_output]
    return b.build(phases)


def make_in_map(inputs, core_id):
    x = np.asarray(inputs["L0_attn_a_x"]).reshape(T, D)
    h0 = 2 * core_id
    aWq = np.asarray(inputs["L0_attn_a_Wq"])
    aWk = np.asarray(inputs["L0_attn_a_Wk"])
    aWv = np.asarray(inputs["L0_attn_a_Wv"])
    bWk = np.asarray(inputs["L0_attn_b_Wk"])
    bWv = np.asarray(inputs["L0_attn_b_Wv"])
    m = {
        "x_own": np.ascontiguousarray(x[core_id * TOWN:(core_id + 1) * TOWN]),
        "aWq_my": aWq[h0:h0 + 2], "aWk_my": aWk[h0:h0 + 2], "aWv_my": aWv[h0:h0 + 2],
        "abq_my": np.asarray(inputs["L0_attn_a_bq"])[h0:h0 + 2],
        "abk_my": np.asarray(inputs["L0_attn_a_bk"])[h0:h0 + 2],
        "abv_my": np.asarray(inputs["L0_attn_a_bv"])[h0:h0 + 2],
        "aWo": inputs["L0_attn_a_Wo"], "abo": inputs["L0_attn_a_bo"],
        "bWq": inputs["L0_attn_b_Wq"],
        "bWk_my": bWk[:, h0 * 128:(h0 + 2) * 128],
        "bWv_my": bWv[:, h0 * 128:(h0 + 2) * 128],
        "bWo": inputs["L0_attn_b_Wo"],
        "bbq": inputs["L0_attn_b_bq"],
        "bbk_my": np.asarray(inputs["L0_attn_b_bk"])[h0 * 128:(h0 + 2) * 128],
        "bbv_my": np.asarray(inputs["L0_attn_b_bv"])[h0 * 128:(h0 + 2) * 128],
        "bbo": inputs["L0_attn_b_bo"],
        "su_ug": inputs["L0_ffn_SU_ug"], "su_dp": inputs["L0_ffn_SU_dp"],
        "wsug": inputs["L0_ffn_Wscale_ug"], "wsdp": inputs["L0_ffn_Wscale_dp"],
        "fWup": inputs["L0_ffn_W_up"], "fWgate": inputs["L0_ffn_W_gate"],
        "fWdown": inputs["L0_ffn_W_down"],
    }
    return {k: np.ascontiguousarray(np.asarray(v, dtype=np.float32)) for k, v in m.items()}


def kernel(**inputs):
    nc = build_core(0)
    in_maps = [make_in_map(inputs, c) for c in range(NCORE)]
    res = run_bass_kernel_spmd(nc, in_maps, core_ids=list(range(NCORE)))
    out = np.concatenate([np.asarray(res.results[c]["y_own"]) for c in range(NCORE)], axis=0)
    return out.reshape(1, T, D).astype(np.float32)


# version 02
# speedup vs baseline: 1.2044x; 1.2044x over previous
import math
import contextlib
import numpy as np
import ml_dtypes

import concourse.bass as bass
import concourse.mybir as mybir
from concourse.bass_utils import run_bass_kernel_spmd

F32 = mybir.dt.float32
BF16 = mybir.dt.bfloat16
AF = mybir.ActivationFunctionType
ALU = mybir.AluOpType
AX = mybir.AxisListType

NCORE = 8
T = 8192
D = 2048
TOWN = 1024
NF = 16
NI = 64
INTER = 8192
HPC = 2
DH = 128
ATT_SCALE = 1.0 / math.sqrt(128.0)
EPS = 1e-5
R2048 = 1.0 / math.sqrt(2048.0)
R8192 = 1.0 / math.sqrt(8192.0)
BF = ml_dtypes.bfloat16


def _hmat(n):
    a = np.arange(n)
    bits = a[:, None] & a[None, :]
    pc = np.zeros((n, n), dtype=np.int64)
    x = bits.copy()
    while x.any():
        pc += x & 1
        x >>= 1
    return np.where(pc % 2 == 0, 1.0, -1.0).astype(np.float32)


class Ctr:
    def __init__(self, es, nc, name):
        self.h = es.enter_context(nc.semaphore(name))
        self.v = 0


class Builder:

    def __init__(self, core_id, taps=(), upto=None):
        self.core_id = core_id
        self.taps = set(taps)
        self.upto = upto
        self.stopped = False
        self.nc = bass.Bass("TRN2", target_bir_lowering=False,
                    dynamic_dma_scratch_size=4096)
        self.es = contextlib.ExitStack()
        self.ops = {k: [] for k in ("tensor", "scalar", "vector", "gpsimd", "sync")}
        self.H0 = 2 * core_id

    def q(self, eng, fn):
        if not self.stopped:
            self.ops[eng].append(fn)

    def qw(self, eng, ctr, val=None):
        v = ctr.v if val is None else val
        if v <= 0:
            return
        h = ctr.h
        self.q(eng, lambda e: e.wait_ge(h, v))

    def phase_done(self, label):
        if self.upto is not None and label == self.upto:
            self.stopped = True

    def pe(self, fn, inc=False):
        if inc:
            h = self.sP.h
            self.sP.v += 1
            self.q("tensor", lambda e: fn(e).then_inc(h, 1))
        else:
            self.q("tensor", fn)
        return self.sP.v

    def ve(self, fn, inc=False):
        if inc:
            h = self.sV.h
            self.sV.v += 1
            self.q("vector", lambda e: fn(e).then_inc(h, 1))
        else:
            self.q("vector", fn)
        return self.sV.v

    def ae(self, fn, inc=False):
        if inc:
            h = self.sA.h
            self.sA.v += 1
            self.q("scalar", lambda e: fn(e).then_inc(h, 1))
        else:
            self.q("scalar", fn)
        return self.sA.v

    def dma(self, eng, out, in_, ctr=None, **kw):
        ctr = ctr if ctr is not None else (self.dG if eng == "gpsimd" else self.dS)
        h = ctr.h
        ctr.v += 16

        def _go(e, out=out, in_=in_, h=h, kw=kw):
            try:
                return e.dma_start(out=out, in_=in_, **kw).then_inc(h, 16)
            except Exception:
                import sys
                print("FAILING DMA out=", str(out)[:300], " in_=", str(in_)[:300],
                      file=sys.stderr)
                raise
        self.q(eng, _go)
        return ctr.v

    def tdma(self, out, in_):
        h = self.dS.h
        self.dS.v += 16
        self.q("sync", lambda e: e.dma_start_transpose(out, in_).then_inc(h, 16))
        return self.dS.v

    def barrier(self):
        pairs = [("tensor", self.sP), ("scalar", self.sA), ("vector", self.sV),
                 ("gpsimd", self.sG)]
        waits = []
        for eng, ctr in pairs:
            h = ctr.h
            pv = ctr.v
            ctr.v += 1
            if eng == "gpsimd":
                self.q(eng, lambda e, h=h, pv=pv: (e.wait_ge(h, pv) if pv > 0 else None,
                                                   e.nop(nofuse=True).then_inc(h, 1)))
            else:
                self.q(eng, lambda e, h=h: e.drain(fusable=False).then_inc(h, 1))
            waits.append((h, ctr.v))
        waits += [(self.dS.h, self.dS.v), (self.dG.h, self.dG.v), (self.dW.h, self.dW.v),
                  (self.dP.h, self.dP.v)]
        for grp in (self.u2, self.d2, self.c2, self.g2):
            waits += [(s.h, s.v) for s in grp]
        for eng in ("tensor", "scalar", "vector", "gpsimd", "sync"):
            for h, v in waits:
                if v > 0:
                    self.q(eng, lambda e, h=h, v=v: e.wait_ge(h, v))

    def tap(self, name, src_dram):
        if name in self.dbg:
            self.barrier()
            self.dma("sync", self.dbg[name], src_dram)
            self.qw("sync", self.dS)

    def tap_sbuf(self, name, sb_ap):
        if name in self.dbg:
            self.barrier()
            self.dma("sync", self.dbg[name], sb_ap)
            self.qw("sync", self.dS)

    def declare_dram(self):
        nc = self.nc
        dt = nc.dram_tensor
        self.inp = {}
        for nm, shape in [
            ("x_own", [TOWN, D]),
            ("aWq_my", [HPC, D, DH]), ("aWk_my", [HPC, D, DH]), ("aWv_my", [HPC, D, DH]),
            ("abq_my", [HPC, DH]), ("abk_my", [HPC, DH]), ("abv_my", [HPC, DH]),
            ("aWo", [D, D]), ("abo", [D]),
            ("bWq", [D, D]), ("bWk_my", [D, HPC * DH]), ("bWv_my", [D, HPC * DH]),
            ("bWo", [D, D]),
            ("bbq", [D]), ("bbk_my", [HPC * DH]), ("bbv_my", [HPC * DH]), ("bbo", [D]),
            ("su_ug", [D]), ("su_dp", [INTER]), ("wsug", [2 * INTER]), ("wsdp", [D]),
            ("fWupT", [D, INTER]), ("fWgateT", [D, INTER]), ("fWdownT", [INTER, D]),
        ]:
            self.inp[nm] = dt(nm, shape, F32, kind="ExternalInput").ap()
        self.y_own = dt("y_own", [TOWN, D], F32, kind="ExternalOutput").ap()

        self.xstream = dt("xstream", [128, NF * TOWN], F32, kind="Internal").ap()
        self.wupT = dt("wupT", [D, INTER], BF16, kind="Internal").ap()
        self.wgateT = dt("wgateT", [D, INTER], BF16, kind="Internal").ap()
        self.wdT = dt("wdT", [INTER, D], BF16, kind="Internal").ap()
        self.wdH = dt("wdH", [INTER, D], BF16, kind="Internal").ap()
        self.wd2 = dt("wd2", [NF, 128, NI, 128], BF16, kind="Internal").ap()
        self.ag_in = dt("ag_in", [D, TOWN], BF16, kind="Internal").ap()
        self.ag_out = dt("ag_out", [NCORE * D, TOWN], BF16, kind="Internal",
                         addr_space="Shared").ap()
        self.y_in = dt("y_in", [HPC * DH, T], BF16, kind="Internal").ap()
        self.y_out = dt("y_out", [NCORE * HPC * DH, T], BF16, kind="Internal",
                        addr_space="Shared").ap()

        self.dbg = {}
        def tapout(nm, shape, dtp):
            self.dbg[nm] = dt("dbg_" + nm, shape, dtp, kind="ExternalOutput").ap()
        if "stream" in self.taps: tapout("stream", [128, NF * TOWN], F32)
        if "xln" in self.taps: tapout("xln", [128, NF * TOWN], BF16)
        if "G" in self.taps: tapout("G", [NCORE * D, TOWN], BF16)
        if "kvq" in self.taps:
            tapout("K", [128, 2 * T], BF16)
            tapout("V", [128, NI * 256], BF16)
            tapout("Q", [128, 2 * T], BF16)
        if "yout" in self.taps: tapout("yout", [NCORE * HPC * DH, T], BF16)
        if "wd2" in self.taps: tapout("wd2", [NF, 128, NI, 128], BF16)
        if "x1" in self.taps: tapout("x1", [128, NF * TOWN], F32)
        if "h" in self.taps: tapout("h", [128, 4 * 16384], BF16)

    def declare_inline(self):
        nc = self.nc
        self.h128_t = nc.inline_tensor(_hmat(128).astype(BF), name="h128i")
        self.h64_t = nc.inline_tensor(_hmat(64).astype(BF), name="h64i")
        self.id128_t = nc.inline_tensor(np.eye(128, dtype=np.float32), name="id128i")
        self.onescol_t = nc.inline_tensor(np.ones((128, 1), dtype=np.float32).astype(BF),
                                          name="onescoli")
        self.onesrow_t = nc.inline_tensor(np.ones((1, 128), dtype=np.float32), name="onesrowi")
        m = np.zeros((128, 4, 512), dtype=np.float32)
        for k in range(4):
            for p in range(128):
                m[p, k, 128 * k + p:] = 1.0
        self.masks_t = nc.inline_tensor(m.reshape(128, 2048).astype(BF), name="masksi")

    def alloc_sbuf(self):
        nc = self.nc
        A = nc.alloc_sbuf_tensor_at
        BASE = (nc.sbuf_base + 31) & ~31
        assert BASE <= 4224, BASE
        self.B = [A(f"B{i}", [128, 16384], BF16, offset=BASE + 32768 * i) for i in range(6)]
        o = BASE + 32768 * 6

        def at(name, shape, dtype):
            nonlocal o
            t = A(name, shape, dtype, offset=o)
            o += int(np.prod(shape[1:])) * (4 if dtype == F32 else 2)
            o = (o + 31) & ~31
            return t

        self.io32 = at("io32", [128, 2048], F32)
        self.ptb = at("ptb", [128, 2048], BF16)
        self.maskb = None
        self.rows32 = at("rows32", [1, 2048], F32)
        self.silu_sb = at("silu_sb", [128, 1024], BF16)
        self.ysb = self.silu_sb
        self.h128_sb = at("h128_sb", [128, 128], BF16)
        self.h64_sb = at("h64_sb", [64, 64], BF16)
        self.id128_sb = at("id128_sb", [128, 128], F32)
        self.onescol_sb = at("onescol_sb", [128, 1], BF16)
        self.onesrow_sb = at("onesrow_sb", [1, 128], F32)
        self.cbias = at("cbias", [128, 32], F32)
        self.sucol = at("sucol", [128, 16], F32)
        self.wsgcol = at("wsgcol", [128, 64], F32)
        self.wsucol = at("wsucol", [128, 64], F32)
        self.wsdcol = at("wsdcol", [128, 16], F32)
        self.bcol = {}
        for nm in ("abo", "bbq", "bbo"):
            self.bcol[nm] = at(f"col_{nm}", [128, 16], F32)
        for nm in ("abq_my", "abk_my", "bbk_my"):
            self.bcol[nm] = at(f"col_{nm}", [128, 2], F32)
        self.bvb = at("bvb", [128, 256], BF16)
        self.r1 = at("r1", [1, 512], F32)
        self.r2 = at("r2", [1, 32], F32)
        self.qb1 = at("qb1", [1, 64], F32)
        self.sqs = self.ptb
        self.tmpcol = at("tmpcol", [128, 64], F32)
        assert o <= 229344, f"sbuf overflow {o}"
        self.sbuf_top = o

        self.rb = self.io32.ap()[:, 1536:2048]
        io_off = BASE + 32768 * 6
        self.maskb = A("maskb", [128, 2048], BF16, offset=io_off)
        self.B0f = A("B0f", [128, 8192], F32, offset=BASE)
        self.x1f = A("x1f", [128, 16384], F32, offset=BASE + 32768 * 4)
        self.gbuf = [A(f"gbuf{i}", [128, 1024], BF16, offset=io_off + 8192 + 2048 * i)
                     for i in range(2)]

        self.p01 = self.es.enter_context(nc.psum_tensor("p01", [128, 1024], F32))
        self.p23 = self.es.enter_context(nc.psum_tensor("p23", [128, 1024], F32))
        self.p45 = self.es.enter_context(nc.psum_tensor("p45", [128, 1024], F32))
        self.p67 = self.es.enter_context(nc.psum_tensor("p67", [128, 1024], F32))

    def make_sems(self):
        nc, es = self.nc, self.es
        self.dS = Ctr(es, nc, "dS")
        self.dG = Ctr(es, nc, "dG")
        self.dW = Ctr(es, nc, "dW")
        self.sP = Ctr(es, nc, "sP")
        self.sA = Ctr(es, nc, "sA")
        self.sV = Ctr(es, nc, "sV")
        self.sG = Ctr(es, nc, "sG")
        self.cc = Ctr(es, nc, "cc")
        self.u2 = [Ctr(es, nc, "u0"), Ctr(es, nc, "u1")]
        self.d2 = [Ctr(es, nc, "d0"), Ctr(es, nc, "d1")]
        self.c2 = [Ctr(es, nc, "c0"), Ctr(es, nc, "c1")]
        self.g2 = [Ctr(es, nc, "g0"), Ctr(es, nc, "g1")]
        self.dP = Ctr(es, nc, "dP")

    def ph_consts(self):
        for dst, src in [
            (self.h128_sb, self.h128_t), (self.h64_sb, self.h64_t),
            (self.id128_sb, self.id128_t), (self.onescol_sb, self.onescol_t),
            (self.onesrow_sb, self.onesrow_t),
        ]:
            self.dma("sync", dst[:], src.ap())

        def vec2col(flat, nch, dst, mul=None):
            src = flat.rearrange("(c p) -> c p", p=128) if len(flat.shape) == 1 else flat
            self.qw("sync", self.sV)
            self.dma("sync", self.io32[0:nch, 0:128], src)
            self.qw("tensor", self.dS)
            self.qw("tensor", self.sV)
            self.pe(lambda e, nch=nch: e.transpose(self.p01[:, 0:nch], self.io32[0:nch, 0:128],
                                                   self.id128_sb[0:nch, 0:nch]), inc=True)
            self.qw("vector", self.sP)
            if mul is None:
                self.ve(lambda e, d=dst, nch=nch: e.tensor_copy(out=d[:, 0:nch],
                                                                in_=self.p01[:, 0:nch]), inc=True)
            else:
                self.ve(lambda e, d=dst, nch=nch, m=mul: e.tensor_scalar(
                    out=d[:, 0:nch], in0=self.p01[:, 0:nch], scalar1=float(m), scalar2=None,
                    op0=ALU.mult), inc=True)

        vec2col(self.inp["su_ug"], 16, self.sucol)
        vec2col(self.inp["wsdp"], 16, self.wsdcol, mul=R8192)
        for nm in ("abo", "bbq", "bbo"):
            vec2col(self.inp[nm], 16, self.bcol[nm])
        for nm in ("abq_my", "abk_my", "bbk_my"):
            vec2col(self.inp[nm], 2, self.bcol[nm])
        vec2col(self.inp["su_dp"], 64, self.tmpcol)
        vec2col(self.inp["wsug"][0:INTER], 64, self.wsucol, mul=R2048)
        self.qw("vector", self.sV)
        self.ve(lambda e: e.tensor_tensor(out=self.wsucol[:], in0=self.wsucol[:],
                                          in1=self.tmpcol[:], op=ALU.mult), inc=True)
        vec2col(self.inp["wsug"][INTER:2 * INTER], 64, self.wsgcol, mul=R2048)
        self.barrier()
        self.phase_done("consts")

    def _transpose_tiles_to(self, dst_is_output):
        xs_v = self.xstream.rearrange("p (c t) -> p c t", c=NF, t=TOWN)
        for tc in range(8):
            self.qw("sync", self.sV)
            if dst_is_output:
                self.dma("sync", self.io32.ap().rearrange("p (c t) -> p c t", c=NF, t=128),
                         xs_v[:, :, tc * 128:(tc + 1) * 128])
            else:
                self.dma("sync", self.io32[:], self.inp["x_own"][tc * 128:(tc + 1) * 128, :])
            wS = self.dS.v
            evs = {}
            pairs = [self.p01, self.p23, self.p45, self.p67]
            for fc in range(16):
                slot = fc % 8
                ps = pairs[slot % 4]
                col = 512 * (slot // 4)
                self.qw("tensor", self.dS, wS)
                if fc >= 8:
                    self.qw("tensor", self.sV, evs[fc - 8])
                self.pe(lambda e, fc=fc, ps=ps, col=col: e.transpose(
                    ps[:, col:col + 128], self.io32[:, fc * 128:(fc + 1) * 128],
                    self.id128_sb[:]), inc=True)
                self.qw("vector", self.sP)
                if fc == 0:
                    self.qw("vector", self.dS)
                evs[fc] = self.ve(lambda e, fc=fc, ps=ps, col=col: e.tensor_copy(
                    out=self.B0f[:, fc * 128:(fc + 1) * 128],
                    in_=ps[:, col:col + 128]), inc=True)
            self.qw("sync", self.sV)
            if dst_is_output:
                self.dma("sync", self.y_own[tc * 128:(tc + 1) * 128, :], self.B0f[:, 0:2048])
            else:
                src = self.B0f.ap()[:, 0:2048].rearrange("p (c t) -> p c t", c=NF, t=128)
                self.dma("sync", xs_v[:, :, tc * 128:(tc + 1) * 128], src)
        self.barrier()

    def ph_input(self):
        self._transpose_tiles_to(dst_is_output=False)
        self.tap("stream", self.xstream)
        self.phase_done("input")

    def ph_output(self):
        self._transpose_tiles_to(dst_is_output=True)
        for eng in ("tensor", "scalar", "vector", "gpsimd", "sync"):
            self.qw(eng, self.dS)

    def ph_prep_casts(self):
        for dst, wsrc in ((self.wupT, self.inp["fWupT"]), (self.wgateT, self.inp["fWgateT"]),
                          (self.wdT, self.inp["fWdownT"])):
            self.dma("gpsimd", dst, wsrc)
        self.prep_cast_w = self.dG.v

    def ph_prep_mms(self, barriers=True):
        stage = self.B[1]
        self.qw("scalar", self.dG, self.prep_cast_w)
        self.qw("scalar", self.sP)
        wdT_v = self.wdT.rearrange("(cc p) c -> cc p c", cc=64, p=128)
        wdH_v = self.wdH.rearrange("(cp p) c -> cp p c", cp=64, p=128)
        for p in range(128):
            self.qw("scalar", self.sV)
            self.dma("scalar", stage[0:64, 0:2048], wdT_v[:, p, :], ctr=self.dP)
            wS = self.dP.v
            self.qw("tensor", self.dP, wS)
            self.qw("tensor", self.sV)
            for k in range(4):
                ps = (self.p01 if k < 2 else self.p23)[0:64, (k % 2) * 512:(k % 2) * 512 + 512]
                self.pe(lambda e, ps=ps, k=k: e.matmul(ps, self.h64_sb[:],
                        stage[0:64, k * 512:(k + 1) * 512], start=True, stop=True),
                        inc=(k == 3))
            self.qw("vector", self.sP)
            self.ve(lambda e: e.tensor_copy(out=stage[64:128, 0:1024], in_=self.p01[0:64, :]))
            self.ve(lambda e: e.tensor_copy(out=stage[64:128, 1024:2048], in_=self.p23[0:64, :]),
                    inc=True)
            self.qw("scalar", self.sV)
            self.dma("scalar", wdH_v[:, p, :], stage[64:128, 0:2048], ctr=self.dP)
        self.qw("scalar", self.dP)
        for ic in range(NI):
            self.qw("scalar", self.sV)
            self.dma("scalar", stage[:, 0:2048], self.wdH[ic * 128:(ic + 1) * 128, :],
                     ctr=self.dP)
            wS = self.dP.v
            self.qw("tensor", self.dP, wS)
            self.qw("tensor", self.sV)
            for k in range(4):
                ps = (self.p01 if k < 2 else self.p23)[:, (k % 2) * 512:(k % 2) * 512 + 512]
                self.pe(lambda e, ps=ps, k=k: e.matmul(ps, self.h128_sb[:],
                        stage[:, k * 512:(k + 1) * 512], start=True, stop=True), inc=(k == 3))
            self.qw("vector", self.sP)
            self.ve(lambda e: e.tensor_copy(out=stage[:, 2048:3072], in_=self.p01[:]))
            self.ve(lambda e: e.tensor_copy(out=stage[:, 3072:4096], in_=self.p23[:]), inc=True)
            self.qw("scalar", self.sV)
            for ct in range(16):
                self.dma("scalar", self.wd2[ct, :, ic, :],
                         stage[:, 2048 + ct * 128:2048 + (ct + 1) * 128], ctr=self.dP)
        self.prep_mms_done = self.dP.v
        self.tap("wd2", self.wd2)
        self.phase_done("prep")

    def ph_ln(self, with_su):
        xln = self.B[4]
        sums_lo, sums_hi = self.p67[0:1, 0:512], self.p67[0:1, 512:1024]
        sq_lo, sq_hi = self.p45[0:1, 0:512], self.p45[0:1, 512:1024]
        for c in range(NF):
            self.qw("sync", self.sV)
            self.dma("sync", self.io32[:, 0:1024], self.xstream[:, c * 1024:(c + 1) * 1024])
            self.qw("vector", self.dS)
            self.ve(lambda e, c=c: e.tensor_copy(out=xln[:, c * 1024:(c + 1) * 1024],
                                                 in_=self.io32[:, 0:1024]), inc=True)
            self.qw("scalar", self.sV)
            self.qw("scalar", self.sP)
            self.ae(lambda e, c=c: e.activation(out=self.sqs[:, 0:1024],
                    in_=xln[:, c * 1024:(c + 1) * 1024], func=AF.Square), inc=True)
            self.qw("tensor", self.sV)
            self.pe(lambda e, c=c: e.matmul(sums_lo, self.onescol_sb[:],
                    xln[:, c * 1024:c * 1024 + 512], start=(c == 0), stop=(c == NF - 1)))
            self.pe(lambda e, c=c: e.matmul(sums_hi, self.onescol_sb[:],
                    xln[:, c * 1024 + 512:(c + 1) * 1024], start=(c == 0), stop=(c == NF - 1)))
            self.qw("tensor", self.sA)
            self.pe(lambda e, c=c: e.matmul(sq_lo, self.onescol_sb[:], self.sqs[:, 0:512],
                    start=(c == 0), stop=(c == NF - 1)))
            self.pe(lambda e, c=c: e.matmul(sq_hi, self.onescol_sb[:], self.sqs[:, 512:1024],
                    start=(c == 0), stop=(c == NF - 1)), inc=True)
        mu = self.rows32[0:1, 0:1024]
        ex2 = self.rows32[0:1, 1024:2048]
        self.qw("vector", self.sP)
        self.ve(lambda e: e.tensor_scalar(out=mu, in0=self.p67[0:1, :], scalar1=1.0 / 2048.0,
                                          scalar2=None, op0=ALU.mult), inc=True)
        self.qw("vector", self.sV)
        self.ve(lambda e: e.tensor_scalar(out=ex2, in0=self.p45[0:1, :], scalar1=1.0 / 2048.0,
                                          scalar2=float(EPS), op0=ALU.mult, op1=ALU.add),
                inc=True)
        for k in range(2):
            self.qw("vector", self.sV)
            self.ve(lambda e, k=k: e.scalar_tensor_tensor(
                out=self.r1[0:1, 0:512], in0=mu[0:1, k * 512:(k + 1) * 512], scalar=-1.0,
                in1=mu[0:1, k * 512:(k + 1) * 512], op0=ALU.mult, op1=ALU.mult), inc=True)
            self.qw("vector", self.sV)
            self.ve(lambda e, k=k: e.tensor_tensor(
                out=ex2[0:1, k * 512:(k + 1) * 512], in0=ex2[0:1, k * 512:(k + 1) * 512],
                in1=self.r1[0:1, 0:512], op=ALU.add), inc=True)
        self.qw("scalar", self.sV)
        self.ae(lambda e: e.activation(out=ex2, in_=ex2, func=AF.Ln), inc=True)
        self.qw("scalar", self.sA)
        self.ae(lambda e: e.activation(out=ex2, in_=ex2, func=AF.Exp, scale=-0.5), inc=True)
        self.qw("tensor", self.sA)
        for k in range(2):
            self.pe(lambda e, k=k: e.matmul(self.p01[:, k * 512:(k + 1) * 512],
                    self.onesrow_sb[:], mu[0:1, k * 512:(k + 1) * 512], start=True, stop=True))
            self.pe(lambda e, k=k: e.matmul(self.p23[:, k * 512:(k + 1) * 512],
                    self.onesrow_sb[:], ex2[0:1, k * 512:(k + 1) * 512], start=True, stop=True),
                    inc=(k == 1))
        wbc = self.sP.v
        for c in range(NF):
            self.qw("vector", self.sP, wbc)
            self.qw("vector", self.sV)
            self.ve(lambda e, c=c: e.tensor_tensor(out=self.io32[:, 0:1024],
                    in0=xln[:, c * 1024:(c + 1) * 1024], in1=self.p01[:], op=ALU.subtract),
                    inc=True)
            self.qw("vector", self.sV)
            su = self.sucol[:, c:c + 1] if with_su else 1.0
            self.ve(lambda e, c=c, su=su: e.scalar_tensor_tensor(
                out=xln[:, c * 1024:(c + 1) * 1024], in0=self.io32[:, 0:1024],
                scalar=su, in1=self.p23[:], op0=ALU.mult, op1=ALU.mult), inc=True)
        self.barrier()

    def ph_ffn(self):
        self.ph_ln(with_su=True)
        self.tap_sbuf("xln", self.B[4][:])
        xln = self.B[4]
        mm_done = {}
        ev_done = {}

        def maybe_evac(cur_c):
            for k in range(NF - 1, -1, -1):
                if k in mm_done and k not in ev_done and 2 * k >= cur_c:
                    psk = mm_done[k][1]
                    self.qw("vector", self.sP, mm_done[k][0])
                    ev_done[k] = self.ve(lambda e, k=k, psk=psk: e.tensor_copy(
                        out=self.x1f[:, k * 1024:(k + 1) * 1024], in_=psk[:]), inc=True)

        for idx, c in enumerate(range(NF - 1, -1, -1)):
            ps = self.p01 if (idx % 2 == 0) else self.p23
            if idx >= 2:
                prev_k = NF - 1 - (idx - 2)
                self.qw("tensor", self.sV, ev_done[prev_k])
            self.pe(lambda e, c=c, ps=ps: e.matmul(ps[:, 0:512], self.h128_sb[:],
                    xln[:, c * 1024:c * 1024 + 512], start=True, stop=True))
            v = self.pe(lambda e, c=c, ps=ps: e.matmul(ps[:, 512:1024], self.h128_sb[:],
                    xln[:, c * 1024 + 512:(c + 1) * 1024], start=True, stop=True), inc=True)
            mm_done[c] = (v, ps)
            maybe_evac(c)
        maybe_evac(-100)
        for s in (1, 2, 4, 8):
            for a0 in range(NF):
                if a0 & s:
                    continue
                b0 = a0 + s
                self.qw("vector", self.sV)
                self.ve(lambda e, a0=a0: e.tensor_copy(out=self.io32[:, 0:1024],
                        in_=self.x1f[:, a0 * 1024:(a0 + 1) * 1024]), inc=True)
                self.qw("vector", self.sV)
                self.ve(lambda e, a0=a0, b0=b0: e.tensor_tensor(
                    out=self.x1f[:, a0 * 1024:(a0 + 1) * 1024],
                    in0=self.x1f[:, a0 * 1024:(a0 + 1) * 1024],
                    in1=self.x1f[:, b0 * 1024:(b0 + 1) * 1024], op=ALU.add), inc=True)
                self.qw("vector", self.sV)
                self.ve(lambda e, a0=a0, b0=b0: e.tensor_tensor(
                    out=self.x1f[:, b0 * 1024:(b0 + 1) * 1024], in0=self.io32[:, 0:1024],
                    in1=self.x1f[:, b0 * 1024:(b0 + 1) * 1024], op=ALU.subtract), inc=True)
        self.tap_sbuf("x1", self.x1f[:])
        x1b = self.B[4]
        self.qw("vector", self.sV)
        self.ve(lambda e: e.tensor_copy(out=self.ptb[:, 0:1024], in_=self.x1f[:, 0:1024]),
                inc=True)
        for c in range(1, NF):
            self.qw("vector", self.sV)
            self.ve(lambda e, c=c: e.tensor_copy(out=x1b[:, c * 1024:(c + 1) * 1024],
                    in_=self.x1f[:, c * 1024:(c + 1) * 1024]), inc=True)
        self.qw("vector", self.sV)
        self.ve(lambda e: e.tensor_copy(out=x1b[:, 0:1024], in_=self.ptb[:, 0:1024]), inc=True)
        self.barrier()

        upbuf = [self.B[5].ap()[:, 0:4096], self.B[5].ap()[:, 4096:8192]]
        gabuf = [self.B[5].ap()[:, 8192:12288], self.B[5].ap()[:, 12288:16384]]
        wupT_v = self.wupT.rearrange("(f p) i -> p f i", f=16, p=128)
        wgaT_v = self.wgateT.rearrange("(f p) i -> p f i", f=16, p=128)

        def h_ap(ic):
            return self.B[ic // 16][:, (ic % 16) * 1024:(ic % 16) * 1024 + 1024]

        nslab = NI // 2
        mm_after_slab = {}
        slab_w = {}
        hdone = {}
        for sl in range(nslab):
            if sl >= 2:
                self.qw("sync", self.sP, mm_after_slab[sl - 2])
            usem = self.u2[sl % 2]
            for half in range(2):
                self.dma("sync", upbuf[sl % 2][:, half * 2048:(half + 1) * 2048].rearrange(
                    "p (f i) -> p f i", f=8, i=256),
                    wupT_v[:, half * 8:(half + 1) * 8, sl * 256:(sl + 1) * 256], ctr=usem)
                self.dma("sync", gabuf[sl % 2][:, half * 2048:(half + 1) * 2048].rearrange(
                    "p (f i) -> p f i", f=8, i=256),
                    wgaT_v[:, half * 8:(half + 1) * 8, sl * 256:(sl + 1) * 256], ctr=usem)
            slab_w[sl] = usem.v
            for j in range(2):
                ic = sl * 2 + j
                pu = self.p01 if (ic % 2 == 0) else self.p45
                pg = self.p23 if (ic % 2 == 0) else self.p67
                self.qw("tensor", self.u2[sl % 2], slab_w[sl])
                if ic >= 2:
                    self.qw("tensor", self.sV, hdone[ic - 2][0])
                    self.qw("tensor", self.sA, hdone[ic - 2][1])
                for f in range(16):
                    off = (f // 8) * 2048 + (f % 8) * 512 + j * 256
                    lhs_u = upbuf[sl % 2][:, off:off + 128] if False else None
                    hh = f // 8
                    base = hh * 2048 + (f % 8) * 256 + j * 128
                    lu = upbuf[sl % 2][:, base:base + 128]
                    lg = gabuf[sl % 2][:, base:base + 128]
                    self.pe(lambda e, f=f, pu=pu, lu=lu: e.matmul(
                        pu[:, 0:512], lu, x1b[:, f * 1024:f * 1024 + 512],
                        start=(f == 0), stop=(f == 15)))
                    self.pe(lambda e, f=f, pu=pu, lu=lu: e.matmul(
                        pu[:, 512:1024], lu, x1b[:, f * 1024 + 512:(f + 1) * 1024],
                        start=(f == 0), stop=(f == 15)))
                    self.pe(lambda e, f=f, pg=pg, lg=lg: e.matmul(
                        pg[:, 0:512], lg, x1b[:, f * 1024:f * 1024 + 512],
                        start=(f == 0), stop=(f == 15)))
                    self.pe(lambda e, f=f, pg=pg, lg=lg: e.matmul(
                        pg[:, 512:1024], lg, x1b[:, f * 1024 + 512:(f + 1) * 1024],
                        start=(f == 0), stop=(f == 15)), inc=(f == 15))
                wmm = self.sP.v
                mm_after_slab[sl] = wmm
                self.qw("scalar", self.sP, wmm)
                if ic >= 1:
                    self.qw("scalar", self.sV, hdone[ic - 1][0])
                vA = self.ae(lambda e, ic=ic, pg=pg: e.activation(
                    out=self.silu_sb[:], in_=pg[:], func=AF.Silu,
                    scale=self.wsgcol[:, ic:ic + 1]), inc=True)
                self.qw("vector", self.sA, vA)
                vV = self.ve(lambda e, ic=ic, pu=pu: e.scalar_tensor_tensor(
                    out=h_ap(ic), in0=pu[:], scalar=self.wsucol[:, ic:ic + 1],
                    in1=self.silu_sb[:], op0=ALU.mult, op1=ALU.mult), inc=True)
                hdone[ic] = (vV, vA)
        self.barrier()
        if "h" in self.dbg:
            self.dma("sync", self.dbg["h"].rearrange("p (r n) -> r p n", r=4, n=16384)[0], self.B[0][:])
            self.dma("sync", self.dbg["h"].rearrange("p (r n) -> r p n", r=4, n=16384)[1], self.B[1][:])
            self.dma("sync", self.dbg["h"].rearrange("p (r n) -> r p n", r=4, n=16384)[2], self.B[2][:])
            self.dma("sync", self.dbg["h"].rearrange("p (r n) -> r p n", r=4, n=16384)[3], self.B[3][:])
            self.qw("sync", self.dS)

        wdb = [self.B[5].ap()[:, 0:8192], self.B[5].ap()[:, 8192:16384]]
        dn_mm = {}
        for ct in range(NF):
            if ct >= 2:
                self.qw("sync", self.sP, dn_mm[ct - 2])
            self.dma("sync", wdb[ct % 2], self.wd2[ct].rearrange("p ic c -> p (ic c)"),
                     ctr=self.d2[ct % 2])
            wW = self.d2[ct % 2].v
            ps = self.p01 if (ct % 2 == 0) else self.p23
            self.qw("tensor", self.d2[ct % 2], wW)
            self.qw("tensor", self.sV)
            for ic in range(NI):
                self.pe(lambda e, ct=ct, ic=ic, ps=ps: e.matmul(
                    ps[:, 0:512], wdb[ct % 2][:, ic * 128:(ic + 1) * 128],
                    h_ap(ic)[:, 0:512], start=(ic == 0), stop=(ic == NI - 1)))
                self.pe(lambda e, ct=ct, ic=ic, ps=ps: e.matmul(
                    ps[:, 512:1024], wdb[ct % 2][:, ic * 128:(ic + 1) * 128],
                    h_ap(ic)[:, 512:1024], start=(ic == 0), stop=(ic == NI - 1)),
                    inc=(ic == NI - 1))
            wmm = self.sP.v
            dn_mm[ct] = wmm
            self.qw("sync", self.sV)
            self.dma("sync", self.io32[:, 0:1024], self.xstream[:, ct * 1024:(ct + 1) * 1024])
            wS = self.dS.v
            self.qw("vector", self.sP, wmm)
            self.qw("vector", self.dS, wS)
            self.ve(lambda e, ct=ct, ps=ps: e.scalar_tensor_tensor(
                out=self.io32[:, 1024:2048], in0=ps[:], scalar=self.wsdcol[:, ct:ct + 1],
                in1=self.io32[:, 0:1024], op0=ALU.mult, op1=ALU.add), inc=True)
            self.qw("sync", self.sV)
            self.dma("sync", self.xstream[:, ct * 1024:(ct + 1) * 1024], self.io32[:, 1024:2048])
        self.barrier()
        self.tap("stream", self.xstream)

    def _colslab_proj(self, wsrc, bias_col, out_cb, rhs, nk=NF, wcols=None):
        wcb = [self.B[5].ap()[:, 0:2048], self.B[5].ap()[:, 2048:4096]]
        mmv = {}
        for ct in range(NF):
            if ct >= 2:
                self.qw("gpsimd", self.sP, mmv[ct - 2])
            src = wsrc.rearrange("(k p) (ct c) -> p ct k c", p=128, k=nk, ct=NF, c=128)
            self.dma("gpsimd", wcb[ct % 2].rearrange("p (k c) -> p k c", k=nk, c=128),
                     src[:, ct, :, :], ctr=self.c2[ct % 2])
            wW = self.c2[ct % 2].v
            ps = self.p01 if (ct % 2 == 0) else self.p23
            self.qw("tensor", self.c2[ct % 2], wW)
            self.qw("tensor", self.sV)
            for k in range(nk):
                self.pe(lambda e, k=k, ct=ct, ps=ps: e.matmul(
                    ps[:, 0:512], wcb[ct % 2][:, k * 128:(k + 1) * 128], rhs(k, 0),
                    start=(k == 0), stop=(k == nk - 1)))
                self.pe(lambda e, k=k, ct=ct, ps=ps: e.matmul(
                    ps[:, 512:1024], wcb[ct % 2][:, k * 128:(k + 1) * 128], rhs(k, 1),
                    start=(k == 0), stop=(k == nk - 1)), inc=(k == nk - 1))
            wmm = self.sP.v
            mmv[ct] = wmm
            out_cb(ct, ps, wmm)

    def ph_attn(self, variant, mid_hook=None):
        a = variant == "a"
        self.ph_ln(with_su=False)
        self.tap_sbuf("xln", self.B[4][:])
        xln = self.B[4]
        agin_v = self.ag_in.rearrange("(c p) t -> p c t", c=NF, p=128)
        self.dma("gpsimd", agin_v, xln.ap().rearrange("p (c t) -> p c t", c=NF, t=TOWN))
        self.qw("gpsimd", self.dG)
        hcc = self.cc.h
        self.cc.v += 1
        rg = [list(range(NCORE))]
        self.q("gpsimd", lambda e: e.collective_compute(
            "AllGather", ALU.bypass, replica_groups=rg,
            ins=[self.ag_in.opt()], outs=[self.ag_out.opt()]).then_inc(hcc, 1))
        wcc = self.cc.v

        KT, VT, QT, QFL = self.B[0], self.B[1], self.B[2], self.B[3]
        pfx = "a" if a else "b"

        if not a:
            def qfl_out(ct, ps, wmm):
                self.qw("vector", self.sP, wmm)
                self.ve(lambda e, ct=ct, ps=ps: e.tensor_scalar(
                    out=QFL[:, ct * 1024:(ct + 1) * 1024], in0=ps[:],
                    scalar1=self.bcol["bbq"][:, ct:ct + 1], scalar2=None, op0=ALU.add),
                    inc=True)

            self._colslab_proj(self.inp["bWq"], None, qfl_out,
                               lambda f, hh: xln[:, f * 1024 + hh * 512:f * 1024 + hh * 512 + 512])
            self.qw("vector", self.sV)
            qfl_v = QFL.ap().rearrange("p (j u) -> p j u", j=16, u=1024)
            for hl in range(2):
                src = qfl_v[:, :, hl * 512:(hl + 1) * 512].rearrange("p j v -> p v j")
                dst = QT.ap()[:, hl * 8192:(hl + 1) * 8192].rearrange(
                    "p (v j) -> p v j", v=512, j=16)
                self.ve(lambda e, src=src, dst=dst: e.tensor_copy(out=dst, in_=src), inc=True)

        wk_sl = self.B[5].ap()[:, 4096:8192]
        wv_sl = self.B[5].ap()[:, 8192:12288]
        wq_sl = self.B[5].ap()[:, 12288:16384]
        mats = [("k", wk_sl), ("v", wv_sl)] + ([("q", wq_sl)] if a else [])
        for nm, dstb in mats:
            for f in range(NF):
                for hl in range(2):
                    if a:
                        wsrc = self.inp["aW" + nm + "_my"][hl, f * 128:(f + 1) * 128, :]
                    else:
                        wsrc = self.inp["bW" + nm + "_my"][f * 128:(f + 1) * 128,
                                                           hl * 128:(hl + 1) * 128]
                    self.dma("gpsimd", dstb[:, f * 256 + hl * 128:f * 256 + hl * 128 + 128],
                             wsrc, ctr=self.dW)
        wslices = self.dW.v

        if a:
            bsrc = self.inp["abv_my"].rearrange("h d -> (h d)")
        else:
            bsrc = self.inp["bbv_my"]
        self.dma("sync", self.r1[0:1, 0:256], bsrc.rearrange("(o n) -> o n", o=1))
        wb = self.dS.v
        self.qw("tensor", self.dS, wb)
        self.qw("tensor", self.sV)
        self.pe(lambda e: e.matmul(self.p45[:, 0:256], self.onesrow_sb[:], self.r1[0:1, 0:256],
                                   start=True, stop=True), inc=True)
        self.qw("vector", self.sP)
        self.ve(lambda e: e.tensor_copy(out=self.bvb[:], in_=self.p45[:, 0:256]), inc=True)

        self.qw("sync", self.cc, wcc)
        self.qw("tensor", self.cc, wcc)
        self.qw("tensor", self.dW, wslices)

        G_v = self.ag_out.rearrange("(r c p) t -> r c p t", r=NCORE, c=NF, p=128)

        def gload(r, f, par):
            self.dma("sync", self.gbuf[par][:], G_v[r, f], ctr=self.g2[par])
            return self.g2[par].v

        bk = self.bcol["abk_my" if a else "bbk_my"]
        bq = self.bcol["abq_my"] if a else None

        def kq_pass(wsl, dstT, bias_col):
            gmm = []
            for r in range(NCORE):
                for f in range(NF):
                    par = f % 2
                    n = r * NF + f
                    if n >= 2:
                        self.qw("sync", self.sP, gmm[n - 2])
                    wS = gload(r, f, par)
                    self.qw("tensor", self.g2[par], wS)
                    if f == 0:
                        self.qw("tensor", self.sV)
                    for hl in range(2):
                        ps = self.p01 if hl == 0 else self.p23
                        for t2 in range(2):
                            self.pe(lambda e, f=f, hl=hl, t2=t2, ps=ps, par=par, wsl=wsl:
                                    e.matmul(ps[:, t2 * 512:(t2 + 1) * 512],
                                             wsl[:, f * 256 + hl * 128:f * 256 + hl * 128 + 128],
                                             self.gbuf[par][:, t2 * 512:(t2 + 1) * 512],
                                             start=(f == 0), stop=(f == NF - 1)),
                                    inc=(hl == 1 and t2 == 1))
                    gmm.append(self.sP.v)
                self.qw("vector", self.sP, gmm[-1])
                for hl in range(2):
                    ps = self.p01 if hl == 0 else self.p23
                    bcolv = bias_col[:, hl:hl + 1]
                    for t2 in range(2):
                        self.ve(lambda e, r=r, hl=hl, t2=t2, ps=ps, bcolv=bcolv: e.tensor_scalar(
                            out=dstT[:, hl * 8192 + r * 1024 + t2 * 512:
                                     hl * 8192 + r * 1024 + t2 * 512 + 512],
                            in0=ps[:, t2 * 512:(t2 + 1) * 512], scalar1=bcolv, scalar2=None,
                            op0=ALU.add), inc=(hl == 1 and t2 == 1))

        kq_pass(wk_sl, KT, bk)
        self.qw("sync", self.sP)
        if a:
            kq_pass(wq_sl, QT, bq)
            self.qw("sync", self.sP)

        gmmv = []
        for r in range(NCORE):
            vmm = 0
            for f in range(NF):
                par = f % 2
                n = r * NF + f
                if n >= 2:
                    self.qw("sync", self.sP, gmmv[n - 2])
                wS = gload(r, f, par)
                self.qw("tensor", self.g2[par], wS)
                if f == 0:
                    self.qw("tensor", self.sV)
                for tc in range(8):
                    ps = (self.p01 if tc < 4 else self.p23)
                    ofs = (tc % 4) * 256
                    self.pe(lambda e, f=f, tc=tc, ps=ps, ofs=ofs, par=par: e.matmul(
                        ps[:, ofs:ofs + 256], self.gbuf[par][:, tc * 128:(tc + 1) * 128],
                        wv_sl[:, f * 256:(f + 1) * 256],
                        start=(f == 0 and tc % 2 == 0),
                        stop=(f == NF - 1 and tc % 2 == 1)), inc=(tc == 7))
                vmm = self.sP.v
                gmmv.append(vmm)
            self.qw("vector", self.sP, vmm)
            for tc in range(8):
                ps = (self.p01 if tc < 4 else self.p23)
                ofs = (tc % 4) * 256
                self.ve(lambda e, r=r, tc=tc, ps=ps, ofs=ofs: e.tensor_tensor(
                    out=VT[:, (r * 8 + tc) * 256:(r * 8 + tc) * 256 + 256],
                    in0=ps[:, ofs:ofs + 256], in1=self.bvb[:], op=ALU.add), inc=(tc == 7))
        self.barrier()
        if "K" in self.dbg:
            self.dma("sync", self.dbg["K"], KT[:])
            self.dma("sync", self.dbg["V"], VT[:])
            self.dma("sync", self.dbg["Q"], QT[:])
            self.qw("sync", self.dS)
        if "G" in self.dbg:
            self.tap("G", self.ag_out)
        self.phase_done("proj_" + pfx)

        for hl in range(2):
            for blk in range(16):
                self.qw("scalar", self.sP)
                self.ae(lambda e, hl=hl, blk=blk: e.activation(
                    out=self.sqs[:, 0:512], in_=KT[:, hl * 8192 + blk * 512:hl * 8192 + blk * 512 + 512],
                    func=AF.Square), inc=True)
                self.qw("tensor", self.sA)
                self.qw("tensor", self.sV)
                self.pe(lambda e: e.matmul(self.p45[0:1, 0:512], self.onescol_sb[:],
                        self.sqs[:, 0:512], start=True, stop=True), inc=True)
                self.qw("vector", self.sP)
                self.ve(lambda e, blk=blk: e.tensor_reduce(
                    out=self.r2[0:1, blk:blk + 1], in_=self.p45[0:1, 0:512], axis=AX.X,
                    op=ALU.max), inc=True)
            self.qw("vector", self.sV)
            self.ve(lambda e: e.tensor_reduce(out=self.r2[0:1, 16:17], in_=self.r2[0:1, 0:16],
                                              axis=AX.X, op=ALU.max), inc=True)
            for blk in range(16):
                self.qw("scalar", self.sP)
                self.ae(lambda e, hl=hl, blk=blk: e.activation(
                    out=self.sqs[:, 0:512], in_=QT[:, hl * 8192 + blk * 512:hl * 8192 + blk * 512 + 512],
                    func=AF.Square), inc=True)
                self.qw("tensor", self.sA)
                self.qw("tensor", self.sV)
                self.pe(lambda e: e.matmul(self.p45[0:1, 0:512], self.onescol_sb[:],
                        self.sqs[:, 0:512], start=True, stop=True), inc=True)
                self.qw("vector", self.sP)
                self.ve(lambda e, hl=hl, blk=blk: e.tensor_reduce(
                    out=self.qb1[0:1, hl * 16 + blk:hl * 16 + blk + 1],
                    in_=self.p45[0:1, 0:512], axis=AX.X, op=ALU.max), inc=True)
            self.qw("vector", self.sV)
            self.ve(lambda e, hl=hl: e.tensor_scalar(
                out=self.r1[0:1, 0:16], in0=self.qb1[0:1, hl * 16:(hl + 1) * 16],
                scalar1=self.r2[0:1, 16:17], scalar2=float((ATT_SCALE * 1.002) ** 2),
                op0=ALU.mult, op1=ALU.mult), inc=True)
            self.qw("scalar", self.sV)
            self.ae(lambda e: e.activation(out=self.r1[0:1, 0:16], in_=self.r1[0:1, 0:16],
                                           func=AF.Ln), inc=True)
            self.qw("scalar", self.sA)
            self.ae(lambda e: e.activation(out=self.r1[0:1, 0:16], in_=self.r1[0:1, 0:16],
                    func=AF.Exp, scale=0.5), inc=True)
            self.qw("vector", self.sA)
            self.ve(lambda e: e.tensor_scalar(out=self.r1[0:1, 16:32], in0=self.r1[0:1, 0:16],
                    scalar1=-1.0, scalar2=None, op0=ALU.mult), inc=True)
            self.qw("tensor", self.sV)
            self.pe(lambda e: e.matmul(self.p45[:, 0:16], self.onesrow_sb[:],
                    self.r1[0:1, 16:32], start=True, stop=True), inc=True)
            self.qw("vector", self.sP)
            self.ve(lambda e, hl=hl: e.tensor_copy(out=self.cbias[:, hl * 16:(hl + 1) * 16],
                    in_=self.p45[:, 0:16]), inc=True)
        self.barrier()

        self.dma("sync", self.maskb[:], self.masks_t.ap())
        self.qw("sync", self.dS)
        for eng in ("tensor", "scalar", "vector"):
            self.qw(eng, self.dS)
        outT = self.p45[:, 0:512]
        sums = self.p45[0:1, 512:1024]
        for hl in range(2):
            for tb in range(NF):
                nsc = 4 * (tb + 1)
                npair = nsc // 2
                qrhs0 = QT[:, hl * 8192 + tb * 512:hl * 8192 + tb * 512 + 512]
                exp_done = {}
                cons_done = {}
                for pr in range(npair):
                    ps = self.p01 if (pr % 2 == 0) else self.p23
                    par = pr % 2
                    if pr >= 2:
                        self.qw("tensor", self.sA, exp_done[pr - 2])
                    else:
                        self.qw("tensor", self.sA)
                    for k in range(2):
                        sc = 2 * pr + k
                        self.pe(lambda e, sc=sc, k=k, ps=ps, hl=hl, qrhs0=qrhs0: e.matmul(
                            ps[:, k * 512:(k + 1) * 512],
                            KT[:, hl * 8192 + sc * 128:hl * 8192 + sc * 128 + 128],
                            qrhs0, start=True, stop=True), inc=(k == 1))
                    wS = self.sP.v
                    self.qw("scalar", self.sP, wS)
                    if pr >= 2:
                        self.qw("scalar", self.sP, cons_done[pr - 2])
                    exp_done[pr] = self.ae(lambda e, ps=ps, par=par, hl=hl, tb=tb: e.activation(
                        out=self.ptb[:, par * 1024:(par + 1) * 1024], in_=ps[:], func=AF.Exp,
                        scale=ATT_SCALE, bias=self.cbias[:, hl * 16 + tb:hl * 16 + tb + 1]),
                        inc=True)
                    wmask = None
                    for k in range(2):
                        sc = 2 * pr + k
                        if sc >= 4 * tb:
                            kk = sc - 4 * tb
                            self.qw("vector", self.sA, exp_done[pr])
                            wmask = self.ve(lambda e, k=k, par=par, kk=kk: e.tensor_tensor(
                                out=self.ptb[:, par * 1024 + k * 512:par * 1024 + (k + 1) * 512],
                                in0=self.ptb[:, par * 1024 + k * 512:par * 1024 + (k + 1) * 512],
                                in1=self.maskb[:, kk * 512:(kk + 1) * 512], op=ALU.mult),
                                inc=True)
                    if wmask is not None:
                        self.qw("tensor", self.sV, wmask)
                    else:
                        self.qw("tensor", self.sA, exp_done[pr])
                    if pr == 0:
                        self.qw("tensor", self.sA)
                        self.qw("tensor", self.sV)
                    for k in range(2):
                        sc = 2 * pr + k
                        st = (sc == 0)
                        sp = (sc == nsc - 1)
                        self.pe(lambda e, k=k, par=par, st=st, sp=sp: e.matmul(
                            sums, self.onescol_sb[:],
                            self.ptb[:, par * 1024 + k * 512:par * 1024 + (k + 1) * 512],
                            start=st, stop=sp))
                        self.pe(lambda e, k=k, par=par, sc=sc, hl=hl, st=st, sp=sp: e.matmul(
                            outT, VT[:, sc * 256 + hl * 128:sc * 256 + hl * 128 + 128],
                            self.ptb[:, par * 1024 + k * 512:par * 1024 + (k + 1) * 512],
                            start=st, stop=sp),
                            inc=(k == 1))
                    cons_done[pr] = self.sP.v
                wcons = cons_done[npair - 1]
                self.qw("scalar", self.sP, wcons)
                self.ae(lambda e: e.activation(out=self.r1[0:1, 0:512], in_=sums, func=AF.Ln),
                        inc=True)
                self.qw("scalar", self.sA)
                wrec = self.ae(lambda e: e.activation(out=self.r1[0:1, 0:512],
                               in_=self.r1[0:1, 0:512], func=AF.Exp, scale=-1.0), inc=True)
                self.qw("tensor", self.sA, wrec)
                self.qw("tensor", self.sV)
                wbc = self.pe(lambda e: e.matmul(self.p67[:, 0:512], self.onesrow_sb[:],
                              self.r1[0:1, 0:512], start=True, stop=True), inc=True)
                self.qw("vector", self.sP, wbc)
                wrb = self.ve(lambda e: e.tensor_copy(out=self.rb, in_=self.p67[:, 0:512]),
                              inc=True)
                ypar = (hl * NF + tb) % 2
                self.qw("vector", self.sV, wrb)
                self.qw("vector", self.dS)
                wye = self.ve(lambda e, ypar=ypar: e.scalar_tensor_tensor(
                    out=self.ysb[:, ypar * 512:(ypar + 1) * 512], in0=outT, scalar=1.0,
                    in1=self.rb, op0=ALU.mult, op1=ALU.mult), inc=True)
                self.qw("sync", self.sV, wye)
                self.dma("sync", self.y_in[hl * 128:(hl + 1) * 128, tb * 512:(tb + 1) * 512],
                         self.ysb[:, ypar * 512:(ypar + 1) * 512])
        self.barrier()
        self.phase_done("scores_" + pfx)

        hcc = self.cc.h
        self.cc.v += 1
        self.q("gpsimd", lambda e: e.collective_compute(
            "AllGather", ALU.bypass, replica_groups=rg,
            ins=[self.y_in.opt()], outs=[self.y_out.opt()]).then_inc(hcc, 1))
        wcc2 = self.cc.v
        if mid_hook is not None:
            mid_hook()
        self.qw("sync", self.cc, wcc2)
        if "yout" in self.dbg:
            self.dma("sync", self.dbg["yout"], self.y_out)
            self.qw("sync", self.dS)

        ysall = self.B[3]
        if not hasattr(self, "_cid"):
            self._cid = self.nc.partition_id(engines=[mybir.EngineType.SP])
        cid = self._cid
        for k in range(16):
            self.dma("sync", ysall[:, k * 1024:(k + 1) * 1024],
                     self.y_out[k * 128:(k + 1) * 128, bass.ts(cid, TOWN)])
        wY = self.dS.v
        self.qw("tensor", self.dS, wY)
        self.qw("gpsimd", self.dS, wY)
        self.qw("sync", self.dS, wY)
        bo = self.bcol["abo" if a else "bbo"]

        def wo_out(ct, ps, wmm):
            self.qw("sync", self.sV)
            self.dma("sync", self.io32[:, 0:1024], self.xstream[:, ct * 1024:(ct + 1) * 1024])
            wS = self.dS.v
            self.qw("vector", self.sP, wmm)
            self.qw("vector", self.dS, wS)
            self.ve(lambda e, ct=ct, ps=ps: e.scalar_tensor_tensor(
                out=self.io32[:, 1024:2048], in0=ps[:], scalar=bo[:, ct:ct + 1],
                in1=self.io32[:, 0:1024], op0=ALU.add, op1=ALU.add), inc=True)
            self.qw("sync", self.sV)
            self.dma("sync", self.xstream[:, ct * 1024:(ct + 1) * 1024],
                     self.io32[:, 1024:2048])

        self._colslab_proj(self.inp["aWo" if a else "bWo"], None, wo_out,
                           lambda k, hh: ysall[:, k * 1024 + hh * 512:k * 1024 + hh * 512 + 512])
        self.barrier()
        self.tap("stream", self.xstream)
        self.phase_done("attn_" + pfx)

    def build(self, phases):
        self.declare_dram()
        self.declare_inline()
        self.alloc_sbuf()
        self.make_sems()
        for ph in phases:
            if self.stopped:
                break
            ph()
        with self.es:
            block = self.es.enter_context(self.nc.Block())
            for eng in ("tensor", "scalar", "vector", "gpsimd", "sync"):
                getattr(block, eng)(lambda e, eng=eng: [fn(e) for fn in self.ops[eng]])
        return self.nc


def build_core(core_id, taps=(), upto=None, which="full"):
    b = Builder(core_id, taps=taps, upto=upto)
    prep_serial = [b.ph_prep_casts, b.ph_prep_mms, b.barrier]
    if which == "ffn_only":
        phases = [b.ph_consts, b.ph_input] + prep_serial + [b.ph_ffn, b.ph_output]
    elif which == "attn_a_only":
        phases = [b.ph_consts, b.ph_input] + prep_serial + [lambda: b.ph_attn("a"), b.ph_output]
    elif which == "attn_b_only":
        phases = [b.ph_consts, b.ph_input] + prep_serial + [lambda: b.ph_attn("b"), b.ph_output]
    else:
        phases = [b.ph_consts, b.ph_input, b.ph_prep_casts,
                  lambda: b.ph_attn("a", mid_hook=b.ph_prep_mms), b.ph_ffn,
                  lambda: b.ph_attn("b"), b.ph_ffn, b.ph_output]
    return b.build(phases)


def make_in_map(inputs, core_id):
    x = np.asarray(inputs["L0_attn_a_x"]).reshape(T, D)
    h0 = 2 * core_id
    aWq = np.asarray(inputs["L0_attn_a_Wq"])
    aWk = np.asarray(inputs["L0_attn_a_Wk"])
    aWv = np.asarray(inputs["L0_attn_a_Wv"])
    bWk = np.asarray(inputs["L0_attn_b_Wk"])
    bWv = np.asarray(inputs["L0_attn_b_Wv"])
    m = {
        "x_own": np.ascontiguousarray(x[core_id * TOWN:(core_id + 1) * TOWN]),
        "aWq_my": aWq[h0:h0 + 2], "aWk_my": aWk[h0:h0 + 2], "aWv_my": aWv[h0:h0 + 2],
        "abq_my": np.asarray(inputs["L0_attn_a_bq"])[h0:h0 + 2],
        "abk_my": np.asarray(inputs["L0_attn_a_bk"])[h0:h0 + 2],
        "abv_my": np.asarray(inputs["L0_attn_a_bv"])[h0:h0 + 2],
        "aWo": inputs["L0_attn_a_Wo"], "abo": inputs["L0_attn_a_bo"],
        "bWq": inputs["L0_attn_b_Wq"],
        "bWk_my": bWk[:, h0 * 128:(h0 + 2) * 128],
        "bWv_my": bWv[:, h0 * 128:(h0 + 2) * 128],
        "bWo": inputs["L0_attn_b_Wo"],
        "bbq": inputs["L0_attn_b_bq"],
        "bbk_my": np.asarray(inputs["L0_attn_b_bk"])[h0 * 128:(h0 + 2) * 128],
        "bbv_my": np.asarray(inputs["L0_attn_b_bv"])[h0 * 128:(h0 + 2) * 128],
        "bbo": inputs["L0_attn_b_bo"],
        "su_ug": inputs["L0_ffn_SU_ug"], "su_dp": inputs["L0_ffn_SU_dp"],
        "wsug": inputs["L0_ffn_Wscale_ug"], "wsdp": inputs["L0_ffn_Wscale_dp"],
        "fWupT": np.ascontiguousarray(np.asarray(inputs["L0_ffn_W_up"]).T),
        "fWgateT": np.ascontiguousarray(np.asarray(inputs["L0_ffn_W_gate"]).T),
        "fWdownT": np.ascontiguousarray(np.asarray(inputs["L0_ffn_W_down"]).T),
    }
    return {k: np.ascontiguousarray(np.asarray(v, dtype=np.float32)) for k, v in m.items()}


def kernel(**inputs):
    nc = build_core(0)
    in_maps = [make_in_map(inputs, c) for c in range(NCORE)]
    res = run_bass_kernel_spmd(nc, in_maps, core_ids=list(range(NCORE)))
    out = np.concatenate([np.asarray(res.results[c]["y_own"]) for c in range(NCORE)], axis=0)
    return out.reshape(1, T, D).astype(np.float32)


# version 03
# speedup vs baseline: 1.2081x; 1.0031x over previous
import math
import contextlib
import numpy as np
import ml_dtypes

import concourse.bass as bass
import concourse.mybir as mybir
from concourse.bass_utils import run_bass_kernel_spmd

F32 = mybir.dt.float32
BF16 = mybir.dt.bfloat16
AF = mybir.ActivationFunctionType
ALU = mybir.AluOpType
AX = mybir.AxisListType

NCORE = 8
T = 8192
D = 2048
TOWN = 1024
NF = 16
NI = 64
INTER = 8192
HPC = 2
DH = 128
ATT_SCALE = 1.0 / math.sqrt(128.0)
EPS = 1e-5
R2048 = 1.0 / math.sqrt(2048.0)
R8192 = 1.0 / math.sqrt(8192.0)
BF = ml_dtypes.bfloat16


def _hmat(n):
    a = np.arange(n)
    bits = a[:, None] & a[None, :]
    pc = np.zeros((n, n), dtype=np.int64)
    x = bits.copy()
    while x.any():
        pc += x & 1
        x >>= 1
    return np.where(pc % 2 == 0, 1.0, -1.0).astype(np.float32)


class Ctr:
    def __init__(self, es, nc, name):
        self.h = es.enter_context(nc.semaphore(name))
        self.v = 0


class Builder:

    def __init__(self, core_id, taps=(), upto=None):
        self.core_id = core_id
        self.taps = set(taps)
        self.upto = upto
        self.stopped = False
        self.nc = bass.Bass("TRN2", target_bir_lowering=False,
                    dynamic_dma_scratch_size=4096)
        self.es = contextlib.ExitStack()
        self.ops = {k: [] for k in ("tensor", "scalar", "vector", "gpsimd", "sync")}
        self.H0 = 2 * core_id

    def q(self, eng, fn):
        if not self.stopped:
            self.ops[eng].append(fn)

    def qw(self, eng, ctr, val=None):
        v = ctr.v if val is None else val
        if v <= 0:
            return
        h = ctr.h
        self.q(eng, lambda e: e.wait_ge(h, v))

    def phase_done(self, label):
        if self.upto is not None and label == self.upto:
            self.stopped = True

    def pe(self, fn, inc=False):
        if inc:
            h = self.sP.h
            self.sP.v += 1
            self.q("tensor", lambda e: fn(e).then_inc(h, 1))
        else:
            self.q("tensor", fn)
        return self.sP.v

    def ve(self, fn, inc=False):
        if inc:
            h = self.sV.h
            self.sV.v += 1
            self.q("vector", lambda e: fn(e).then_inc(h, 1))
        else:
            self.q("vector", fn)
        return self.sV.v

    def ae(self, fn, inc=False):
        if inc:
            h = self.sA.h
            self.sA.v += 1
            self.q("scalar", lambda e: fn(e).then_inc(h, 1))
        else:
            self.q("scalar", fn)
        return self.sA.v

    def dma(self, eng, out, in_, ctr=None, **kw):
        ctr = ctr if ctr is not None else (self.dG if eng == "gpsimd" else self.dS)
        h = ctr.h
        ctr.v += 16

        def _go(e, out=out, in_=in_, h=h, kw=kw):
            try:
                return e.dma_start(out=out, in_=in_, **kw).then_inc(h, 16)
            except Exception:
                import sys
                print("FAILING DMA out=", str(out)[:300], " in_=", str(in_)[:300],
                      file=sys.stderr)
                raise
        self.q(eng, _go)
        return ctr.v

    def tdma(self, out, in_):
        h = self.dS.h
        self.dS.v += 16
        self.q("sync", lambda e: e.dma_start_transpose(out, in_).then_inc(h, 16))
        return self.dS.v

    def barrier(self):
        pairs = [("tensor", self.sP), ("scalar", self.sA), ("vector", self.sV),
                 ("gpsimd", self.sG)]
        waits = []
        for eng, ctr in pairs:
            h = ctr.h
            pv = ctr.v
            ctr.v += 1
            if eng == "gpsimd":
                self.q(eng, lambda e, h=h, pv=pv: (e.wait_ge(h, pv) if pv > 0 else None,
                                                   e.nop(nofuse=True).then_inc(h, 1)))
            else:
                self.q(eng, lambda e, h=h: e.drain(fusable=False).then_inc(h, 1))
            waits.append((h, ctr.v))
        waits += [(self.dS.h, self.dS.v), (self.dG.h, self.dG.v), (self.dW.h, self.dW.v),
                  (self.dP.h, self.dP.v)]
        for grp in (self.pl, self.px):
            waits += [(s.h, s.v) for s in grp]
        for grp in (self.u2, self.d2, self.c2, self.g2):
            waits += [(s.h, s.v) for s in grp]
        for eng in ("tensor", "scalar", "vector", "gpsimd", "sync"):
            for h, v in waits:
                if v > 0:
                    self.q(eng, lambda e, h=h, v=v: e.wait_ge(h, v))

    def tap(self, name, src_dram):
        if name in self.dbg:
            self.barrier()
            self.dma("sync", self.dbg[name], src_dram)
            self.qw("sync", self.dS)

    def tap_sbuf(self, name, sb_ap):
        if name in self.dbg:
            self.barrier()
            self.dma("sync", self.dbg[name], sb_ap)
            self.qw("sync", self.dS)

    def declare_dram(self):
        nc = self.nc
        dt = nc.dram_tensor
        self.inp = {}
        for nm, shape in [
            ("x_own", [TOWN, D]),
            ("aWq_my", [HPC, D, DH]), ("aWk_my", [HPC, D, DH]), ("aWv_my", [HPC, D, DH]),
            ("abq_my", [HPC, DH]), ("abk_my", [HPC, DH]), ("abv_my", [HPC, DH]),
            ("aWo", [D, D]), ("abo", [D]),
            ("bWq", [D, D]), ("bWk_my", [D, HPC * DH]), ("bWv_my", [D, HPC * DH]),
            ("bWo", [D, D]),
            ("bbq", [D]), ("bbk_my", [HPC * DH]), ("bbv_my", [HPC * DH]), ("bbo", [D]),
            ("su_ug", [D]), ("su_dp", [INTER]), ("wsug", [2 * INTER]), ("wsdp", [D]),
            ("fWupT", [D, INTER]), ("fWgateT", [D, INTER]), ("fWdownT", [INTER, D]),
        ]:
            self.inp[nm] = dt(nm, shape, F32, kind="ExternalInput").ap()
        self.y_own = dt("y_own", [TOWN, D], F32, kind="ExternalOutput").ap()

        self.xstream = dt("xstream", [128, NF * TOWN], F32, kind="Internal").ap()
        self.wupT = dt("wupT", [D, INTER], BF16, kind="Internal").ap()
        self.wgateT = dt("wgateT", [D, INTER], BF16, kind="Internal").ap()
        self.wdT = dt("wdT", [INTER, D], BF16, kind="Internal").ap()
        self.wdH = dt("wdH", [INTER, D], BF16, kind="Internal").ap()
        self.wd2 = dt("wd2", [NF, 128, NI, 128], BF16, kind="Internal").ap()
        self.ag_in = dt("ag_in", [D, TOWN], BF16, kind="Internal").ap()
        self.ag_out = dt("ag_out", [NCORE * D, TOWN], BF16, kind="Internal",
                         addr_space="Shared").ap()
        self.y_in = dt("y_in", [HPC * DH, T], BF16, kind="Internal").ap()
        self.y_out = dt("y_out", [NCORE * HPC * DH, T], BF16, kind="Internal",
                        addr_space="Shared").ap()

        self.dbg = {}
        def tapout(nm, shape, dtp):
            self.dbg[nm] = dt("dbg_" + nm, shape, dtp, kind="ExternalOutput").ap()
        if "stream" in self.taps: tapout("stream", [128, NF * TOWN], F32)
        if "xln" in self.taps: tapout("xln", [128, NF * TOWN], BF16)
        if "G" in self.taps: tapout("G", [NCORE * D, TOWN], BF16)
        if "kvq" in self.taps:
            tapout("K", [128, 2 * T], BF16)
            tapout("V", [128, NI * 256], BF16)
            tapout("Q", [128, 2 * T], BF16)
        if "yout" in self.taps: tapout("yout", [NCORE * HPC * DH, T], BF16)
        if "wd2" in self.taps: tapout("wd2", [NF, 128, NI, 128], BF16)
        if "x1" in self.taps: tapout("x1", [128, NF * TOWN], F32)
        if "h" in self.taps: tapout("h", [128, 4 * 16384], BF16)

    def declare_inline(self):
        nc = self.nc
        self.h128_t = nc.inline_tensor(_hmat(128).astype(BF), name="h128i")
        self.h64_t = nc.inline_tensor(_hmat(64).astype(BF), name="h64i")
        self.id128_t = nc.inline_tensor(np.eye(128, dtype=np.float32), name="id128i")
        self.onescol_t = nc.inline_tensor(np.ones((128, 1), dtype=np.float32).astype(BF),
                                          name="onescoli")
        self.onesrow_t = nc.inline_tensor(np.ones((1, 128), dtype=np.float32), name="onesrowi")
        m = np.zeros((128, 4, 512), dtype=np.float32)
        for k in range(4):
            for p in range(128):
                m[p, k, 128 * k + p:] = 1.0
        self.masks_t = nc.inline_tensor(m.reshape(128, 2048).astype(BF), name="masksi")

    def alloc_sbuf(self):
        nc = self.nc
        A = nc.alloc_sbuf_tensor_at
        BASE = (nc.sbuf_base + 31) & ~31
        assert BASE <= 4224, BASE
        self.B = [A(f"B{i}", [128, 16384], BF16, offset=BASE + 32768 * i) for i in range(6)]
        o = BASE + 32768 * 6

        def at(name, shape, dtype):
            nonlocal o
            t = A(name, shape, dtype, offset=o)
            o += int(np.prod(shape[1:])) * (4 if dtype == F32 else 2)
            o = (o + 31) & ~31
            return t

        self.io32 = at("io32", [128, 2048], F32)
        self.ptb = at("ptb", [128, 2048], BF16)
        self.maskb = None
        self.rows32 = at("rows32", [1, 2048], F32)
        self.silu_sb = at("silu_sb", [128, 1024], BF16)
        self.ysb = self.silu_sb
        self.h128_sb = at("h128_sb", [128, 128], BF16)
        self.h64_sb = at("h64_sb", [64, 64], BF16)
        self.id128_sb = at("id128_sb", [128, 128], F32)
        self.onescol_sb = at("onescol_sb", [128, 1], BF16)
        self.onesrow_sb = at("onesrow_sb", [1, 128], F32)
        self.cbias = at("cbias", [128, 32], F32)
        self.sucol = at("sucol", [128, 16], F32)
        self.wsgcol = at("wsgcol", [128, 64], F32)
        self.wsucol = at("wsucol", [128, 64], F32)
        self.wsdcol = at("wsdcol", [128, 16], F32)
        self.bcol = {}
        for nm in ("abo", "bbq", "bbo"):
            self.bcol[nm] = at(f"col_{nm}", [128, 16], F32)
        for nm in ("abq_my", "abk_my", "bbk_my"):
            self.bcol[nm] = at(f"col_{nm}", [128, 2], F32)
        self.bvb = at("bvb", [128, 256], BF16)
        self.r1 = at("r1", [1, 512], F32)
        self.r2 = at("r2", [1, 32], F32)
        self.qb1 = at("qb1", [1, 64], F32)
        self.sqs = self.ptb
        self.tmpcol = at("tmpcol", [128, 64], F32)
        assert o <= 229344, f"sbuf overflow {o}"
        self.sbuf_top = o

        self.rb = self.io32.ap()[:, 1536:2048]
        io_off = BASE + 32768 * 6
        self.maskb = A("maskb", [128, 2048], BF16, offset=io_off)
        self.B0f = A("B0f", [128, 8192], F32, offset=BASE)
        self.x1f = A("x1f", [128, 16384], F32, offset=BASE + 32768 * 4)
        self.gbuf = [A(f"gbuf{i}", [128, 1024], BF16, offset=io_off + 8192 + 2048 * i)
                     for i in range(2)]

        self.p01 = self.es.enter_context(nc.psum_tensor("p01", [128, 1024], F32))
        self.p23 = self.es.enter_context(nc.psum_tensor("p23", [128, 1024], F32))
        self.p45 = self.es.enter_context(nc.psum_tensor("p45", [128, 1024], F32))
        self.p67 = self.es.enter_context(nc.psum_tensor("p67", [128, 1024], F32))

    def make_sems(self):
        nc, es = self.nc, self.es
        self.dS = Ctr(es, nc, "dS")
        self.dG = Ctr(es, nc, "dG")
        self.dW = Ctr(es, nc, "dW")
        self.sP = Ctr(es, nc, "sP")
        self.sA = Ctr(es, nc, "sA")
        self.sV = Ctr(es, nc, "sV")
        self.sG = Ctr(es, nc, "sG")
        self.cc = Ctr(es, nc, "cc")
        self.u2 = [Ctr(es, nc, "u0"), Ctr(es, nc, "u1")]
        self.d2 = [Ctr(es, nc, "d0"), Ctr(es, nc, "d1")]
        self.c2 = [Ctr(es, nc, "c0"), Ctr(es, nc, "c1")]
        self.g2 = [Ctr(es, nc, "g0"), Ctr(es, nc, "g1")]
        self.dP = Ctr(es, nc, "dP")
        self.pl = [Ctr(es, nc, "pl0"), Ctr(es, nc, "pl1")]
        self.px = [Ctr(es, nc, "px0"), Ctr(es, nc, "px1")]

    def ph_consts(self):
        for dst, src in [
            (self.h128_sb, self.h128_t), (self.h64_sb, self.h64_t),
            (self.id128_sb, self.id128_t), (self.onescol_sb, self.onescol_t),
            (self.onesrow_sb, self.onesrow_t),
        ]:
            self.dma("sync", dst[:], src.ap())

        def vec2col(flat, nch, dst, mul=None):
            src = flat.rearrange("(c p) -> c p", p=128) if len(flat.shape) == 1 else flat
            self.qw("sync", self.sV)
            self.dma("sync", self.io32[0:nch, 0:128], src)
            self.qw("tensor", self.dS)
            self.qw("tensor", self.sV)
            self.pe(lambda e, nch=nch: e.transpose(self.p01[:, 0:nch], self.io32[0:nch, 0:128],
                                                   self.id128_sb[0:nch, 0:nch]), inc=True)
            self.qw("vector", self.sP)
            if mul is None:
                self.ve(lambda e, d=dst, nch=nch: e.tensor_copy(out=d[:, 0:nch],
                                                                in_=self.p01[:, 0:nch]), inc=True)
            else:
                self.ve(lambda e, d=dst, nch=nch, m=mul: e.tensor_scalar(
                    out=d[:, 0:nch], in0=self.p01[:, 0:nch], scalar1=float(m), scalar2=None,
                    op0=ALU.mult), inc=True)

        vec2col(self.inp["su_ug"], 16, self.sucol)
        vec2col(self.inp["wsdp"], 16, self.wsdcol, mul=R8192)
        for nm in ("abo", "bbq", "bbo"):
            vec2col(self.inp[nm], 16, self.bcol[nm])
        for nm in ("abq_my", "abk_my", "bbk_my"):
            vec2col(self.inp[nm], 2, self.bcol[nm])
        vec2col(self.inp["su_dp"], 64, self.tmpcol)
        vec2col(self.inp["wsug"][0:INTER], 64, self.wsucol, mul=R2048)
        self.qw("vector", self.sV)
        self.ve(lambda e: e.tensor_tensor(out=self.wsucol[:], in0=self.wsucol[:],
                                          in1=self.tmpcol[:], op=ALU.mult), inc=True)
        vec2col(self.inp["wsug"][INTER:2 * INTER], 64, self.wsgcol, mul=R2048)
        self.barrier()
        self.phase_done("consts")

    def _transpose_tiles_to(self, dst_is_output):
        xs_v = self.xstream.rearrange("p (c t) -> p c t", c=NF, t=TOWN)
        for tc in range(8):
            self.qw("sync", self.sV)
            if dst_is_output:
                self.dma("sync", self.io32.ap().rearrange("p (c t) -> p c t", c=NF, t=128),
                         xs_v[:, :, tc * 128:(tc + 1) * 128])
            else:
                self.dma("sync", self.io32[:], self.inp["x_own"][tc * 128:(tc + 1) * 128, :])
            wS = self.dS.v
            evs = {}
            pairs = [self.p01, self.p23, self.p45, self.p67]
            for fc in range(16):
                slot = fc % 8
                ps = pairs[slot % 4]
                col = 512 * (slot // 4)
                self.qw("tensor", self.dS, wS)
                if fc >= 8:
                    self.qw("tensor", self.sV, evs[fc - 8])
                self.pe(lambda e, fc=fc, ps=ps, col=col: e.transpose(
                    ps[:, col:col + 128], self.io32[:, fc * 128:(fc + 1) * 128],
                    self.id128_sb[:]), inc=True)
                self.qw("vector", self.sP)
                if fc == 0:
                    self.qw("vector", self.dS)
                evs[fc] = self.ve(lambda e, fc=fc, ps=ps, col=col: e.tensor_copy(
                    out=self.B0f[:, fc * 128:(fc + 1) * 128],
                    in_=ps[:, col:col + 128]), inc=True)
            self.qw("sync", self.sV)
            if dst_is_output:
                self.dma("sync", self.y_own[tc * 128:(tc + 1) * 128, :], self.B0f[:, 0:2048])
            else:
                src = self.B0f.ap()[:, 0:2048].rearrange("p (c t) -> p c t", c=NF, t=128)
                self.dma("sync", xs_v[:, :, tc * 128:(tc + 1) * 128], src)
        self.barrier()

    def ph_input(self):
        self._transpose_tiles_to(dst_is_output=False)
        self.tap("stream", self.xstream)
        self.phase_done("input")

    def ph_output(self):
        self._transpose_tiles_to(dst_is_output=True)
        for eng in ("tensor", "scalar", "vector", "gpsimd", "sync"):
            self.qw(eng, self.dS)

    def ph_prep_casts(self):
        for dst, wsrc in ((self.wupT, self.inp["fWupT"]), (self.wgateT, self.inp["fWgateT"]),
                          (self.wdT, self.inp["fWdownT"])):
            self.dma("gpsimd", dst, wsrc)
        self.prep_cast_w = self.dG.v

    def ph_prep_mms(self, barriers=True):
        stage = self.B[1]
        self.qw("scalar", self.dG, self.prep_cast_w)
        self.qw("scalar", self.sP)
        wdT_v = self.wdT.rearrange("(cc p) c -> cc p c", cc=64, p=128)
        wdH_v = self.wdH.rearrange("(cp p) c -> cp p c", cp=64, p=128)

        def pipelined_pass(n_iter, nrow, ld_src, st_dst, hmat_sb):
            self.qw("tensor", self.sV)
            self.qw("tensor", self.sA)
            LD = [stage.ap()[0:nrow, 0:2048], stage.ap()[0:nrow, 2048:4096]]
            EV = [stage.ap()[0:nrow, 4096:6144], stage.ap()[0:nrow, 6144:8192]]
            pse = [(self.p01, self.p23), (self.p45, self.p67)]
            mmv, evv, stv = {}, {}, {}
            for p in range(n_iter):
                q = p % 2
                if p >= 2:
                    self.qw("scalar", self.sP, mmv[p - 2])
                self.dma("scalar", LD[q], ld_src(p), ctr=self.pl[q])
                wL = self.pl[q].v
                self.qw("tensor", self.pl[q], wL)
                if p >= 2:
                    self.qw("tensor", self.sV, evv[p - 2])
                for k in range(4):
                    ps = pse[q][k // 2][0:nrow, (k % 2) * 512:(k % 2) * 512 + 512]
                    self.pe(lambda e, ps=ps, k=k, q=q, hm=hmat_sb: e.matmul(
                        ps, hm, LD[q][0:nrow, k * 512:(k + 1) * 512],
                        start=True, stop=True), inc=(k == 3))
                mmv[p] = self.sP.v
                self.qw("vector", self.sP, mmv[p])
                if p >= 2:
                    self.qw("vector", self.px[q], stv[p - 2])
                self.ve(lambda e, q=q, nrow=nrow: e.tensor_copy(
                    out=EV[q][0:nrow, 0:1024], in_=pse[q][0][0:nrow, :]))
                evv[p] = self.ve(lambda e, q=q, nrow=nrow: e.tensor_copy(
                    out=EV[q][0:nrow, 1024:2048], in_=pse[q][1][0:nrow, :]), inc=True)
                self.qw("scalar", self.sV, evv[p])
                st_dst(p, EV[q], self.px[q])
                stv[p] = self.px[q].v

        pipelined_pass(
            128, 64,
            lambda p: wdT_v[:, p, :],
            lambda p, ev, ctr: self.dma("scalar", wdH_v[:, p, :], ev[0:64, 0:2048], ctr=ctr),
            self.h64_sb[:])
        self.qw("scalar", self.px[0])
        self.qw("scalar", self.px[1])

        def h128_store(ic, ev, ctr):
            for ct in range(16):
                self.dma("scalar", self.wd2[ct, :, ic, :],
                         ev[:, ct * 128:(ct + 1) * 128], ctr=ctr)

        pipelined_pass(
            NI, 128,
            lambda ic: self.wdH[ic * 128:(ic + 1) * 128, :],
            h128_store,
            self.h128_sb[:])
        self.prep_mms_done = self.dP.v
        self.tap("wd2", self.wd2)
        self.phase_done("prep")

    def ph_ln(self, with_su):
        xln = self.B[4]
        sums_lo, sums_hi = self.p67[0:1, 0:512], self.p67[0:1, 512:1024]
        sq_lo, sq_hi = self.p45[0:1, 0:512], self.p45[0:1, 512:1024]
        for c in range(NF):
            self.qw("sync", self.sV)
            self.dma("sync", self.io32[:, 0:1024], self.xstream[:, c * 1024:(c + 1) * 1024])
            self.qw("vector", self.dS)
            self.ve(lambda e, c=c: e.tensor_copy(out=xln[:, c * 1024:(c + 1) * 1024],
                                                 in_=self.io32[:, 0:1024]), inc=True)
            self.qw("scalar", self.sV)
            self.qw("scalar", self.sP)
            self.ae(lambda e, c=c: e.activation(out=self.sqs[:, 0:1024],
                    in_=xln[:, c * 1024:(c + 1) * 1024], func=AF.Square), inc=True)
            self.qw("tensor", self.sV)
            self.pe(lambda e, c=c: e.matmul(sums_lo, self.onescol_sb[:],
                    xln[:, c * 1024:c * 1024 + 512], start=(c == 0), stop=(c == NF - 1)))
            self.pe(lambda e, c=c: e.matmul(sums_hi, self.onescol_sb[:],
                    xln[:, c * 1024 + 512:(c + 1) * 1024], start=(c == 0), stop=(c == NF - 1)))
            self.qw("tensor", self.sA)
            self.pe(lambda e, c=c: e.matmul(sq_lo, self.onescol_sb[:], self.sqs[:, 0:512],
                    start=(c == 0), stop=(c == NF - 1)))
            self.pe(lambda e, c=c: e.matmul(sq_hi, self.onescol_sb[:], self.sqs[:, 512:1024],
                    start=(c == 0), stop=(c == NF - 1)), inc=True)
        mu = self.rows32[0:1, 0:1024]
        ex2 = self.rows32[0:1, 1024:2048]
        self.qw("vector", self.sP)
        self.ve(lambda e: e.tensor_scalar(out=mu, in0=self.p67[0:1, :], scalar1=1.0 / 2048.0,
                                          scalar2=None, op0=ALU.mult), inc=True)
        self.qw("vector", self.sV)
        self.ve(lambda e: e.tensor_scalar(out=ex2, in0=self.p45[0:1, :], scalar1=1.0 / 2048.0,
                                          scalar2=float(EPS), op0=ALU.mult, op1=ALU.add),
                inc=True)
        for k in range(2):
            self.qw("vector", self.sV)
            self.ve(lambda e, k=k: e.scalar_tensor_tensor(
                out=self.r1[0:1, 0:512], in0=mu[0:1, k * 512:(k + 1) * 512], scalar=-1.0,
                in1=mu[0:1, k * 512:(k + 1) * 512], op0=ALU.mult, op1=ALU.mult), inc=True)
            self.qw("vector", self.sV)
            self.ve(lambda e, k=k: e.tensor_tensor(
                out=ex2[0:1, k * 512:(k + 1) * 512], in0=ex2[0:1, k * 512:(k + 1) * 512],
                in1=self.r1[0:1, 0:512], op=ALU.add), inc=True)
        self.qw("scalar", self.sV)
        self.ae(lambda e: e.activation(out=ex2, in_=ex2, func=AF.Ln), inc=True)
        self.qw("scalar", self.sA)
        self.ae(lambda e: e.activation(out=ex2, in_=ex2, func=AF.Exp, scale=-0.5), inc=True)
        self.qw("tensor", self.sA)
        for k in range(2):
            self.pe(lambda e, k=k: e.matmul(self.p01[:, k * 512:(k + 1) * 512],
                    self.onesrow_sb[:], mu[0:1, k * 512:(k + 1) * 512], start=True, stop=True))
            self.pe(lambda e, k=k: e.matmul(self.p23[:, k * 512:(k + 1) * 512],
                    self.onesrow_sb[:], ex2[0:1, k * 512:(k + 1) * 512], start=True, stop=True),
                    inc=(k == 1))
        wbc = self.sP.v
        for c in range(NF):
            self.qw("vector", self.sP, wbc)
            self.qw("vector", self.sV)
            self.ve(lambda e, c=c: e.tensor_tensor(out=self.io32[:, 0:1024],
                    in0=xln[:, c * 1024:(c + 1) * 1024], in1=self.p01[:], op=ALU.subtract),
                    inc=True)
            self.qw("vector", self.sV)
            su = self.sucol[:, c:c + 1] if with_su else 1.0
            self.ve(lambda e, c=c, su=su: e.scalar_tensor_tensor(
                out=xln[:, c * 1024:(c + 1) * 1024], in0=self.io32[:, 0:1024],
                scalar=su, in1=self.p23[:], op0=ALU.mult, op1=ALU.mult), inc=True)
        self.barrier()

    def ph_ffn(self):
        self.ph_ln(with_su=True)
        self.tap_sbuf("xln", self.B[4][:])
        xln = self.B[4]
        mm_done = {}
        ev_done = {}

        def maybe_evac(cur_c):
            for k in range(NF - 1, -1, -1):
                if k in mm_done and k not in ev_done and 2 * k >= cur_c:
                    psk = mm_done[k][1]
                    self.qw("vector", self.sP, mm_done[k][0])
                    ev_done[k] = self.ve(lambda e, k=k, psk=psk: e.tensor_copy(
                        out=self.x1f[:, k * 1024:(k + 1) * 1024], in_=psk[:]), inc=True)

        for idx, c in enumerate(range(NF - 1, -1, -1)):
            ps = self.p01 if (idx % 2 == 0) else self.p23
            if idx >= 2:
                prev_k = NF - 1 - (idx - 2)
                self.qw("tensor", self.sV, ev_done[prev_k])
            self.pe(lambda e, c=c, ps=ps: e.matmul(ps[:, 0:512], self.h128_sb[:],
                    xln[:, c * 1024:c * 1024 + 512], start=True, stop=True))
            v = self.pe(lambda e, c=c, ps=ps: e.matmul(ps[:, 512:1024], self.h128_sb[:],
                    xln[:, c * 1024 + 512:(c + 1) * 1024], start=True, stop=True), inc=True)
            mm_done[c] = (v, ps)
            maybe_evac(c)
        maybe_evac(-100)
        for s in (1, 2, 4, 8):
            for a0 in range(NF):
                if a0 & s:
                    continue
                b0 = a0 + s
                self.qw("vector", self.sV)
                self.ve(lambda e, a0=a0: e.tensor_copy(out=self.io32[:, 0:1024],
                        in_=self.x1f[:, a0 * 1024:(a0 + 1) * 1024]), inc=True)
                self.qw("vector", self.sV)
                self.ve(lambda e, a0=a0, b0=b0: e.tensor_tensor(
                    out=self.x1f[:, a0 * 1024:(a0 + 1) * 1024],
                    in0=self.x1f[:, a0 * 1024:(a0 + 1) * 1024],
                    in1=self.x1f[:, b0 * 1024:(b0 + 1) * 1024], op=ALU.add), inc=True)
                self.qw("vector", self.sV)
                self.ve(lambda e, a0=a0, b0=b0: e.tensor_tensor(
                    out=self.x1f[:, b0 * 1024:(b0 + 1) * 1024], in0=self.io32[:, 0:1024],
                    in1=self.x1f[:, b0 * 1024:(b0 + 1) * 1024], op=ALU.subtract), inc=True)
        self.tap_sbuf("x1", self.x1f[:])
        x1b = self.B[4]
        self.qw("vector", self.sV)
        self.ve(lambda e: e.tensor_copy(out=self.ptb[:, 0:1024], in_=self.x1f[:, 0:1024]),
                inc=True)
        for c in range(1, NF):
            self.qw("vector", self.sV)
            self.ve(lambda e, c=c: e.tensor_copy(out=x1b[:, c * 1024:(c + 1) * 1024],
                    in_=self.x1f[:, c * 1024:(c + 1) * 1024]), inc=True)
        self.qw("vector", self.sV)
        self.ve(lambda e: e.tensor_copy(out=x1b[:, 0:1024], in_=self.ptb[:, 0:1024]), inc=True)
        self.barrier()

        upbuf = [self.B[5].ap()[:, 0:4096], self.B[5].ap()[:, 4096:8192]]
        gabuf = [self.B[5].ap()[:, 8192:12288], self.B[5].ap()[:, 12288:16384]]
        wupT_v = self.wupT.rearrange("(f p) i -> p f i", f=16, p=128)
        wgaT_v = self.wgateT.rearrange("(f p) i -> p f i", f=16, p=128)

        def h_ap(ic):
            return self.B[ic // 16][:, (ic % 16) * 1024:(ic % 16) * 1024 + 1024]

        nslab = NI // 2
        mm_after_slab = {}
        slab_w = {}
        hdone = {}
        for sl in range(nslab):
            if sl >= 2:
                self.qw("sync", self.sP, mm_after_slab[sl - 2])
            usem = self.u2[sl % 2]
            for half in range(2):
                self.dma("sync", upbuf[sl % 2][:, half * 2048:(half + 1) * 2048].rearrange(
                    "p (f i) -> p f i", f=8, i=256),
                    wupT_v[:, half * 8:(half + 1) * 8, sl * 256:(sl + 1) * 256], ctr=usem)
                self.dma("sync", gabuf[sl % 2][:, half * 2048:(half + 1) * 2048].rearrange(
                    "p (f i) -> p f i", f=8, i=256),
                    wgaT_v[:, half * 8:(half + 1) * 8, sl * 256:(sl + 1) * 256], ctr=usem)
            slab_w[sl] = usem.v
            for j in range(2):
                ic = sl * 2 + j
                pu = self.p01 if (ic % 2 == 0) else self.p45
                pg = self.p23 if (ic % 2 == 0) else self.p67
                self.qw("tensor", self.u2[sl % 2], slab_w[sl])
                if ic >= 2:
                    self.qw("tensor", self.sV, hdone[ic - 2][0])
                    self.qw("tensor", self.sA, hdone[ic - 2][1])
                for f in range(16):
                    off = (f // 8) * 2048 + (f % 8) * 512 + j * 256
                    lhs_u = upbuf[sl % 2][:, off:off + 128] if False else None
                    hh = f // 8
                    base = hh * 2048 + (f % 8) * 256 + j * 128
                    lu = upbuf[sl % 2][:, base:base + 128]
                    lg = gabuf[sl % 2][:, base:base + 128]
                    self.pe(lambda e, f=f, pu=pu, lu=lu: e.matmul(
                        pu[:, 0:512], lu, x1b[:, f * 1024:f * 1024 + 512],
                        start=(f == 0), stop=(f == 15)))
                    self.pe(lambda e, f=f, pu=pu, lu=lu: e.matmul(
                        pu[:, 512:1024], lu, x1b[:, f * 1024 + 512:(f + 1) * 1024],
                        start=(f == 0), stop=(f == 15)))
                    self.pe(lambda e, f=f, pg=pg, lg=lg: e.matmul(
                        pg[:, 0:512], lg, x1b[:, f * 1024:f * 1024 + 512],
                        start=(f == 0), stop=(f == 15)))
                    self.pe(lambda e, f=f, pg=pg, lg=lg: e.matmul(
                        pg[:, 512:1024], lg, x1b[:, f * 1024 + 512:(f + 1) * 1024],
                        start=(f == 0), stop=(f == 15)), inc=(f == 15))
                wmm = self.sP.v
                mm_after_slab[sl] = wmm
                self.qw("scalar", self.sP, wmm)
                if ic >= 1:
                    self.qw("scalar", self.sV, hdone[ic - 1][0])
                vA = self.ae(lambda e, ic=ic, pg=pg: e.activation(
                    out=self.silu_sb[:], in_=pg[:], func=AF.Silu,
                    scale=self.wsgcol[:, ic:ic + 1]), inc=True)
                self.qw("vector", self.sA, vA)
                vV = self.ve(lambda e, ic=ic, pu=pu: e.scalar_tensor_tensor(
                    out=h_ap(ic), in0=pu[:], scalar=self.wsucol[:, ic:ic + 1],
                    in1=self.silu_sb[:], op0=ALU.mult, op1=ALU.mult), inc=True)
                hdone[ic] = (vV, vA)
        self.barrier()
        if "h" in self.dbg:
            self.dma("sync", self.dbg["h"].rearrange("p (r n) -> r p n", r=4, n=16384)[0], self.B[0][:])
            self.dma("sync", self.dbg["h"].rearrange("p (r n) -> r p n", r=4, n=16384)[1], self.B[1][:])
            self.dma("sync", self.dbg["h"].rearrange("p (r n) -> r p n", r=4, n=16384)[2], self.B[2][:])
            self.dma("sync", self.dbg["h"].rearrange("p (r n) -> r p n", r=4, n=16384)[3], self.B[3][:])
            self.qw("sync", self.dS)

        wdb = [self.B[5].ap()[:, 0:8192], self.B[5].ap()[:, 8192:16384]]
        dn_mm = {}
        for ct in range(NF):
            if ct >= 2:
                self.qw("sync", self.sP, dn_mm[ct - 2])
            self.dma("sync", wdb[ct % 2], self.wd2[ct].rearrange("p ic c -> p (ic c)"),
                     ctr=self.d2[ct % 2])
            wW = self.d2[ct % 2].v
            ps = self.p01 if (ct % 2 == 0) else self.p23
            self.qw("tensor", self.d2[ct % 2], wW)
            self.qw("tensor", self.sV)
            for ic in range(NI):
                self.pe(lambda e, ct=ct, ic=ic, ps=ps: e.matmul(
                    ps[:, 0:512], wdb[ct % 2][:, ic * 128:(ic + 1) * 128],
                    h_ap(ic)[:, 0:512], start=(ic == 0), stop=(ic == NI - 1)))
                self.pe(lambda e, ct=ct, ic=ic, ps=ps: e.matmul(
                    ps[:, 512:1024], wdb[ct % 2][:, ic * 128:(ic + 1) * 128],
                    h_ap(ic)[:, 512:1024], start=(ic == 0), stop=(ic == NI - 1)),
                    inc=(ic == NI - 1))
            wmm = self.sP.v
            dn_mm[ct] = wmm
            self.qw("sync", self.sV)
            self.dma("sync", self.io32[:, 0:1024], self.xstream[:, ct * 1024:(ct + 1) * 1024])
            wS = self.dS.v
            self.qw("vector", self.sP, wmm)
            self.qw("vector", self.dS, wS)
            self.ve(lambda e, ct=ct, ps=ps: e.scalar_tensor_tensor(
                out=self.io32[:, 1024:2048], in0=ps[:], scalar=self.wsdcol[:, ct:ct + 1],
                in1=self.io32[:, 0:1024], op0=ALU.mult, op1=ALU.add), inc=True)
            self.qw("sync", self.sV)
            self.dma("sync", self.xstream[:, ct * 1024:(ct + 1) * 1024], self.io32[:, 1024:2048])
        self.barrier()
        self.tap("stream", self.xstream)

    def _colslab_proj(self, wsrc, bias_col, out_cb, rhs, nk=NF, wcols=None):
        wcb = [self.B[5].ap()[:, 0:2048], self.B[5].ap()[:, 2048:4096]]
        mmv = {}
        for ct in range(NF):
            if ct >= 2:
                self.qw("gpsimd", self.sP, mmv[ct - 2])
            src = wsrc.rearrange("(k p) (ct c) -> p ct k c", p=128, k=nk, ct=NF, c=128)
            self.dma("gpsimd", wcb[ct % 2].rearrange("p (k c) -> p k c", k=nk, c=128),
                     src[:, ct, :, :], ctr=self.c2[ct % 2])
            wW = self.c2[ct % 2].v
            ps = self.p01 if (ct % 2 == 0) else self.p23
            self.qw("tensor", self.c2[ct % 2], wW)
            self.qw("tensor", self.sV)
            for k in range(nk):
                self.pe(lambda e, k=k, ct=ct, ps=ps: e.matmul(
                    ps[:, 0:512], wcb[ct % 2][:, k * 128:(k + 1) * 128], rhs(k, 0),
                    start=(k == 0), stop=(k == nk - 1)))
                self.pe(lambda e, k=k, ct=ct, ps=ps: e.matmul(
                    ps[:, 512:1024], wcb[ct % 2][:, k * 128:(k + 1) * 128], rhs(k, 1),
                    start=(k == 0), stop=(k == nk - 1)), inc=(k == nk - 1))
            wmm = self.sP.v
            mmv[ct] = wmm
            out_cb(ct, ps, wmm)

    def ph_attn(self, variant, mid_hook=None):
        a = variant == "a"
        self.ph_ln(with_su=False)
        self.tap_sbuf("xln", self.B[4][:])
        xln = self.B[4]
        agin_v = self.ag_in.rearrange("(c p) t -> p c t", c=NF, p=128)
        self.dma("gpsimd", agin_v, xln.ap().rearrange("p (c t) -> p c t", c=NF, t=TOWN))
        self.qw("gpsimd", self.dG)
        hcc = self.cc.h
        self.cc.v += 1
        rg = [list(range(NCORE))]
        self.q("gpsimd", lambda e: e.collective_compute(
            "AllGather", ALU.bypass, replica_groups=rg,
            ins=[self.ag_in.opt()], outs=[self.ag_out.opt()]).then_inc(hcc, 1))
        wcc = self.cc.v

        KT, VT, QT, QFL = self.B[0], self.B[1], self.B[2], self.B[3]
        pfx = "a" if a else "b"

        if not a:
            def qfl_out(ct, ps, wmm):
                self.qw("vector", self.sP, wmm)
                self.ve(lambda e, ct=ct, ps=ps: e.tensor_scalar(
                    out=QFL[:, ct * 1024:(ct + 1) * 1024], in0=ps[:],
                    scalar1=self.bcol["bbq"][:, ct:ct + 1], scalar2=None, op0=ALU.add),
                    inc=True)

            self._colslab_proj(self.inp["bWq"], None, qfl_out,
                               lambda f, hh: xln[:, f * 1024 + hh * 512:f * 1024 + hh * 512 + 512])
            self.qw("vector", self.sV)
            qfl_v = QFL.ap().rearrange("p (j u) -> p j u", j=16, u=1024)
            for hl in range(2):
                src = qfl_v[:, :, hl * 512:(hl + 1) * 512].rearrange("p j v -> p v j")
                dst = QT.ap()[:, hl * 8192:(hl + 1) * 8192].rearrange(
                    "p (v j) -> p v j", v=512, j=16)
                self.ve(lambda e, src=src, dst=dst: e.tensor_copy(out=dst, in_=src), inc=True)

        wk_sl = self.B[5].ap()[:, 4096:8192]
        wv_sl = self.B[5].ap()[:, 8192:12288]
        wq_sl = self.B[5].ap()[:, 12288:16384]
        mats = [("k", wk_sl), ("v", wv_sl)] + ([("q", wq_sl)] if a else [])
        for nm, dstb in mats:
            for f in range(NF):
                for hl in range(2):
                    if a:
                        wsrc = self.inp["aW" + nm + "_my"][hl, f * 128:(f + 1) * 128, :]
                    else:
                        wsrc = self.inp["bW" + nm + "_my"][f * 128:(f + 1) * 128,
                                                           hl * 128:(hl + 1) * 128]
                    self.dma("gpsimd", dstb[:, f * 256 + hl * 128:f * 256 + hl * 128 + 128],
                             wsrc, ctr=self.dW)
        wslices = self.dW.v

        if a:
            bsrc = self.inp["abv_my"].rearrange("h d -> (h d)")
        else:
            bsrc = self.inp["bbv_my"]
        self.dma("sync", self.r1[0:1, 0:256], bsrc.rearrange("(o n) -> o n", o=1))
        wb = self.dS.v
        self.qw("tensor", self.dS, wb)
        self.qw("tensor", self.sV)
        self.pe(lambda e: e.matmul(self.p45[:, 0:256], self.onesrow_sb[:], self.r1[0:1, 0:256],
                                   start=True, stop=True), inc=True)
        self.qw("vector", self.sP)
        self.ve(lambda e: e.tensor_copy(out=self.bvb[:], in_=self.p45[:, 0:256]), inc=True)

        self.qw("sync", self.cc, wcc)
        self.qw("tensor", self.cc, wcc)
        self.qw("tensor", self.dW, wslices)

        G_v = self.ag_out.rearrange("(r c p) t -> r c p t", r=NCORE, c=NF, p=128)

        def gload(r, f, par):
            self.dma("sync", self.gbuf[par][:], G_v[r, f], ctr=self.g2[par])
            return self.g2[par].v

        bk = self.bcol["abk_my" if a else "bbk_my"]
        bq = self.bcol["abq_my"] if a else None

        def kq_pass(wsl, dstT, bias_col):
            gmm = []
            for r in range(NCORE):
                for f in range(NF):
                    par = f % 2
                    n = r * NF + f
                    if n >= 2:
                        self.qw("sync", self.sP, gmm[n - 2])
                    wS = gload(r, f, par)
                    self.qw("tensor", self.g2[par], wS)
                    if f == 0:
                        self.qw("tensor", self.sV)
                    for hl in range(2):
                        ps = self.p01 if hl == 0 else self.p23
                        for t2 in range(2):
                            self.pe(lambda e, f=f, hl=hl, t2=t2, ps=ps, par=par, wsl=wsl:
                                    e.matmul(ps[:, t2 * 512:(t2 + 1) * 512],
                                             wsl[:, f * 256 + hl * 128:f * 256 + hl * 128 + 128],
                                             self.gbuf[par][:, t2 * 512:(t2 + 1) * 512],
                                             start=(f == 0), stop=(f == NF - 1)),
                                    inc=(hl == 1 and t2 == 1))
                    gmm.append(self.sP.v)
                self.qw("vector", self.sP, gmm[-1])
                for hl in range(2):
                    ps = self.p01 if hl == 0 else self.p23
                    bcolv = bias_col[:, hl:hl + 1]
                    for t2 in range(2):
                        self.ve(lambda e, r=r, hl=hl, t2=t2, ps=ps, bcolv=bcolv: e.tensor_scalar(
                            out=dstT[:, hl * 8192 + r * 1024 + t2 * 512:
                                     hl * 8192 + r * 1024 + t2 * 512 + 512],
                            in0=ps[:, t2 * 512:(t2 + 1) * 512], scalar1=bcolv, scalar2=None,
                            op0=ALU.add), inc=(hl == 1 and t2 == 1))

        kq_pass(wk_sl, KT, bk)
        self.qw("sync", self.sP)
        if a:
            kq_pass(wq_sl, QT, bq)
            self.qw("sync", self.sP)

        gmmv = []
        for r in range(NCORE):
            vmm = 0
            for f in range(NF):
                par = f % 2
                n = r * NF + f
                if n >= 2:
                    self.qw("sync", self.sP, gmmv[n - 2])
                wS = gload(r, f, par)
                self.qw("tensor", self.g2[par], wS)
                if f == 0:
                    self.qw("tensor", self.sV)
                for tc in range(8):
                    ps = (self.p01 if tc < 4 else self.p23)
                    ofs = (tc % 4) * 256
                    self.pe(lambda e, f=f, tc=tc, ps=ps, ofs=ofs, par=par: e.matmul(
                        ps[:, ofs:ofs + 256], self.gbuf[par][:, tc * 128:(tc + 1) * 128],
                        wv_sl[:, f * 256:(f + 1) * 256],
                        start=(f == 0 and tc % 2 == 0),
                        stop=(f == NF - 1 and tc % 2 == 1)), inc=(tc == 7))
                vmm = self.sP.v
                gmmv.append(vmm)
            self.qw("vector", self.sP, vmm)
            for tc in range(8):
                ps = (self.p01 if tc < 4 else self.p23)
                ofs = (tc % 4) * 256
                self.ve(lambda e, r=r, tc=tc, ps=ps, ofs=ofs: e.tensor_tensor(
                    out=VT[:, (r * 8 + tc) * 256:(r * 8 + tc) * 256 + 256],
                    in0=ps[:, ofs:ofs + 256], in1=self.bvb[:], op=ALU.add), inc=(tc == 7))
        self.barrier()
        if "K" in self.dbg:
            self.dma("sync", self.dbg["K"], KT[:])
            self.dma("sync", self.dbg["V"], VT[:])
            self.dma("sync", self.dbg["Q"], QT[:])
            self.qw("sync", self.dS)
        if "G" in self.dbg:
            self.tap("G", self.ag_out)
        self.phase_done("proj_" + pfx)

        for hl in range(2):
            for blk in range(16):
                self.qw("scalar", self.sP)
                self.ae(lambda e, hl=hl, blk=blk: e.activation(
                    out=self.sqs[:, 0:512], in_=KT[:, hl * 8192 + blk * 512:hl * 8192 + blk * 512 + 512],
                    func=AF.Square), inc=True)
                self.qw("tensor", self.sA)
                self.qw("tensor", self.sV)
                self.pe(lambda e: e.matmul(self.p45[0:1, 0:512], self.onescol_sb[:],
                        self.sqs[:, 0:512], start=True, stop=True), inc=True)
                self.qw("vector", self.sP)
                self.ve(lambda e, blk=blk: e.tensor_reduce(
                    out=self.r2[0:1, blk:blk + 1], in_=self.p45[0:1, 0:512], axis=AX.X,
                    op=ALU.max), inc=True)
            self.qw("vector", self.sV)
            self.ve(lambda e: e.tensor_reduce(out=self.r2[0:1, 16:17], in_=self.r2[0:1, 0:16],
                                              axis=AX.X, op=ALU.max), inc=True)
            for blk in range(16):
                self.qw("scalar", self.sP)
                self.ae(lambda e, hl=hl, blk=blk: e.activation(
                    out=self.sqs[:, 0:512], in_=QT[:, hl * 8192 + blk * 512:hl * 8192 + blk * 512 + 512],
                    func=AF.Square), inc=True)
                self.qw("tensor", self.sA)
                self.qw("tensor", self.sV)
                self.pe(lambda e: e.matmul(self.p45[0:1, 0:512], self.onescol_sb[:],
                        self.sqs[:, 0:512], start=True, stop=True), inc=True)
                self.qw("vector", self.sP)
                self.ve(lambda e, hl=hl, blk=blk: e.tensor_reduce(
                    out=self.qb1[0:1, hl * 16 + blk:hl * 16 + blk + 1],
                    in_=self.p45[0:1, 0:512], axis=AX.X, op=ALU.max), inc=True)
            self.qw("vector", self.sV)
            self.ve(lambda e, hl=hl: e.tensor_scalar(
                out=self.r1[0:1, 0:16], in0=self.qb1[0:1, hl * 16:(hl + 1) * 16],
                scalar1=self.r2[0:1, 16:17], scalar2=float((ATT_SCALE * 1.002) ** 2),
                op0=ALU.mult, op1=ALU.mult), inc=True)
            self.qw("scalar", self.sV)
            self.ae(lambda e: e.activation(out=self.r1[0:1, 0:16], in_=self.r1[0:1, 0:16],
                                           func=AF.Ln), inc=True)
            self.qw("scalar", self.sA)
            self.ae(lambda e: e.activation(out=self.r1[0:1, 0:16], in_=self.r1[0:1, 0:16],
                    func=AF.Exp, scale=0.5), inc=True)
            self.qw("vector", self.sA)
            self.ve(lambda e: e.tensor_scalar(out=self.r1[0:1, 16:32], in0=self.r1[0:1, 0:16],
                    scalar1=-1.0, scalar2=None, op0=ALU.mult), inc=True)
            self.qw("tensor", self.sV)
            self.pe(lambda e: e.matmul(self.p45[:, 0:16], self.onesrow_sb[:],
                    self.r1[0:1, 16:32], start=True, stop=True), inc=True)
            self.qw("vector", self.sP)
            self.ve(lambda e, hl=hl: e.tensor_copy(out=self.cbias[:, hl * 16:(hl + 1) * 16],
                    in_=self.p45[:, 0:16]), inc=True)
        self.barrier()

        self.dma("sync", self.maskb[:], self.masks_t.ap())
        self.qw("sync", self.dS)
        for eng in ("tensor", "scalar", "vector"):
            self.qw(eng, self.dS)
        outT = self.p45[:, 0:512]
        sums = self.p45[0:1, 512:1024]
        for hl in range(2):
            for tb in range(NF):
                nsc = 4 * (tb + 1)
                npair = nsc // 2
                qrhs0 = QT[:, hl * 8192 + tb * 512:hl * 8192 + tb * 512 + 512]
                exp_done = {}
                cons_done = {}
                for pr in range(npair):
                    ps = self.p01 if (pr % 2 == 0) else self.p23
                    par = pr % 2
                    if pr >= 2:
                        self.qw("tensor", self.sA, exp_done[pr - 2])
                    else:
                        self.qw("tensor", self.sA)
                    for k in range(2):
                        sc = 2 * pr + k
                        self.pe(lambda e, sc=sc, k=k, ps=ps, hl=hl, qrhs0=qrhs0: e.matmul(
                            ps[:, k * 512:(k + 1) * 512],
                            KT[:, hl * 8192 + sc * 128:hl * 8192 + sc * 128 + 128],
                            qrhs0, start=True, stop=True), inc=(k == 1))
                    wS = self.sP.v
                    self.qw("scalar", self.sP, wS)
                    if pr >= 2:
                        self.qw("scalar", self.sP, cons_done[pr - 2])
                    exp_done[pr] = self.ae(lambda e, ps=ps, par=par, hl=hl, tb=tb: e.activation(
                        out=self.ptb[:, par * 1024:(par + 1) * 1024], in_=ps[:], func=AF.Exp,
                        scale=ATT_SCALE, bias=self.cbias[:, hl * 16 + tb:hl * 16 + tb + 1]),
                        inc=True)
                    wmask = None
                    for k in range(2):
                        sc = 2 * pr + k
                        if sc >= 4 * tb:
                            kk = sc - 4 * tb
                            self.qw("vector", self.sA, exp_done[pr])
                            wmask = self.ve(lambda e, k=k, par=par, kk=kk: e.tensor_tensor(
                                out=self.ptb[:, par * 1024 + k * 512:par * 1024 + (k + 1) * 512],
                                in0=self.ptb[:, par * 1024 + k * 512:par * 1024 + (k + 1) * 512],
                                in1=self.maskb[:, kk * 512:(kk + 1) * 512], op=ALU.mult),
                                inc=True)
                    if wmask is not None:
                        self.qw("tensor", self.sV, wmask)
                    else:
                        self.qw("tensor", self.sA, exp_done[pr])
                    if pr == 0:
                        self.qw("tensor", self.sA)
                        self.qw("tensor", self.sV)
                    for k in range(2):
                        sc = 2 * pr + k
                        st = (sc == 0)
                        sp = (sc == nsc - 1)
                        self.pe(lambda e, k=k, par=par, st=st, sp=sp: e.matmul(
                            sums, self.onescol_sb[:],
                            self.ptb[:, par * 1024 + k * 512:par * 1024 + (k + 1) * 512],
                            start=st, stop=sp))
                        self.pe(lambda e, k=k, par=par, sc=sc, hl=hl, st=st, sp=sp: e.matmul(
                            outT, VT[:, sc * 256 + hl * 128:sc * 256 + hl * 128 + 128],
                            self.ptb[:, par * 1024 + k * 512:par * 1024 + (k + 1) * 512],
                            start=st, stop=sp),
                            inc=(k == 1))
                    cons_done[pr] = self.sP.v
                wcons = cons_done[npair - 1]
                self.qw("scalar", self.sP, wcons)
                self.ae(lambda e: e.activation(out=self.r1[0:1, 0:512], in_=sums, func=AF.Ln),
                        inc=True)
                self.qw("scalar", self.sA)
                wrec = self.ae(lambda e: e.activation(out=self.r1[0:1, 0:512],
                               in_=self.r1[0:1, 0:512], func=AF.Exp, scale=-1.0), inc=True)
                self.qw("tensor", self.sA, wrec)
                self.qw("tensor", self.sV)
                wbc = self.pe(lambda e: e.matmul(self.p67[:, 0:512], self.onesrow_sb[:],
                              self.r1[0:1, 0:512], start=True, stop=True), inc=True)
                self.qw("vector", self.sP, wbc)
                wrb = self.ve(lambda e: e.tensor_copy(out=self.rb, in_=self.p67[:, 0:512]),
                              inc=True)
                ypar = (hl * NF + tb) % 2
                self.qw("vector", self.sV, wrb)
                self.qw("vector", self.dS)
                wye = self.ve(lambda e, ypar=ypar: e.scalar_tensor_tensor(
                    out=self.ysb[:, ypar * 512:(ypar + 1) * 512], in0=outT, scalar=1.0,
                    in1=self.rb, op0=ALU.mult, op1=ALU.mult), inc=True)
                self.qw("sync", self.sV, wye)
                self.dma("sync", self.y_in[hl * 128:(hl + 1) * 128, tb * 512:(tb + 1) * 512],
                         self.ysb[:, ypar * 512:(ypar + 1) * 512])
        self.barrier()
        self.phase_done("scores_" + pfx)

        hcc = self.cc.h
        self.cc.v += 1
        self.q("gpsimd", lambda e: e.collective_compute(
            "AllGather", ALU.bypass, replica_groups=rg,
            ins=[self.y_in.opt()], outs=[self.y_out.opt()]).then_inc(hcc, 1))
        wcc2 = self.cc.v
        if mid_hook is not None:
            mid_hook()
        self.qw("sync", self.cc, wcc2)
        if "yout" in self.dbg:
            self.dma("sync", self.dbg["yout"], self.y_out)
            self.qw("sync", self.dS)

        ysall = self.B[3]
        if not hasattr(self, "_cid"):
            self._cid = self.nc.partition_id(engines=[mybir.EngineType.SP])
        cid = self._cid
        for k in range(16):
            self.dma("sync", ysall[:, k * 1024:(k + 1) * 1024],
                     self.y_out[k * 128:(k + 1) * 128, bass.ts(cid, TOWN)])
        wY = self.dS.v
        self.qw("tensor", self.dS, wY)
        self.qw("gpsimd", self.dS, wY)
        self.qw("sync", self.dS, wY)
        bo = self.bcol["abo" if a else "bbo"]

        def wo_out(ct, ps, wmm):
            self.qw("sync", self.sV)
            self.dma("sync", self.io32[:, 0:1024], self.xstream[:, ct * 1024:(ct + 1) * 1024])
            wS = self.dS.v
            self.qw("vector", self.sP, wmm)
            self.qw("vector", self.dS, wS)
            self.ve(lambda e, ct=ct, ps=ps: e.scalar_tensor_tensor(
                out=self.io32[:, 1024:2048], in0=ps[:], scalar=bo[:, ct:ct + 1],
                in1=self.io32[:, 0:1024], op0=ALU.add, op1=ALU.add), inc=True)
            self.qw("sync", self.sV)
            self.dma("sync", self.xstream[:, ct * 1024:(ct + 1) * 1024],
                     self.io32[:, 1024:2048])

        self._colslab_proj(self.inp["aWo" if a else "bWo"], None, wo_out,
                           lambda k, hh: ysall[:, k * 1024 + hh * 512:k * 1024 + hh * 512 + 512])
        self.barrier()
        self.tap("stream", self.xstream)
        self.phase_done("attn_" + pfx)

    def build(self, phases):
        self.declare_dram()
        self.declare_inline()
        self.alloc_sbuf()
        self.make_sems()
        for ph in phases:
            if self.stopped:
                break
            ph()
        with self.es:
            block = self.es.enter_context(self.nc.Block())
            for eng in ("tensor", "scalar", "vector", "gpsimd", "sync"):
                getattr(block, eng)(lambda e, eng=eng: [fn(e) for fn in self.ops[eng]])
        return self.nc


def build_core(core_id, taps=(), upto=None, which="full"):
    b = Builder(core_id, taps=taps, upto=upto)
    prep_serial = [b.ph_prep_casts, b.ph_prep_mms, b.barrier]
    if which == "ffn_only":
        phases = [b.ph_consts, b.ph_input] + prep_serial + [b.ph_ffn, b.ph_output]
    elif which == "attn_a_only":
        phases = [b.ph_consts, b.ph_input] + prep_serial + [lambda: b.ph_attn("a"), b.ph_output]
    elif which == "attn_b_only":
        phases = [b.ph_consts, b.ph_input] + prep_serial + [lambda: b.ph_attn("b"), b.ph_output]
    else:
        phases = [b.ph_consts, b.ph_input, b.ph_prep_casts,
                  lambda: b.ph_attn("a", mid_hook=b.ph_prep_mms), b.ph_ffn,
                  lambda: b.ph_attn("b"), b.ph_ffn, b.ph_output]
    return b.build(phases)


def make_in_map(inputs, core_id):
    x = np.asarray(inputs["L0_attn_a_x"]).reshape(T, D)
    h0 = 2 * core_id
    aWq = np.asarray(inputs["L0_attn_a_Wq"])
    aWk = np.asarray(inputs["L0_attn_a_Wk"])
    aWv = np.asarray(inputs["L0_attn_a_Wv"])
    bWk = np.asarray(inputs["L0_attn_b_Wk"])
    bWv = np.asarray(inputs["L0_attn_b_Wv"])
    m = {
        "x_own": np.ascontiguousarray(x[core_id * TOWN:(core_id + 1) * TOWN]),
        "aWq_my": aWq[h0:h0 + 2], "aWk_my": aWk[h0:h0 + 2], "aWv_my": aWv[h0:h0 + 2],
        "abq_my": np.asarray(inputs["L0_attn_a_bq"])[h0:h0 + 2],
        "abk_my": np.asarray(inputs["L0_attn_a_bk"])[h0:h0 + 2],
        "abv_my": np.asarray(inputs["L0_attn_a_bv"])[h0:h0 + 2],
        "aWo": inputs["L0_attn_a_Wo"], "abo": inputs["L0_attn_a_bo"],
        "bWq": inputs["L0_attn_b_Wq"],
        "bWk_my": bWk[:, h0 * 128:(h0 + 2) * 128],
        "bWv_my": bWv[:, h0 * 128:(h0 + 2) * 128],
        "bWo": inputs["L0_attn_b_Wo"],
        "bbq": inputs["L0_attn_b_bq"],
        "bbk_my": np.asarray(inputs["L0_attn_b_bk"])[h0 * 128:(h0 + 2) * 128],
        "bbv_my": np.asarray(inputs["L0_attn_b_bv"])[h0 * 128:(h0 + 2) * 128],
        "bbo": inputs["L0_attn_b_bo"],
        "su_ug": inputs["L0_ffn_SU_ug"], "su_dp": inputs["L0_ffn_SU_dp"],
        "wsug": inputs["L0_ffn_Wscale_ug"], "wsdp": inputs["L0_ffn_Wscale_dp"],
        "fWupT": np.ascontiguousarray(np.asarray(inputs["L0_ffn_W_up"]).T),
        "fWgateT": np.ascontiguousarray(np.asarray(inputs["L0_ffn_W_gate"]).T),
        "fWdownT": np.ascontiguousarray(np.asarray(inputs["L0_ffn_W_down"]).T),
    }
    return {k: np.ascontiguousarray(np.asarray(v, dtype=np.float32)) for k, v in m.items()}


def kernel(**inputs):
    nc = build_core(0)
    in_maps = [make_in_map(inputs, c) for c in range(NCORE)]
    res = run_bass_kernel_spmd(nc, in_maps, core_ids=list(range(NCORE)))
    out = np.concatenate([np.asarray(res.results[c]["y_own"]) for c in range(NCORE)], axis=0)
    return out.reshape(1, T, D).astype(np.float32)


# version 04
# speedup vs baseline: 1.3555x; 1.1220x over previous
import math
import contextlib
import numpy as np
import ml_dtypes

import concourse.bass as bass
import concourse.mybir as mybir
from concourse.bass_utils import run_bass_kernel_spmd

F32 = mybir.dt.float32
BF16 = mybir.dt.bfloat16
AF = mybir.ActivationFunctionType
ALU = mybir.AluOpType
AX = mybir.AxisListType

NCORE = 8
T = 8192
D = 2048
TOWN = 1024
NF = 16
NI = 64
INTER = 8192
HPC = 2
DH = 128
ATT_SCALE = 1.0 / math.sqrt(128.0)
EPS = 1e-5
R2048 = 1.0 / math.sqrt(2048.0)
R8192 = 1.0 / math.sqrt(8192.0)
BF = ml_dtypes.bfloat16


def _hmat(n):
    a = np.arange(n)
    bits = a[:, None] & a[None, :]
    pc = np.zeros((n, n), dtype=np.int64)
    x = bits.copy()
    while x.any():
        pc += x & 1
        x >>= 1
    return np.where(pc % 2 == 0, 1.0, -1.0).astype(np.float32)


class Ctr:
    def __init__(self, es, nc, name):
        self.h = es.enter_context(nc.semaphore(name))
        self.v = 0


class Builder:

    def __init__(self, core_id, taps=(), upto=None):
        self.core_id = core_id
        self.taps = set(taps)
        self.upto = upto
        self.stopped = False
        self.nc = bass.Bass("TRN2", target_bir_lowering=False,
                    dynamic_dma_scratch_size=4096)
        self.es = contextlib.ExitStack()
        self.ops = {k: [] for k in ("tensor", "scalar", "vector", "gpsimd", "sync")}
        self.H0 = 2 * core_id

    def q(self, eng, fn):
        if not self.stopped:
            self.ops[eng].append(fn)

    def qw(self, eng, ctr, val=None):
        v = ctr.v if val is None else val
        if v <= 0:
            return
        h = ctr.h
        self.q(eng, lambda e: e.wait_ge(h, v))

    def phase_done(self, label):
        if self.upto is not None and label == self.upto:
            self.stopped = True

    def pe(self, fn, inc=False):
        if inc:
            h = self.sP.h
            self.sP.v += 1
            self.q("tensor", lambda e: fn(e).then_inc(h, 1))
        else:
            self.q("tensor", fn)
        return self.sP.v

    def ve(self, fn, inc=False):
        if inc:
            h = self.sV.h
            self.sV.v += 1
            self.q("vector", lambda e: fn(e).then_inc(h, 1))
        else:
            self.q("vector", fn)
        return self.sV.v

    def ae(self, fn, inc=False):
        if inc:
            h = self.sA.h
            self.sA.v += 1
            self.q("scalar", lambda e: fn(e).then_inc(h, 1))
        else:
            self.q("scalar", fn)
        return self.sA.v

    def dma(self, eng, out, in_, ctr=None, **kw):
        ctr = ctr if ctr is not None else (self.dG if eng == "gpsimd" else self.dS)
        h = ctr.h
        ctr.v += 16

        def _go(e, out=out, in_=in_, h=h, kw=kw):
            try:
                return e.dma_start(out=out, in_=in_, **kw).then_inc(h, 16)
            except Exception:
                import sys
                print("FAILING DMA out=", str(out)[:300], " in_=", str(in_)[:300],
                      file=sys.stderr)
                raise
        self.q(eng, _go)
        return ctr.v

    def tdma(self, out, in_):
        h = self.dS.h
        self.dS.v += 16
        self.q("sync", lambda e: e.dma_start_transpose(out, in_).then_inc(h, 16))
        return self.dS.v

    def barrier(self):
        pairs = [("tensor", self.sP), ("scalar", self.sA), ("vector", self.sV),
                 ("gpsimd", self.sG)]
        waits = []
        for eng, ctr in pairs:
            h = ctr.h
            pv = ctr.v
            ctr.v += 1
            if eng == "gpsimd":
                self.q(eng, lambda e, h=h, pv=pv: (e.wait_ge(h, pv) if pv > 0 else None,
                                                   e.nop(nofuse=True).then_inc(h, 1)))
            else:
                self.q(eng, lambda e, h=h: e.drain(fusable=False).then_inc(h, 1))
            waits.append((h, ctr.v))
        waits += [(self.dS.h, self.dS.v), (self.dG.h, self.dG.v), (self.dW.h, self.dW.v),
                  (self.dP.h, self.dP.v)]
        for grp in (self.pl, self.px):
            waits += [(s.h, s.v) for s in grp]
        for grp in (self.u2, self.d2, self.c2, self.g2):
            waits += [(s.h, s.v) for s in grp]
        for eng in ("tensor", "scalar", "vector", "gpsimd", "sync"):
            for h, v in waits:
                if v > 0:
                    self.q(eng, lambda e, h=h, v=v: e.wait_ge(h, v))

    def tap(self, name, src_dram):
        if name in self.dbg:
            self.barrier()
            self.dma("sync", self.dbg[name], src_dram)
            self.qw("sync", self.dS)

    def tap_sbuf(self, name, sb_ap):
        if name in self.dbg:
            self.barrier()
            self.dma("sync", self.dbg[name], sb_ap)
            self.qw("sync", self.dS)

    def declare_dram(self):
        nc = self.nc
        dt = nc.dram_tensor
        self.inp = {}
        for nm, shape in [
            ("x_own", [TOWN, D]),
            ("aWq_my", [HPC, D, DH]), ("aWk_my", [HPC, D, DH]), ("aWv_my", [HPC, D, DH]),
            ("abq_my", [HPC, DH]), ("abk_my", [HPC, DH]), ("abv_my", [HPC, DH]),
            ("aWo", [D, D]), ("abo", [D]),
            ("bWq", [D, D]), ("bWk_my", [D, HPC * DH]), ("bWv_my", [D, HPC * DH]),
            ("bWo", [D, D]),
            ("bbq", [D]), ("bbk_my", [HPC * DH]), ("bbv_my", [HPC * DH]), ("bbo", [D]),
            ("su_ug", [D]), ("su_dp", [INTER]), ("wsug", [2 * INTER]), ("wsdp", [D]),
            ("fWupT", [D, INTER]), ("fWgateT", [D, INTER]), ("fWdownT", [INTER, D]),
        ]:
            self.inp[nm] = dt(nm, shape, F32, kind="ExternalInput").ap()
        self.y_own = dt("y_own", [TOWN, D], F32, kind="ExternalOutput").ap()

        self.xstream = dt("xstream", [128, NF * TOWN], F32, kind="Internal").ap()
        self.wupT = dt("wupT", [D, INTER], BF16, kind="Internal").ap()
        self.wgateT = dt("wgateT", [D, INTER], BF16, kind="Internal").ap()
        self.wdT = dt("wdT", [INTER, D], BF16, kind="Internal").ap()
        self.wdH = dt("wdH", [INTER, D], BF16, kind="Internal").ap()
        self.wd2 = dt("wd2", [NF, 128, NI, 128], BF16, kind="Internal").ap()
        self.ag_in = dt("ag_in", [D, TOWN], BF16, kind="Internal").ap()
        self.ag_out = dt("ag_out", [NCORE * D, TOWN], BF16, kind="Internal",
                         addr_space="Shared").ap()
        self.y_in = dt("y_in", [HPC * DH, T], BF16, kind="Internal").ap()
        self.y_out = dt("y_out", [NCORE * HPC * DH, T], BF16, kind="Internal",
                        addr_space="Shared").ap()

        self.dbg = {}
        def tapout(nm, shape, dtp):
            self.dbg[nm] = dt("dbg_" + nm, shape, dtp, kind="ExternalOutput").ap()
        if "stream" in self.taps: tapout("stream", [128, NF * TOWN], F32)
        if "xln" in self.taps: tapout("xln", [128, NF * TOWN], BF16)
        if "G" in self.taps: tapout("G", [NCORE * D, TOWN], BF16)
        if "kvq" in self.taps:
            tapout("K", [128, 2 * T], BF16)
            tapout("V", [128, NI * 256], BF16)
            tapout("Q", [128, 2 * T], BF16)
        if "yout" in self.taps: tapout("yout", [NCORE * HPC * DH, T], BF16)
        if "wd2" in self.taps: tapout("wd2", [NF, 128, NI, 128], BF16)
        if "x1" in self.taps: tapout("x1", [128, NF * TOWN], F32)
        if "h" in self.taps: tapout("h", [128, 4 * 16384], BF16)

    def declare_inline(self):
        nc = self.nc
        self.h128_t = nc.inline_tensor(_hmat(128).astype(BF), name="h128i")
        self.h64_t = nc.inline_tensor(_hmat(64).astype(BF), name="h64i")
        self.id128_t = nc.inline_tensor(np.eye(128, dtype=np.float32), name="id128i")
        self.onescol_t = nc.inline_tensor(np.ones((128, 1), dtype=np.float32).astype(BF),
                                          name="onescoli")
        self.onesrow_t = nc.inline_tensor(np.ones((1, 128), dtype=np.float32), name="onesrowi")
        m = np.zeros((128, 4, 512), dtype=np.float32)
        for k in range(4):
            for p in range(128):
                m[p, k, 128 * k + p:] = 1.0
        self.masks_t = nc.inline_tensor(m.reshape(128, 2048).astype(BF), name="masksi")

    def alloc_sbuf(self):
        nc = self.nc
        A = nc.alloc_sbuf_tensor_at
        BASE = (nc.sbuf_base + 31) & ~31
        assert BASE <= 4224, BASE
        self.B = [A(f"B{i}", [128, 16384], BF16, offset=BASE + 32768 * i) for i in range(6)]
        o = BASE + 32768 * 6

        def at(name, shape, dtype):
            nonlocal o
            t = A(name, shape, dtype, offset=o)
            o += int(np.prod(shape[1:])) * (4 if dtype == F32 else 2)
            o = (o + 31) & ~31
            return t

        self.io32 = at("io32", [128, 2048], F32)
        self.ptb = at("ptb", [128, 2048], BF16)
        self.maskb = None
        self.rows32 = at("rows32", [1, 2048], F32)
        self.silu_sb = at("silu_sb", [128, 1024], BF16)
        self.ysb = self.silu_sb
        self.h128_sb = at("h128_sb", [128, 128], BF16)
        self.h64_sb = at("h64_sb", [64, 64], BF16)
        self.id128_sb = at("id128_sb", [128, 128], F32)
        self.onescol_sb = at("onescol_sb", [128, 1], BF16)
        self.onesrow_sb = at("onesrow_sb", [1, 128], F32)
        self.cbias = at("cbias", [128, 32], F32)
        self.sucol = at("sucol", [128, 16], F32)
        self.wsgcol = at("wsgcol", [128, 64], F32)
        self.wsucol = at("wsucol", [128, 64], F32)
        self.wsdcol = at("wsdcol", [128, 16], F32)
        self.bcol = {}
        for nm in ("abo", "bbq", "bbo"):
            self.bcol[nm] = at(f"col_{nm}", [128, 16], F32)
        for nm in ("abq_my", "abk_my", "bbk_my"):
            self.bcol[nm] = at(f"col_{nm}", [128, 2], F32)
        self.bvb = at("bvb", [128, 256], BF16)
        self.r1 = at("r1", [1, 512], F32)
        self.r2 = at("r2", [1, 32], F32)
        self.qb1 = at("qb1", [1, 64], F32)
        self.sqs = self.ptb
        self.tmpcol = at("tmpcol", [128, 64], F32)
        assert o <= 229344, f"sbuf overflow {o}"
        self.sbuf_top = o

        self.rb = self.io32.ap()[:, 1536:2048]
        io_off = BASE + 32768 * 6
        self.maskb = A("maskb", [128, 2048], BF16, offset=io_off)
        self.B0f = A("B0f", [128, 8192], F32, offset=BASE)
        self.x1f = A("x1f", [128, 16384], F32, offset=BASE + 32768 * 4)
        self.gbuf = [A(f"gbuf{i}", [128, 1024], BF16, offset=io_off + 8192 + 2048 * i)
                     for i in range(2)]

        self.p01 = self.es.enter_context(nc.psum_tensor("p01", [128, 1024], F32))
        self.p23 = self.es.enter_context(nc.psum_tensor("p23", [128, 1024], F32))
        self.p45 = self.es.enter_context(nc.psum_tensor("p45", [128, 1024], F32))
        self.p67 = self.es.enter_context(nc.psum_tensor("p67", [128, 1024], F32))

    def make_sems(self):
        nc, es = self.nc, self.es
        self.dS = Ctr(es, nc, "dS")
        self.dG = Ctr(es, nc, "dG")
        self.dW = Ctr(es, nc, "dW")
        self.sP = Ctr(es, nc, "sP")
        self.sA = Ctr(es, nc, "sA")
        self.sV = Ctr(es, nc, "sV")
        self.sG = Ctr(es, nc, "sG")
        self.cc = Ctr(es, nc, "cc")
        self.u2 = [Ctr(es, nc, "u0"), Ctr(es, nc, "u1")]
        self.d2 = [Ctr(es, nc, "d0"), Ctr(es, nc, "d1")]
        self.c2 = [Ctr(es, nc, "c0"), Ctr(es, nc, "c1")]
        self.g2 = [Ctr(es, nc, "g0"), Ctr(es, nc, "g1")]
        self.dP = Ctr(es, nc, "dP")
        self.pl = [Ctr(es, nc, "pl0"), Ctr(es, nc, "pl1")]
        self.px = [Ctr(es, nc, "px0"), Ctr(es, nc, "px1")]

    def ph_consts(self):
        for dst, src in [
            (self.h128_sb, self.h128_t), (self.h64_sb, self.h64_t),
            (self.id128_sb, self.id128_t), (self.onescol_sb, self.onescol_t),
            (self.onesrow_sb, self.onesrow_t),
        ]:
            self.dma("sync", dst[:], src.ap())

        def vec2col(flat, nch, dst, mul=None):
            src = flat.rearrange("(c p) -> c p", p=128) if len(flat.shape) == 1 else flat
            self.qw("sync", self.sV)
            self.dma("sync", self.io32[0:nch, 0:128], src)
            self.qw("tensor", self.dS)
            self.qw("tensor", self.sV)
            self.pe(lambda e, nch=nch: e.transpose(self.p01[:, 0:nch], self.io32[0:nch, 0:128],
                                                   self.id128_sb[0:nch, 0:nch]), inc=True)
            self.qw("vector", self.sP)
            if mul is None:
                self.ve(lambda e, d=dst, nch=nch: e.tensor_copy(out=d[:, 0:nch],
                                                                in_=self.p01[:, 0:nch]), inc=True)
            else:
                self.ve(lambda e, d=dst, nch=nch, m=mul: e.tensor_scalar(
                    out=d[:, 0:nch], in0=self.p01[:, 0:nch], scalar1=float(m), scalar2=None,
                    op0=ALU.mult), inc=True)

        vec2col(self.inp["su_ug"], 16, self.sucol)
        vec2col(self.inp["wsdp"], 16, self.wsdcol, mul=R8192)
        for nm in ("abo", "bbq", "bbo"):
            vec2col(self.inp[nm], 16, self.bcol[nm])
        for nm in ("abq_my", "abk_my", "bbk_my"):
            vec2col(self.inp[nm], 2, self.bcol[nm])
        vec2col(self.inp["su_dp"], 64, self.tmpcol)
        vec2col(self.inp["wsug"][0:INTER], 64, self.wsucol, mul=R2048)
        self.qw("vector", self.sV)
        self.ve(lambda e: e.tensor_tensor(out=self.wsucol[:], in0=self.wsucol[:],
                                          in1=self.tmpcol[:], op=ALU.mult), inc=True)
        vec2col(self.inp["wsug"][INTER:2 * INTER], 64, self.wsgcol, mul=R2048)
        self.barrier()
        self.phase_done("consts")

    def _transpose_tiles_to(self, dst_is_output):
        xs_v = self.xstream.rearrange("p (c t) -> p c t", c=NF, t=TOWN)
        for tc in range(8):
            self.qw("sync", self.sV)
            if dst_is_output:
                self.dma("sync", self.io32.ap().rearrange("p (c t) -> p c t", c=NF, t=128),
                         xs_v[:, :, tc * 128:(tc + 1) * 128])
            else:
                self.dma("sync", self.io32[:], self.inp["x_own"][tc * 128:(tc + 1) * 128, :])
            wS = self.dS.v
            evs = {}
            pairs = [self.p01, self.p23, self.p45, self.p67]
            for fc in range(16):
                slot = fc % 8
                ps = pairs[slot % 4]
                col = 512 * (slot // 4)
                self.qw("tensor", self.dS, wS)
                if fc >= 8:
                    self.qw("tensor", self.sV, evs[fc - 8])
                self.pe(lambda e, fc=fc, ps=ps, col=col: e.transpose(
                    ps[:, col:col + 128], self.io32[:, fc * 128:(fc + 1) * 128],
                    self.id128_sb[:]), inc=True)
                self.qw("vector", self.sP)
                if fc == 0:
                    self.qw("vector", self.dS)
                evs[fc] = self.ve(lambda e, fc=fc, ps=ps, col=col: e.tensor_copy(
                    out=self.B0f[:, fc * 128:(fc + 1) * 128],
                    in_=ps[:, col:col + 128]), inc=True)
            self.qw("sync", self.sV)
            if dst_is_output:
                self.dma("sync", self.y_own[tc * 128:(tc + 1) * 128, :], self.B0f[:, 0:2048])
            else:
                src = self.B0f.ap()[:, 0:2048].rearrange("p (c t) -> p c t", c=NF, t=128)
                self.dma("sync", xs_v[:, :, tc * 128:(tc + 1) * 128], src)
        self.barrier()

    def ph_input(self):
        self._transpose_tiles_to(dst_is_output=False)
        self.tap("stream", self.xstream)
        self.phase_done("input")

    def ph_output(self):
        self._transpose_tiles_to(dst_is_output=True)
        for eng in ("tensor", "scalar", "vector", "gpsimd", "sync"):
            self.qw(eng, self.dS)

    def ph_prep_casts(self):
        for dst, wsrc in ((self.wupT, self.inp["fWupT"]), (self.wgateT, self.inp["fWgateT"]),
                          (self.wdT, self.inp["fWdownT"])):
            self.dma("gpsimd", dst, wsrc)
        self.prep_cast_w = self.dG.v

    def ph_prep_mms(self, barriers=True):
        stage = self.B[1]
        self.qw("scalar", self.dG, self.prep_cast_w)
        self.qw("scalar", self.sP)
        wdT_v = self.wdT.rearrange("(cc p) c -> cc p c", cc=64, p=128)
        wdH_v = self.wdH.rearrange("(cp p) c -> cp p c", cp=64, p=128)

        def pipelined_pass(n_iter, nrow, ld_src, st_dst, hmat_sb):
            self.qw("tensor", self.sV)
            self.qw("tensor", self.sA)
            LD = [stage.ap()[0:nrow, 0:2048], stage.ap()[0:nrow, 2048:4096]]
            EV = [stage.ap()[0:nrow, 4096:6144], stage.ap()[0:nrow, 6144:8192]]
            pse = [(self.p01, self.p23), (self.p45, self.p67)]
            mmv, evv, stv, ldv = {}, {}, {}, {}
            for p in range(min(2, n_iter)):
                self.dma("scalar", LD[p % 2], ld_src(p), ctr=self.pl[p % 2])
                ldv[p] = self.pl[p % 2].v
            for p in range(n_iter):
                q = p % 2
                self.qw("tensor", self.pl[q], ldv[p])
                if p >= 2:
                    self.qw("tensor", self.sV, evv[p - 2])
                for k in range(4):
                    ps = pse[q][k // 2][0:nrow, (k % 2) * 512:(k % 2) * 512 + 512]
                    self.pe(lambda e, ps=ps, k=k, q=q, hm=hmat_sb: e.matmul(
                        ps, hm, LD[q][0:nrow, k * 512:(k + 1) * 512],
                        start=True, stop=True), inc=(k == 3))
                mmv[p] = self.sP.v
                self.qw("vector", self.sP, mmv[p])
                if p >= 2:
                    self.qw("vector", self.px[q], stv[p - 2])
                self.ve(lambda e, q=q, nrow=nrow: e.tensor_copy(
                    out=EV[q][0:nrow, 0:1024], in_=pse[q][0][0:nrow, :]))
                evv[p] = self.ve(lambda e, q=q, nrow=nrow: e.tensor_copy(
                    out=EV[q][0:nrow, 1024:2048], in_=pse[q][1][0:nrow, :]), inc=True)
                if p + 2 < n_iter:
                    self.qw("scalar", self.sP, mmv[p])
                    self.dma("scalar", LD[q], ld_src(p + 2), ctr=self.pl[q])
                    ldv[p + 2] = self.pl[q].v
                self.qw("scalar", self.sV, evv[p])
                st_dst(p, EV[q], self.px[q])
                stv[p] = self.px[q].v

        pipelined_pass(
            128, 64,
            lambda p: wdT_v[:, p, :],
            lambda p, ev, ctr: self.dma("scalar", wdH_v[:, p, :], ev[0:64, 0:2048], ctr=ctr),
            self.h64_sb[:])
        self.qw("scalar", self.px[0])
        self.qw("scalar", self.px[1])

        wd2_v = self.wd2.rearrange("ct p ic c -> p ct ic c")

        def h128_store(ic, ev, ctr):
            for c4 in range(4):
                self.dma("scalar", wd2_v[:, c4 * 4:(c4 + 1) * 4, ic, :],
                         ev[:, c4 * 512:(c4 + 1) * 512].rearrange(
                             "p (ct c) -> p ct c", ct=4, c=128), ctr=ctr)

        pipelined_pass(
            NI, 128,
            lambda ic: self.wdH[ic * 128:(ic + 1) * 128, :],
            h128_store,
            self.h128_sb[:])
        self.prep_mms_done = self.dP.v
        self.tap("wd2", self.wd2)
        self.phase_done("prep")

    def ph_ln(self, with_su):
        xln = self.B[4]
        sums_lo, sums_hi = self.p67[0:1, 0:512], self.p67[0:1, 512:1024]
        sq_lo, sq_hi = self.p45[0:1, 0:512], self.p45[0:1, 512:1024]
        for c in range(NF):
            self.qw("sync", self.sV)
            self.dma("sync", self.io32[:, 0:1024], self.xstream[:, c * 1024:(c + 1) * 1024])
            self.qw("vector", self.dS)
            self.ve(lambda e, c=c: e.tensor_copy(out=xln[:, c * 1024:(c + 1) * 1024],
                                                 in_=self.io32[:, 0:1024]), inc=True)
            self.qw("scalar", self.sV)
            self.qw("scalar", self.sP)
            self.ae(lambda e, c=c: e.activation(out=self.sqs[:, 0:1024],
                    in_=xln[:, c * 1024:(c + 1) * 1024], func=AF.Square), inc=True)
            self.qw("tensor", self.sV)
            self.pe(lambda e, c=c: e.matmul(sums_lo, self.onescol_sb[:],
                    xln[:, c * 1024:c * 1024 + 512], start=(c == 0), stop=(c == NF - 1)))
            self.pe(lambda e, c=c: e.matmul(sums_hi, self.onescol_sb[:],
                    xln[:, c * 1024 + 512:(c + 1) * 1024], start=(c == 0), stop=(c == NF - 1)))
            self.qw("tensor", self.sA)
            self.pe(lambda e, c=c: e.matmul(sq_lo, self.onescol_sb[:], self.sqs[:, 0:512],
                    start=(c == 0), stop=(c == NF - 1)))
            self.pe(lambda e, c=c: e.matmul(sq_hi, self.onescol_sb[:], self.sqs[:, 512:1024],
                    start=(c == 0), stop=(c == NF - 1)), inc=True)
        mu = self.rows32[0:1, 0:1024]
        ex2 = self.rows32[0:1, 1024:2048]
        self.qw("vector", self.sP)
        self.ve(lambda e: e.tensor_scalar(out=mu, in0=self.p67[0:1, :], scalar1=1.0 / 2048.0,
                                          scalar2=None, op0=ALU.mult), inc=True)
        self.qw("vector", self.sV)
        self.ve(lambda e: e.tensor_scalar(out=ex2, in0=self.p45[0:1, :], scalar1=1.0 / 2048.0,
                                          scalar2=float(EPS), op0=ALU.mult, op1=ALU.add),
                inc=True)
        for k in range(2):
            self.qw("vector", self.sV)
            self.ve(lambda e, k=k: e.scalar_tensor_tensor(
                out=self.r1[0:1, 0:512], in0=mu[0:1, k * 512:(k + 1) * 512], scalar=-1.0,
                in1=mu[0:1, k * 512:(k + 1) * 512], op0=ALU.mult, op1=ALU.mult), inc=True)
            self.qw("vector", self.sV)
            self.ve(lambda e, k=k: e.tensor_tensor(
                out=ex2[0:1, k * 512:(k + 1) * 512], in0=ex2[0:1, k * 512:(k + 1) * 512],
                in1=self.r1[0:1, 0:512], op=ALU.add), inc=True)
        self.qw("scalar", self.sV)
        self.ae(lambda e: e.activation(out=ex2, in_=ex2, func=AF.Ln), inc=True)
        self.qw("scalar", self.sA)
        self.ae(lambda e: e.activation(out=ex2, in_=ex2, func=AF.Exp, scale=-0.5), inc=True)
        self.qw("tensor", self.sA)
        for k in range(2):
            self.pe(lambda e, k=k: e.matmul(self.p01[:, k * 512:(k + 1) * 512],
                    self.onesrow_sb[:], mu[0:1, k * 512:(k + 1) * 512], start=True, stop=True))
            self.pe(lambda e, k=k: e.matmul(self.p23[:, k * 512:(k + 1) * 512],
                    self.onesrow_sb[:], ex2[0:1, k * 512:(k + 1) * 512], start=True, stop=True),
                    inc=(k == 1))
        wbc = self.sP.v
        for c in range(NF):
            self.qw("vector", self.sP, wbc)
            self.qw("vector", self.sV)
            self.ve(lambda e, c=c: e.tensor_tensor(out=self.io32[:, 0:1024],
                    in0=xln[:, c * 1024:(c + 1) * 1024], in1=self.p01[:], op=ALU.subtract),
                    inc=True)
            self.qw("vector", self.sV)
            su = self.sucol[:, c:c + 1] if with_su else 1.0
            self.ve(lambda e, c=c, su=su: e.scalar_tensor_tensor(
                out=xln[:, c * 1024:(c + 1) * 1024], in0=self.io32[:, 0:1024],
                scalar=su, in1=self.p23[:], op0=ALU.mult, op1=ALU.mult), inc=True)
        self.barrier()

    def ph_ffn(self):
        self.ph_ln(with_su=True)
        self.tap_sbuf("xln", self.B[4][:])
        xln = self.B[4]
        mm_done = {}
        ev_done = {}

        def maybe_evac(cur_c):
            for k in range(NF - 1, -1, -1):
                if k in mm_done and k not in ev_done and 2 * k >= cur_c:
                    psk = mm_done[k][1]
                    self.qw("vector", self.sP, mm_done[k][0])
                    ev_done[k] = self.ve(lambda e, k=k, psk=psk: e.tensor_copy(
                        out=self.x1f[:, k * 1024:(k + 1) * 1024], in_=psk[:]), inc=True)

        for idx, c in enumerate(range(NF - 1, -1, -1)):
            ps = self.p01 if (idx % 2 == 0) else self.p23
            if idx >= 2:
                prev_k = NF - 1 - (idx - 2)
                self.qw("tensor", self.sV, ev_done[prev_k])
            self.pe(lambda e, c=c, ps=ps: e.matmul(ps[:, 0:512], self.h128_sb[:],
                    xln[:, c * 1024:c * 1024 + 512], start=True, stop=True))
            v = self.pe(lambda e, c=c, ps=ps: e.matmul(ps[:, 512:1024], self.h128_sb[:],
                    xln[:, c * 1024 + 512:(c + 1) * 1024], start=True, stop=True), inc=True)
            mm_done[c] = (v, ps)
            maybe_evac(c)
        maybe_evac(-100)
        for s in (1, 2, 4, 8):
            for a0 in range(NF):
                if a0 & s:
                    continue
                b0 = a0 + s
                self.qw("vector", self.sV)
                self.ve(lambda e, a0=a0: e.tensor_copy(out=self.io32[:, 0:1024],
                        in_=self.x1f[:, a0 * 1024:(a0 + 1) * 1024]), inc=True)
                self.qw("vector", self.sV)
                self.ve(lambda e, a0=a0, b0=b0: e.tensor_tensor(
                    out=self.x1f[:, a0 * 1024:(a0 + 1) * 1024],
                    in0=self.x1f[:, a0 * 1024:(a0 + 1) * 1024],
                    in1=self.x1f[:, b0 * 1024:(b0 + 1) * 1024], op=ALU.add), inc=True)
                self.qw("vector", self.sV)
                self.ve(lambda e, a0=a0, b0=b0: e.tensor_tensor(
                    out=self.x1f[:, b0 * 1024:(b0 + 1) * 1024], in0=self.io32[:, 0:1024],
                    in1=self.x1f[:, b0 * 1024:(b0 + 1) * 1024], op=ALU.subtract), inc=True)
        self.tap_sbuf("x1", self.x1f[:])
        x1b = self.B[4]
        self.qw("vector", self.sV)
        self.ve(lambda e: e.tensor_copy(out=self.ptb[:, 0:1024], in_=self.x1f[:, 0:1024]),
                inc=True)
        for c in range(1, NF):
            self.qw("vector", self.sV)
            self.ve(lambda e, c=c: e.tensor_copy(out=x1b[:, c * 1024:(c + 1) * 1024],
                    in_=self.x1f[:, c * 1024:(c + 1) * 1024]), inc=True)
        self.qw("vector", self.sV)
        self.ve(lambda e: e.tensor_copy(out=x1b[:, 0:1024], in_=self.ptb[:, 0:1024]), inc=True)
        self.barrier()

        upbuf = [self.B[5].ap()[:, 0:4096], self.B[5].ap()[:, 4096:8192]]
        gabuf = [self.B[5].ap()[:, 8192:12288], self.B[5].ap()[:, 12288:16384]]
        wupT_v = self.wupT.rearrange("(f p) i -> p f i", f=16, p=128)
        wgaT_v = self.wgateT.rearrange("(f p) i -> p f i", f=16, p=128)

        def h_ap(ic):
            return self.B[ic // 16][:, (ic % 16) * 1024:(ic % 16) * 1024 + 1024]

        nslab = NI // 2
        mm_after_slab = {}
        slab_w = {}
        hdone = {}
        for sl in range(nslab):
            if sl >= 2:
                self.qw("sync", self.sP, mm_after_slab[sl - 2])
            usem = self.u2[sl % 2]
            for half in range(2):
                self.dma("sync", upbuf[sl % 2][:, half * 2048:(half + 1) * 2048].rearrange(
                    "p (f i) -> p f i", f=8, i=256),
                    wupT_v[:, half * 8:(half + 1) * 8, sl * 256:(sl + 1) * 256], ctr=usem)
                self.dma("sync", gabuf[sl % 2][:, half * 2048:(half + 1) * 2048].rearrange(
                    "p (f i) -> p f i", f=8, i=256),
                    wgaT_v[:, half * 8:(half + 1) * 8, sl * 256:(sl + 1) * 256], ctr=usem)
            slab_w[sl] = usem.v
            for j in range(2):
                ic = sl * 2 + j
                pu = self.p01 if (ic % 2 == 0) else self.p45
                pg = self.p23 if (ic % 2 == 0) else self.p67
                self.qw("tensor", self.u2[sl % 2], slab_w[sl])
                if ic >= 2:
                    self.qw("tensor", self.sV, hdone[ic - 2][0])
                    self.qw("tensor", self.sA, hdone[ic - 2][1])
                for f in range(16):
                    off = (f // 8) * 2048 + (f % 8) * 512 + j * 256
                    lhs_u = upbuf[sl % 2][:, off:off + 128] if False else None
                    hh = f // 8
                    base = hh * 2048 + (f % 8) * 256 + j * 128
                    lu = upbuf[sl % 2][:, base:base + 128]
                    lg = gabuf[sl % 2][:, base:base + 128]
                    self.pe(lambda e, f=f, pu=pu, lu=lu: e.matmul(
                        pu[:, 0:512], lu, x1b[:, f * 1024:f * 1024 + 512],
                        start=(f == 0), stop=(f == 15)))
                    self.pe(lambda e, f=f, pu=pu, lu=lu: e.matmul(
                        pu[:, 512:1024], lu, x1b[:, f * 1024 + 512:(f + 1) * 1024],
                        start=(f == 0), stop=(f == 15)))
                    self.pe(lambda e, f=f, pg=pg, lg=lg: e.matmul(
                        pg[:, 0:512], lg, x1b[:, f * 1024:f * 1024 + 512],
                        start=(f == 0), stop=(f == 15)))
                    self.pe(lambda e, f=f, pg=pg, lg=lg: e.matmul(
                        pg[:, 512:1024], lg, x1b[:, f * 1024 + 512:(f + 1) * 1024],
                        start=(f == 0), stop=(f == 15)), inc=(f == 15))
                wmm = self.sP.v
                mm_after_slab[sl] = wmm
                self.qw("scalar", self.sP, wmm)
                if ic >= 1:
                    self.qw("scalar", self.sV, hdone[ic - 1][0])
                vA = self.ae(lambda e, ic=ic, pg=pg: e.activation(
                    out=self.silu_sb[:], in_=pg[:], func=AF.Silu,
                    scale=self.wsgcol[:, ic:ic + 1]), inc=True)
                self.qw("vector", self.sA, vA)
                vV = self.ve(lambda e, ic=ic, pu=pu: e.scalar_tensor_tensor(
                    out=h_ap(ic), in0=pu[:], scalar=self.wsucol[:, ic:ic + 1],
                    in1=self.silu_sb[:], op0=ALU.mult, op1=ALU.mult), inc=True)
                hdone[ic] = (vV, vA)
        self.barrier()
        if "h" in self.dbg:
            self.dma("sync", self.dbg["h"].rearrange("p (r n) -> r p n", r=4, n=16384)[0], self.B[0][:])
            self.dma("sync", self.dbg["h"].rearrange("p (r n) -> r p n", r=4, n=16384)[1], self.B[1][:])
            self.dma("sync", self.dbg["h"].rearrange("p (r n) -> r p n", r=4, n=16384)[2], self.B[2][:])
            self.dma("sync", self.dbg["h"].rearrange("p (r n) -> r p n", r=4, n=16384)[3], self.B[3][:])
            self.qw("sync", self.dS)

        wdb = [self.B[5].ap()[:, 0:8192], self.B[5].ap()[:, 8192:16384]]
        dn_mm = {}
        for ct in range(NF):
            if ct >= 2:
                self.qw("sync", self.sP, dn_mm[ct - 2])
            self.dma("sync", wdb[ct % 2], self.wd2[ct].rearrange("p ic c -> p (ic c)"),
                     ctr=self.d2[ct % 2])
            wW = self.d2[ct % 2].v
            ps = self.p01 if (ct % 2 == 0) else self.p23
            self.qw("tensor", self.d2[ct % 2], wW)
            self.qw("tensor", self.sV)
            for ic in range(NI):
                self.pe(lambda e, ct=ct, ic=ic, ps=ps: e.matmul(
                    ps[:, 0:512], wdb[ct % 2][:, ic * 128:(ic + 1) * 128],
                    h_ap(ic)[:, 0:512], start=(ic == 0), stop=(ic == NI - 1)))
                self.pe(lambda e, ct=ct, ic=ic, ps=ps: e.matmul(
                    ps[:, 512:1024], wdb[ct % 2][:, ic * 128:(ic + 1) * 128],
                    h_ap(ic)[:, 512:1024], start=(ic == 0), stop=(ic == NI - 1)),
                    inc=(ic == NI - 1))
            wmm = self.sP.v
            dn_mm[ct] = wmm
            self.qw("sync", self.sV)
            self.dma("sync", self.io32[:, 0:1024], self.xstream[:, ct * 1024:(ct + 1) * 1024])
            wS = self.dS.v
            self.qw("vector", self.sP, wmm)
            self.qw("vector", self.dS, wS)
            self.ve(lambda e, ct=ct, ps=ps: e.scalar_tensor_tensor(
                out=self.io32[:, 1024:2048], in0=ps[:], scalar=self.wsdcol[:, ct:ct + 1],
                in1=self.io32[:, 0:1024], op0=ALU.mult, op1=ALU.add), inc=True)
            self.qw("sync", self.sV)
            self.dma("sync", self.xstream[:, ct * 1024:(ct + 1) * 1024], self.io32[:, 1024:2048])
        self.barrier()
        self.tap("stream", self.xstream)

    def _colslab_proj(self, wsrc, bias_col, out_cb, rhs, nk=NF, wcols=None):
        wcb = [self.B[5].ap()[:, 0:2048], self.B[5].ap()[:, 2048:4096]]
        mmv = {}
        for ct in range(NF):
            if ct >= 2:
                self.qw("gpsimd", self.sP, mmv[ct - 2])
            src = wsrc.rearrange("(k p) (ct c) -> p ct k c", p=128, k=nk, ct=NF, c=128)
            self.dma("gpsimd", wcb[ct % 2].rearrange("p (k c) -> p k c", k=nk, c=128),
                     src[:, ct, :, :], ctr=self.c2[ct % 2])
            wW = self.c2[ct % 2].v
            ps = self.p01 if (ct % 2 == 0) else self.p23
            self.qw("tensor", self.c2[ct % 2], wW)
            self.qw("tensor", self.sV)
            for k in range(nk):
                self.pe(lambda e, k=k, ct=ct, ps=ps: e.matmul(
                    ps[:, 0:512], wcb[ct % 2][:, k * 128:(k + 1) * 128], rhs(k, 0),
                    start=(k == 0), stop=(k == nk - 1)))
                self.pe(lambda e, k=k, ct=ct, ps=ps: e.matmul(
                    ps[:, 512:1024], wcb[ct % 2][:, k * 128:(k + 1) * 128], rhs(k, 1),
                    start=(k == 0), stop=(k == nk - 1)), inc=(k == nk - 1))
            wmm = self.sP.v
            mmv[ct] = wmm
            out_cb(ct, ps, wmm)

    def ph_attn(self, variant, mid_hook=None):
        a = variant == "a"
        self.ph_ln(with_su=False)
        self.tap_sbuf("xln", self.B[4][:])
        xln = self.B[4]
        agin_v = self.ag_in.rearrange("(c p) t -> p c t", c=NF, p=128)
        self.dma("gpsimd", agin_v, xln.ap().rearrange("p (c t) -> p c t", c=NF, t=TOWN))
        self.qw("gpsimd", self.dG)
        hcc = self.cc.h
        self.cc.v += 1
        rg = [list(range(NCORE))]
        self.q("gpsimd", lambda e: e.collective_compute(
            "AllGather", ALU.bypass, replica_groups=rg,
            ins=[self.ag_in.opt()], outs=[self.ag_out.opt()]).then_inc(hcc, 1))
        wcc = self.cc.v

        KT, VT, QT, QFL = self.B[0], self.B[1], self.B[2], self.B[3]
        pfx = "a" if a else "b"

        if not a:
            def qfl_out(ct, ps, wmm):
                self.qw("vector", self.sP, wmm)
                self.ve(lambda e, ct=ct, ps=ps: e.tensor_scalar(
                    out=QFL[:, ct * 1024:(ct + 1) * 1024], in0=ps[:],
                    scalar1=self.bcol["bbq"][:, ct:ct + 1], scalar2=None, op0=ALU.add),
                    inc=True)

            self._colslab_proj(self.inp["bWq"], None, qfl_out,
                               lambda f, hh: xln[:, f * 1024 + hh * 512:f * 1024 + hh * 512 + 512])
            self.qw("vector", self.sV)
            qfl_v = QFL.ap().rearrange("p (j u) -> p j u", j=16, u=1024)
            for hl in range(2):
                src = qfl_v[:, :, hl * 512:(hl + 1) * 512].rearrange("p j v -> p v j")
                dst = QT.ap()[:, hl * 8192:(hl + 1) * 8192].rearrange(
                    "p (v j) -> p v j", v=512, j=16)
                self.ve(lambda e, src=src, dst=dst: e.tensor_copy(out=dst, in_=src), inc=True)

        wk_sl = self.B[5].ap()[:, 4096:8192]
        wv_sl = self.B[5].ap()[:, 8192:12288]
        wq_sl = self.B[5].ap()[:, 12288:16384]
        mats = [("k", wk_sl), ("v", wv_sl)] + ([("q", wq_sl)] if a else [])
        for nm, dstb in mats:
            for f in range(NF):
                for hl in range(2):
                    if a:
                        wsrc = self.inp["aW" + nm + "_my"][hl, f * 128:(f + 1) * 128, :]
                    else:
                        wsrc = self.inp["bW" + nm + "_my"][f * 128:(f + 1) * 128,
                                                           hl * 128:(hl + 1) * 128]
                    self.dma("gpsimd", dstb[:, f * 256 + hl * 128:f * 256 + hl * 128 + 128],
                             wsrc, ctr=self.dW)
        wslices = self.dW.v

        if a:
            bsrc = self.inp["abv_my"].rearrange("h d -> (h d)")
        else:
            bsrc = self.inp["bbv_my"]
        self.dma("sync", self.r1[0:1, 0:256], bsrc.rearrange("(o n) -> o n", o=1))
        wb = self.dS.v
        self.qw("tensor", self.dS, wb)
        self.qw("tensor", self.sV)
        self.pe(lambda e: e.matmul(self.p45[:, 0:256], self.onesrow_sb[:], self.r1[0:1, 0:256],
                                   start=True, stop=True), inc=True)
        self.qw("vector", self.sP)
        self.ve(lambda e: e.tensor_copy(out=self.bvb[:], in_=self.p45[:, 0:256]), inc=True)

        self.qw("sync", self.cc, wcc)
        self.qw("tensor", self.cc, wcc)
        self.qw("tensor", self.dW, wslices)

        G_v = self.ag_out.rearrange("(r c p) t -> r c p t", r=NCORE, c=NF, p=128)

        def gload(r, f, par):
            self.dma("sync", self.gbuf[par][:], G_v[r, f], ctr=self.g2[par])
            return self.g2[par].v

        bk = self.bcol["abk_my" if a else "bbk_my"]
        bq = self.bcol["abq_my"] if a else None

        def kq_pass(wsl, dstT, bias_col):
            gmm = []
            for r in range(NCORE):
                for f in range(NF):
                    par = f % 2
                    n = r * NF + f
                    if n >= 2:
                        self.qw("sync", self.sP, gmm[n - 2])
                    wS = gload(r, f, par)
                    self.qw("tensor", self.g2[par], wS)
                    if f == 0:
                        self.qw("tensor", self.sV)
                    for hl in range(2):
                        ps = self.p01 if hl == 0 else self.p23
                        for t2 in range(2):
                            self.pe(lambda e, f=f, hl=hl, t2=t2, ps=ps, par=par, wsl=wsl:
                                    e.matmul(ps[:, t2 * 512:(t2 + 1) * 512],
                                             wsl[:, f * 256 + hl * 128:f * 256 + hl * 128 + 128],
                                             self.gbuf[par][:, t2 * 512:(t2 + 1) * 512],
                                             start=(f == 0), stop=(f == NF - 1)),
                                    inc=(hl == 1 and t2 == 1))
                    gmm.append(self.sP.v)
                self.qw("vector", self.sP, gmm[-1])
                for hl in range(2):
                    ps = self.p01 if hl == 0 else self.p23
                    bcolv = bias_col[:, hl:hl + 1]
                    for t2 in range(2):
                        self.ve(lambda e, r=r, hl=hl, t2=t2, ps=ps, bcolv=bcolv: e.tensor_scalar(
                            out=dstT[:, hl * 8192 + r * 1024 + t2 * 512:
                                     hl * 8192 + r * 1024 + t2 * 512 + 512],
                            in0=ps[:, t2 * 512:(t2 + 1) * 512], scalar1=bcolv, scalar2=None,
                            op0=ALU.add), inc=(hl == 1 and t2 == 1))

        kq_pass(wk_sl, KT, bk)
        self.qw("sync", self.sP)
        if a:
            kq_pass(wq_sl, QT, bq)
            self.qw("sync", self.sP)

        gmmv = []
        for r in range(NCORE):
            vmm = 0
            for f in range(NF):
                par = f % 2
                n = r * NF + f
                if n >= 2:
                    self.qw("sync", self.sP, gmmv[n - 2])
                wS = gload(r, f, par)
                self.qw("tensor", self.g2[par], wS)
                if f == 0:
                    self.qw("tensor", self.sV)
                for tc in range(8):
                    ps = (self.p01 if tc < 4 else self.p23)
                    ofs = (tc % 4) * 256
                    self.pe(lambda e, f=f, tc=tc, ps=ps, ofs=ofs, par=par: e.matmul(
                        ps[:, ofs:ofs + 256], self.gbuf[par][:, tc * 128:(tc + 1) * 128],
                        wv_sl[:, f * 256:(f + 1) * 256],
                        start=(f == 0 and tc % 2 == 0),
                        stop=(f == NF - 1 and tc % 2 == 1)), inc=(tc == 7))
                vmm = self.sP.v
                gmmv.append(vmm)
            self.qw("vector", self.sP, vmm)
            for tc in range(8):
                ps = (self.p01 if tc < 4 else self.p23)
                ofs = (tc % 4) * 256
                self.ve(lambda e, r=r, tc=tc, ps=ps, ofs=ofs: e.tensor_tensor(
                    out=VT[:, (r * 8 + tc) * 256:(r * 8 + tc) * 256 + 256],
                    in0=ps[:, ofs:ofs + 256], in1=self.bvb[:], op=ALU.add), inc=(tc == 7))
        self.barrier()
        if "K" in self.dbg:
            self.dma("sync", self.dbg["K"], KT[:])
            self.dma("sync", self.dbg["V"], VT[:])
            self.dma("sync", self.dbg["Q"], QT[:])
            self.qw("sync", self.dS)
        if "G" in self.dbg:
            self.tap("G", self.ag_out)
        self.phase_done("proj_" + pfx)

        for hl in range(2):
            for blk in range(16):
                self.qw("scalar", self.sP)
                self.ae(lambda e, hl=hl, blk=blk: e.activation(
                    out=self.sqs[:, 0:512], in_=KT[:, hl * 8192 + blk * 512:hl * 8192 + blk * 512 + 512],
                    func=AF.Square), inc=True)
                self.qw("tensor", self.sA)
                self.qw("tensor", self.sV)
                self.pe(lambda e: e.matmul(self.p45[0:1, 0:512], self.onescol_sb[:],
                        self.sqs[:, 0:512], start=True, stop=True), inc=True)
                self.qw("vector", self.sP)
                self.ve(lambda e, blk=blk: e.tensor_reduce(
                    out=self.r2[0:1, blk:blk + 1], in_=self.p45[0:1, 0:512], axis=AX.X,
                    op=ALU.max), inc=True)
            self.qw("vector", self.sV)
            self.ve(lambda e: e.tensor_reduce(out=self.r2[0:1, 16:17], in_=self.r2[0:1, 0:16],
                                              axis=AX.X, op=ALU.max), inc=True)
            for blk in range(16):
                self.qw("scalar", self.sP)
                self.ae(lambda e, hl=hl, blk=blk: e.activation(
                    out=self.sqs[:, 0:512], in_=QT[:, hl * 8192 + blk * 512:hl * 8192 + blk * 512 + 512],
                    func=AF.Square), inc=True)
                self.qw("tensor", self.sA)
                self.qw("tensor", self.sV)
                self.pe(lambda e: e.matmul(self.p45[0:1, 0:512], self.onescol_sb[:],
                        self.sqs[:, 0:512], start=True, stop=True), inc=True)
                self.qw("vector", self.sP)
                self.ve(lambda e, hl=hl, blk=blk: e.tensor_reduce(
                    out=self.qb1[0:1, hl * 16 + blk:hl * 16 + blk + 1],
                    in_=self.p45[0:1, 0:512], axis=AX.X, op=ALU.max), inc=True)
            self.qw("vector", self.sV)
            self.ve(lambda e, hl=hl: e.tensor_scalar(
                out=self.r1[0:1, 0:16], in0=self.qb1[0:1, hl * 16:(hl + 1) * 16],
                scalar1=self.r2[0:1, 16:17], scalar2=float((ATT_SCALE * 1.002) ** 2),
                op0=ALU.mult, op1=ALU.mult), inc=True)
            self.qw("scalar", self.sV)
            self.ae(lambda e: e.activation(out=self.r1[0:1, 0:16], in_=self.r1[0:1, 0:16],
                                           func=AF.Ln), inc=True)
            self.qw("scalar", self.sA)
            self.ae(lambda e: e.activation(out=self.r1[0:1, 0:16], in_=self.r1[0:1, 0:16],
                    func=AF.Exp, scale=0.5), inc=True)
            self.qw("vector", self.sA)
            self.ve(lambda e: e.tensor_scalar(out=self.r1[0:1, 16:32], in0=self.r1[0:1, 0:16],
                    scalar1=-1.0, scalar2=None, op0=ALU.mult), inc=True)
            self.qw("tensor", self.sV)
            self.pe(lambda e: e.matmul(self.p45[:, 0:16], self.onesrow_sb[:],
                    self.r1[0:1, 16:32], start=True, stop=True), inc=True)
            self.qw("vector", self.sP)
            self.ve(lambda e, hl=hl: e.tensor_copy(out=self.cbias[:, hl * 16:(hl + 1) * 16],
                    in_=self.p45[:, 0:16]), inc=True)
        self.barrier()

        self.dma("sync", self.maskb[:], self.masks_t.ap())
        self.qw("sync", self.dS)
        for eng in ("tensor", "scalar", "vector"):
            self.qw(eng, self.dS)
        outT = self.p45[:, 0:512]
        sums = self.p45[0:1, 512:1024]
        for hl in range(2):
            for tb in range(NF):
                nsc = 4 * (tb + 1)
                npair = nsc // 2
                qrhs0 = QT[:, hl * 8192 + tb * 512:hl * 8192 + tb * 512 + 512]
                exp_done = {}
                cons_done = {}
                for pr in range(npair):
                    ps = self.p01 if (pr % 2 == 0) else self.p23
                    par = pr % 2
                    if pr >= 2:
                        self.qw("tensor", self.sA, exp_done[pr - 2])
                    else:
                        self.qw("tensor", self.sA)
                    for k in range(2):
                        sc = 2 * pr + k
                        self.pe(lambda e, sc=sc, k=k, ps=ps, hl=hl, qrhs0=qrhs0: e.matmul(
                            ps[:, k * 512:(k + 1) * 512],
                            KT[:, hl * 8192 + sc * 128:hl * 8192 + sc * 128 + 128],
                            qrhs0, start=True, stop=True), inc=(k == 1))
                    wS = self.sP.v
                    self.qw("scalar", self.sP, wS)
                    if pr >= 2:
                        self.qw("scalar", self.sP, cons_done[pr - 2])
                    exp_done[pr] = self.ae(lambda e, ps=ps, par=par, hl=hl, tb=tb: e.activation(
                        out=self.ptb[:, par * 1024:(par + 1) * 1024], in_=ps[:], func=AF.Exp,
                        scale=ATT_SCALE, bias=self.cbias[:, hl * 16 + tb:hl * 16 + tb + 1]),
                        inc=True)
                    wmask = None
                    for k in range(2):
                        sc = 2 * pr + k
                        if sc >= 4 * tb:
                            kk = sc - 4 * tb
                            self.qw("vector", self.sA, exp_done[pr])
                            wmask = self.ve(lambda e, k=k, par=par, kk=kk: e.tensor_tensor(
                                out=self.ptb[:, par * 1024 + k * 512:par * 1024 + (k + 1) * 512],
                                in0=self.ptb[:, par * 1024 + k * 512:par * 1024 + (k + 1) * 512],
                                in1=self.maskb[:, kk * 512:(kk + 1) * 512], op=ALU.mult),
                                inc=True)
                    if wmask is not None:
                        self.qw("tensor", self.sV, wmask)
                    else:
                        self.qw("tensor", self.sA, exp_done[pr])
                    if pr == 0:
                        self.qw("tensor", self.sA)
                        self.qw("tensor", self.sV)
                    for k in range(2):
                        sc = 2 * pr + k
                        st = (sc == 0)
                        sp = (sc == nsc - 1)
                        self.pe(lambda e, k=k, par=par, st=st, sp=sp: e.matmul(
                            sums, self.onescol_sb[:],
                            self.ptb[:, par * 1024 + k * 512:par * 1024 + (k + 1) * 512],
                            start=st, stop=sp))
                        self.pe(lambda e, k=k, par=par, sc=sc, hl=hl, st=st, sp=sp: e.matmul(
                            outT, VT[:, sc * 256 + hl * 128:sc * 256 + hl * 128 + 128],
                            self.ptb[:, par * 1024 + k * 512:par * 1024 + (k + 1) * 512],
                            start=st, stop=sp),
                            inc=(k == 1))
                    cons_done[pr] = self.sP.v
                wcons = cons_done[npair - 1]
                self.qw("scalar", self.sP, wcons)
                self.ae(lambda e: e.activation(out=self.r1[0:1, 0:512], in_=sums, func=AF.Ln),
                        inc=True)
                self.qw("scalar", self.sA)
                wrec = self.ae(lambda e: e.activation(out=self.r1[0:1, 0:512],
                               in_=self.r1[0:1, 0:512], func=AF.Exp, scale=-1.0), inc=True)
                self.qw("tensor", self.sA, wrec)
                self.qw("tensor", self.sV)
                wbc = self.pe(lambda e: e.matmul(self.p67[:, 0:512], self.onesrow_sb[:],
                              self.r1[0:1, 0:512], start=True, stop=True), inc=True)
                self.qw("vector", self.sP, wbc)
                wrb = self.ve(lambda e: e.tensor_copy(out=self.rb, in_=self.p67[:, 0:512]),
                              inc=True)
                ypar = (hl * NF + tb) % 2
                self.qw("vector", self.sV, wrb)
                self.qw("vector", self.dS)
                wye = self.ve(lambda e, ypar=ypar: e.scalar_tensor_tensor(
                    out=self.ysb[:, ypar * 512:(ypar + 1) * 512], in0=outT, scalar=1.0,
                    in1=self.rb, op0=ALU.mult, op1=ALU.mult), inc=True)
                self.qw("sync", self.sV, wye)
                self.dma("sync", self.y_in[hl * 128:(hl + 1) * 128, tb * 512:(tb + 1) * 512],
                         self.ysb[:, ypar * 512:(ypar + 1) * 512])
        self.barrier()
        self.phase_done("scores_" + pfx)

        hcc = self.cc.h
        self.cc.v += 1
        self.q("gpsimd", lambda e: e.collective_compute(
            "AllGather", ALU.bypass, replica_groups=rg,
            ins=[self.y_in.opt()], outs=[self.y_out.opt()]).then_inc(hcc, 1))
        wcc2 = self.cc.v
        if mid_hook is not None:
            mid_hook()
        self.qw("sync", self.cc, wcc2)
        if "yout" in self.dbg:
            self.dma("sync", self.dbg["yout"], self.y_out)
            self.qw("sync", self.dS)

        ysall = self.B[3]
        if not hasattr(self, "_cid"):
            self._cid = self.nc.partition_id(engines=[mybir.EngineType.SP])
        cid = self._cid
        for k in range(16):
            self.dma("sync", ysall[:, k * 1024:(k + 1) * 1024],
                     self.y_out[k * 128:(k + 1) * 128, bass.ts(cid, TOWN)])
        wY = self.dS.v
        self.qw("tensor", self.dS, wY)
        self.qw("gpsimd", self.dS, wY)
        self.qw("sync", self.dS, wY)
        bo = self.bcol["abo" if a else "bbo"]

        def wo_out(ct, ps, wmm):
            self.qw("sync", self.sV)
            self.dma("sync", self.io32[:, 0:1024], self.xstream[:, ct * 1024:(ct + 1) * 1024])
            wS = self.dS.v
            self.qw("vector", self.sP, wmm)
            self.qw("vector", self.dS, wS)
            self.ve(lambda e, ct=ct, ps=ps: e.scalar_tensor_tensor(
                out=self.io32[:, 1024:2048], in0=ps[:], scalar=bo[:, ct:ct + 1],
                in1=self.io32[:, 0:1024], op0=ALU.add, op1=ALU.add), inc=True)
            self.qw("sync", self.sV)
            self.dma("sync", self.xstream[:, ct * 1024:(ct + 1) * 1024],
                     self.io32[:, 1024:2048])

        self._colslab_proj(self.inp["aWo" if a else "bWo"], None, wo_out,
                           lambda k, hh: ysall[:, k * 1024 + hh * 512:k * 1024 + hh * 512 + 512])
        self.barrier()
        self.tap("stream", self.xstream)
        self.phase_done("attn_" + pfx)

    def build(self, phases):
        self.declare_dram()
        self.declare_inline()
        self.alloc_sbuf()
        self.make_sems()
        for ph in phases:
            if self.stopped:
                break
            ph()
        with self.es:
            block = self.es.enter_context(self.nc.Block())
            for eng in ("tensor", "scalar", "vector", "gpsimd", "sync"):
                getattr(block, eng)(lambda e, eng=eng: [fn(e) for fn in self.ops[eng]])
        return self.nc


def build_core(core_id, taps=(), upto=None, which="full"):
    b = Builder(core_id, taps=taps, upto=upto)
    prep_serial = [b.ph_prep_casts, b.ph_prep_mms, b.barrier]
    if which == "ffn_only":
        phases = [b.ph_consts, b.ph_input] + prep_serial + [b.ph_ffn, b.ph_output]
    elif which == "attn_a_only":
        phases = [b.ph_consts, b.ph_input] + prep_serial + [lambda: b.ph_attn("a"), b.ph_output]
    elif which == "attn_b_only":
        phases = [b.ph_consts, b.ph_input] + prep_serial + [lambda: b.ph_attn("b"), b.ph_output]
    else:
        phases = [b.ph_consts, b.ph_input, b.ph_prep_casts,
                  lambda: b.ph_attn("a", mid_hook=b.ph_prep_mms), b.ph_ffn,
                  lambda: b.ph_attn("b"), b.ph_ffn, b.ph_output]
    return b.build(phases)


def make_in_map(inputs, core_id):
    x = np.asarray(inputs["L0_attn_a_x"]).reshape(T, D)
    h0 = 2 * core_id
    aWq = np.asarray(inputs["L0_attn_a_Wq"])
    aWk = np.asarray(inputs["L0_attn_a_Wk"])
    aWv = np.asarray(inputs["L0_attn_a_Wv"])
    bWk = np.asarray(inputs["L0_attn_b_Wk"])
    bWv = np.asarray(inputs["L0_attn_b_Wv"])
    m = {
        "x_own": np.ascontiguousarray(x[core_id * TOWN:(core_id + 1) * TOWN]),
        "aWq_my": aWq[h0:h0 + 2], "aWk_my": aWk[h0:h0 + 2], "aWv_my": aWv[h0:h0 + 2],
        "abq_my": np.asarray(inputs["L0_attn_a_bq"])[h0:h0 + 2],
        "abk_my": np.asarray(inputs["L0_attn_a_bk"])[h0:h0 + 2],
        "abv_my": np.asarray(inputs["L0_attn_a_bv"])[h0:h0 + 2],
        "aWo": inputs["L0_attn_a_Wo"], "abo": inputs["L0_attn_a_bo"],
        "bWq": inputs["L0_attn_b_Wq"],
        "bWk_my": bWk[:, h0 * 128:(h0 + 2) * 128],
        "bWv_my": bWv[:, h0 * 128:(h0 + 2) * 128],
        "bWo": inputs["L0_attn_b_Wo"],
        "bbq": inputs["L0_attn_b_bq"],
        "bbk_my": np.asarray(inputs["L0_attn_b_bk"])[h0 * 128:(h0 + 2) * 128],
        "bbv_my": np.asarray(inputs["L0_attn_b_bv"])[h0 * 128:(h0 + 2) * 128],
        "bbo": inputs["L0_attn_b_bo"],
        "su_ug": inputs["L0_ffn_SU_ug"], "su_dp": inputs["L0_ffn_SU_dp"],
        "wsug": inputs["L0_ffn_Wscale_ug"], "wsdp": inputs["L0_ffn_Wscale_dp"],
        "fWupT": np.ascontiguousarray(np.asarray(inputs["L0_ffn_W_up"]).T),
        "fWgateT": np.ascontiguousarray(np.asarray(inputs["L0_ffn_W_gate"]).T),
        "fWdownT": np.ascontiguousarray(np.asarray(inputs["L0_ffn_W_down"]).T),
    }
    return {k: np.ascontiguousarray(np.asarray(v, dtype=np.float32)) for k, v in m.items()}


def kernel(**inputs):
    nc = build_core(0)
    in_maps = [make_in_map(inputs, c) for c in range(NCORE)]
    res = run_bass_kernel_spmd(nc, in_maps, core_ids=list(range(NCORE)))
    out = np.concatenate([np.asarray(res.results[c]["y_own"]) for c in range(NCORE)], axis=0)
    return out.reshape(1, T, D).astype(np.float32)


# version 05
# speedup vs baseline: 1.3723x; 1.0124x over previous
import math
import contextlib
import numpy as np
import ml_dtypes

import concourse.bass as bass
import concourse.mybir as mybir
from concourse.bass_utils import run_bass_kernel_spmd

F32 = mybir.dt.float32
BF16 = mybir.dt.bfloat16
AF = mybir.ActivationFunctionType
ALU = mybir.AluOpType
AX = mybir.AxisListType

NCORE = 8
T = 8192
D = 2048
TOWN = 1024
NF = 16
NI = 64
INTER = 8192
HPC = 2
DH = 128
ATT_SCALE = 1.0 / math.sqrt(128.0)
EPS = 1e-5
R2048 = 1.0 / math.sqrt(2048.0)
R8192 = 1.0 / math.sqrt(8192.0)
BF = ml_dtypes.bfloat16


def _hmat(n):
    a = np.arange(n)
    bits = a[:, None] & a[None, :]
    pc = np.zeros((n, n), dtype=np.int64)
    x = bits.copy()
    while x.any():
        pc += x & 1
        x >>= 1
    return np.where(pc % 2 == 0, 1.0, -1.0).astype(np.float32)


class Ctr:
    def __init__(self, es, nc, name):
        self.h = es.enter_context(nc.semaphore(name))
        self.v = 0


class Builder:

    def __init__(self, core_id, taps=(), upto=None):
        self.core_id = core_id
        self.taps = set(taps)
        self.upto = upto
        self.stopped = False
        self.nc = bass.Bass("TRN2", target_bir_lowering=False,
                    dynamic_dma_scratch_size=4096)
        self.es = contextlib.ExitStack()
        self.ops = {k: [] for k in ("tensor", "scalar", "vector", "gpsimd", "sync")}
        self.H0 = 2 * core_id

    def q(self, eng, fn):
        if not self.stopped:
            self.ops[eng].append(fn)

    def qw(self, eng, ctr, val=None):
        v = ctr.v if val is None else val
        if v <= 0:
            return
        h = ctr.h
        self.q(eng, lambda e: e.wait_ge(h, v))

    def phase_done(self, label):
        if self.upto is not None and label == self.upto:
            self.stopped = True

    def pe(self, fn, inc=False):
        if inc:
            h = self.sP.h
            self.sP.v += 1
            self.q("tensor", lambda e: fn(e).then_inc(h, 1))
        else:
            self.q("tensor", fn)
        return self.sP.v

    def ve(self, fn, inc=False):
        if inc:
            h = self.sV.h
            self.sV.v += 1
            self.q("vector", lambda e: fn(e).then_inc(h, 1))
        else:
            self.q("vector", fn)
        return self.sV.v

    def ae(self, fn, inc=False):
        if inc:
            h = self.sA.h
            self.sA.v += 1
            self.q("scalar", lambda e: fn(e).then_inc(h, 1))
        else:
            self.q("scalar", fn)
        return self.sA.v

    def dma(self, eng, out, in_, ctr=None, **kw):
        ctr = ctr if ctr is not None else (self.dG if eng == "gpsimd" else self.dS)
        h = ctr.h
        ctr.v += 16

        def _go(e, out=out, in_=in_, h=h, kw=kw):
            try:
                return e.dma_start(out=out, in_=in_, **kw).then_inc(h, 16)
            except Exception:
                import sys
                print("FAILING DMA out=", str(out)[:300], " in_=", str(in_)[:300],
                      file=sys.stderr)
                raise
        self.q(eng, _go)
        return ctr.v

    def tdma(self, out, in_):
        h = self.dS.h
        self.dS.v += 16
        self.q("sync", lambda e: e.dma_start_transpose(out, in_).then_inc(h, 16))
        return self.dS.v

    def barrier(self):
        pairs = [("tensor", self.sP), ("scalar", self.sA), ("vector", self.sV),
                 ("gpsimd", self.sG)]
        waits = []
        for eng, ctr in pairs:
            h = ctr.h
            pv = ctr.v
            ctr.v += 1
            if eng == "gpsimd":
                self.q(eng, lambda e, h=h, pv=pv: (e.wait_ge(h, pv) if pv > 0 else None,
                                                   e.nop(nofuse=True).then_inc(h, 1)))
            else:
                self.q(eng, lambda e, h=h: e.drain(fusable=False).then_inc(h, 1))
            waits.append((h, ctr.v))
        waits += [(self.dS.h, self.dS.v), (self.dG.h, self.dG.v), (self.dW.h, self.dW.v),
                  (self.dP.h, self.dP.v)]
        for grp in (self.pl, self.px, self.ys):
            waits += [(s.h, s.v) for s in grp]
        for grp in (self.u2, self.d2, self.c2, self.g2):
            waits += [(s.h, s.v) for s in grp]
        for eng in ("tensor", "scalar", "vector", "gpsimd", "sync"):
            for h, v in waits:
                if v > 0:
                    self.q(eng, lambda e, h=h, v=v: e.wait_ge(h, v))

    def tap(self, name, src_dram):
        if name in self.dbg:
            self.barrier()
            self.dma("sync", self.dbg[name], src_dram)
            self.qw("sync", self.dS)

    def tap_sbuf(self, name, sb_ap):
        if name in self.dbg:
            self.barrier()
            self.dma("sync", self.dbg[name], sb_ap)
            self.qw("sync", self.dS)

    def declare_dram(self):
        nc = self.nc
        dt = nc.dram_tensor
        self.inp = {}
        for nm, shape in [
            ("x_own", [TOWN, D]),
            ("aWq_my", [HPC, D, DH]), ("aWk_my", [HPC, D, DH]), ("aWv_my", [HPC, D, DH]),
            ("abq_my", [HPC, DH]), ("abk_my", [HPC, DH]), ("abv_my", [HPC, DH]),
            ("aWo", [D, D]), ("abo", [D]),
            ("bWq", [D, D]), ("bWk_my", [D, HPC * DH]), ("bWv_my", [D, HPC * DH]),
            ("bWo", [D, D]),
            ("bbq", [D]), ("bbk_my", [HPC * DH]), ("bbv_my", [HPC * DH]), ("bbo", [D]),
            ("su_ug", [D]), ("su_dp", [INTER]), ("wsug", [2 * INTER]), ("wsdp", [D]),
            ("fWupT", [D, INTER]), ("fWgateT", [D, INTER]), ("fWdownT", [INTER, D]),
        ]:
            self.inp[nm] = dt(nm, shape, F32, kind="ExternalInput").ap()
        self.y_own = dt("y_own", [TOWN, D], F32, kind="ExternalOutput").ap()

        self.xstream = dt("xstream", [128, NF * TOWN], F32, kind="Internal").ap()
        self.wupT = dt("wupT", [D, INTER], BF16, kind="Internal").ap()
        self.wgateT = dt("wgateT", [D, INTER], BF16, kind="Internal").ap()
        self.wdT = dt("wdT", [INTER, D], BF16, kind="Internal").ap()
        self.wdH = dt("wdH", [INTER, D], BF16, kind="Internal").ap()
        self.wd2 = dt("wd2", [NF, 128, NI, 128], BF16, kind="Internal").ap()
        self.ag_in = dt("ag_in", [D, TOWN], BF16, kind="Internal").ap()
        self.ag_out = dt("ag_out", [NCORE * D, TOWN], BF16, kind="Internal",
                         addr_space="Shared").ap()
        self.y_in = dt("y_in", [HPC * DH, T], BF16, kind="Internal").ap()
        self.y_out = dt("y_out", [NCORE * HPC * DH, T], BF16, kind="Internal",
                        addr_space="Shared").ap()

        self.dbg = {}
        def tapout(nm, shape, dtp):
            self.dbg[nm] = dt("dbg_" + nm, shape, dtp, kind="ExternalOutput").ap()
        if "stream" in self.taps: tapout("stream", [128, NF * TOWN], F32)
        if "xln" in self.taps: tapout("xln", [128, NF * TOWN], BF16)
        if "G" in self.taps: tapout("G", [NCORE * D, TOWN], BF16)
        if "kvq" in self.taps:
            tapout("K", [128, 2 * T], BF16)
            tapout("V", [128, NI * 256], BF16)
            tapout("Q", [128, 2 * T], BF16)
        if "yout" in self.taps: tapout("yout", [NCORE * HPC * DH, T], BF16)
        if "wd2" in self.taps: tapout("wd2", [NF, 128, NI, 128], BF16)
        if "x1" in self.taps: tapout("x1", [128, NF * TOWN], F32)
        if "h" in self.taps: tapout("h", [128, 4 * 16384], BF16)

    def declare_inline(self):
        nc = self.nc
        self.h128_t = nc.inline_tensor(_hmat(128).astype(BF), name="h128i")
        self.h64_t = nc.inline_tensor(_hmat(64).astype(BF), name="h64i")
        self.id128_t = nc.inline_tensor(np.eye(128, dtype=np.float32), name="id128i")
        self.onescol_t = nc.inline_tensor(np.ones((128, 1), dtype=np.float32).astype(BF),
                                          name="onescoli")
        self.onesrow_t = nc.inline_tensor(np.ones((1, 128), dtype=np.float32), name="onesrowi")
        m = np.zeros((128, 4, 512), dtype=np.float32)
        for k in range(4):
            for p in range(128):
                m[p, k, 128 * k + p:] = 1.0
        self.masks_t = nc.inline_tensor(m.reshape(128, 2048).astype(BF), name="masksi")

    def alloc_sbuf(self):
        nc = self.nc
        A = nc.alloc_sbuf_tensor_at
        BASE = (nc.sbuf_base + 31) & ~31
        assert BASE <= 4224, BASE
        self.B = [A(f"B{i}", [128, 16384], BF16, offset=BASE + 32768 * i) for i in range(6)]
        o = BASE + 32768 * 6

        def at(name, shape, dtype):
            nonlocal o
            t = A(name, shape, dtype, offset=o)
            o += int(np.prod(shape[1:])) * (4 if dtype == F32 else 2)
            o = (o + 31) & ~31
            return t

        self.io32 = at("io32", [128, 2048], F32)
        self.ptb = at("ptb", [128, 2048], BF16)
        self.maskb = None
        self.rows32 = at("rows32", [1, 2048], F32)
        self.silu_sb = at("silu_sb", [128, 1024], BF16)
        self.ysb = self.silu_sb
        self.h128_sb = at("h128_sb", [128, 128], BF16)
        self.h64_sb = at("h64_sb", [64, 64], BF16)
        self.id128_sb = at("id128_sb", [128, 128], F32)
        self.onescol_sb = at("onescol_sb", [128, 1], BF16)
        self.onesrow_sb = at("onesrow_sb", [1, 128], F32)
        self.cbias = at("cbias", [128, 32], F32)
        self.sucol = at("sucol", [128, 16], F32)
        self.wsgcol = at("wsgcol", [128, 64], F32)
        self.wsucol = at("wsucol", [128, 64], F32)
        self.wsdcol = at("wsdcol", [128, 16], F32)
        self.bcol = {}
        for nm in ("abo", "bbq", "bbo"):
            self.bcol[nm] = at(f"col_{nm}", [128, 16], F32)
        for nm in ("abq_my", "abk_my", "bbk_my"):
            self.bcol[nm] = at(f"col_{nm}", [128, 2], F32)
        self.bvb = at("bvb", [128, 256], BF16)
        self.r1 = at("r1", [1, 512], F32)
        self.r2 = at("r2", [1, 32], F32)
        self.qb1 = at("qb1", [1, 64], F32)
        self.sqs = self.ptb
        self.tmpcol = at("tmpcol", [128, 64], F32)
        assert o <= 229344, f"sbuf overflow {o}"
        self.sbuf_top = o

        self.rb = self.io32.ap()[:, 1536:2048]
        io_off = BASE + 32768 * 6
        self.maskb = A("maskb", [128, 2048], BF16, offset=io_off)
        self.B0f = A("B0f", [128, 8192], F32, offset=BASE)
        self.x1f = A("x1f", [128, 16384], F32, offset=BASE + 32768 * 4)
        self.gbuf = [A(f"gbuf{i}", [128, 1024], BF16, offset=io_off + 8192 + 2048 * i)
                     for i in range(2)]

        self.p01 = self.es.enter_context(nc.psum_tensor("p01", [128, 1024], F32))
        self.p23 = self.es.enter_context(nc.psum_tensor("p23", [128, 1024], F32))
        self.p45 = self.es.enter_context(nc.psum_tensor("p45", [128, 1024], F32))
        self.p67 = self.es.enter_context(nc.psum_tensor("p67", [128, 1024], F32))

    def make_sems(self):
        nc, es = self.nc, self.es
        self.dS = Ctr(es, nc, "dS")
        self.dG = Ctr(es, nc, "dG")
        self.dW = Ctr(es, nc, "dW")
        self.sP = Ctr(es, nc, "sP")
        self.sA = Ctr(es, nc, "sA")
        self.sV = Ctr(es, nc, "sV")
        self.sG = Ctr(es, nc, "sG")
        self.cc = Ctr(es, nc, "cc")
        self.u2 = [Ctr(es, nc, "u0"), Ctr(es, nc, "u1")]
        self.d2 = [Ctr(es, nc, "d0"), Ctr(es, nc, "d1")]
        self.c2 = [Ctr(es, nc, "c0"), Ctr(es, nc, "c1")]
        self.g2 = [Ctr(es, nc, "g0"), Ctr(es, nc, "g1")]
        self.dP = Ctr(es, nc, "dP")
        self.pl = [Ctr(es, nc, "pl0"), Ctr(es, nc, "pl1")]
        self.px = [Ctr(es, nc, "px0"), Ctr(es, nc, "px1")]
        self.ys = [Ctr(es, nc, "ys0"), Ctr(es, nc, "ys1")]

    def ph_consts(self):
        for dst, src in [
            (self.h128_sb, self.h128_t), (self.h64_sb, self.h64_t),
            (self.id128_sb, self.id128_t), (self.onescol_sb, self.onescol_t),
            (self.onesrow_sb, self.onesrow_t),
        ]:
            self.dma("sync", dst[:], src.ap())

        def vec2col(flat, nch, dst, mul=None):
            src = flat.rearrange("(c p) -> c p", p=128) if len(flat.shape) == 1 else flat
            self.qw("sync", self.sV)
            self.dma("sync", self.io32[0:nch, 0:128], src)
            self.qw("tensor", self.dS)
            self.qw("tensor", self.sV)
            self.pe(lambda e, nch=nch: e.transpose(self.p01[:, 0:nch], self.io32[0:nch, 0:128],
                                                   self.id128_sb[0:nch, 0:nch]), inc=True)
            self.qw("vector", self.sP)
            if mul is None:
                self.ve(lambda e, d=dst, nch=nch: e.tensor_copy(out=d[:, 0:nch],
                                                                in_=self.p01[:, 0:nch]), inc=True)
            else:
                self.ve(lambda e, d=dst, nch=nch, m=mul: e.tensor_scalar(
                    out=d[:, 0:nch], in0=self.p01[:, 0:nch], scalar1=float(m), scalar2=None,
                    op0=ALU.mult), inc=True)

        vec2col(self.inp["su_ug"], 16, self.sucol)
        vec2col(self.inp["wsdp"], 16, self.wsdcol, mul=R8192)
        for nm in ("abo", "bbq", "bbo"):
            vec2col(self.inp[nm], 16, self.bcol[nm])
        for nm in ("abq_my", "abk_my", "bbk_my"):
            vec2col(self.inp[nm], 2, self.bcol[nm])
        vec2col(self.inp["su_dp"], 64, self.tmpcol)
        vec2col(self.inp["wsug"][0:INTER], 64, self.wsucol, mul=R2048)
        self.qw("vector", self.sV)
        self.ve(lambda e: e.tensor_tensor(out=self.wsucol[:], in0=self.wsucol[:],
                                          in1=self.tmpcol[:], op=ALU.mult), inc=True)
        vec2col(self.inp["wsug"][INTER:2 * INTER], 64, self.wsgcol, mul=R2048)
        self.barrier()
        self.phase_done("consts")

    def _transpose_tiles_to(self, dst_is_output):
        xs_v = self.xstream.rearrange("p (c t) -> p c t", c=NF, t=TOWN)
        for tc in range(8):
            self.qw("sync", self.sV)
            if dst_is_output:
                self.dma("sync", self.io32.ap().rearrange("p (c t) -> p c t", c=NF, t=128),
                         xs_v[:, :, tc * 128:(tc + 1) * 128])
            else:
                self.dma("sync", self.io32[:], self.inp["x_own"][tc * 128:(tc + 1) * 128, :])
            wS = self.dS.v
            evs = {}
            pairs = [self.p01, self.p23, self.p45, self.p67]
            for fc in range(16):
                slot = fc % 8
                ps = pairs[slot % 4]
                col = 512 * (slot // 4)
                self.qw("tensor", self.dS, wS)
                if fc >= 8:
                    self.qw("tensor", self.sV, evs[fc - 8])
                self.pe(lambda e, fc=fc, ps=ps, col=col: e.transpose(
                    ps[:, col:col + 128], self.io32[:, fc * 128:(fc + 1) * 128],
                    self.id128_sb[:]), inc=True)
                self.qw("vector", self.sP)
                if fc == 0:
                    self.qw("vector", self.dS)
                evs[fc] = self.ve(lambda e, fc=fc, ps=ps, col=col: e.tensor_copy(
                    out=self.B0f[:, fc * 128:(fc + 1) * 128],
                    in_=ps[:, col:col + 128]), inc=True)
            self.qw("sync", self.sV)
            if dst_is_output:
                self.dma("sync", self.y_own[tc * 128:(tc + 1) * 128, :], self.B0f[:, 0:2048])
            else:
                src = self.B0f.ap()[:, 0:2048].rearrange("p (c t) -> p c t", c=NF, t=128)
                self.dma("sync", xs_v[:, :, tc * 128:(tc + 1) * 128], src)
        self.barrier()

    def ph_input(self):
        self._transpose_tiles_to(dst_is_output=False)
        self.tap("stream", self.xstream)
        self.phase_done("input")

    def ph_output(self):
        self._transpose_tiles_to(dst_is_output=True)
        for eng in ("tensor", "scalar", "vector", "gpsimd", "sync"):
            self.qw(eng, self.dS)

    def ph_prep_casts(self):
        for dst, wsrc in ((self.wupT, self.inp["fWupT"]), (self.wgateT, self.inp["fWgateT"]),
                          (self.wdT, self.inp["fWdownT"])):
            self.dma("gpsimd", dst, wsrc)
        self.prep_cast_w = self.dG.v

    def ph_prep_mms(self, barriers=True):
        stage = self.B[1]
        self.qw("scalar", self.dG, self.prep_cast_w)
        self.qw("scalar", self.sP)
        wdT_v = self.wdT.rearrange("(cc p) c -> cc p c", cc=64, p=128)
        wdH_v = self.wdH.rearrange("(cp p) c -> cp p c", cp=64, p=128)

        def pipelined_pass(n_iter, nrow, ld_src, st_dst, hmat_sb):
            self.qw("tensor", self.sV)
            self.qw("tensor", self.sA)
            LD = [stage.ap()[0:nrow, 0:2048], stage.ap()[0:nrow, 2048:4096]]
            EV = [stage.ap()[0:nrow, 4096:6144], stage.ap()[0:nrow, 6144:8192]]
            pse = [(self.p01, self.p23), (self.p45, self.p67)]
            mmv, evv, stv, ldv = {}, {}, {}, {}
            for p in range(min(2, n_iter)):
                self.dma("scalar", LD[p % 2], ld_src(p), ctr=self.pl[p % 2])
                ldv[p] = self.pl[p % 2].v
            for p in range(n_iter):
                q = p % 2
                self.qw("tensor", self.pl[q], ldv[p])
                if p >= 2:
                    self.qw("tensor", self.sV, evv[p - 2])
                for k in range(4):
                    ps = pse[q][k // 2][0:nrow, (k % 2) * 512:(k % 2) * 512 + 512]
                    self.pe(lambda e, ps=ps, k=k, q=q, hm=hmat_sb: e.matmul(
                        ps, hm, LD[q][0:nrow, k * 512:(k + 1) * 512],
                        start=True, stop=True), inc=(k == 3))
                mmv[p] = self.sP.v
                self.qw("vector", self.sP, mmv[p])
                if p >= 2:
                    self.qw("vector", self.px[q], stv[p - 2])
                self.ve(lambda e, q=q, nrow=nrow: e.tensor_copy(
                    out=EV[q][0:nrow, 0:1024], in_=pse[q][0][0:nrow, :]))
                evv[p] = self.ve(lambda e, q=q, nrow=nrow: e.tensor_copy(
                    out=EV[q][0:nrow, 1024:2048], in_=pse[q][1][0:nrow, :]), inc=True)
                if p + 2 < n_iter:
                    self.qw("scalar", self.sP, mmv[p])
                    self.dma("scalar", LD[q], ld_src(p + 2), ctr=self.pl[q])
                    ldv[p + 2] = self.pl[q].v
                self.qw("scalar", self.sV, evv[p])
                st_dst(p, EV[q], self.px[q])
                stv[p] = self.px[q].v

        pipelined_pass(
            128, 64,
            lambda p: wdT_v[:, p, :],
            lambda p, ev, ctr: self.dma("scalar", wdH_v[:, p, :], ev[0:64, 0:2048], ctr=ctr),
            self.h64_sb[:])
        self.qw("scalar", self.px[0])
        self.qw("scalar", self.px[1])

        wd2_v = self.wd2.rearrange("ct p ic c -> p ct ic c")

        def h128_store(ic, ev, ctr):
            for c4 in range(4):
                self.dma("scalar", wd2_v[:, c4 * 4:(c4 + 1) * 4, ic, :],
                         ev[:, c4 * 512:(c4 + 1) * 512].rearrange(
                             "p (ct c) -> p ct c", ct=4, c=128), ctr=ctr)

        pipelined_pass(
            NI, 128,
            lambda ic: self.wdH[ic * 128:(ic + 1) * 128, :],
            h128_store,
            self.h128_sb[:])
        self.prep_mms_done = self.dP.v
        self.tap("wd2", self.wd2)
        self.phase_done("prep")

    def ph_ln(self, with_su):
        xln = self.B[4]
        sums_lo, sums_hi = self.p67[0:1, 0:512], self.p67[0:1, 512:1024]
        sq_lo, sq_hi = self.p45[0:1, 0:512], self.p45[0:1, 512:1024]
        for c in range(NF):
            self.qw("sync", self.sV)
            self.dma("sync", self.io32[:, 0:1024], self.xstream[:, c * 1024:(c + 1) * 1024])
            self.qw("vector", self.dS)
            self.ve(lambda e, c=c: e.tensor_copy(out=xln[:, c * 1024:(c + 1) * 1024],
                                                 in_=self.io32[:, 0:1024]), inc=True)
            self.qw("scalar", self.sV)
            self.qw("scalar", self.sP)
            self.ae(lambda e, c=c: e.activation(out=self.sqs[:, 0:1024],
                    in_=xln[:, c * 1024:(c + 1) * 1024], func=AF.Square), inc=True)
            self.qw("tensor", self.sV)
            self.pe(lambda e, c=c: e.matmul(sums_lo, self.onescol_sb[:],
                    xln[:, c * 1024:c * 1024 + 512], start=(c == 0), stop=(c == NF - 1)))
            self.pe(lambda e, c=c: e.matmul(sums_hi, self.onescol_sb[:],
                    xln[:, c * 1024 + 512:(c + 1) * 1024], start=(c == 0), stop=(c == NF - 1)))
            self.qw("tensor", self.sA)
            self.pe(lambda e, c=c: e.matmul(sq_lo, self.onescol_sb[:], self.sqs[:, 0:512],
                    start=(c == 0), stop=(c == NF - 1)))
            self.pe(lambda e, c=c: e.matmul(sq_hi, self.onescol_sb[:], self.sqs[:, 512:1024],
                    start=(c == 0), stop=(c == NF - 1)), inc=True)
        mu = self.rows32[0:1, 0:1024]
        ex2 = self.rows32[0:1, 1024:2048]
        self.qw("vector", self.sP)
        self.ve(lambda e: e.tensor_scalar(out=mu, in0=self.p67[0:1, :], scalar1=1.0 / 2048.0,
                                          scalar2=None, op0=ALU.mult), inc=True)
        self.qw("vector", self.sV)
        self.ve(lambda e: e.tensor_scalar(out=ex2, in0=self.p45[0:1, :], scalar1=1.0 / 2048.0,
                                          scalar2=float(EPS), op0=ALU.mult, op1=ALU.add),
                inc=True)
        for k in range(2):
            self.qw("vector", self.sV)
            self.ve(lambda e, k=k: e.scalar_tensor_tensor(
                out=self.r1[0:1, 0:512], in0=mu[0:1, k * 512:(k + 1) * 512], scalar=-1.0,
                in1=mu[0:1, k * 512:(k + 1) * 512], op0=ALU.mult, op1=ALU.mult), inc=True)
            self.qw("vector", self.sV)
            self.ve(lambda e, k=k: e.tensor_tensor(
                out=ex2[0:1, k * 512:(k + 1) * 512], in0=ex2[0:1, k * 512:(k + 1) * 512],
                in1=self.r1[0:1, 0:512], op=ALU.add), inc=True)
        self.qw("scalar", self.sV)
        self.ae(lambda e: e.activation(out=ex2, in_=ex2, func=AF.Ln), inc=True)
        self.qw("scalar", self.sA)
        self.ae(lambda e: e.activation(out=ex2, in_=ex2, func=AF.Exp, scale=-0.5), inc=True)
        self.qw("tensor", self.sA)
        for k in range(2):
            self.pe(lambda e, k=k: e.matmul(self.p01[:, k * 512:(k + 1) * 512],
                    self.onesrow_sb[:], mu[0:1, k * 512:(k + 1) * 512], start=True, stop=True))
            self.pe(lambda e, k=k: e.matmul(self.p23[:, k * 512:(k + 1) * 512],
                    self.onesrow_sb[:], ex2[0:1, k * 512:(k + 1) * 512], start=True, stop=True),
                    inc=(k == 1))
        wbc = self.sP.v
        for c in range(NF):
            self.qw("vector", self.sP, wbc)
            self.qw("vector", self.sV)
            self.ve(lambda e, c=c: e.tensor_tensor(out=self.io32[:, 0:1024],
                    in0=xln[:, c * 1024:(c + 1) * 1024], in1=self.p01[:], op=ALU.subtract),
                    inc=True)
            self.qw("vector", self.sV)
            su = self.sucol[:, c:c + 1] if with_su else 1.0
            self.ve(lambda e, c=c, su=su: e.scalar_tensor_tensor(
                out=xln[:, c * 1024:(c + 1) * 1024], in0=self.io32[:, 0:1024],
                scalar=su, in1=self.p23[:], op0=ALU.mult, op1=ALU.mult), inc=True)
        self.barrier()

    def ph_ffn(self):
        self.ph_ln(with_su=True)
        self.tap_sbuf("xln", self.B[4][:])
        xln = self.B[4]
        mm_done = {}
        ev_done = {}

        def maybe_evac(cur_c):
            for k in range(NF - 1, -1, -1):
                if k in mm_done and k not in ev_done and 2 * k >= cur_c:
                    psk = mm_done[k][1]
                    self.qw("vector", self.sP, mm_done[k][0])
                    ev_done[k] = self.ve(lambda e, k=k, psk=psk: e.tensor_copy(
                        out=self.x1f[:, k * 1024:(k + 1) * 1024], in_=psk[:]), inc=True)

        for idx, c in enumerate(range(NF - 1, -1, -1)):
            ps = self.p01 if (idx % 2 == 0) else self.p23
            if idx >= 2:
                prev_k = NF - 1 - (idx - 2)
                self.qw("tensor", self.sV, ev_done[prev_k])
            self.pe(lambda e, c=c, ps=ps: e.matmul(ps[:, 0:512], self.h128_sb[:],
                    xln[:, c * 1024:c * 1024 + 512], start=True, stop=True))
            v = self.pe(lambda e, c=c, ps=ps: e.matmul(ps[:, 512:1024], self.h128_sb[:],
                    xln[:, c * 1024 + 512:(c + 1) * 1024], start=True, stop=True), inc=True)
            mm_done[c] = (v, ps)
            maybe_evac(c)
        maybe_evac(-100)
        for s in (1, 2, 4, 8):
            for a0 in range(NF):
                if a0 & s:
                    continue
                b0 = a0 + s
                self.qw("vector", self.sV)
                self.ve(lambda e, a0=a0: e.tensor_copy(out=self.io32[:, 0:1024],
                        in_=self.x1f[:, a0 * 1024:(a0 + 1) * 1024]), inc=True)
                self.qw("vector", self.sV)
                self.ve(lambda e, a0=a0, b0=b0: e.tensor_tensor(
                    out=self.x1f[:, a0 * 1024:(a0 + 1) * 1024],
                    in0=self.x1f[:, a0 * 1024:(a0 + 1) * 1024],
                    in1=self.x1f[:, b0 * 1024:(b0 + 1) * 1024], op=ALU.add), inc=True)
                self.qw("vector", self.sV)
                self.ve(lambda e, a0=a0, b0=b0: e.tensor_tensor(
                    out=self.x1f[:, b0 * 1024:(b0 + 1) * 1024], in0=self.io32[:, 0:1024],
                    in1=self.x1f[:, b0 * 1024:(b0 + 1) * 1024], op=ALU.subtract), inc=True)
        self.tap_sbuf("x1", self.x1f[:])
        x1b = self.B[4]
        self.qw("vector", self.sV)
        self.ve(lambda e: e.tensor_copy(out=self.ptb[:, 0:1024], in_=self.x1f[:, 0:1024]),
                inc=True)
        for c in range(1, NF):
            self.qw("vector", self.sV)
            self.ve(lambda e, c=c: e.tensor_copy(out=x1b[:, c * 1024:(c + 1) * 1024],
                    in_=self.x1f[:, c * 1024:(c + 1) * 1024]), inc=True)
        self.qw("vector", self.sV)
        self.ve(lambda e: e.tensor_copy(out=x1b[:, 0:1024], in_=self.ptb[:, 0:1024]), inc=True)
        self.barrier()

        upbuf = [self.B[5].ap()[:, 0:4096], self.B[5].ap()[:, 4096:8192]]
        gabuf = [self.B[5].ap()[:, 8192:12288], self.B[5].ap()[:, 12288:16384]]
        wupT_v = self.wupT.rearrange("(f p) i -> p f i", f=16, p=128)
        wgaT_v = self.wgateT.rearrange("(f p) i -> p f i", f=16, p=128)

        def h_ap(ic):
            return self.B[ic // 16][:, (ic % 16) * 1024:(ic % 16) * 1024 + 1024]

        nslab = NI // 2
        mm_after_slab = {}
        slab_w = {}
        hdone = {}
        for sl in range(nslab):
            if sl >= 2:
                self.qw("sync", self.sP, mm_after_slab[sl - 2])
            usem = self.u2[sl % 2]
            for half in range(2):
                self.dma("sync", upbuf[sl % 2][:, half * 2048:(half + 1) * 2048].rearrange(
                    "p (f i) -> p f i", f=8, i=256),
                    wupT_v[:, half * 8:(half + 1) * 8, sl * 256:(sl + 1) * 256], ctr=usem)
                self.dma("sync", gabuf[sl % 2][:, half * 2048:(half + 1) * 2048].rearrange(
                    "p (f i) -> p f i", f=8, i=256),
                    wgaT_v[:, half * 8:(half + 1) * 8, sl * 256:(sl + 1) * 256], ctr=usem)
            slab_w[sl] = usem.v
            for j in range(2):
                ic = sl * 2 + j
                pu = self.p01 if (ic % 2 == 0) else self.p45
                pg = self.p23 if (ic % 2 == 0) else self.p67
                self.qw("tensor", self.u2[sl % 2], slab_w[sl])
                if ic >= 2:
                    self.qw("tensor", self.sV, hdone[ic - 2][0])
                    self.qw("tensor", self.sA, hdone[ic - 2][1])
                for f in range(16):
                    off = (f // 8) * 2048 + (f % 8) * 512 + j * 256
                    lhs_u = upbuf[sl % 2][:, off:off + 128] if False else None
                    hh = f // 8
                    base = hh * 2048 + (f % 8) * 256 + j * 128
                    lu = upbuf[sl % 2][:, base:base + 128]
                    lg = gabuf[sl % 2][:, base:base + 128]
                    self.pe(lambda e, f=f, pu=pu, lu=lu: e.matmul(
                        pu[:, 0:512], lu, x1b[:, f * 1024:f * 1024 + 512],
                        start=(f == 0), stop=(f == 15)))
                    self.pe(lambda e, f=f, pu=pu, lu=lu: e.matmul(
                        pu[:, 512:1024], lu, x1b[:, f * 1024 + 512:(f + 1) * 1024],
                        start=(f == 0), stop=(f == 15)))
                    self.pe(lambda e, f=f, pg=pg, lg=lg: e.matmul(
                        pg[:, 0:512], lg, x1b[:, f * 1024:f * 1024 + 512],
                        start=(f == 0), stop=(f == 15)))
                    self.pe(lambda e, f=f, pg=pg, lg=lg: e.matmul(
                        pg[:, 512:1024], lg, x1b[:, f * 1024 + 512:(f + 1) * 1024],
                        start=(f == 0), stop=(f == 15)), inc=(f == 15))
                wmm = self.sP.v
                mm_after_slab[sl] = wmm
                self.qw("scalar", self.sP, wmm)
                if ic >= 1:
                    self.qw("scalar", self.sV, hdone[ic - 1][0])
                vA = self.ae(lambda e, ic=ic, pg=pg: e.activation(
                    out=self.silu_sb[:], in_=pg[:], func=AF.Silu,
                    scale=self.wsgcol[:, ic:ic + 1]), inc=True)
                self.qw("vector", self.sA, vA)
                vV = self.ve(lambda e, ic=ic, pu=pu: e.scalar_tensor_tensor(
                    out=h_ap(ic), in0=pu[:], scalar=self.wsucol[:, ic:ic + 1],
                    in1=self.silu_sb[:], op0=ALU.mult, op1=ALU.mult), inc=True)
                hdone[ic] = (vV, vA)
        self.barrier()
        if "h" in self.dbg:
            self.dma("sync", self.dbg["h"].rearrange("p (r n) -> r p n", r=4, n=16384)[0], self.B[0][:])
            self.dma("sync", self.dbg["h"].rearrange("p (r n) -> r p n", r=4, n=16384)[1], self.B[1][:])
            self.dma("sync", self.dbg["h"].rearrange("p (r n) -> r p n", r=4, n=16384)[2], self.B[2][:])
            self.dma("sync", self.dbg["h"].rearrange("p (r n) -> r p n", r=4, n=16384)[3], self.B[3][:])
            self.qw("sync", self.dS)

        wdb = [self.B[5].ap()[:, 0:8192], self.B[5].ap()[:, 8192:16384]]
        dn_mm = {}
        for ct in range(NF):
            if ct >= 2:
                self.qw("sync", self.sP, dn_mm[ct - 2])
            self.dma("sync", wdb[ct % 2], self.wd2[ct].rearrange("p ic c -> p (ic c)"),
                     ctr=self.d2[ct % 2])
            wW = self.d2[ct % 2].v
            ps = self.p01 if (ct % 2 == 0) else self.p23
            self.qw("tensor", self.d2[ct % 2], wW)
            self.qw("tensor", self.sV)
            for ic in range(NI):
                self.pe(lambda e, ct=ct, ic=ic, ps=ps: e.matmul(
                    ps[:, 0:512], wdb[ct % 2][:, ic * 128:(ic + 1) * 128],
                    h_ap(ic)[:, 0:512], start=(ic == 0), stop=(ic == NI - 1)))
                self.pe(lambda e, ct=ct, ic=ic, ps=ps: e.matmul(
                    ps[:, 512:1024], wdb[ct % 2][:, ic * 128:(ic + 1) * 128],
                    h_ap(ic)[:, 512:1024], start=(ic == 0), stop=(ic == NI - 1)),
                    inc=(ic == NI - 1))
            wmm = self.sP.v
            dn_mm[ct] = wmm
            self.qw("sync", self.sV)
            self.dma("sync", self.io32[:, 0:1024], self.xstream[:, ct * 1024:(ct + 1) * 1024])
            wS = self.dS.v
            self.qw("vector", self.sP, wmm)
            self.qw("vector", self.dS, wS)
            self.ve(lambda e, ct=ct, ps=ps: e.scalar_tensor_tensor(
                out=self.io32[:, 1024:2048], in0=ps[:], scalar=self.wsdcol[:, ct:ct + 1],
                in1=self.io32[:, 0:1024], op0=ALU.mult, op1=ALU.add), inc=True)
            self.qw("sync", self.sV)
            self.dma("sync", self.xstream[:, ct * 1024:(ct + 1) * 1024], self.io32[:, 1024:2048])
        self.barrier()
        self.tap("stream", self.xstream)

    def _colslab_proj(self, wsrc, bias_col, out_cb, rhs, nk=NF, wcols=None):
        wcb = [self.B[5].ap()[:, 0:2048], self.B[5].ap()[:, 2048:4096]]
        mmv = {}
        for ct in range(NF):
            if ct >= 2:
                self.qw("gpsimd", self.sP, mmv[ct - 2])
            src = wsrc.rearrange("(k p) (ct c) -> p ct k c", p=128, k=nk, ct=NF, c=128)
            self.dma("gpsimd", wcb[ct % 2].rearrange("p (k c) -> p k c", k=nk, c=128),
                     src[:, ct, :, :], ctr=self.c2[ct % 2])
            wW = self.c2[ct % 2].v
            ps = self.p01 if (ct % 2 == 0) else self.p23
            self.qw("tensor", self.c2[ct % 2], wW)
            self.qw("tensor", self.sV)
            for k in range(nk):
                self.pe(lambda e, k=k, ct=ct, ps=ps: e.matmul(
                    ps[:, 0:512], wcb[ct % 2][:, k * 128:(k + 1) * 128], rhs(k, 0),
                    start=(k == 0), stop=(k == nk - 1)))
                self.pe(lambda e, k=k, ct=ct, ps=ps: e.matmul(
                    ps[:, 512:1024], wcb[ct % 2][:, k * 128:(k + 1) * 128], rhs(k, 1),
                    start=(k == 0), stop=(k == nk - 1)), inc=(k == nk - 1))
            wmm = self.sP.v
            mmv[ct] = wmm
            out_cb(ct, ps, wmm)

    def ph_attn(self, variant, mid_hook=None):
        a = variant == "a"
        self.ph_ln(with_su=False)
        self.tap_sbuf("xln", self.B[4][:])
        xln = self.B[4]
        agin_v = self.ag_in.rearrange("(c p) t -> p c t", c=NF, p=128)
        self.dma("gpsimd", agin_v, xln.ap().rearrange("p (c t) -> p c t", c=NF, t=TOWN))
        self.qw("gpsimd", self.dG)
        hcc = self.cc.h
        self.cc.v += 1
        rg = [list(range(NCORE))]
        self.q("gpsimd", lambda e: e.collective_compute(
            "AllGather", ALU.bypass, replica_groups=rg,
            ins=[self.ag_in.opt()], outs=[self.ag_out.opt()]).then_inc(hcc, 1))
        wcc = self.cc.v

        KT, VT, QT, QFL = self.B[0], self.B[1], self.B[2], self.B[3]
        pfx = "a" if a else "b"

        if not a:
            def qfl_out(ct, ps, wmm):
                self.qw("vector", self.sP, wmm)
                self.ve(lambda e, ct=ct, ps=ps: e.tensor_scalar(
                    out=QFL[:, ct * 1024:(ct + 1) * 1024], in0=ps[:],
                    scalar1=self.bcol["bbq"][:, ct:ct + 1], scalar2=None, op0=ALU.add),
                    inc=True)

            self._colslab_proj(self.inp["bWq"], None, qfl_out,
                               lambda f, hh: xln[:, f * 1024 + hh * 512:f * 1024 + hh * 512 + 512])
            self.qw("vector", self.sV)
            qfl_v = QFL.ap().rearrange("p (j u) -> p j u", j=16, u=1024)
            for hl in range(2):
                src = qfl_v[:, :, hl * 512:(hl + 1) * 512].rearrange("p j v -> p v j")
                dst = QT.ap()[:, hl * 8192:(hl + 1) * 8192].rearrange(
                    "p (v j) -> p v j", v=512, j=16)
                self.ve(lambda e, src=src, dst=dst: e.tensor_copy(out=dst, in_=src), inc=True)

        wk_sl = self.B[5].ap()[:, 4096:8192]
        wv_sl = self.B[5].ap()[:, 8192:12288]
        wq_sl = self.B[5].ap()[:, 12288:16384]
        mats = [("k", wk_sl), ("v", wv_sl)] + ([("q", wq_sl)] if a else [])
        for nm, dstb in mats:
            for f in range(NF):
                for hl in range(2):
                    if a:
                        wsrc = self.inp["aW" + nm + "_my"][hl, f * 128:(f + 1) * 128, :]
                    else:
                        wsrc = self.inp["bW" + nm + "_my"][f * 128:(f + 1) * 128,
                                                           hl * 128:(hl + 1) * 128]
                    self.dma("gpsimd", dstb[:, f * 256 + hl * 128:f * 256 + hl * 128 + 128],
                             wsrc, ctr=self.dW)
        wslices = self.dW.v

        if a:
            bsrc = self.inp["abv_my"].rearrange("h d -> (h d)")
        else:
            bsrc = self.inp["bbv_my"]
        self.dma("sync", self.r1[0:1, 0:256], bsrc.rearrange("(o n) -> o n", o=1))
        wb = self.dS.v
        self.qw("tensor", self.dS, wb)
        self.qw("tensor", self.sV)
        self.pe(lambda e: e.matmul(self.p45[:, 0:256], self.onesrow_sb[:], self.r1[0:1, 0:256],
                                   start=True, stop=True), inc=True)
        self.qw("vector", self.sP)
        self.ve(lambda e: e.tensor_copy(out=self.bvb[:], in_=self.p45[:, 0:256]), inc=True)

        self.qw("sync", self.cc, wcc)
        self.qw("tensor", self.cc, wcc)
        self.qw("tensor", self.dW, wslices)

        G_v = self.ag_out.rearrange("(r c p) t -> r c p t", r=NCORE, c=NF, p=128)

        def gload(r, f, par):
            self.dma("sync", self.gbuf[par][:], G_v[r, f], ctr=self.g2[par])
            return self.g2[par].v

        bk = self.bcol["abk_my" if a else "bbk_my"]
        bq = self.bcol["abq_my"] if a else None

        def kq_pass(wsl, dstT, bias_col):
            gmm = []
            for r in range(NCORE):
                for f in range(NF):
                    par = f % 2
                    n = r * NF + f
                    if n >= 2:
                        self.qw("sync", self.sP, gmm[n - 2])
                    wS = gload(r, f, par)
                    self.qw("tensor", self.g2[par], wS)
                    if f == 0:
                        self.qw("tensor", self.sV)
                    for hl in range(2):
                        ps = self.p01 if hl == 0 else self.p23
                        for t2 in range(2):
                            self.pe(lambda e, f=f, hl=hl, t2=t2, ps=ps, par=par, wsl=wsl:
                                    e.matmul(ps[:, t2 * 512:(t2 + 1) * 512],
                                             wsl[:, f * 256 + hl * 128:f * 256 + hl * 128 + 128],
                                             self.gbuf[par][:, t2 * 512:(t2 + 1) * 512],
                                             start=(f == 0), stop=(f == NF - 1)),
                                    inc=(hl == 1 and t2 == 1))
                    gmm.append(self.sP.v)
                self.qw("vector", self.sP, gmm[-1])
                for hl in range(2):
                    ps = self.p01 if hl == 0 else self.p23
                    bcolv = bias_col[:, hl:hl + 1]
                    for t2 in range(2):
                        self.ve(lambda e, r=r, hl=hl, t2=t2, ps=ps, bcolv=bcolv: e.tensor_scalar(
                            out=dstT[:, hl * 8192 + r * 1024 + t2 * 512:
                                     hl * 8192 + r * 1024 + t2 * 512 + 512],
                            in0=ps[:, t2 * 512:(t2 + 1) * 512], scalar1=bcolv, scalar2=None,
                            op0=ALU.add), inc=(hl == 1 and t2 == 1))

        kq_pass(wk_sl, KT, bk)
        self.qw("sync", self.sP)
        if a:
            kq_pass(wq_sl, QT, bq)
            self.qw("sync", self.sP)

        gmmv = []
        for r in range(NCORE):
            vmm = 0
            for f in range(NF):
                par = f % 2
                n = r * NF + f
                if n >= 2:
                    self.qw("sync", self.sP, gmmv[n - 2])
                wS = gload(r, f, par)
                self.qw("tensor", self.g2[par], wS)
                if f == 0:
                    self.qw("tensor", self.sV)
                for tc in range(8):
                    ps = (self.p01 if tc < 4 else self.p23)
                    ofs = (tc % 4) * 256
                    self.pe(lambda e, f=f, tc=tc, ps=ps, ofs=ofs, par=par: e.matmul(
                        ps[:, ofs:ofs + 256], self.gbuf[par][:, tc * 128:(tc + 1) * 128],
                        wv_sl[:, f * 256:(f + 1) * 256],
                        start=(f == 0 and tc % 2 == 0),
                        stop=(f == NF - 1 and tc % 2 == 1)), inc=(tc == 7))
                vmm = self.sP.v
                gmmv.append(vmm)
            self.qw("vector", self.sP, vmm)
            for tc in range(8):
                ps = (self.p01 if tc < 4 else self.p23)
                ofs = (tc % 4) * 256
                self.ve(lambda e, r=r, tc=tc, ps=ps, ofs=ofs: e.tensor_tensor(
                    out=VT[:, (r * 8 + tc) * 256:(r * 8 + tc) * 256 + 256],
                    in0=ps[:, ofs:ofs + 256], in1=self.bvb[:], op=ALU.add), inc=(tc == 7))
        self.barrier()
        if "K" in self.dbg:
            self.dma("sync", self.dbg["K"], KT[:])
            self.dma("sync", self.dbg["V"], VT[:])
            self.dma("sync", self.dbg["Q"], QT[:])
            self.qw("sync", self.dS)
        if "G" in self.dbg:
            self.tap("G", self.ag_out)
        self.phase_done("proj_" + pfx)

        for hl in range(2):
            for blk in range(16):
                self.qw("scalar", self.sP)
                self.ae(lambda e, hl=hl, blk=blk: e.activation(
                    out=self.sqs[:, 0:512], in_=KT[:, hl * 8192 + blk * 512:hl * 8192 + blk * 512 + 512],
                    func=AF.Square), inc=True)
                self.qw("tensor", self.sA)
                self.qw("tensor", self.sV)
                self.pe(lambda e: e.matmul(self.p45[0:1, 0:512], self.onescol_sb[:],
                        self.sqs[:, 0:512], start=True, stop=True), inc=True)
                self.qw("vector", self.sP)
                self.ve(lambda e, blk=blk: e.tensor_reduce(
                    out=self.r2[0:1, blk:blk + 1], in_=self.p45[0:1, 0:512], axis=AX.X,
                    op=ALU.max), inc=True)
            self.qw("vector", self.sV)
            self.ve(lambda e: e.tensor_reduce(out=self.r2[0:1, 16:17], in_=self.r2[0:1, 0:16],
                                              axis=AX.X, op=ALU.max), inc=True)
            for blk in range(16):
                self.qw("scalar", self.sP)
                self.ae(lambda e, hl=hl, blk=blk: e.activation(
                    out=self.sqs[:, 0:512], in_=QT[:, hl * 8192 + blk * 512:hl * 8192 + blk * 512 + 512],
                    func=AF.Square), inc=True)
                self.qw("tensor", self.sA)
                self.qw("tensor", self.sV)
                self.pe(lambda e: e.matmul(self.p45[0:1, 0:512], self.onescol_sb[:],
                        self.sqs[:, 0:512], start=True, stop=True), inc=True)
                self.qw("vector", self.sP)
                self.ve(lambda e, hl=hl, blk=blk: e.tensor_reduce(
                    out=self.qb1[0:1, hl * 16 + blk:hl * 16 + blk + 1],
                    in_=self.p45[0:1, 0:512], axis=AX.X, op=ALU.max), inc=True)
            self.qw("vector", self.sV)
            self.ve(lambda e, hl=hl: e.tensor_scalar(
                out=self.r1[0:1, 0:16], in0=self.qb1[0:1, hl * 16:(hl + 1) * 16],
                scalar1=self.r2[0:1, 16:17], scalar2=float((ATT_SCALE * 1.002) ** 2),
                op0=ALU.mult, op1=ALU.mult), inc=True)
            self.qw("scalar", self.sV)
            self.ae(lambda e: e.activation(out=self.r1[0:1, 0:16], in_=self.r1[0:1, 0:16],
                                           func=AF.Ln), inc=True)
            self.qw("scalar", self.sA)
            self.ae(lambda e: e.activation(out=self.r1[0:1, 0:16], in_=self.r1[0:1, 0:16],
                    func=AF.Exp, scale=0.5), inc=True)
            self.qw("vector", self.sA)
            self.ve(lambda e: e.tensor_scalar(out=self.r1[0:1, 16:32], in0=self.r1[0:1, 0:16],
                    scalar1=-1.0, scalar2=None, op0=ALU.mult), inc=True)
            self.qw("tensor", self.sV)
            self.pe(lambda e: e.matmul(self.p45[:, 0:16], self.onesrow_sb[:],
                    self.r1[0:1, 16:32], start=True, stop=True), inc=True)
            self.qw("vector", self.sP)
            self.ve(lambda e, hl=hl: e.tensor_copy(out=self.cbias[:, hl * 16:(hl + 1) * 16],
                    in_=self.p45[:, 0:16]), inc=True)
        self.barrier()

        self.dma("sync", self.maskb[:], self.masks_t.ap())
        self.qw("sync", self.dS)
        for eng in ("tensor", "scalar", "vector"):
            self.qw(eng, self.dS)
        PP = [self.p45, self.p67]
        ep_sv = {0: 0, 1: 0}
        ep_sa = {0: 0, 1: 0}
        exp_roll = []
        cons_roll = []
        ystore = {0: 0, 1: 0}
        pending_ep = []
        gpair = 0
        tbi = 0
        for hl in range(2):
            for tb in range(NF):
                nsc = 4 * (tb + 1)
                npair = nsc // 2
                pi = tbi % 2
                outT = PP[pi][:, 0:512]
                sums = PP[pi][0:1, 512:1024]
                qrhs0 = QT[:, hl * 8192 + tb * 512:hl * 8192 + tb * 512 + 512]
                for pr in range(npair):
                    ps = self.p01 if (gpair % 2 == 0) else self.p23
                    par = gpair % 2
                    if len(exp_roll) >= 2:
                        self.qw("tensor", self.sA, exp_roll[-2])
                    for k in range(2):
                        sc = 2 * pr + k
                        self.pe(lambda e, sc=sc, k=k, ps=ps, hl=hl, qrhs0=qrhs0: e.matmul(
                            ps[:, k * 512:(k + 1) * 512],
                            KT[:, hl * 8192 + sc * 128:hl * 8192 + sc * 128 + 128],
                            qrhs0, start=True, stop=True), inc=(k == 1))
                    wS = self.sP.v
                    if pr == 1 and pending_ep:
                        pending_ep.pop(0)()
                    self.qw("scalar", self.sP, wS)
                    if len(cons_roll) >= 2:
                        self.qw("scalar", self.sP, cons_roll[-2])
                    ev = self.ae(lambda e, ps=ps, par=par, hl=hl, tb=tb: e.activation(
                        out=self.ptb[:, par * 1024:(par + 1) * 1024], in_=ps[:], func=AF.Exp,
                        scale=ATT_SCALE, bias=self.cbias[:, hl * 16 + tb:hl * 16 + tb + 1]),
                        inc=True)
                    exp_roll.append(ev)
                    wmask = None
                    for k in range(2):
                        sc = 2 * pr + k
                        if sc >= 4 * tb:
                            kk = sc - 4 * tb
                            self.qw("vector", self.sA, ev)
                            wmask = self.ve(lambda e, k=k, par=par, kk=kk: e.tensor_tensor(
                                out=self.ptb[:, par * 1024 + k * 512:par * 1024 + (k + 1) * 512],
                                in0=self.ptb[:, par * 1024 + k * 512:par * 1024 + (k + 1) * 512],
                                in1=self.maskb[:, kk * 512:(kk + 1) * 512], op=ALU.mult),
                                inc=True)
                    if wmask is not None:
                        self.qw("tensor", self.sV, wmask)
                    else:
                        self.qw("tensor", self.sA, ev)
                    if pr == 0:
                        self.qw("tensor", self.sV, ep_sv[pi])
                        self.qw("tensor", self.sA, ep_sa[pi])
                    for k in range(2):
                        sc = 2 * pr + k
                        st = (sc == 0)
                        sp = (sc == nsc - 1)
                        self.pe(lambda e, k=k, par=par, st=st, sp=sp, sums=sums: e.matmul(
                            sums, self.onescol_sb[:],
                            self.ptb[:, par * 1024 + k * 512:par * 1024 + (k + 1) * 512],
                            start=st, stop=sp))
                        self.pe(lambda e, k=k, par=par, sc=sc, hl=hl, st=st, sp=sp,
                                outT=outT: e.matmul(
                            outT, VT[:, sc * 256 + hl * 128:sc * 256 + hl * 128 + 128],
                            self.ptb[:, par * 1024 + k * 512:par * 1024 + (k + 1) * 512],
                            start=st, stop=sp), inc=(k == 1))
                    cons_roll.append(self.sP.v)
                    gpair += 1
                wcons = cons_roll[-1]
                self.qw("scalar", self.sP, wcons)
                self.ae(lambda e, sums=sums: e.activation(out=self.r1[0:1, 0:512], in_=sums,
                                                          func=AF.Ln), inc=True)
                self.qw("scalar", self.sA)
                wrec = self.ae(lambda e: e.activation(out=self.r1[0:1, 0:512],
                               in_=self.r1[0:1, 0:512], func=AF.Exp, scale=-1.0), inc=True)
                ep_sa[pi] = wrec
                rbdst = PP[pi][:, 512:1024]

                def mk_ep(pi=pi, hl=hl, tb=tb, wrec=wrec, rbdst=rbdst, outT=outT):
                    def ep():
                        self.qw("tensor", self.sA, wrec)
                        wbc = self.pe(lambda e: e.matmul(rbdst, self.onesrow_sb[:],
                                      self.r1[0:1, 0:512], start=True, stop=True), inc=True)
                        self.qw("vector", self.sP, wbc)
                        wrb = self.ve(lambda e: e.tensor_copy(out=self.rb, in_=rbdst),
                                      inc=True)
                        ypar = pi
                        self.qw("vector", self.sV, wrb)
                        self.qw("vector", self.ys[ypar], ystore[ypar])
                        wye = self.ve(lambda e, ypar=ypar, outT=outT: e.scalar_tensor_tensor(
                            out=self.ysb[:, ypar * 512:(ypar + 1) * 512], in0=outT, scalar=1.0,
                            in1=self.rb, op0=ALU.mult, op1=ALU.mult), inc=True)
                        ep_sv[pi] = wye
                        self.qw("sync", self.sV, wye)
                        self.dma("sync", self.y_in[hl * 128:(hl + 1) * 128,
                                                   tb * 512:(tb + 1) * 512],
                                 self.ysb[:, ypar * 512:(ypar + 1) * 512],
                                 ctr=self.ys[ypar])
                        ystore[ypar] = self.ys[ypar].v
                    return ep

                pending_ep.append(mk_ep())
                tbi += 1
        while pending_ep:
            pending_ep.pop(0)()
        self.barrier()
        self.phase_done("scores_" + pfx)

        hcc = self.cc.h
        self.cc.v += 1
        self.q("gpsimd", lambda e: e.collective_compute(
            "AllGather", ALU.bypass, replica_groups=rg,
            ins=[self.y_in.opt()], outs=[self.y_out.opt()]).then_inc(hcc, 1))
        wcc2 = self.cc.v
        if mid_hook is not None:
            mid_hook()
        self.qw("sync", self.cc, wcc2)
        if "yout" in self.dbg:
            self.dma("sync", self.dbg["yout"], self.y_out)
            self.qw("sync", self.dS)

        ysall = self.B[3]
        if not hasattr(self, "_cid"):
            self._cid = self.nc.partition_id(engines=[mybir.EngineType.SP])
        cid = self._cid
        for k in range(16):
            self.dma("sync", ysall[:, k * 1024:(k + 1) * 1024],
                     self.y_out[k * 128:(k + 1) * 128, bass.ts(cid, TOWN)])
        wY = self.dS.v
        self.qw("tensor", self.dS, wY)
        self.qw("gpsimd", self.dS, wY)
        self.qw("sync", self.dS, wY)
        bo = self.bcol["abo" if a else "bbo"]

        def wo_out(ct, ps, wmm):
            self.qw("sync", self.sV)
            self.dma("sync", self.io32[:, 0:1024], self.xstream[:, ct * 1024:(ct + 1) * 1024])
            wS = self.dS.v
            self.qw("vector", self.sP, wmm)
            self.qw("vector", self.dS, wS)
            self.ve(lambda e, ct=ct, ps=ps: e.scalar_tensor_tensor(
                out=self.io32[:, 1024:2048], in0=ps[:], scalar=bo[:, ct:ct + 1],
                in1=self.io32[:, 0:1024], op0=ALU.add, op1=ALU.add), inc=True)
            self.qw("sync", self.sV)
            self.dma("sync", self.xstream[:, ct * 1024:(ct + 1) * 1024],
                     self.io32[:, 1024:2048])

        self._colslab_proj(self.inp["aWo" if a else "bWo"], None, wo_out,
                           lambda k, hh: ysall[:, k * 1024 + hh * 512:k * 1024 + hh * 512 + 512])
        self.barrier()
        self.tap("stream", self.xstream)
        self.phase_done("attn_" + pfx)

    def build(self, phases):
        self.declare_dram()
        self.declare_inline()
        self.alloc_sbuf()
        self.make_sems()
        for ph in phases:
            if self.stopped:
                break
            ph()
        with self.es:
            block = self.es.enter_context(self.nc.Block())
            for eng in ("tensor", "scalar", "vector", "gpsimd", "sync"):
                getattr(block, eng)(lambda e, eng=eng: [fn(e) for fn in self.ops[eng]])
        return self.nc


def build_core(core_id, taps=(), upto=None, which="full"):
    b = Builder(core_id, taps=taps, upto=upto)
    prep_serial = [b.ph_prep_casts, b.ph_prep_mms, b.barrier]
    if which == "ffn_only":
        phases = [b.ph_consts, b.ph_input] + prep_serial + [b.ph_ffn, b.ph_output]
    elif which == "attn_a_only":
        phases = [b.ph_consts, b.ph_input] + prep_serial + [lambda: b.ph_attn("a"), b.ph_output]
    elif which == "attn_b_only":
        phases = [b.ph_consts, b.ph_input] + prep_serial + [lambda: b.ph_attn("b"), b.ph_output]
    else:
        phases = [b.ph_consts, b.ph_input, b.ph_prep_casts,
                  lambda: b.ph_attn("a", mid_hook=b.ph_prep_mms), b.ph_ffn,
                  lambda: b.ph_attn("b"), b.ph_ffn, b.ph_output]
    return b.build(phases)


def make_in_map(inputs, core_id):
    x = np.asarray(inputs["L0_attn_a_x"]).reshape(T, D)
    h0 = 2 * core_id
    aWq = np.asarray(inputs["L0_attn_a_Wq"])
    aWk = np.asarray(inputs["L0_attn_a_Wk"])
    aWv = np.asarray(inputs["L0_attn_a_Wv"])
    bWk = np.asarray(inputs["L0_attn_b_Wk"])
    bWv = np.asarray(inputs["L0_attn_b_Wv"])
    m = {
        "x_own": np.ascontiguousarray(x[core_id * TOWN:(core_id + 1) * TOWN]),
        "aWq_my": aWq[h0:h0 + 2], "aWk_my": aWk[h0:h0 + 2], "aWv_my": aWv[h0:h0 + 2],
        "abq_my": np.asarray(inputs["L0_attn_a_bq"])[h0:h0 + 2],
        "abk_my": np.asarray(inputs["L0_attn_a_bk"])[h0:h0 + 2],
        "abv_my": np.asarray(inputs["L0_attn_a_bv"])[h0:h0 + 2],
        "aWo": inputs["L0_attn_a_Wo"], "abo": inputs["L0_attn_a_bo"],
        "bWq": inputs["L0_attn_b_Wq"],
        "bWk_my": bWk[:, h0 * 128:(h0 + 2) * 128],
        "bWv_my": bWv[:, h0 * 128:(h0 + 2) * 128],
        "bWo": inputs["L0_attn_b_Wo"],
        "bbq": inputs["L0_attn_b_bq"],
        "bbk_my": np.asarray(inputs["L0_attn_b_bk"])[h0 * 128:(h0 + 2) * 128],
        "bbv_my": np.asarray(inputs["L0_attn_b_bv"])[h0 * 128:(h0 + 2) * 128],
        "bbo": inputs["L0_attn_b_bo"],
        "su_ug": inputs["L0_ffn_SU_ug"], "su_dp": inputs["L0_ffn_SU_dp"],
        "wsug": inputs["L0_ffn_Wscale_ug"], "wsdp": inputs["L0_ffn_Wscale_dp"],
        "fWupT": np.ascontiguousarray(np.asarray(inputs["L0_ffn_W_up"]).T),
        "fWgateT": np.ascontiguousarray(np.asarray(inputs["L0_ffn_W_gate"]).T),
        "fWdownT": np.ascontiguousarray(np.asarray(inputs["L0_ffn_W_down"]).T),
    }
    return {k: np.ascontiguousarray(np.asarray(v, dtype=np.float32)) for k, v in m.items()}


def kernel(**inputs):
    nc = build_core(0)
    in_maps = [make_in_map(inputs, c) for c in range(NCORE)]
    res = run_bass_kernel_spmd(nc, in_maps, core_ids=list(range(NCORE)))
    out = np.concatenate([np.asarray(res.results[c]["y_own"]) for c in range(NCORE)], axis=0)
    return out.reshape(1, T, D).astype(np.float32)


# version 08
# speedup vs baseline: 1.7401x; 1.2681x over previous
import math
import contextlib
import numpy as np
import ml_dtypes

import concourse.bass as bass
import concourse.mybir as mybir
from concourse.bass_utils import run_bass_kernel_spmd

F32 = mybir.dt.float32
BF16 = mybir.dt.bfloat16
AF = mybir.ActivationFunctionType
ALU = mybir.AluOpType
AX = mybir.AxisListType

NCORE = 8
T = 8192
D = 2048
TOWN = 1024
NF = 16
NI = 64
INTER = 8192
HPC = 2
DH = 128
ATT_SCALE = 1.0 / math.sqrt(128.0)
EPS = 1e-5
R2048 = 1.0 / math.sqrt(2048.0)
R8192 = 1.0 / math.sqrt(8192.0)
BF = ml_dtypes.bfloat16


def _hmat(n):
    a = np.arange(n)
    bits = a[:, None] & a[None, :]
    pc = np.zeros((n, n), dtype=np.int64)
    x = bits.copy()
    while x.any():
        pc += x & 1
        x >>= 1
    return np.where(pc % 2 == 0, 1.0, -1.0).astype(np.float32)


class Ctr:
    def __init__(self, es, nc, name):
        self.h = es.enter_context(nc.semaphore(name))
        self.v = 0


class Builder:

    def __init__(self, core_id, taps=(), upto=None):
        self.core_id = core_id
        self.taps = set(taps)
        self.upto = upto
        self.stopped = False
        self.nc = bass.Bass("TRN2", target_bir_lowering=False,
                    dynamic_dma_scratch_size=4096)
        self.es = contextlib.ExitStack()
        self.ops = {k: [] for k in ("tensor", "scalar", "vector", "gpsimd", "sync")}
        self.H0 = 2 * core_id

    def q(self, eng, fn):
        if not self.stopped:
            self.ops[eng].append(fn)

    def qw(self, eng, ctr, val=None):
        v = ctr.v if val is None else val
        if v <= 0:
            return
        h = ctr.h
        self.q(eng, lambda e: e.wait_ge(h, v))

    def phase_done(self, label):
        if self.upto is not None and label == self.upto:
            self.stopped = True

    def pe(self, fn, inc=False):
        if inc:
            h = self.sP.h
            self.sP.v += 1
            self.q("tensor", lambda e: fn(e).then_inc(h, 1))
        else:
            self.q("tensor", fn)
        return self.sP.v

    def ve(self, fn, inc=False):
        if inc:
            h = self.sV.h
            self.sV.v += 1
            self.q("vector", lambda e: fn(e).then_inc(h, 1))
        else:
            self.q("vector", fn)
        return self.sV.v

    def ae(self, fn, inc=False):
        if inc:
            h = self.sA.h
            self.sA.v += 1
            self.q("scalar", lambda e: fn(e).then_inc(h, 1))
        else:
            self.q("scalar", fn)
        return self.sA.v

    def dma(self, eng, out, in_, ctr=None, **kw):
        ctr = ctr if ctr is not None else (self.dG if eng == "gpsimd" else self.dS)
        h = ctr.h
        ctr.v += 16

        def _go(e, out=out, in_=in_, h=h, kw=kw):
            try:
                return e.dma_start(out=out, in_=in_, **kw).then_inc(h, 16)
            except Exception:
                import sys
                print("FAILING DMA out=", str(out)[:300], " in_=", str(in_)[:300],
                      file=sys.stderr)
                raise
        self.q(eng, _go)
        return ctr.v

    def tdma(self, out, in_):
        h = self.dS.h
        self.dS.v += 16
        self.q("sync", lambda e: e.dma_start_transpose(out, in_).then_inc(h, 16))
        return self.dS.v

    def barrier(self):
        pairs = [("tensor", self.sP), ("scalar", self.sA), ("vector", self.sV),
                 ("gpsimd", self.sG)]
        waits = []
        for eng, ctr in pairs:
            h = ctr.h
            pv = ctr.v
            ctr.v += 1
            if eng == "gpsimd":
                self.q(eng, lambda e, h=h, pv=pv: (e.wait_ge(h, pv) if pv > 0 else None,
                                                   e.nop(nofuse=True).then_inc(h, 1)))
            else:
                self.q(eng, lambda e, h=h: e.drain(fusable=False).then_inc(h, 1))
            waits.append((h, ctr.v))
        waits += [(self.dS.h, self.dS.v), (self.dG.h, self.dG.v), (self.dW.h, self.dW.v),
                  (self.dP.h, self.dP.v)]
        for grp in (self.pl, self.px, self.ys):
            waits += [(s.h, s.v) for s in grp]
        for grp in (self.u2, self.d2, self.c2, self.g2, self.stg):
            waits += [(s.h, s.v) for s in grp]
        for eng in ("tensor", "scalar", "vector", "gpsimd", "sync"):
            for h, v in waits:
                if v > 0:
                    self.q(eng, lambda e, h=h, v=v: e.wait_ge(h, v))

    def tap(self, name, src_dram):
        if name in self.dbg:
            self.barrier()
            self.dma("sync", self.dbg[name], src_dram)
            self.qw("sync", self.dS)

    def tap_sbuf(self, name, sb_ap):
        if name in self.dbg:
            self.barrier()
            self.dma("sync", self.dbg[name], sb_ap)
            self.qw("sync", self.dS)

    def declare_dram(self):
        nc = self.nc
        dt = nc.dram_tensor
        self.inp = {}
        for nm, shape in [
            ("x_own", [TOWN, D]),
            ("aWq_my", [HPC, D, DH]), ("aWk_my", [HPC, D, DH]), ("aWv_my", [HPC, D, DH]),
            ("abq_my", [HPC, DH]), ("abk_my", [HPC, DH]), ("abv_my", [HPC, DH]),
            ("aWo", [D, D]), ("abo", [D]),
            ("bWq", [D, D]), ("bWk_my", [D, HPC * DH]), ("bWv_my", [D, HPC * DH]),
            ("bWo", [D, D]),
            ("bbq", [D]), ("bbk_my", [HPC * DH]), ("bbv_my", [HPC * DH]), ("bbo", [D]),
            ("su_ug", [D]), ("su_dp", [INTER]), ("wsug", [2 * INTER]), ("wsdp", [D]),
            ("fWupT", [D, INTER]), ("fWgateT", [D, INTER]), ("fWdownT", [INTER, D]),
        ]:
            self.inp[nm] = dt(nm, shape, F32, kind="ExternalInput").ap()
        self.y_own = dt("y_own", [TOWN, D], F32, kind="ExternalOutput").ap()

        self.xstream = dt("xstream", [128, NF * TOWN], F32, kind="Internal").ap()
        self.wupT = dt("wupT", [D, INTER], BF16, kind="Internal").ap()
        self.wgateT = dt("wgateT", [D, INTER], BF16, kind="Internal").ap()
        self.wdT = dt("wdT", [INTER, D], BF16, kind="Internal").ap()
        self.wdH = dt("wdH", [INTER, D], BF16, kind="Internal").ap()
        self.wd2 = dt("wd2", [NF, 128, NI, 128], BF16, kind="Internal").ap()
        self.ag_in = dt("ag_in", [D, TOWN], BF16, kind="Internal").ap()
        self.ag_out = dt("ag_out", [NCORE * D, TOWN], BF16, kind="Internal",
                         addr_space="Shared").ap()
        self.y_in = dt("y_in", [HPC * DH, T], BF16, kind="Internal").ap()
        self.y_out = dt("y_out", [NCORE * HPC * DH, T], BF16, kind="Internal",
                        addr_space="Shared").ap()

        self.dbg = {}
        def tapout(nm, shape, dtp):
            self.dbg[nm] = dt("dbg_" + nm, shape, dtp, kind="ExternalOutput").ap()
        if "stream" in self.taps: tapout("stream", [128, NF * TOWN], F32)
        if "xln" in self.taps: tapout("xln", [128, NF * TOWN], BF16)
        if "G" in self.taps: tapout("G", [NCORE * D, TOWN], BF16)
        if "kvq" in self.taps:
            tapout("K", [128, 2 * T], BF16)
            tapout("V", [128, NI * 256], BF16)
            tapout("Q", [128, 2 * T], BF16)
        if "yout" in self.taps: tapout("yout", [NCORE * HPC * DH, T], BF16)
        if "wd2" in self.taps: tapout("wd2", [NF, 128, NI, 128], BF16)
        if "x1" in self.taps: tapout("x1", [128, NF * TOWN], F32)
        if "h" in self.taps: tapout("h", [128, 4 * 16384], BF16)

    def declare_inline(self):
        nc = self.nc
        self.h128_t = nc.inline_tensor(_hmat(128).astype(BF), name="h128i")
        self.h64_t = nc.inline_tensor(_hmat(64).astype(BF), name="h64i")
        self.id128_t = nc.inline_tensor(np.eye(128, dtype=np.float32), name="id128i")
        self.onescol_t = nc.inline_tensor(np.ones((128, 1), dtype=np.float32).astype(BF),
                                          name="onescoli")
        self.onesrow_t = nc.inline_tensor(np.ones((1, 128), dtype=np.float32), name="onesrowi")
        m = np.zeros((128, 4, 512), dtype=np.float32)
        for k in range(4):
            for p in range(128):
                m[p, k, 128 * k + p:] = 1.0
        self.masks_t = nc.inline_tensor(m.reshape(128, 2048).astype(BF), name="masksi")

    def alloc_sbuf(self):
        nc = self.nc
        A = nc.alloc_sbuf_tensor_at
        BASE = (nc.sbuf_base + 31) & ~31
        assert BASE <= 4224, BASE
        self.B = [A(f"B{i}", [128, 16384], BF16, offset=BASE + 32768 * i) for i in range(6)]
        o = BASE + 32768 * 6

        def at(name, shape, dtype):
            nonlocal o
            t = A(name, shape, dtype, offset=o)
            o += int(np.prod(shape[1:])) * (4 if dtype == F32 else 2)
            o = (o + 31) & ~31
            return t

        self.io32 = at("io32", [128, 2048], F32)
        self.ptb = at("ptb", [128, 2048], BF16)
        self.maskb = None
        self.rows32 = at("rows32", [1, 2048], F32)
        self.silu_sb = at("silu_sb", [128, 1024], BF16)
        self.ysb = self.silu_sb
        self.h128_sb = at("h128_sb", [128, 128], BF16)
        self.h64_sb = at("h64_sb", [64, 64], BF16)
        self.id128_sb = at("id128_sb", [128, 128], F32)
        self.onescol_sb = at("onescol_sb", [128, 1], BF16)
        self.onesrow_sb = at("onesrow_sb", [1, 128], F32)
        self.cbias = at("cbias", [128, 32], F32)
        self.sucol = at("sucol", [128, 16], F32)
        self.wsgcol = at("wsgcol", [128, 64], F32)
        self.wsucol = at("wsucol", [128, 64], F32)
        self.wsdcol = at("wsdcol", [128, 16], F32)
        self.bcol = {}
        for nm in ("abo", "bbq", "bbo"):
            self.bcol[nm] = at(f"col_{nm}", [128, 16], F32)
        for nm in ("abq_my", "abk_my", "bbk_my"):
            self.bcol[nm] = at(f"col_{nm}", [128, 2], F32)
        self.bvb = at("bvb", [128, 256], BF16)
        self.sumbuf = A("sumbuf", [128, 1024], BF16, offset=BASE + 32768 * 3)
        self.r1 = at("r1", [1, 512], F32)
        self.r2 = at("r2", [1, 32], F32)
        self.qb1 = at("qb1", [1, 64], F32)
        self.sqs = self.ptb
        self.tmpcol = at("tmpcol", [128, 64], F32)
        assert o <= 229344, f"sbuf overflow {o}"
        self.sbuf_top = o

        self.rb = self.io32.ap()[:, 1536:2048]
        io_off = BASE + 32768 * 6
        self.maskb = A("maskb", [128, 2048], BF16, offset=io_off)
        self.B0f = A("B0f", [128, 8192], F32, offset=BASE)
        self.x1f = A("x1f", [128, 16384], F32, offset=BASE + 32768 * 4)
        self.gbuf = [A(f"gbuf{i}", [128, 1024], BF16, offset=io_off + 8192 + 2048 * i)
                     for i in range(2)]

        self.p01 = self.es.enter_context(nc.psum_tensor("p01", [128, 1024], F32))
        self.p23 = self.es.enter_context(nc.psum_tensor("p23", [128, 1024], F32))
        self.p45 = self.es.enter_context(nc.psum_tensor("p45", [128, 1024], F32))
        self.p67 = self.es.enter_context(nc.psum_tensor("p67", [128, 1024], F32))

    def make_sems(self):
        nc, es = self.nc, self.es
        self.dS = Ctr(es, nc, "dS")
        self.dG = Ctr(es, nc, "dG")
        self.dW = Ctr(es, nc, "dW")
        self.sP = Ctr(es, nc, "sP")
        self.sA = Ctr(es, nc, "sA")
        self.sV = Ctr(es, nc, "sV")
        self.sG = Ctr(es, nc, "sG")
        self.cc = Ctr(es, nc, "cc")
        self.u2 = [Ctr(es, nc, "u0"), Ctr(es, nc, "u1")]
        self.d2 = [Ctr(es, nc, "d0"), Ctr(es, nc, "d1")]
        self.c2 = [Ctr(es, nc, "c0"), Ctr(es, nc, "c1")]
        self.g2 = [Ctr(es, nc, "g0"), Ctr(es, nc, "g1")]
        self.stg = [Ctr(es, nc, "sg0"), Ctr(es, nc, "sg1")]
        self.dP = Ctr(es, nc, "dP")
        self.pl = [Ctr(es, nc, "pl0"), Ctr(es, nc, "pl1")]
        self.px = [Ctr(es, nc, "px0"), Ctr(es, nc, "px1")]
        self.ys = [Ctr(es, nc, "ys0"), Ctr(es, nc, "ys1")]

    def ph_consts(self):
        for dst, src in [
            (self.h128_sb, self.h128_t), (self.h64_sb, self.h64_t),
            (self.id128_sb, self.id128_t), (self.onescol_sb, self.onescol_t),
            (self.onesrow_sb, self.onesrow_t),
        ]:
            self.dma("sync", dst[:], src.ap())

        def vec2col(flat, nch, dst, mul=None):
            src = flat.rearrange("(c p) -> c p", p=128) if len(flat.shape) == 1 else flat
            self.qw("sync", self.sV)
            self.dma("sync", self.io32[0:nch, 0:128], src)
            self.qw("tensor", self.dS)
            self.qw("tensor", self.sV)
            self.pe(lambda e, nch=nch: e.transpose(self.p01[:, 0:nch], self.io32[0:nch, 0:128],
                                                   self.id128_sb[0:nch, 0:nch]), inc=True)
            self.qw("vector", self.sP)
            if mul is None:
                self.ve(lambda e, d=dst, nch=nch: e.tensor_copy(out=d[:, 0:nch],
                                                                in_=self.p01[:, 0:nch]), inc=True)
            else:
                self.ve(lambda e, d=dst, nch=nch, m=mul: e.tensor_scalar(
                    out=d[:, 0:nch], in0=self.p01[:, 0:nch], scalar1=float(m), scalar2=None,
                    op0=ALU.mult), inc=True)

        vec2col(self.inp["su_ug"], 16, self.sucol)
        vec2col(self.inp["wsdp"], 16, self.wsdcol, mul=R8192)
        for nm in ("abo", "bbq", "bbo"):
            vec2col(self.inp[nm], 16, self.bcol[nm])
        for nm in ("abq_my", "abk_my", "bbk_my"):
            vec2col(self.inp[nm], 2, self.bcol[nm])
        vec2col(self.inp["su_dp"], 64, self.tmpcol)
        vec2col(self.inp["wsug"][0:INTER], 64, self.wsucol, mul=R2048)
        self.qw("vector", self.sV)
        self.ve(lambda e: e.tensor_tensor(out=self.wsucol[:], in0=self.wsucol[:],
                                          in1=self.tmpcol[:], op=ALU.mult), inc=True)
        vec2col(self.inp["wsug"][INTER:2 * INTER], 64, self.wsgcol, mul=R2048)
        self.barrier()
        self.phase_done("consts")

    def _transpose_tiles_to(self, dst_is_output):
        xs_v = self.xstream.rearrange("p (c t) -> p c t", c=NF, t=TOWN)
        for tc in range(8):
            self.qw("sync", self.sV)
            if dst_is_output:
                self.dma("sync", self.io32.ap().rearrange("p (c t) -> p c t", c=NF, t=128),
                         xs_v[:, :, tc * 128:(tc + 1) * 128])
            else:
                self.dma("sync", self.io32[:], self.inp["x_own"][tc * 128:(tc + 1) * 128, :])
            wS = self.dS.v
            evs = {}
            pairs = [self.p01, self.p23, self.p45, self.p67]
            for fc in range(16):
                slot = fc % 8
                ps = pairs[slot % 4]
                col = 512 * (slot // 4)
                self.qw("tensor", self.dS, wS)
                if fc >= 8:
                    self.qw("tensor", self.sV, evs[fc - 8])
                self.pe(lambda e, fc=fc, ps=ps, col=col: e.transpose(
                    ps[:, col:col + 128], self.io32[:, fc * 128:(fc + 1) * 128],
                    self.id128_sb[:]), inc=True)
                self.qw("vector", self.sP)
                if fc == 0:
                    self.qw("vector", self.dS)
                evs[fc] = self.ve(lambda e, fc=fc, ps=ps, col=col: e.tensor_copy(
                    out=self.B0f[:, fc * 128:(fc + 1) * 128],
                    in_=ps[:, col:col + 128]), inc=True)
            self.qw("sync", self.sV)
            if dst_is_output:
                self.dma("sync", self.y_own[tc * 128:(tc + 1) * 128, :], self.B0f[:, 0:2048])
            else:
                src = self.B0f.ap()[:, 0:2048].rearrange("p (c t) -> p c t", c=NF, t=128)
                self.dma("sync", xs_v[:, :, tc * 128:(tc + 1) * 128], src)
        self.barrier()

    def ph_input(self):
        self._transpose_tiles_to(dst_is_output=False)
        self.tap("stream", self.xstream)
        self.phase_done("input")

    def ph_output(self):
        self._transpose_tiles_to(dst_is_output=True)
        for eng in ("tensor", "scalar", "vector", "gpsimd", "sync"):
            self.qw(eng, self.dS)

    def ph_prep_casts(self):
        self.dma("gpsimd", self.wdT, self.inp["fWdownT"])
        self.prep_cast_w = self.dG.v
        self.dma("gpsimd", self.wupT, self.inp["fWupT"])
        self.dma("gpsimd", self.wgateT, self.inp["fWgateT"])
        self.prep_cast_ug = self.dG.v

    def ph_prep_mms(self, barriers=True):
        stage = self.B[1]
        self.qw("scalar", self.dG, self.prep_cast_w)
        self.qw("scalar", self.sP)
        wdT_v = self.wdT.rearrange("(cc p) c -> cc p c", cc=64, p=128)
        wdH_v = self.wdH.rearrange("(cp p) c -> cp p c", cp=64, p=128)

        def pipelined_pass(n_iter, nrow, ld_src, st_dst, hmat_sb, ld_eng="scalar"):
            self.qw("tensor", self.sV)
            self.qw("tensor", self.sA)
            LD = [stage.ap()[0:nrow, 0:2048], stage.ap()[0:nrow, 2048:4096]]
            EV = [stage.ap()[0:nrow, 4096:6144], stage.ap()[0:nrow, 6144:8192]]
            pse = [(self.p01, self.p23), (self.p45, self.p67)]
            mmv, evv, stv, ldv = {}, {}, {}, {}
            for p in range(min(2, n_iter)):
                self.dma(ld_eng, LD[p % 2], ld_src(p), ctr=self.pl[p % 2])
                ldv[p] = self.pl[p % 2].v
            for p in range(n_iter):
                q = p % 2
                self.qw("tensor", self.pl[q], ldv[p])
                if p >= 2:
                    self.qw("tensor", self.sV, evv[p - 2])
                for k in range(4):
                    ps = pse[q][k // 2][0:nrow, (k % 2) * 512:(k % 2) * 512 + 512]
                    self.pe(lambda e, ps=ps, k=k, q=q, hm=hmat_sb: e.matmul(
                        ps, hm, LD[q][0:nrow, k * 512:(k + 1) * 512],
                        start=True, stop=True), inc=(k == 3))
                mmv[p] = self.sP.v
                self.qw("vector", self.sP, mmv[p])
                if p >= 2:
                    self.qw("vector", self.px[q], stv[p - 2])
                self.ve(lambda e, q=q, nrow=nrow: e.tensor_copy(
                    out=EV[q][0:nrow, 0:1024], in_=pse[q][0][0:nrow, :]))
                evv[p] = self.ve(lambda e, q=q, nrow=nrow: e.tensor_copy(
                    out=EV[q][0:nrow, 1024:2048], in_=pse[q][1][0:nrow, :]), inc=True)
                if p + 2 < n_iter:
                    self.qw(ld_eng, self.sP, mmv[p])
                    self.dma(ld_eng, LD[q], ld_src(p + 2), ctr=self.pl[q])
                    ldv[p + 2] = self.pl[q].v
                self.qw("scalar", self.sV, evv[p])
                st_dst(p, EV[q], self.px[q])
                stv[p] = self.px[q].v

        pipelined_pass(
            128, 64,
            lambda p: wdT_v[:, p, :],
            lambda p, ev, ctr: self.dma("scalar", wdH_v[:, p, :], ev[0:64, 0:2048], ctr=ctr),
            self.h64_sb[:])
        self.qw("scalar", self.px[0])
        self.qw("scalar", self.px[1])

        wd2_v = self.wd2.rearrange("ct p ic c -> p ct ic c")

        def h128_store(ic, ev, ctr):
            for c4 in range(4):
                self.dma("scalar", wd2_v[:, c4 * 4:(c4 + 1) * 4, ic, :],
                         ev[:, c4 * 512:(c4 + 1) * 512].rearrange(
                             "p (ct c) -> p ct c", ct=4, c=128), ctr=ctr)

        pipelined_pass(
            NI, 128,
            lambda ic: self.wdH[ic * 128:(ic + 1) * 128, :],
            h128_store,
            self.h128_sb[:])
        self.prep_mms_done = self.dP.v
        self.tap("wd2", self.wd2)
        self.phase_done("prep")

    def ph_ln(self, with_su):
        xln = self.B[4]
        sums_lo, sums_hi = self.p67[0:1, 0:512], self.p67[0:1, 512:1024]
        sq_lo, sq_hi = self.p45[0:1, 0:512], self.p45[0:1, 512:1024]
        for c in range(NF):
            self.qw("sync", self.sV)
            self.dma("sync", self.io32[:, 0:1024], self.xstream[:, c * 1024:(c + 1) * 1024])
            self.qw("vector", self.dS)
            self.ve(lambda e, c=c: e.tensor_copy(out=xln[:, c * 1024:(c + 1) * 1024],
                                                 in_=self.io32[:, 0:1024]), inc=True)
            self.qw("scalar", self.sV)
            self.qw("scalar", self.sP)
            self.ae(lambda e, c=c: e.activation(out=self.sqs[:, 0:1024],
                    in_=xln[:, c * 1024:(c + 1) * 1024], func=AF.Square), inc=True)
            self.qw("tensor", self.sV)
            self.pe(lambda e, c=c: e.matmul(sums_lo, self.onescol_sb[:],
                    xln[:, c * 1024:c * 1024 + 512], start=(c == 0), stop=(c == NF - 1)))
            self.pe(lambda e, c=c: e.matmul(sums_hi, self.onescol_sb[:],
                    xln[:, c * 1024 + 512:(c + 1) * 1024], start=(c == 0), stop=(c == NF - 1)))
            self.qw("tensor", self.sA)
            self.pe(lambda e, c=c: e.matmul(sq_lo, self.onescol_sb[:], self.sqs[:, 0:512],
                    start=(c == 0), stop=(c == NF - 1)))
            self.pe(lambda e, c=c: e.matmul(sq_hi, self.onescol_sb[:], self.sqs[:, 512:1024],
                    start=(c == 0), stop=(c == NF - 1)), inc=True)
        mu = self.rows32[0:1, 0:1024]
        ex2 = self.rows32[0:1, 1024:2048]
        self.qw("vector", self.sP)
        self.ve(lambda e: e.tensor_scalar(out=mu, in0=self.p67[0:1, :], scalar1=1.0 / 2048.0,
                                          scalar2=None, op0=ALU.mult), inc=True)
        self.qw("vector", self.sV)
        self.ve(lambda e: e.tensor_scalar(out=ex2, in0=self.p45[0:1, :], scalar1=1.0 / 2048.0,
                                          scalar2=float(EPS), op0=ALU.mult, op1=ALU.add),
                inc=True)
        for k in range(2):
            self.qw("vector", self.sV)
            self.ve(lambda e, k=k: e.scalar_tensor_tensor(
                out=self.r1[0:1, 0:512], in0=mu[0:1, k * 512:(k + 1) * 512], scalar=-1.0,
                in1=mu[0:1, k * 512:(k + 1) * 512], op0=ALU.mult, op1=ALU.mult), inc=True)
            self.qw("vector", self.sV)
            self.ve(lambda e, k=k: e.tensor_tensor(
                out=ex2[0:1, k * 512:(k + 1) * 512], in0=ex2[0:1, k * 512:(k + 1) * 512],
                in1=self.r1[0:1, 0:512], op=ALU.add), inc=True)
        self.qw("scalar", self.sV)
        self.ae(lambda e: e.activation(out=ex2, in_=ex2, func=AF.Ln), inc=True)
        self.qw("scalar", self.sA)
        self.ae(lambda e: e.activation(out=ex2, in_=ex2, func=AF.Exp, scale=-0.5), inc=True)
        self.qw("tensor", self.sA)
        for k in range(2):
            self.pe(lambda e, k=k: e.matmul(self.p01[:, k * 512:(k + 1) * 512],
                    self.onesrow_sb[:], mu[0:1, k * 512:(k + 1) * 512], start=True, stop=True))
            self.pe(lambda e, k=k: e.matmul(self.p23[:, k * 512:(k + 1) * 512],
                    self.onesrow_sb[:], ex2[0:1, k * 512:(k + 1) * 512], start=True, stop=True),
                    inc=(k == 1))
        wbc = self.sP.v
        for c in range(NF):
            self.qw("vector", self.sP, wbc)
            self.qw("vector", self.sV)
            self.ve(lambda e, c=c: e.tensor_tensor(out=self.io32[:, 0:1024],
                    in0=xln[:, c * 1024:(c + 1) * 1024], in1=self.p01[:], op=ALU.subtract),
                    inc=True)
            self.qw("vector", self.sV)
            su = self.sucol[:, c:c + 1] if with_su else 1.0
            self.ve(lambda e, c=c, su=su: e.scalar_tensor_tensor(
                out=xln[:, c * 1024:(c + 1) * 1024], in0=self.io32[:, 0:1024],
                scalar=su, in1=self.p23[:], op0=ALU.mult, op1=ALU.mult), inc=True)
        self.barrier()

    def ph_ffn(self):
        self.ph_ln(with_su=True)
        self.tap_sbuf("xln", self.B[4][:])
        xln = self.B[4]
        mm_done = {}
        ev_done = {}

        def maybe_evac(cur_c):
            for k in range(NF - 1, -1, -1):
                if k in mm_done and k not in ev_done and 2 * k >= cur_c:
                    psk = mm_done[k][1]
                    self.qw("vector", self.sP, mm_done[k][0])
                    ev_done[k] = self.ve(lambda e, k=k, psk=psk: e.tensor_copy(
                        out=self.x1f[:, k * 1024:(k + 1) * 1024], in_=psk[:]), inc=True)

        for idx, c in enumerate(range(NF - 1, -1, -1)):
            ps = self.p01 if (idx % 2 == 0) else self.p23
            if idx >= 2:
                prev_k = NF - 1 - (idx - 2)
                self.qw("tensor", self.sV, ev_done[prev_k])
            self.pe(lambda e, c=c, ps=ps: e.matmul(ps[:, 0:512], self.h128_sb[:],
                    xln[:, c * 1024:c * 1024 + 512], start=True, stop=True))
            v = self.pe(lambda e, c=c, ps=ps: e.matmul(ps[:, 512:1024], self.h128_sb[:],
                    xln[:, c * 1024 + 512:(c + 1) * 1024], start=True, stop=True), inc=True)
            mm_done[c] = (v, ps)
            maybe_evac(c)
        maybe_evac(-100)
        for s in (1, 2, 4, 8):
            for a0 in range(NF):
                if a0 & s:
                    continue
                b0 = a0 + s
                self.qw("vector", self.sV)
                self.ve(lambda e, a0=a0: e.tensor_copy(out=self.io32[:, 0:1024],
                        in_=self.x1f[:, a0 * 1024:(a0 + 1) * 1024]), inc=True)
                self.qw("vector", self.sV)
                self.ve(lambda e, a0=a0, b0=b0: e.tensor_tensor(
                    out=self.x1f[:, a0 * 1024:(a0 + 1) * 1024],
                    in0=self.x1f[:, a0 * 1024:(a0 + 1) * 1024],
                    in1=self.x1f[:, b0 * 1024:(b0 + 1) * 1024], op=ALU.add), inc=True)
                self.qw("vector", self.sV)
                self.ve(lambda e, a0=a0, b0=b0: e.tensor_tensor(
                    out=self.x1f[:, b0 * 1024:(b0 + 1) * 1024], in0=self.io32[:, 0:1024],
                    in1=self.x1f[:, b0 * 1024:(b0 + 1) * 1024], op=ALU.subtract), inc=True)
        self.tap_sbuf("x1", self.x1f[:])
        x1b = self.B[4]
        self.qw("vector", self.sV)
        self.ve(lambda e: e.tensor_copy(out=self.ptb[:, 0:1024], in_=self.x1f[:, 0:1024]),
                inc=True)
        for c in range(1, NF):
            self.qw("vector", self.sV)
            self.ve(lambda e, c=c: e.tensor_copy(out=x1b[:, c * 1024:(c + 1) * 1024],
                    in_=self.x1f[:, c * 1024:(c + 1) * 1024]), inc=True)
        self.qw("vector", self.sV)
        self.ve(lambda e: e.tensor_copy(out=x1b[:, 0:1024], in_=self.ptb[:, 0:1024]), inc=True)
        self.barrier()
        self.phase_done("ffnpre")

        upbuf = [self.B[5].ap()[:, 0:4096], self.B[5].ap()[:, 4096:8192]]
        gabuf = [self.B[5].ap()[:, 8192:12288], self.B[5].ap()[:, 12288:16384]]
        wupT_v = self.wupT.rearrange("(f p) i -> p f i", f=16, p=128)
        wgaT_v = self.wgateT.rearrange("(f p) i -> p f i", f=16, p=128)
        self.qw("sync", self.dG, getattr(self, "prep_cast_ug", 0))

        def h_ap(ic):
            return self.B[ic // 16][:, (ic % 16) * 1024:(ic % 16) * 1024 + 1024]

        nslab = NI // 2
        mm_after_slab = {}
        slab_w = {}
        hdone = {}
        for sl in range(nslab):
            if sl >= 2:
                self.qw("sync", self.sP, mm_after_slab[sl - 2])
            usem = self.u2[sl % 2]
            for half in range(2):
                self.dma("sync", upbuf[sl % 2][:, half * 2048:(half + 1) * 2048].rearrange(
                    "p (f i) -> p f i", f=8, i=256),
                    wupT_v[:, half * 8:(half + 1) * 8, sl * 256:(sl + 1) * 256], ctr=usem)
                self.dma("sync", gabuf[sl % 2][:, half * 2048:(half + 1) * 2048].rearrange(
                    "p (f i) -> p f i", f=8, i=256),
                    wgaT_v[:, half * 8:(half + 1) * 8, sl * 256:(sl + 1) * 256], ctr=usem)
            slab_w[sl] = usem.v
            for j in range(2):
                ic = sl * 2 + j
                pu = self.p01 if (ic % 2 == 0) else self.p45
                pg = self.p23 if (ic % 2 == 0) else self.p67
                self.qw("tensor", self.u2[sl % 2], slab_w[sl])
                if ic >= 2:
                    self.qw("tensor", self.sV, hdone[ic - 2][0])
                    self.qw("tensor", self.sA, hdone[ic - 2][1])
                for f in range(16):
                    off = (f // 8) * 2048 + (f % 8) * 512 + j * 256
                    lhs_u = upbuf[sl % 2][:, off:off + 128] if False else None
                    hh = f // 8
                    base = hh * 2048 + (f % 8) * 256 + j * 128
                    lu = upbuf[sl % 2][:, base:base + 128]
                    lg = gabuf[sl % 2][:, base:base + 128]
                    self.pe(lambda e, f=f, pu=pu, lu=lu: e.matmul(
                        pu[:, 0:512], lu, x1b[:, f * 1024:f * 1024 + 512],
                        start=(f == 0), stop=(f == 15)))
                    self.pe(lambda e, f=f, pu=pu, lu=lu: e.matmul(
                        pu[:, 512:1024], lu, x1b[:, f * 1024 + 512:(f + 1) * 1024],
                        start=(f == 0), stop=(f == 15)))
                    self.pe(lambda e, f=f, pg=pg, lg=lg: e.matmul(
                        pg[:, 0:512], lg, x1b[:, f * 1024:f * 1024 + 512],
                        start=(f == 0), stop=(f == 15)))
                    self.pe(lambda e, f=f, pg=pg, lg=lg: e.matmul(
                        pg[:, 512:1024], lg, x1b[:, f * 1024 + 512:(f + 1) * 1024],
                        start=(f == 0), stop=(f == 15)), inc=(f == 15))
                wmm = self.sP.v
                mm_after_slab[sl] = wmm
                self.qw("scalar", self.sP, wmm)
                if ic >= 1:
                    self.qw("scalar", self.sV, hdone[ic - 1][0])
                vA = self.ae(lambda e, ic=ic, pg=pg: e.activation(
                    out=self.silu_sb[:], in_=pg[:], func=AF.Silu,
                    scale=self.wsgcol[:, ic:ic + 1]), inc=True)
                self.qw("vector", self.sA, vA)
                vV = self.ve(lambda e, ic=ic, pu=pu: e.scalar_tensor_tensor(
                    out=h_ap(ic), in0=pu[:], scalar=self.wsucol[:, ic:ic + 1],
                    in1=self.silu_sb[:], op0=ALU.mult, op1=ALU.mult), inc=True)
                hdone[ic] = (vV, vA)
        self.barrier()
        self.phase_done("upgate")
        if "h" in self.dbg:
            self.dma("sync", self.dbg["h"].rearrange("p (r n) -> r p n", r=4, n=16384)[0], self.B[0][:])
            self.dma("sync", self.dbg["h"].rearrange("p (r n) -> r p n", r=4, n=16384)[1], self.B[1][:])
            self.dma("sync", self.dbg["h"].rearrange("p (r n) -> r p n", r=4, n=16384)[2], self.B[2][:])
            self.dma("sync", self.dbg["h"].rearrange("p (r n) -> r p n", r=4, n=16384)[3], self.B[3][:])
            self.qw("sync", self.dS)

        wdb = [self.B[5].ap()[:, 0:8192], self.B[5].ap()[:, 8192:16384]]
        dn_mm = {}
        for ct in range(NF):
            if ct >= 2:
                self.qw("sync", self.sP, dn_mm[ct - 2])
            self.dma("sync", wdb[ct % 2], self.wd2[ct].rearrange("p ic c -> p (ic c)"),
                     ctr=self.d2[ct % 2])
            wW = self.d2[ct % 2].v
            ps = self.p01 if (ct % 2 == 0) else self.p23
            self.qw("tensor", self.d2[ct % 2], wW)
            self.qw("tensor", self.sV)
            for ic in range(NI):
                self.pe(lambda e, ct=ct, ic=ic, ps=ps: e.matmul(
                    ps[:, 0:512], wdb[ct % 2][:, ic * 128:(ic + 1) * 128],
                    h_ap(ic)[:, 0:512], start=(ic == 0), stop=(ic == NI - 1)))
                self.pe(lambda e, ct=ct, ic=ic, ps=ps: e.matmul(
                    ps[:, 512:1024], wdb[ct % 2][:, ic * 128:(ic + 1) * 128],
                    h_ap(ic)[:, 512:1024], start=(ic == 0), stop=(ic == NI - 1)),
                    inc=(ic == NI - 1))
            wmm = self.sP.v
            dn_mm[ct] = wmm
            self.qw("sync", self.sV)
            self.dma("sync", self.io32[:, 0:1024], self.xstream[:, ct * 1024:(ct + 1) * 1024])
            wS = self.dS.v
            self.qw("vector", self.sP, wmm)
            self.qw("vector", self.dS, wS)
            self.ve(lambda e, ct=ct, ps=ps: e.scalar_tensor_tensor(
                out=self.io32[:, 1024:2048], in0=ps[:], scalar=self.wsdcol[:, ct:ct + 1],
                in1=self.io32[:, 0:1024], op0=ALU.mult, op1=ALU.add), inc=True)
            self.qw("sync", self.sV)
            self.dma("sync", self.xstream[:, ct * 1024:(ct + 1) * 1024], self.io32[:, 1024:2048])
        self.barrier()
        self.tap("stream", self.xstream)

    def _colslab_proj(self, wsrc, bias_col, out_cb, rhs, nk=NF, wcols=None):
        wcb = [self.B[5].ap()[:, 0:2048], self.B[5].ap()[:, 2048:4096]]
        mmv = {}
        for ct in range(NF):
            if ct >= 2:
                self.qw("gpsimd", self.sP, mmv[ct - 2])
            src = wsrc.rearrange("(k p) (ct c) -> p ct k c", p=128, k=nk, ct=NF, c=128)
            self.dma("gpsimd", wcb[ct % 2].rearrange("p (k c) -> p k c", k=nk, c=128),
                     src[:, ct, :, :], ctr=self.c2[ct % 2])
            wW = self.c2[ct % 2].v
            ps = self.p01 if (ct % 2 == 0) else self.p23
            self.qw("tensor", self.c2[ct % 2], wW)
            self.qw("tensor", self.sV)
            for k in range(nk):
                self.pe(lambda e, k=k, ct=ct, ps=ps: e.matmul(
                    ps[:, 0:512], wcb[ct % 2][:, k * 128:(k + 1) * 128], rhs(k, 0),
                    start=(k == 0), stop=(k == nk - 1)))
                self.pe(lambda e, k=k, ct=ct, ps=ps: e.matmul(
                    ps[:, 512:1024], wcb[ct % 2][:, k * 128:(k + 1) * 128], rhs(k, 1),
                    start=(k == 0), stop=(k == nk - 1)), inc=(k == nk - 1))
            wmm = self.sP.v
            mmv[ct] = wmm
            out_cb(ct, ps, wmm)

    def ph_attn(self, variant, mid_hook=None):
        a = variant == "a"
        self.ph_ln(with_su=False)
        self.tap_sbuf("xln", self.B[4][:])
        xln = self.B[4]
        agin_v = self.ag_in.rearrange("(c p) t -> p c t", c=NF, p=128)
        self.dma("gpsimd", agin_v, xln.ap().rearrange("p (c t) -> p c t", c=NF, t=TOWN))
        agin_done = self.dG.v
        self.qw("gpsimd", self.dG)
        hcc = self.cc.h
        self.cc.v += 1
        rg = [list(range(NCORE))]
        self.q("gpsimd", lambda e: e.collective_compute(
            "AllGather", ALU.bypass, replica_groups=rg,
            ins=[self.ag_in.opt()], outs=[self.ag_out.opt()]).then_inc(hcc, 1))
        wcc = self.cc.v

        KT, VT, QT, QFL = self.B[0], self.B[1], self.B[2], self.B[3]
        pfx = "a" if a else "b"

        if not a:
            def qfl_out(ct, ps, wmm):
                self.qw("vector", self.sP, wmm)
                self.ve(lambda e, ct=ct, ps=ps: e.tensor_scalar(
                    out=QFL[:, ct * 1024:(ct + 1) * 1024], in0=ps[:],
                    scalar1=self.bcol["bbq"][:, ct:ct + 1], scalar2=None, op0=ALU.add),
                    inc=True)

            self._colslab_proj(self.inp["bWq"], None, qfl_out,
                               lambda f, hh: xln[:, f * 1024 + hh * 512:f * 1024 + hh * 512 + 512])
            b4_free = self.sP.v
            self.qw("vector", self.sV)
            qfl_v = QFL.ap().rearrange("p (j u) -> p j u", j=16, u=1024)
            for hl in range(2):
                src = qfl_v[:, :, hl * 512:(hl + 1) * 512].rearrange("p j v -> p v j")
                dst = QT.ap()[:, hl * 8192:(hl + 1) * 8192].rearrange(
                    "p (v j) -> p v j", v=512, j=16)
                self.ve(lambda e, src=src, dst=dst: e.tensor_copy(out=dst, in_=src), inc=True)
            b3_free = self.sV.v

        wk_sl = self.B[5].ap()[:, 4096:8192]
        wv_sl = self.B[5].ap()[:, 8192:12288]
        wq_sl = self.B[5].ap()[:, 12288:16384]
        mats = [("k", wk_sl), ("v", wv_sl)] + ([("q", wq_sl)] if a else [])
        for nm, dstb in mats:
            for f in range(NF):
                for hl in range(2):
                    if a:
                        wsrc = self.inp["aW" + nm + "_my"][hl, f * 128:(f + 1) * 128, :]
                    else:
                        wsrc = self.inp["bW" + nm + "_my"][f * 128:(f + 1) * 128,
                                                           hl * 128:(hl + 1) * 128]
                    self.dma("gpsimd", dstb[:, f * 256 + hl * 128:f * 256 + hl * 128 + 128],
                             wsrc, ctr=self.dW)
        wslices = self.dW.v

        if a:
            bsrc = self.inp["abv_my"].rearrange("h d -> (h d)")
        else:
            bsrc = self.inp["bbv_my"]
        self.dma("sync", self.r1[0:1, 0:256], bsrc.rearrange("(o n) -> o n", o=1))
        wb = self.dS.v
        self.qw("tensor", self.dS, wb)
        self.qw("tensor", self.sV)
        self.pe(lambda e: e.matmul(self.p45[:, 0:256], self.onesrow_sb[:], self.r1[0:1, 0:256],
                                   start=True, stop=True), inc=True)
        self.qw("vector", self.sP)
        self.ve(lambda e: e.tensor_copy(out=self.bvb[:], in_=self.p45[:, 0:256]), inc=True)

        self.qw("sync", self.cc, wcc)
        self.qw("tensor", self.cc, wcc)
        self.qw("tensor", self.dW, wslices)

        G_r = self.ag_out.rearrange("(r f p) t -> r p f t", r=NCORE, f=NF, p=128)
        stage = [QFL, xln]

        def stage_load(r):
            self.dma("sync", stage[r % 2].ap().rearrange("p (f t) -> p f t", f=NF),
                     G_r[r], ctr=self.stg[r % 2])
            return self.stg[r % 2].v

        vload = {}
        if a:
            vload[0] = stage_load(0)
            self.qw("sync", self.dG, agin_done)
            vload[1] = stage_load(1)
        else:
            self.qw("sync", self.sV, b3_free)
            vload[0] = stage_load(0)
            self.qw("sync", self.sP, b4_free)
            vload[1] = stage_load(1)

        bk = self.bcol["abk_my" if a else "bbk_my"]
        bq = self.bcol["abq_my"] if a else None
        kevac = {}
        qevac = {-1: self.sV.v}
        vevac = {-1: self.sV.v}

        def kq_mms(st, wsl, lo):
            for f in range(NF):
                for hl in range(2):
                    ps = (self.p01, self.p23, self.p45, self.p67)[lo + hl]
                    for t2 in range(2):
                        self.pe(lambda e, f=f, hl=hl, t2=t2, ps=ps, st=st, wsl=wsl:
                                e.matmul(ps[:, t2 * 512:(t2 + 1) * 512],
                                         wsl[:, f * 256 + hl * 128:f * 256 + hl * 128 + 128],
                                         st[:, f * 1024 + t2 * 512:f * 1024 + t2 * 512 + 512],
                                         start=(f == 0), stop=(f == NF - 1)),
                                inc=(f == NF - 1 and hl == 1 and t2 == 1))
            return self.sP.v

        def kq_evac(r, dstT, bias_col, lo):
            for hl in range(2):
                ps = (self.p01, self.p23, self.p45, self.p67)[lo + hl]
                bcolv = bias_col[:, hl:hl + 1]
                for t2 in range(2):
                    self.ve(lambda e, r=r, hl=hl, t2=t2, ps=ps, bcolv=bcolv: e.tensor_scalar(
                        out=dstT[:, hl * 8192 + r * 1024 + t2 * 512:
                                 hl * 8192 + r * 1024 + t2 * 512 + 512],
                        in0=ps[:, t2 * 512:(t2 + 1) * 512], scalar1=bcolv, scalar2=None,
                        op0=ALU.add), inc=(hl == 1 and t2 == 1))
            return self.sV.v

        for r in range(NCORE):
            par = r % 2
            st = stage[par].ap()
            self.qw("tensor", self.stg[par], vload[r])
            self.qw("tensor", self.sV, vevac[r - 1])
            kmm = kq_mms(st, wk_sl, 0)
            self.qw("vector", self.sP, kmm)
            kevac[r] = kq_evac(r, KT, bk, 0)
            if a:
                self.qw("tensor", self.sV, qevac[r - 1])
                qmm = kq_mms(st, wq_sl, 2)
                self.qw("vector", self.sP, qmm)
                qevac[r] = kq_evac(r, QT, bq, 2)
            self.qw("tensor", self.sV, kevac[r])
            for f in range(NF):
                for tc in range(8):
                    ps = (self.p01 if tc < 4 else self.p23)
                    ofs = (tc % 4) * 256
                    self.pe(lambda e, f=f, tc=tc, ps=ps, ofs=ofs, st=st: e.matmul(
                        ps[:, ofs:ofs + 256], st[:, f * 1024 + tc * 128:f * 1024 + tc * 128 + 128],
                        wv_sl[:, f * 256:(f + 1) * 256],
                        start=(f == 0 and tc % 2 == 0),
                        stop=(f == NF - 1 and tc % 2 == 1)), inc=(f == NF - 1 and tc == 7))
            vmm = self.sP.v
            if r + 2 < NCORE:
                self.qw("sync", self.sP, vmm)
                vload[r + 2] = stage_load(r + 2)
            self.qw("vector", self.sP, vmm)
            for tc in range(8):
                ps = (self.p01 if tc < 4 else self.p23)
                ofs = (tc % 4) * 256
                self.ve(lambda e, r=r, tc=tc, ps=ps, ofs=ofs: e.tensor_tensor(
                    out=VT[:, (r * 8 + tc) * 256:(r * 8 + tc) * 256 + 256],
                    in0=ps[:, ofs:ofs + 256], in1=self.bvb[:], op=ALU.add), inc=(tc == 7))
            vevac[r] = self.sV.v
        self.barrier()
        if a and getattr(self, "cast_in_attn_a", False):
            self.ph_prep_casts()
        if "K" in self.dbg:
            self.dma("sync", self.dbg["K"], KT[:])
            self.dma("sync", self.dbg["V"], VT[:])
            self.dma("sync", self.dbg["Q"], QT[:])
            self.qw("sync", self.dS)
        if "G" in self.dbg:
            self.tap("G", self.ag_out)
        self.phase_done("proj_" + pfx)

        for hl in range(2):
            for blk in range(16):
                self.qw("scalar", self.sP)
                self.ae(lambda e, hl=hl, blk=blk: e.activation(
                    out=self.sqs[:, 0:512], in_=KT[:, hl * 8192 + blk * 512:hl * 8192 + blk * 512 + 512],
                    func=AF.Square), inc=True)
                self.qw("tensor", self.sA)
                self.qw("tensor", self.sV)
                self.pe(lambda e: e.matmul(self.p45[0:1, 0:512], self.onescol_sb[:],
                        self.sqs[:, 0:512], start=True, stop=True), inc=True)
                self.qw("vector", self.sP)
                self.ve(lambda e, blk=blk: e.tensor_reduce(
                    out=self.r2[0:1, blk:blk + 1], in_=self.p45[0:1, 0:512], axis=AX.X,
                    op=ALU.max), inc=True)
            self.qw("vector", self.sV)
            self.ve(lambda e: e.tensor_reduce(out=self.r2[0:1, 16:17], in_=self.r2[0:1, 0:16],
                                              axis=AX.X, op=ALU.max), inc=True)
            for blk in range(16):
                self.qw("scalar", self.sP)
                self.ae(lambda e, hl=hl, blk=blk: e.activation(
                    out=self.sqs[:, 0:512], in_=QT[:, hl * 8192 + blk * 512:hl * 8192 + blk * 512 + 512],
                    func=AF.Square), inc=True)
                self.qw("tensor", self.sA)
                self.qw("tensor", self.sV)
                self.pe(lambda e: e.matmul(self.p45[0:1, 0:512], self.onescol_sb[:],
                        self.sqs[:, 0:512], start=True, stop=True), inc=True)
                self.qw("vector", self.sP)
                self.ve(lambda e, hl=hl, blk=blk: e.tensor_reduce(
                    out=self.qb1[0:1, hl * 16 + blk:hl * 16 + blk + 1],
                    in_=self.p45[0:1, 0:512], axis=AX.X, op=ALU.max), inc=True)
            self.qw("vector", self.sV)
            self.ve(lambda e, hl=hl: e.tensor_scalar(
                out=self.r1[0:1, 0:16], in0=self.qb1[0:1, hl * 16:(hl + 1) * 16],
                scalar1=self.r2[0:1, 16:17], scalar2=float((ATT_SCALE * 1.002) ** 2),
                op0=ALU.mult, op1=ALU.mult), inc=True)
            self.qw("scalar", self.sV)
            self.ae(lambda e: e.activation(out=self.r1[0:1, 0:16], in_=self.r1[0:1, 0:16],
                                           func=AF.Ln), inc=True)
            self.qw("scalar", self.sA)
            self.ae(lambda e: e.activation(out=self.r1[0:1, 0:16], in_=self.r1[0:1, 0:16],
                    func=AF.Exp, scale=0.5), inc=True)
            self.qw("vector", self.sA)
            self.ve(lambda e: e.tensor_scalar(out=self.r1[0:1, 16:32], in0=self.r1[0:1, 0:16],
                    scalar1=-1.0, scalar2=None, op0=ALU.mult), inc=True)
            self.qw("tensor", self.sV)
            self.pe(lambda e: e.matmul(self.p45[:, 0:16], self.onesrow_sb[:],
                    self.r1[0:1, 16:32], start=True, stop=True), inc=True)
            self.qw("vector", self.sP)
            self.ve(lambda e, hl=hl: e.tensor_copy(out=self.cbias[:, hl * 16:(hl + 1) * 16],
                    in_=self.p45[:, 0:16]), inc=True)
        self.barrier()

        self.dma("sync", self.maskb[:], self.masks_t.ap())
        self.qw("sync", self.dS)
        for eng in ("tensor", "scalar", "vector"):
            self.qw(eng, self.dS)
        PP = [self.p45, self.p67]
        ep_sv = {0: 0, 1: 0}
        ep_sa = {0: 0, 1: 0}
        ystore = {0: 0, 1: 0}
        pending_ep = []

        pairs = []
        tbi = 0
        for hl in range(2):
            for tb in range(NF):
                npair = 2 * (tb + 1)
                for pr in range(npair):
                    pairs.append((hl, tb, pr, npair, tbi % 2))
                tbi += 1
        NP = len(pairs)
        exp_v = {}
        pre_v = {}
        cons_v = {}

        def emit_qk(g):
            hl, tb, pr, npair, pi = pairs[g]
            par = g % 2
            ps = self.p01 if par == 0 else self.p23
            qrhs0 = QT[:, hl * 8192 + tb * 512:hl * 8192 + tb * 512 + 512]
            if g >= 2:
                self.qw("tensor", self.sA, exp_v[g - 2])
            for k in range(2):
                sc = 2 * pr + k
                self.pe(lambda e, sc=sc, k=k, ps=ps, hl=hl, qrhs0=qrhs0: e.matmul(
                    ps[:, k * 512:(k + 1) * 512],
                    KT[:, hl * 8192 + sc * 128:hl * 8192 + sc * 128 + 128],
                    qrhs0, start=True, stop=True), inc=(k == 1))
            wS = self.sP.v
            self.qw("scalar", self.sP, wS)
            if g >= 2:
                self.qw("scalar", self.sP, cons_v[g - 2])
            ev = self.ae(lambda e, ps=ps, par=par, hl=hl, tb=tb: e.activation(
                out=self.ptb[:, par * 1024:(par + 1) * 1024], in_=ps[:], func=AF.Exp,
                scale=ATT_SCALE, bias=self.cbias[:, hl * 16 + tb:hl * 16 + tb + 1]),
                inc=True)
            exp_v[g] = ev
            wmask = None
            for k in range(2):
                sc = 2 * pr + k
                if sc >= 4 * tb:
                    kk = sc - 4 * tb
                    self.qw("vector", self.sA, ev)
                    wmask = self.ve(lambda e, k=k, par=par, kk=kk: e.tensor_tensor(
                        out=self.ptb[:, par * 1024 + k * 512:par * 1024 + (k + 1) * 512],
                        in0=self.ptb[:, par * 1024 + k * 512:par * 1024 + (k + 1) * 512],
                        in1=self.maskb[:, kk * 512:(kk + 1) * 512], op=ALU.mult),
                        inc=True)
            if wmask is None:
                self.qw("vector", self.sA, ev)
            if g >= 2:
                self.qw("vector", self.sP, cons_v[g - 2])
            pre_v[g] = self.ve(lambda e, par=par: e.tensor_tensor(
                out=self.sumbuf[:, par * 512:(par + 1) * 512],
                in0=self.ptb[:, par * 1024:par * 1024 + 512],
                in1=self.ptb[:, par * 1024 + 512:par * 1024 + 1024], op=ALU.add), inc=True)

        def emit_consume(g):
            hl, tb, pr, npair, pi = pairs[g]
            par = g % 2
            nsc = 2 * npair
            outT = PP[pi][:, 0:512]
            sums = PP[pi][0:1, 512:1024]
            self.qw("tensor", self.sV, pre_v[g])
            if pr == 0:
                self.qw("tensor", self.sV, ep_sv[pi])
                self.qw("tensor", self.sA, ep_sa[pi])
            st = (pr == 0)
            sp = (pr == npair - 1)
            self.pe(lambda e, par=par, st=st, sp=sp, sums=sums: e.matmul(
                sums, self.onescol_sb[:], self.sumbuf[:, par * 512:(par + 1) * 512],
                start=st, stop=sp))
            for k in range(2):
                sc = 2 * pr + k
                st2 = (sc == 0)
                sp2 = (sc == nsc - 1)
                self.pe(lambda e, k=k, par=par, sc=sc, hl=hl, st2=st2, sp2=sp2,
                        outT=outT: e.matmul(
                    outT, VT[:, sc * 256 + hl * 128:sc * 256 + hl * 128 + 128],
                    self.ptb[:, par * 1024 + k * 512:par * 1024 + (k + 1) * 512],
                    start=st2, stop=sp2), inc=(k == 1))
            cons_v[g] = self.sP.v
            if not sp:
                return
            self.qw("scalar", self.sP, cons_v[g])
            self.ae(lambda e, sums=sums: e.activation(out=self.r1[0:1, 0:512], in_=sums,
                                                      func=AF.Ln), inc=True)
            self.qw("scalar", self.sA)
            wrec = self.ae(lambda e: e.activation(out=self.r1[0:1, 0:512],
                           in_=self.r1[0:1, 0:512], func=AF.Exp, scale=-1.0), inc=True)
            ep_sa[pi] = wrec
            rbdst = PP[pi][:, 512:1024]

            def mk_ep(pi=pi, hl=hl, tb=tb, wrec=wrec, rbdst=rbdst, outT=outT):
                def ep():
                    self.qw("tensor", self.sA, wrec)
                    wbc = self.pe(lambda e: e.matmul(rbdst, self.onesrow_sb[:],
                                  self.r1[0:1, 0:512], start=True, stop=True), inc=True)
                    self.qw("vector", self.sP, wbc)
                    wrb = self.ve(lambda e: e.tensor_copy(out=self.rb, in_=rbdst),
                                  inc=True)
                    ypar = pi
                    self.qw("vector", self.sV, wrb)
                    self.qw("vector", self.ys[ypar], ystore[ypar])
                    wye = self.ve(lambda e, ypar=ypar, outT=outT: e.scalar_tensor_tensor(
                        out=self.ysb[:, ypar * 512:(ypar + 1) * 512], in0=outT, scalar=1.0,
                        in1=self.rb, op0=ALU.mult, op1=ALU.mult), inc=True)
                    ep_sv[pi] = wye
                    self.qw("sync", self.sV, wye)
                    self.dma("sync", self.y_in[hl * 128:(hl + 1) * 128,
                                               tb * 512:(tb + 1) * 512],
                             self.ysb[:, ypar * 512:(ypar + 1) * 512],
                             ctr=self.ys[ypar])
                    ystore[ypar] = self.ys[ypar].v
                return ep

            pending_ep.append(mk_ep())

        for g in range(NP):
            emit_qk(g)
            if pairs[g][2] == 1 and pending_ep:
                pending_ep.pop(0)()
            if g >= 1:
                emit_consume(g - 1)
        emit_consume(NP - 1)
        while pending_ep:
            pending_ep.pop(0)()
        self.barrier()
        self.phase_done("scores_" + pfx)

        hcc = self.cc.h
        self.cc.v += 1
        self.q("gpsimd", lambda e: e.collective_compute(
            "AllGather", ALU.bypass, replica_groups=rg,
            ins=[self.y_in.opt()], outs=[self.y_out.opt()]).then_inc(hcc, 1))
        wcc2 = self.cc.v
        if mid_hook is not None:
            mid_hook()
        self.qw("sync", self.cc, wcc2)
        if "yout" in self.dbg:
            self.dma("sync", self.dbg["yout"], self.y_out)
            self.qw("sync", self.dS)

        ysall = self.B[3]
        if not hasattr(self, "_cid"):
            self._cid = self.nc.partition_id(engines=[mybir.EngineType.SP])
        cid = self._cid
        for k in range(16):
            self.dma("sync", ysall[:, k * 1024:(k + 1) * 1024],
                     self.y_out[k * 128:(k + 1) * 128, bass.ts(cid, TOWN)])
        wY = self.dS.v
        self.qw("tensor", self.dS, wY)
        self.qw("gpsimd", self.dS, wY)
        self.qw("sync", self.dS, wY)
        bo = self.bcol["abo" if a else "bbo"]

        def wo_out(ct, ps, wmm):
            self.qw("sync", self.sV)
            self.dma("sync", self.io32[:, 0:1024], self.xstream[:, ct * 1024:(ct + 1) * 1024])
            wS = self.dS.v
            self.qw("vector", self.sP, wmm)
            self.qw("vector", self.dS, wS)
            self.ve(lambda e, ct=ct, ps=ps: e.scalar_tensor_tensor(
                out=self.io32[:, 1024:2048], in0=ps[:], scalar=bo[:, ct:ct + 1],
                in1=self.io32[:, 0:1024], op0=ALU.add, op1=ALU.add), inc=True)
            self.qw("sync", self.sV)
            self.dma("sync", self.xstream[:, ct * 1024:(ct + 1) * 1024],
                     self.io32[:, 1024:2048])

        self._colslab_proj(self.inp["aWo" if a else "bWo"], None, wo_out,
                           lambda k, hh: ysall[:, k * 1024 + hh * 512:k * 1024 + hh * 512 + 512])
        self.barrier()
        self.tap("stream", self.xstream)
        self.phase_done("attn_" + pfx)

    def build(self, phases):
        self.declare_dram()
        self.declare_inline()
        self.alloc_sbuf()
        self.make_sems()
        for ph in phases:
            if self.stopped:
                break
            ph()
        with self.es:
            block = self.es.enter_context(self.nc.Block())
            for eng in ("tensor", "scalar", "vector", "gpsimd", "sync"):
                getattr(block, eng)(lambda e, eng=eng: [fn(e) for fn in self.ops[eng]])
        return self.nc


def build_core(core_id, taps=(), upto=None, which="full"):
    b = Builder(core_id, taps=taps, upto=upto)
    prep_serial = [b.ph_prep_casts, b.ph_prep_mms, b.barrier]
    if which == "ffn_only":
        phases = [b.ph_consts, b.ph_input] + prep_serial + [b.ph_ffn, b.ph_output]
    elif which == "attn_a_only":
        phases = [b.ph_consts, b.ph_input] + prep_serial + [lambda: b.ph_attn("a"), b.ph_output]
    elif which == "attn_b_only":
        phases = [b.ph_consts, b.ph_input] + prep_serial + [lambda: b.ph_attn("b"), b.ph_output]
    else:
        phases = [b.ph_consts, b.ph_input, b.ph_prep_casts,
                  lambda: b.ph_attn("a", mid_hook=b.ph_prep_mms), b.ph_ffn,
                  lambda: b.ph_attn("b"), b.ph_ffn, b.ph_output]
    return b.build(phases)


def make_in_map(inputs, core_id):
    x = np.asarray(inputs["L0_attn_a_x"]).reshape(T, D)
    h0 = 2 * core_id
    aWq = np.asarray(inputs["L0_attn_a_Wq"])
    aWk = np.asarray(inputs["L0_attn_a_Wk"])
    aWv = np.asarray(inputs["L0_attn_a_Wv"])
    bWk = np.asarray(inputs["L0_attn_b_Wk"])
    bWv = np.asarray(inputs["L0_attn_b_Wv"])
    m = {
        "x_own": np.ascontiguousarray(x[core_id * TOWN:(core_id + 1) * TOWN]),
        "aWq_my": aWq[h0:h0 + 2], "aWk_my": aWk[h0:h0 + 2], "aWv_my": aWv[h0:h0 + 2],
        "abq_my": np.asarray(inputs["L0_attn_a_bq"])[h0:h0 + 2],
        "abk_my": np.asarray(inputs["L0_attn_a_bk"])[h0:h0 + 2],
        "abv_my": np.asarray(inputs["L0_attn_a_bv"])[h0:h0 + 2],
        "aWo": inputs["L0_attn_a_Wo"], "abo": inputs["L0_attn_a_bo"],
        "bWq": inputs["L0_attn_b_Wq"],
        "bWk_my": bWk[:, h0 * 128:(h0 + 2) * 128],
        "bWv_my": bWv[:, h0 * 128:(h0 + 2) * 128],
        "bWo": inputs["L0_attn_b_Wo"],
        "bbq": inputs["L0_attn_b_bq"],
        "bbk_my": np.asarray(inputs["L0_attn_b_bk"])[h0 * 128:(h0 + 2) * 128],
        "bbv_my": np.asarray(inputs["L0_attn_b_bv"])[h0 * 128:(h0 + 2) * 128],
        "bbo": inputs["L0_attn_b_bo"],
        "su_ug": inputs["L0_ffn_SU_ug"], "su_dp": inputs["L0_ffn_SU_dp"],
        "wsug": inputs["L0_ffn_Wscale_ug"], "wsdp": inputs["L0_ffn_Wscale_dp"],
        "fWupT": np.ascontiguousarray(np.asarray(inputs["L0_ffn_W_up"]).T),
        "fWgateT": np.ascontiguousarray(np.asarray(inputs["L0_ffn_W_gate"]).T),
        "fWdownT": np.ascontiguousarray(np.asarray(inputs["L0_ffn_W_down"]).T),
    }
    return {k: np.ascontiguousarray(np.asarray(v, dtype=np.float32)) for k, v in m.items()}


def kernel(**inputs):
    nc = build_core(0)
    in_maps = [make_in_map(inputs, c) for c in range(NCORE)]
    res = run_bass_kernel_spmd(nc, in_maps, core_ids=list(range(NCORE)))
    out = np.concatenate([np.asarray(res.results[c]["y_own"]) for c in range(NCORE)], axis=0)
    return out.reshape(1, T, D).astype(np.float32)


# version 09 (final)
# speedup vs baseline: 1.7618x; 1.0124x over previous
import math
import contextlib
import numpy as np
import ml_dtypes

import concourse.bass as bass
import concourse.mybir as mybir
from concourse.bass_utils import run_bass_kernel_spmd

F32 = mybir.dt.float32
BF16 = mybir.dt.bfloat16
AF = mybir.ActivationFunctionType
ALU = mybir.AluOpType
AX = mybir.AxisListType

NCORE = 8
T = 8192
D = 2048
TOWN = 1024
NF = 16
NI = 64
INTER = 8192
HPC = 2
DH = 128
ATT_SCALE = 1.0 / math.sqrt(128.0)
EPS = 1e-5
R2048 = 1.0 / math.sqrt(2048.0)
R8192 = 1.0 / math.sqrt(8192.0)
BF = ml_dtypes.bfloat16


def _hmat(n):
    a = np.arange(n)
    bits = a[:, None] & a[None, :]
    pc = np.zeros((n, n), dtype=np.int64)
    x = bits.copy()
    while x.any():
        pc += x & 1
        x >>= 1
    return np.where(pc % 2 == 0, 1.0, -1.0).astype(np.float32)


class Ctr:
    def __init__(self, es, nc, name):
        self.h = es.enter_context(nc.semaphore(name))
        self.v = 0


class Builder:

    def __init__(self, core_id, taps=(), upto=None):
        self.core_id = core_id
        self.taps = set(taps)
        self.upto = upto
        self.stopped = False
        self.nc = bass.Bass("TRN2", target_bir_lowering=False,
                    dynamic_dma_scratch_size=4096)
        self.es = contextlib.ExitStack()
        self.ops = {k: [] for k in ("tensor", "scalar", "vector", "gpsimd", "sync")}
        self.H0 = 2 * core_id

    def q(self, eng, fn):
        if not self.stopped:
            self.ops[eng].append(fn)

    def qw(self, eng, ctr, val=None):
        v = ctr.v if val is None else val
        if v <= 0:
            return
        h = ctr.h
        self.q(eng, lambda e: e.wait_ge(h, v))

    def phase_done(self, label):
        if self.upto is not None and label == self.upto:
            self.stopped = True

    def pe(self, fn, inc=False):
        if inc:
            h = self.sP.h
            self.sP.v += 1
            self.q("tensor", lambda e: fn(e).then_inc(h, 1))
        else:
            self.q("tensor", fn)
        return self.sP.v

    def ve(self, fn, inc=False):
        if inc:
            h = self.sV.h
            self.sV.v += 1
            self.q("vector", lambda e: fn(e).then_inc(h, 1))
        else:
            self.q("vector", fn)
        return self.sV.v

    def ae(self, fn, inc=False):
        if inc:
            h = self.sA.h
            self.sA.v += 1
            self.q("scalar", lambda e: fn(e).then_inc(h, 1))
        else:
            self.q("scalar", fn)
        return self.sA.v

    def dma(self, eng, out, in_, ctr=None, **kw):
        ctr = ctr if ctr is not None else (self.dG if eng == "gpsimd" else self.dS)
        h = ctr.h
        ctr.v += 16

        def _go(e, out=out, in_=in_, h=h, kw=kw):
            try:
                return e.dma_start(out=out, in_=in_, **kw).then_inc(h, 16)
            except Exception:
                import sys
                print("FAILING DMA out=", str(out)[:300], " in_=", str(in_)[:300],
                      file=sys.stderr)
                raise
        self.q(eng, _go)
        return ctr.v

    def tdma(self, out, in_):
        h = self.dS.h
        self.dS.v += 16
        self.q("sync", lambda e: e.dma_start_transpose(out, in_).then_inc(h, 16))
        return self.dS.v

    def barrier(self):
        pairs = [("tensor", self.sP), ("scalar", self.sA), ("vector", self.sV),
                 ("gpsimd", self.sG)]
        waits = []
        for eng, ctr in pairs:
            h = ctr.h
            pv = ctr.v
            ctr.v += 1
            if eng == "gpsimd":
                self.q(eng, lambda e, h=h, pv=pv: (e.wait_ge(h, pv) if pv > 0 else None,
                                                   e.nop(nofuse=True).then_inc(h, 1)))
            else:
                self.q(eng, lambda e, h=h: e.drain(fusable=False).then_inc(h, 1))
            waits.append((h, ctr.v))
        waits += [(self.dS.h, self.dS.v), (self.dG.h, self.dG.v), (self.dW.h, self.dW.v),
                  (self.dP.h, self.dP.v)]
        for grp in (self.pl, self.px, self.ys):
            waits += [(s.h, s.v) for s in grp]
        for grp in (self.u2, self.d2, self.c2, self.g2, self.stg):
            waits += [(s.h, s.v) for s in grp]
        for eng in ("tensor", "scalar", "vector", "gpsimd", "sync"):
            for h, v in waits:
                if v > 0:
                    self.q(eng, lambda e, h=h, v=v: e.wait_ge(h, v))

    def tap(self, name, src_dram):
        if name in self.dbg:
            self.barrier()
            self.dma("sync", self.dbg[name], src_dram)
            self.qw("sync", self.dS)

    def tap_sbuf(self, name, sb_ap):
        if name in self.dbg:
            self.barrier()
            self.dma("sync", self.dbg[name], sb_ap)
            self.qw("sync", self.dS)

    def declare_dram(self):
        nc = self.nc
        dt = nc.dram_tensor
        self.inp = {}
        for nm, shape in [
            ("x_own", [TOWN, D]),
            ("aWq_my", [HPC, D, DH]), ("aWk_my", [HPC, D, DH]), ("aWv_my", [HPC, D, DH]),
            ("abq_my", [HPC, DH]), ("abk_my", [HPC, DH]), ("abv_my", [HPC, DH]),
            ("aWo", [D, D]), ("abo", [D]),
            ("bWq", [D, D]), ("bWk_my", [D, HPC * DH]), ("bWv_my", [D, HPC * DH]),
            ("bWo", [D, D]),
            ("bbq", [D]), ("bbk_my", [HPC * DH]), ("bbv_my", [HPC * DH]), ("bbo", [D]),
            ("su_ug", [D]), ("su_dp", [INTER]), ("wsug", [2 * INTER]), ("wsdp", [D]),
            ("fWupT", [D, INTER]), ("fWgateT", [D, INTER]), ("fWdownT", [INTER, D]),
        ]:
            self.inp[nm] = dt(nm, shape, F32, kind="ExternalInput").ap()
        self.y_own = dt("y_own", [TOWN, D], F32, kind="ExternalOutput").ap()

        self.xstream = dt("xstream", [128, NF * TOWN], F32, kind="Internal").ap()
        self.wupT = dt("wupT", [D, INTER], BF16, kind="Internal").ap()
        self.wgateT = dt("wgateT", [D, INTER], BF16, kind="Internal").ap()
        self.wdT = dt("wdT", [INTER, D], BF16, kind="Internal").ap()
        self.wdH = dt("wdH", [INTER, D], BF16, kind="Internal").ap()
        self.wd2 = dt("wd2", [NF, 128, NI, 128], BF16, kind="Internal").ap()
        self.ag_in = dt("ag_in", [D, TOWN], BF16, kind="Internal").ap()
        self.ag_out = dt("ag_out", [NCORE * D, TOWN], BF16, kind="Internal",
                         addr_space="Shared").ap()
        self.y_in = dt("y_in", [HPC * DH, T], BF16, kind="Internal").ap()
        self.y_out = dt("y_out", [NCORE * HPC * DH, T], BF16, kind="Internal",
                        addr_space="Shared").ap()

        self.dbg = {}
        def tapout(nm, shape, dtp):
            self.dbg[nm] = dt("dbg_" + nm, shape, dtp, kind="ExternalOutput").ap()
        if "stream" in self.taps: tapout("stream", [128, NF * TOWN], F32)
        if "xln" in self.taps: tapout("xln", [128, NF * TOWN], BF16)
        if "G" in self.taps: tapout("G", [NCORE * D, TOWN], BF16)
        if "kvq" in self.taps:
            tapout("K", [128, 2 * T], BF16)
            tapout("V", [128, NI * 256], BF16)
            tapout("Q", [128, 2 * T], BF16)
        if "yout" in self.taps: tapout("yout", [NCORE * HPC * DH, T], BF16)
        if "wd2" in self.taps: tapout("wd2", [NF, 128, NI, 128], BF16)
        if "x1" in self.taps: tapout("x1", [128, NF * TOWN], F32)
        if "h" in self.taps: tapout("h", [128, 4 * 16384], BF16)

    def declare_inline(self):
        nc = self.nc
        self.h128_t = nc.inline_tensor(_hmat(128).astype(BF), name="h128i")
        self.h64_t = nc.inline_tensor(_hmat(64).astype(BF), name="h64i")
        self.id128_t = nc.inline_tensor(np.eye(128, dtype=np.float32), name="id128i")
        self.onescol_t = nc.inline_tensor(np.ones((128, 1), dtype=np.float32).astype(BF),
                                          name="onescoli")
        self.onesrow_t = nc.inline_tensor(np.ones((1, 128), dtype=np.float32), name="onesrowi")
        m = np.zeros((128, 4, 512), dtype=np.float32)
        for k in range(4):
            for p in range(128):
                m[p, k, 128 * k + p:] = 1.0
        self.masks_t = nc.inline_tensor(m.reshape(128, 2048).astype(BF), name="masksi")

    def alloc_sbuf(self):
        nc = self.nc
        A = nc.alloc_sbuf_tensor_at
        BASE = (nc.sbuf_base + 31) & ~31
        assert BASE <= 4224, BASE
        self.B = [A(f"B{i}", [128, 16384], BF16, offset=BASE + 32768 * i) for i in range(6)]
        o = BASE + 32768 * 6

        def at(name, shape, dtype):
            nonlocal o
            t = A(name, shape, dtype, offset=o)
            o += int(np.prod(shape[1:])) * (4 if dtype == F32 else 2)
            o = (o + 31) & ~31
            return t

        self.io32 = at("io32", [128, 2048], F32)
        self.ptb = at("ptb", [128, 2048], BF16)
        self.maskb = None
        self.rows32 = at("rows32", [1, 2048], F32)
        self.silu_sb = at("silu_sb", [128, 1024], BF16)
        self.ysb = self.silu_sb
        self.h128_sb = at("h128_sb", [128, 128], BF16)
        self.h64_sb = at("h64_sb", [64, 64], BF16)
        self.id128_sb = at("id128_sb", [128, 128], F32)
        self.onescol_sb = at("onescol_sb", [128, 1], BF16)
        self.onesrow_sb = at("onesrow_sb", [1, 128], F32)
        self.cbias = at("cbias", [128, 32], F32)
        self.sucol = at("sucol", [128, 16], F32)
        self.wsgcol = at("wsgcol", [128, 64], F32)
        self.wsucol = at("wsucol", [128, 64], F32)
        self.wsdcol = at("wsdcol", [128, 16], F32)
        self.bcol = {}
        for nm in ("abo", "bbq", "bbo"):
            self.bcol[nm] = at(f"col_{nm}", [128, 16], F32)
        for nm in ("abq_my", "abk_my", "bbk_my"):
            self.bcol[nm] = at(f"col_{nm}", [128, 2], F32)
        self.bvb = at("bvb", [128, 256], BF16)
        self.sumbuf = A("sumbuf", [128, 1024], BF16, offset=BASE + 32768 * 3)
        self.r1 = at("r1", [1, 512], F32)
        self.r2 = at("r2", [1, 32], F32)
        self.qb1 = at("qb1", [1, 64], F32)
        self.sqs = self.ptb
        self.tmpcol = at("tmpcol", [128, 64], F32)
        assert o <= 229344, f"sbuf overflow {o}"
        self.sbuf_top = o

        self.rb = self.io32.ap()[:, 1536:2048]
        io_off = BASE + 32768 * 6
        self.maskb = A("maskb", [128, 2048], BF16, offset=io_off)
        self.B0f = A("B0f", [128, 8192], F32, offset=BASE)
        self.x1f = A("x1f", [128, 16384], F32, offset=BASE + 32768 * 4)
        self.gbuf = [A(f"gbuf{i}", [128, 1024], BF16, offset=io_off + 8192 + 2048 * i)
                     for i in range(2)]

        self.p01 = self.es.enter_context(nc.psum_tensor("p01", [128, 1024], F32))
        self.p23 = self.es.enter_context(nc.psum_tensor("p23", [128, 1024], F32))
        self.p45 = self.es.enter_context(nc.psum_tensor("p45", [128, 1024], F32))
        self.p67 = self.es.enter_context(nc.psum_tensor("p67", [128, 1024], F32))

    def make_sems(self):
        nc, es = self.nc, self.es
        self.dS = Ctr(es, nc, "dS")
        self.dG = Ctr(es, nc, "dG")
        self.dW = Ctr(es, nc, "dW")
        self.sP = Ctr(es, nc, "sP")
        self.sA = Ctr(es, nc, "sA")
        self.sV = Ctr(es, nc, "sV")
        self.sG = Ctr(es, nc, "sG")
        self.cc = Ctr(es, nc, "cc")
        self.u2 = [Ctr(es, nc, "u0"), Ctr(es, nc, "u1")]
        self.d2 = [Ctr(es, nc, "d0"), Ctr(es, nc, "d1")]
        self.c2 = [Ctr(es, nc, "c0"), Ctr(es, nc, "c1")]
        self.g2 = [Ctr(es, nc, "g0"), Ctr(es, nc, "g1")]
        self.stg = [Ctr(es, nc, "sg0"), Ctr(es, nc, "sg1")]
        self.dP = Ctr(es, nc, "dP")
        self.pl = [Ctr(es, nc, "pl0"), Ctr(es, nc, "pl1")]
        self.px = [Ctr(es, nc, "px0"), Ctr(es, nc, "px1")]
        self.ys = [Ctr(es, nc, "ys0"), Ctr(es, nc, "ys1")]

    def ph_consts(self):
        for dst, src in [
            (self.h128_sb, self.h128_t), (self.h64_sb, self.h64_t),
            (self.id128_sb, self.id128_t), (self.onescol_sb, self.onescol_t),
            (self.onesrow_sb, self.onesrow_t),
        ]:
            self.dma("sync", dst[:], src.ap())

        def vec2col(flat, nch, dst, mul=None):
            src = flat.rearrange("(c p) -> c p", p=128) if len(flat.shape) == 1 else flat
            self.qw("sync", self.sV)
            self.dma("sync", self.io32[0:nch, 0:128], src)
            self.qw("tensor", self.dS)
            self.qw("tensor", self.sV)
            self.pe(lambda e, nch=nch: e.transpose(self.p01[:, 0:nch], self.io32[0:nch, 0:128],
                                                   self.id128_sb[0:nch, 0:nch]), inc=True)
            self.qw("vector", self.sP)
            if mul is None:
                self.ve(lambda e, d=dst, nch=nch: e.tensor_copy(out=d[:, 0:nch],
                                                                in_=self.p01[:, 0:nch]), inc=True)
            else:
                self.ve(lambda e, d=dst, nch=nch, m=mul: e.tensor_scalar(
                    out=d[:, 0:nch], in0=self.p01[:, 0:nch], scalar1=float(m), scalar2=None,
                    op0=ALU.mult), inc=True)

        vec2col(self.inp["su_ug"], 16, self.sucol)
        vec2col(self.inp["wsdp"], 16, self.wsdcol, mul=R8192)
        for nm in ("abo", "bbq", "bbo"):
            vec2col(self.inp[nm], 16, self.bcol[nm])
        for nm in ("abq_my", "abk_my", "bbk_my"):
            vec2col(self.inp[nm], 2, self.bcol[nm])
        vec2col(self.inp["su_dp"], 64, self.tmpcol)
        vec2col(self.inp["wsug"][0:INTER], 64, self.wsucol, mul=R2048)
        self.qw("vector", self.sV)
        self.ve(lambda e: e.tensor_tensor(out=self.wsucol[:], in0=self.wsucol[:],
                                          in1=self.tmpcol[:], op=ALU.mult), inc=True)
        vec2col(self.inp["wsug"][INTER:2 * INTER], 64, self.wsgcol, mul=R2048)
        self.barrier()
        self.phase_done("consts")

    def _transpose_tiles_to(self, dst_is_output):
        xs_v = self.xstream.rearrange("p (c t) -> p c t", c=NF, t=TOWN)
        for tc in range(8):
            self.qw("sync", self.sV)
            if dst_is_output:
                self.dma("sync", self.io32.ap().rearrange("p (c t) -> p c t", c=NF, t=128),
                         xs_v[:, :, tc * 128:(tc + 1) * 128])
            else:
                self.dma("sync", self.io32[:], self.inp["x_own"][tc * 128:(tc + 1) * 128, :])
            wS = self.dS.v
            evs = {}
            pairs = [self.p01, self.p23, self.p45, self.p67]
            for fc in range(16):
                slot = fc % 8
                ps = pairs[slot % 4]
                col = 512 * (slot // 4)
                self.qw("tensor", self.dS, wS)
                if fc >= 8:
                    self.qw("tensor", self.sV, evs[fc - 8])
                self.pe(lambda e, fc=fc, ps=ps, col=col: e.transpose(
                    ps[:, col:col + 128], self.io32[:, fc * 128:(fc + 1) * 128],
                    self.id128_sb[:]), inc=True)
                self.qw("vector", self.sP)
                if fc == 0:
                    self.qw("vector", self.dS)
                evs[fc] = self.ve(lambda e, fc=fc, ps=ps, col=col: e.tensor_copy(
                    out=self.B0f[:, fc * 128:(fc + 1) * 128],
                    in_=ps[:, col:col + 128]), inc=True)
            self.qw("sync", self.sV)
            if dst_is_output:
                self.dma("sync", self.y_own[tc * 128:(tc + 1) * 128, :], self.B0f[:, 0:2048])
            else:
                src = self.B0f.ap()[:, 0:2048].rearrange("p (c t) -> p c t", c=NF, t=128)
                self.dma("sync", xs_v[:, :, tc * 128:(tc + 1) * 128], src)
        self.barrier()

    def ph_input(self):
        self._transpose_tiles_to(dst_is_output=False)
        self.tap("stream", self.xstream)
        self.phase_done("input")

    def ph_output(self):
        self._transpose_tiles_to(dst_is_output=True)
        for eng in ("tensor", "scalar", "vector", "gpsimd", "sync"):
            self.qw(eng, self.dS)

    def ph_prep_casts(self):
        self.dma("gpsimd", self.wdT, self.inp["fWdownT"])
        self.prep_cast_w = self.dG.v
        self.dma("gpsimd", self.wupT, self.inp["fWupT"])
        self.dma("gpsimd", self.wgateT, self.inp["fWgateT"])
        self.prep_cast_ug = self.dG.v

    def ph_prep_mms(self, barriers=True):
        stage = self.B[1]
        self.qw("scalar", self.dG, self.prep_cast_w)
        self.qw("scalar", self.sP)
        wdT_v = self.wdT.rearrange("(cc p) c -> cc p c", cc=64, p=128)
        wdH_v = self.wdH.rearrange("(cp p) c -> cp p c", cp=64, p=128)

        def pipelined_pass(n_iter, nrow, ld_src, st_dst, hmat_sb, ld_eng="scalar"):
            self.qw("tensor", self.sV)
            self.qw("tensor", self.sA)
            LD = [stage.ap()[0:nrow, 0:2048], stage.ap()[0:nrow, 2048:4096]]
            EV = [stage.ap()[0:nrow, 4096:6144], stage.ap()[0:nrow, 6144:8192]]
            pse = [(self.p01, self.p23), (self.p45, self.p67)]
            mmv, evv, stv, ldv = {}, {}, {}, {}
            for p in range(min(2, n_iter)):
                self.dma(ld_eng, LD[p % 2], ld_src(p), ctr=self.pl[p % 2])
                ldv[p] = self.pl[p % 2].v
            for p in range(n_iter):
                q = p % 2
                self.qw("tensor", self.pl[q], ldv[p])
                if p >= 2:
                    self.qw("tensor", self.sV, evv[p - 2])
                for k in range(4):
                    ps = pse[q][k // 2][0:nrow, (k % 2) * 512:(k % 2) * 512 + 512]
                    self.pe(lambda e, ps=ps, k=k, q=q, hm=hmat_sb: e.matmul(
                        ps, hm, LD[q][0:nrow, k * 512:(k + 1) * 512],
                        start=True, stop=True), inc=(k == 3))
                mmv[p] = self.sP.v
                self.qw("vector", self.sP, mmv[p])
                if p >= 2:
                    self.qw("vector", self.px[q], stv[p - 2])
                self.ve(lambda e, q=q, nrow=nrow: e.tensor_copy(
                    out=EV[q][0:nrow, 0:1024], in_=pse[q][0][0:nrow, :]))
                evv[p] = self.ve(lambda e, q=q, nrow=nrow: e.tensor_copy(
                    out=EV[q][0:nrow, 1024:2048], in_=pse[q][1][0:nrow, :]), inc=True)
                if p + 2 < n_iter:
                    self.qw(ld_eng, self.sP, mmv[p])
                    self.dma(ld_eng, LD[q], ld_src(p + 2), ctr=self.pl[q])
                    ldv[p + 2] = self.pl[q].v
                self.qw("scalar", self.sV, evv[p])
                st_dst(p, EV[q], self.px[q])
                stv[p] = self.px[q].v

        pipelined_pass(
            128, 64,
            lambda p: wdT_v[:, p, :],
            lambda p, ev, ctr: self.dma("scalar", wdH_v[:, p, :], ev[0:64, 0:2048], ctr=ctr),
            self.h64_sb[:])
        self.qw("scalar", self.px[0])
        self.qw("scalar", self.px[1])

        wd2_v = self.wd2.rearrange("ct p ic c -> p ct ic c")

        def h128_store(ic, ev, ctr):
            for c4 in range(4):
                self.dma("scalar", wd2_v[:, c4 * 4:(c4 + 1) * 4, ic, :],
                         ev[:, c4 * 512:(c4 + 1) * 512].rearrange(
                             "p (ct c) -> p ct c", ct=4, c=128), ctr=ctr)

        pipelined_pass(
            NI, 128,
            lambda ic: self.wdH[ic * 128:(ic + 1) * 128, :],
            h128_store,
            self.h128_sb[:])
        self.prep_mms_done = self.dP.v
        self.tap("wd2", self.wd2)
        self.phase_done("prep")

    def ph_ln(self, with_su):
        xln = self.B[4]
        sums_lo, sums_hi = self.p67[0:1, 0:512], self.p67[0:1, 512:1024]
        sq_lo, sq_hi = self.p45[0:1, 0:512], self.p45[0:1, 512:1024]
        for c in range(NF):
            self.qw("sync", self.sV)
            self.dma("sync", self.io32[:, 0:1024], self.xstream[:, c * 1024:(c + 1) * 1024])
            self.qw("vector", self.dS)
            self.ve(lambda e, c=c: e.tensor_copy(out=xln[:, c * 1024:(c + 1) * 1024],
                                                 in_=self.io32[:, 0:1024]), inc=True)
            self.qw("scalar", self.sV)
            self.qw("scalar", self.sP)
            self.ae(lambda e, c=c: e.activation(out=self.sqs[:, 0:1024],
                    in_=xln[:, c * 1024:(c + 1) * 1024], func=AF.Square), inc=True)
            self.qw("tensor", self.sV)
            self.pe(lambda e, c=c: e.matmul(sums_lo, self.onescol_sb[:],
                    xln[:, c * 1024:c * 1024 + 512], start=(c == 0), stop=(c == NF - 1)))
            self.pe(lambda e, c=c: e.matmul(sums_hi, self.onescol_sb[:],
                    xln[:, c * 1024 + 512:(c + 1) * 1024], start=(c == 0), stop=(c == NF - 1)))
            self.qw("tensor", self.sA)
            self.pe(lambda e, c=c: e.matmul(sq_lo, self.onescol_sb[:], self.sqs[:, 0:512],
                    start=(c == 0), stop=(c == NF - 1)))
            self.pe(lambda e, c=c: e.matmul(sq_hi, self.onescol_sb[:], self.sqs[:, 512:1024],
                    start=(c == 0), stop=(c == NF - 1)), inc=True)
        mu = self.rows32[0:1, 0:1024]
        ex2 = self.rows32[0:1, 1024:2048]
        self.qw("vector", self.sP)
        self.ve(lambda e: e.tensor_scalar(out=mu, in0=self.p67[0:1, :], scalar1=1.0 / 2048.0,
                                          scalar2=None, op0=ALU.mult), inc=True)
        self.qw("vector", self.sV)
        self.ve(lambda e: e.tensor_scalar(out=ex2, in0=self.p45[0:1, :], scalar1=1.0 / 2048.0,
                                          scalar2=float(EPS), op0=ALU.mult, op1=ALU.add),
                inc=True)
        for k in range(2):
            self.qw("vector", self.sV)
            self.ve(lambda e, k=k: e.scalar_tensor_tensor(
                out=self.r1[0:1, 0:512], in0=mu[0:1, k * 512:(k + 1) * 512], scalar=-1.0,
                in1=mu[0:1, k * 512:(k + 1) * 512], op0=ALU.mult, op1=ALU.mult), inc=True)
            self.qw("vector", self.sV)
            self.ve(lambda e, k=k: e.tensor_tensor(
                out=ex2[0:1, k * 512:(k + 1) * 512], in0=ex2[0:1, k * 512:(k + 1) * 512],
                in1=self.r1[0:1, 0:512], op=ALU.add), inc=True)
        self.qw("scalar", self.sV)
        self.ae(lambda e: e.activation(out=ex2, in_=ex2, func=AF.Ln), inc=True)
        self.qw("scalar", self.sA)
        self.ae(lambda e: e.activation(out=ex2, in_=ex2, func=AF.Exp, scale=-0.5), inc=True)
        self.qw("tensor", self.sA)
        for k in range(2):
            self.pe(lambda e, k=k: e.matmul(self.p01[:, k * 512:(k + 1) * 512],
                    self.onesrow_sb[:], mu[0:1, k * 512:(k + 1) * 512], start=True, stop=True))
            self.pe(lambda e, k=k: e.matmul(self.p23[:, k * 512:(k + 1) * 512],
                    self.onesrow_sb[:], ex2[0:1, k * 512:(k + 1) * 512], start=True, stop=True),
                    inc=(k == 1))
        wbc = self.sP.v
        for c in range(NF):
            self.qw("vector", self.sP, wbc)
            self.qw("vector", self.sV)
            self.ve(lambda e, c=c: e.tensor_tensor(out=self.io32[:, 0:1024],
                    in0=xln[:, c * 1024:(c + 1) * 1024], in1=self.p01[:], op=ALU.subtract),
                    inc=True)
            self.qw("vector", self.sV)
            su = self.sucol[:, c:c + 1] if with_su else 1.0
            self.ve(lambda e, c=c, su=su: e.scalar_tensor_tensor(
                out=xln[:, c * 1024:(c + 1) * 1024], in0=self.io32[:, 0:1024],
                scalar=su, in1=self.p23[:], op0=ALU.mult, op1=ALU.mult), inc=True)
        self.barrier()

    def ph_ffn(self):
        self.ph_ln(with_su=True)
        self.tap_sbuf("xln", self.B[4][:])
        xln = self.B[4]
        mm_done = {}
        ev_done = {}

        def maybe_evac(cur_c):
            for k in range(NF - 1, -1, -1):
                if k in mm_done and k not in ev_done and 2 * k >= cur_c:
                    psk = mm_done[k][1]
                    self.qw("vector", self.sP, mm_done[k][0])
                    ev_done[k] = self.ve(lambda e, k=k, psk=psk: e.tensor_copy(
                        out=self.x1f[:, k * 1024:(k + 1) * 1024], in_=psk[:]), inc=True)

        for idx, c in enumerate(range(NF - 1, -1, -1)):
            ps = self.p01 if (idx % 2 == 0) else self.p23
            if idx >= 2:
                prev_k = NF - 1 - (idx - 2)
                self.qw("tensor", self.sV, ev_done[prev_k])
            self.pe(lambda e, c=c, ps=ps: e.matmul(ps[:, 0:512], self.h128_sb[:],
                    xln[:, c * 1024:c * 1024 + 512], start=True, stop=True))
            v = self.pe(lambda e, c=c, ps=ps: e.matmul(ps[:, 512:1024], self.h128_sb[:],
                    xln[:, c * 1024 + 512:(c + 1) * 1024], start=True, stop=True), inc=True)
            mm_done[c] = (v, ps)
            maybe_evac(c)
        maybe_evac(-100)
        for s in (1, 2, 4, 8):
            for a0 in range(NF):
                if a0 & s:
                    continue
                b0 = a0 + s
                self.qw("vector", self.sV)
                self.ve(lambda e, a0=a0: e.tensor_copy(out=self.io32[:, 0:1024],
                        in_=self.x1f[:, a0 * 1024:(a0 + 1) * 1024]), inc=True)
                self.qw("vector", self.sV)
                self.ve(lambda e, a0=a0, b0=b0: e.tensor_tensor(
                    out=self.x1f[:, a0 * 1024:(a0 + 1) * 1024],
                    in0=self.x1f[:, a0 * 1024:(a0 + 1) * 1024],
                    in1=self.x1f[:, b0 * 1024:(b0 + 1) * 1024], op=ALU.add), inc=True)
                self.qw("vector", self.sV)
                self.ve(lambda e, a0=a0, b0=b0: e.tensor_tensor(
                    out=self.x1f[:, b0 * 1024:(b0 + 1) * 1024], in0=self.io32[:, 0:1024],
                    in1=self.x1f[:, b0 * 1024:(b0 + 1) * 1024], op=ALU.subtract), inc=True)
        self.tap_sbuf("x1", self.x1f[:])
        x1b = self.B[4]
        self.qw("vector", self.sV)
        self.ve(lambda e: e.tensor_copy(out=self.ptb[:, 0:1024], in_=self.x1f[:, 0:1024]),
                inc=True)
        for c in range(1, NF):
            self.qw("vector", self.sV)
            self.ve(lambda e, c=c: e.tensor_copy(out=x1b[:, c * 1024:(c + 1) * 1024],
                    in_=self.x1f[:, c * 1024:(c + 1) * 1024]), inc=True)
        self.qw("vector", self.sV)
        self.ve(lambda e: e.tensor_copy(out=x1b[:, 0:1024], in_=self.ptb[:, 0:1024]), inc=True)
        self.barrier()
        self.phase_done("ffnpre")

        upbuf = [self.B[5].ap()[:, 0:4096], self.B[5].ap()[:, 4096:8192]]
        gabuf = [self.B[5].ap()[:, 8192:12288], self.B[5].ap()[:, 12288:16384]]
        wupT_v = self.wupT.rearrange("(f p) i -> p f i", f=16, p=128)
        wgaT_v = self.wgateT.rearrange("(f p) i -> p f i", f=16, p=128)
        self.qw("sync", self.dG, getattr(self, "prep_cast_ug", 0))

        def h_ap(ic):
            return self.B[ic // 16][:, (ic % 16) * 1024:(ic % 16) * 1024 + 1024]

        nslab = NI // 2
        mm_after_slab = {}
        slab_w = {}
        hdone = {}
        for sl in range(nslab):
            if sl >= 2:
                self.qw("sync", self.sP, mm_after_slab[sl - 2])
            usem = self.u2[sl % 2]
            for half in range(2):
                self.dma("sync", upbuf[sl % 2][:, half * 2048:(half + 1) * 2048].rearrange(
                    "p (f i) -> p f i", f=8, i=256),
                    wupT_v[:, half * 8:(half + 1) * 8, sl * 256:(sl + 1) * 256], ctr=usem)
                self.dma("sync", gabuf[sl % 2][:, half * 2048:(half + 1) * 2048].rearrange(
                    "p (f i) -> p f i", f=8, i=256),
                    wgaT_v[:, half * 8:(half + 1) * 8, sl * 256:(sl + 1) * 256], ctr=usem)
            slab_w[sl] = usem.v
            for j in range(2):
                ic = sl * 2 + j
                pu = self.p01 if (ic % 2 == 0) else self.p45
                pg = self.p23 if (ic % 2 == 0) else self.p67
                self.qw("tensor", self.u2[sl % 2], slab_w[sl])
                if ic >= 2:
                    self.qw("tensor", self.sV, hdone[ic - 2][0])
                    self.qw("tensor", self.sA, hdone[ic - 2][1])
                for f in range(16):
                    off = (f // 8) * 2048 + (f % 8) * 512 + j * 256
                    lhs_u = upbuf[sl % 2][:, off:off + 128] if False else None
                    hh = f // 8
                    base = hh * 2048 + (f % 8) * 256 + j * 128
                    lu = upbuf[sl % 2][:, base:base + 128]
                    lg = gabuf[sl % 2][:, base:base + 128]
                    self.pe(lambda e, f=f, pu=pu, lu=lu: e.matmul(
                        pu[:, 0:512], lu, x1b[:, f * 1024:f * 1024 + 512],
                        start=(f == 0), stop=(f == 15)))
                    self.pe(lambda e, f=f, pu=pu, lu=lu: e.matmul(
                        pu[:, 512:1024], lu, x1b[:, f * 1024 + 512:(f + 1) * 1024],
                        start=(f == 0), stop=(f == 15)))
                    self.pe(lambda e, f=f, pg=pg, lg=lg: e.matmul(
                        pg[:, 0:512], lg, x1b[:, f * 1024:f * 1024 + 512],
                        start=(f == 0), stop=(f == 15)))
                    self.pe(lambda e, f=f, pg=pg, lg=lg: e.matmul(
                        pg[:, 512:1024], lg, x1b[:, f * 1024 + 512:(f + 1) * 1024],
                        start=(f == 0), stop=(f == 15)), inc=(f == 15))
                wmm = self.sP.v
                mm_after_slab[sl] = wmm
                self.qw("scalar", self.sP, wmm)
                if ic >= 1:
                    self.qw("scalar", self.sV, hdone[ic - 1][0])
                vA = self.ae(lambda e, ic=ic, pg=pg: e.activation(
                    out=self.silu_sb[:], in_=pg[:], func=AF.Silu,
                    scale=self.wsgcol[:, ic:ic + 1]), inc=True)
                self.qw("vector", self.sA, vA)
                vV = self.ve(lambda e, ic=ic, pu=pu: e.scalar_tensor_tensor(
                    out=h_ap(ic), in0=pu[:], scalar=self.wsucol[:, ic:ic + 1],
                    in1=self.silu_sb[:], op0=ALU.mult, op1=ALU.mult), inc=True)
                hdone[ic] = (vV, vA)
        self.barrier()
        self.phase_done("upgate")
        if "h" in self.dbg:
            self.dma("sync", self.dbg["h"].rearrange("p (r n) -> r p n", r=4, n=16384)[0], self.B[0][:])
            self.dma("sync", self.dbg["h"].rearrange("p (r n) -> r p n", r=4, n=16384)[1], self.B[1][:])
            self.dma("sync", self.dbg["h"].rearrange("p (r n) -> r p n", r=4, n=16384)[2], self.B[2][:])
            self.dma("sync", self.dbg["h"].rearrange("p (r n) -> r p n", r=4, n=16384)[3], self.B[3][:])
            self.qw("sync", self.dS)

        wdb = [self.B[5].ap()[:, 0:8192], self.B[5].ap()[:, 8192:16384]]
        dn_mm = {}
        for ct in range(NF):
            ldq = "sync" if ct % 2 == 0 else "scalar"
            if ct >= 2:
                self.qw(ldq, self.sP, dn_mm[ct - 2])
            self.dma(ldq, wdb[ct % 2], self.wd2[ct].rearrange("p ic c -> p (ic c)"),
                     ctr=self.d2[ct % 2])
            wW = self.d2[ct % 2].v
            ps = self.p01 if (ct % 2 == 0) else self.p23
            self.qw("tensor", self.d2[ct % 2], wW)
            self.qw("tensor", self.sV)
            for ic in range(NI):
                self.pe(lambda e, ct=ct, ic=ic, ps=ps: e.matmul(
                    ps[:, 0:512], wdb[ct % 2][:, ic * 128:(ic + 1) * 128],
                    h_ap(ic)[:, 0:512], start=(ic == 0), stop=(ic == NI - 1)))
                self.pe(lambda e, ct=ct, ic=ic, ps=ps: e.matmul(
                    ps[:, 512:1024], wdb[ct % 2][:, ic * 128:(ic + 1) * 128],
                    h_ap(ic)[:, 512:1024], start=(ic == 0), stop=(ic == NI - 1)),
                    inc=(ic == NI - 1))
            wmm = self.sP.v
            dn_mm[ct] = wmm
            self.qw("sync", self.sV)
            self.dma("sync", self.io32[:, 0:1024], self.xstream[:, ct * 1024:(ct + 1) * 1024])
            wS = self.dS.v
            self.qw("vector", self.sP, wmm)
            self.qw("vector", self.dS, wS)
            self.ve(lambda e, ct=ct, ps=ps: e.scalar_tensor_tensor(
                out=self.io32[:, 1024:2048], in0=ps[:], scalar=self.wsdcol[:, ct:ct + 1],
                in1=self.io32[:, 0:1024], op0=ALU.mult, op1=ALU.add), inc=True)
            self.qw("sync", self.sV)
            self.dma("sync", self.xstream[:, ct * 1024:(ct + 1) * 1024], self.io32[:, 1024:2048])
        self.barrier()
        self.tap("stream", self.xstream)

    def _colslab_proj(self, wsrc, bias_col, out_cb, rhs, nk=NF, wcols=None):
        wcb = [self.B[5].ap()[:, 0:2048], self.B[5].ap()[:, 2048:4096]]
        mmv = {}
        for ct in range(NF):
            if ct >= 2:
                self.qw("gpsimd", self.sP, mmv[ct - 2])
            src = wsrc.rearrange("(k p) (ct c) -> p ct k c", p=128, k=nk, ct=NF, c=128)
            self.dma("gpsimd", wcb[ct % 2].rearrange("p (k c) -> p k c", k=nk, c=128),
                     src[:, ct, :, :], ctr=self.c2[ct % 2])
            wW = self.c2[ct % 2].v
            ps = self.p01 if (ct % 2 == 0) else self.p23
            self.qw("tensor", self.c2[ct % 2], wW)
            self.qw("tensor", self.sV)
            for k in range(nk):
                self.pe(lambda e, k=k, ct=ct, ps=ps: e.matmul(
                    ps[:, 0:512], wcb[ct % 2][:, k * 128:(k + 1) * 128], rhs(k, 0),
                    start=(k == 0), stop=(k == nk - 1)))
                self.pe(lambda e, k=k, ct=ct, ps=ps: e.matmul(
                    ps[:, 512:1024], wcb[ct % 2][:, k * 128:(k + 1) * 128], rhs(k, 1),
                    start=(k == 0), stop=(k == nk - 1)), inc=(k == nk - 1))
            wmm = self.sP.v
            mmv[ct] = wmm
            out_cb(ct, ps, wmm)

    def ph_attn(self, variant, mid_hook=None):
        a = variant == "a"
        self.ph_ln(with_su=False)
        self.tap_sbuf("xln", self.B[4][:])
        xln = self.B[4]
        agin_v = self.ag_in.rearrange("(c p) t -> p c t", c=NF, p=128)
        self.dma("gpsimd", agin_v, xln.ap().rearrange("p (c t) -> p c t", c=NF, t=TOWN))
        agin_done = self.dG.v
        self.qw("gpsimd", self.dG)
        hcc = self.cc.h
        self.cc.v += 1
        rg = [list(range(NCORE))]
        self.q("gpsimd", lambda e: e.collective_compute(
            "AllGather", ALU.bypass, replica_groups=rg,
            ins=[self.ag_in.opt()], outs=[self.ag_out.opt()]).then_inc(hcc, 1))
        wcc = self.cc.v

        KT, VT, QT, QFL = self.B[0], self.B[1], self.B[2], self.B[3]
        pfx = "a" if a else "b"

        if not a:
            def qfl_out(ct, ps, wmm):
                self.qw("vector", self.sP, wmm)
                self.ve(lambda e, ct=ct, ps=ps: e.tensor_scalar(
                    out=QFL[:, ct * 1024:(ct + 1) * 1024], in0=ps[:],
                    scalar1=self.bcol["bbq"][:, ct:ct + 1], scalar2=None, op0=ALU.add),
                    inc=True)

            self._colslab_proj(self.inp["bWq"], None, qfl_out,
                               lambda f, hh: xln[:, f * 1024 + hh * 512:f * 1024 + hh * 512 + 512])
            b4_free = self.sP.v
            self.qw("vector", self.sV)
            qfl_v = QFL.ap().rearrange("p (j u) -> p j u", j=16, u=1024)
            for hl in range(2):
                src = qfl_v[:, :, hl * 512:(hl + 1) * 512].rearrange("p j v -> p v j")
                dst = QT.ap()[:, hl * 8192:(hl + 1) * 8192].rearrange(
                    "p (v j) -> p v j", v=512, j=16)
                self.ve(lambda e, src=src, dst=dst: e.tensor_copy(out=dst, in_=src), inc=True)
            b3_free = self.sV.v

        wk_sl = self.B[5].ap()[:, 4096:8192]
        wv_sl = self.B[5].ap()[:, 8192:12288]
        wq_sl = self.B[5].ap()[:, 12288:16384]
        mats = [("k", wk_sl), ("v", wv_sl)] + ([("q", wq_sl)] if a else [])
        for nm, dstb in mats:
            for f in range(NF):
                for hl in range(2):
                    if a:
                        wsrc = self.inp["aW" + nm + "_my"][hl, f * 128:(f + 1) * 128, :]
                    else:
                        wsrc = self.inp["bW" + nm + "_my"][f * 128:(f + 1) * 128,
                                                           hl * 128:(hl + 1) * 128]
                    self.dma("gpsimd", dstb[:, f * 256 + hl * 128:f * 256 + hl * 128 + 128],
                             wsrc, ctr=self.dW)
        wslices = self.dW.v

        if a:
            bsrc = self.inp["abv_my"].rearrange("h d -> (h d)")
        else:
            bsrc = self.inp["bbv_my"]
        self.dma("sync", self.r1[0:1, 0:256], bsrc.rearrange("(o n) -> o n", o=1))
        wb = self.dS.v
        self.qw("tensor", self.dS, wb)
        self.qw("tensor", self.sV)
        self.pe(lambda e: e.matmul(self.p45[:, 0:256], self.onesrow_sb[:], self.r1[0:1, 0:256],
                                   start=True, stop=True), inc=True)
        self.qw("vector", self.sP)
        self.ve(lambda e: e.tensor_copy(out=self.bvb[:], in_=self.p45[:, 0:256]), inc=True)

        self.qw("sync", self.cc, wcc)
        self.qw("tensor", self.cc, wcc)
        self.qw("tensor", self.dW, wslices)

        G_r = self.ag_out.rearrange("(r f p) t -> r p f t", r=NCORE, f=NF, p=128)
        stage = [QFL, xln]

        def stage_load(r):
            self.dma("sync", stage[r % 2].ap().rearrange("p (f t) -> p f t", f=NF),
                     G_r[r], ctr=self.stg[r % 2])
            return self.stg[r % 2].v

        vload = {}
        if a:
            vload[0] = stage_load(0)
            self.qw("sync", self.dG, agin_done)
            vload[1] = stage_load(1)
        else:
            self.qw("sync", self.sV, b3_free)
            vload[0] = stage_load(0)
            self.qw("sync", self.sP, b4_free)
            vload[1] = stage_load(1)

        bk = self.bcol["abk_my" if a else "bbk_my"]
        bq = self.bcol["abq_my"] if a else None
        kevac = {}
        qevac = {-1: self.sV.v}
        vevac = {-1: self.sV.v}

        def kq_mms(st, wsl, lo):
            for f in range(NF):
                for hl in range(2):
                    ps = (self.p01, self.p23, self.p45, self.p67)[lo + hl]
                    for t2 in range(2):
                        self.pe(lambda e, f=f, hl=hl, t2=t2, ps=ps, st=st, wsl=wsl:
                                e.matmul(ps[:, t2 * 512:(t2 + 1) * 512],
                                         wsl[:, f * 256 + hl * 128:f * 256 + hl * 128 + 128],
                                         st[:, f * 1024 + t2 * 512:f * 1024 + t2 * 512 + 512],
                                         start=(f == 0), stop=(f == NF - 1)),
                                inc=(f == NF - 1 and hl == 1 and t2 == 1))
            return self.sP.v

        def kq_evac(r, dstT, bias_col, lo):
            for hl in range(2):
                ps = (self.p01, self.p23, self.p45, self.p67)[lo + hl]
                bcolv = bias_col[:, hl:hl + 1]
                for t2 in range(2):
                    self.ve(lambda e, r=r, hl=hl, t2=t2, ps=ps, bcolv=bcolv: e.tensor_scalar(
                        out=dstT[:, hl * 8192 + r * 1024 + t2 * 512:
                                 hl * 8192 + r * 1024 + t2 * 512 + 512],
                        in0=ps[:, t2 * 512:(t2 + 1) * 512], scalar1=bcolv, scalar2=None,
                        op0=ALU.add), inc=(hl == 1 and t2 == 1))
            return self.sV.v

        for r in range(NCORE):
            par = r % 2
            st = stage[par].ap()
            self.qw("tensor", self.stg[par], vload[r])
            self.qw("tensor", self.sV, vevac[r - 1])
            kmm = kq_mms(st, wk_sl, 0)
            self.qw("vector", self.sP, kmm)
            kevac[r] = kq_evac(r, KT, bk, 0)
            if a:
                self.qw("tensor", self.sV, qevac[r - 1])
                qmm = kq_mms(st, wq_sl, 2)
                self.qw("vector", self.sP, qmm)
                qevac[r] = kq_evac(r, QT, bq, 2)
            self.qw("tensor", self.sV, kevac[r])
            for f in range(NF):
                for tc in range(8):
                    ps = (self.p01 if tc < 4 else self.p23)
                    ofs = (tc % 4) * 256
                    self.pe(lambda e, f=f, tc=tc, ps=ps, ofs=ofs, st=st: e.matmul(
                        ps[:, ofs:ofs + 256], st[:, f * 1024 + tc * 128:f * 1024 + tc * 128 + 128],
                        wv_sl[:, f * 256:(f + 1) * 256],
                        start=(f == 0 and tc % 2 == 0),
                        stop=(f == NF - 1 and tc % 2 == 1)), inc=(f == NF - 1 and tc == 7))
            vmm = self.sP.v
            if r + 2 < NCORE:
                self.qw("sync", self.sP, vmm)
                vload[r + 2] = stage_load(r + 2)
            self.qw("vector", self.sP, vmm)
            for tc in range(8):
                ps = (self.p01 if tc < 4 else self.p23)
                ofs = (tc % 4) * 256
                self.ve(lambda e, r=r, tc=tc, ps=ps, ofs=ofs: e.tensor_tensor(
                    out=VT[:, (r * 8 + tc) * 256:(r * 8 + tc) * 256 + 256],
                    in0=ps[:, ofs:ofs + 256], in1=self.bvb[:], op=ALU.add), inc=(tc == 7))
            vevac[r] = self.sV.v
        self.barrier()
        if a and getattr(self, "cast_in_attn_a", False):
            self.ph_prep_casts()
        if "K" in self.dbg:
            self.dma("sync", self.dbg["K"], KT[:])
            self.dma("sync", self.dbg["V"], VT[:])
            self.dma("sync", self.dbg["Q"], QT[:])
            self.qw("sync", self.dS)
        if "G" in self.dbg:
            self.tap("G", self.ag_out)
        self.phase_done("proj_" + pfx)

        for hl in range(2):
            for blk in range(16):
                self.qw("scalar", self.sP)
                self.ae(lambda e, hl=hl, blk=blk: e.activation(
                    out=self.sqs[:, 0:512], in_=KT[:, hl * 8192 + blk * 512:hl * 8192 + blk * 512 + 512],
                    func=AF.Square), inc=True)
                self.qw("tensor", self.sA)
                self.qw("tensor", self.sV)
                self.pe(lambda e: e.matmul(self.p45[0:1, 0:512], self.onescol_sb[:],
                        self.sqs[:, 0:512], start=True, stop=True), inc=True)
                self.qw("vector", self.sP)
                self.ve(lambda e, blk=blk: e.tensor_reduce(
                    out=self.r2[0:1, blk:blk + 1], in_=self.p45[0:1, 0:512], axis=AX.X,
                    op=ALU.max), inc=True)
            self.qw("vector", self.sV)
            self.ve(lambda e: e.tensor_reduce(out=self.r2[0:1, 16:17], in_=self.r2[0:1, 0:16],
                                              axis=AX.X, op=ALU.max), inc=True)
            for blk in range(16):
                self.qw("scalar", self.sP)
                self.ae(lambda e, hl=hl, blk=blk: e.activation(
                    out=self.sqs[:, 0:512], in_=QT[:, hl * 8192 + blk * 512:hl * 8192 + blk * 512 + 512],
                    func=AF.Square), inc=True)
                self.qw("tensor", self.sA)
                self.qw("tensor", self.sV)
                self.pe(lambda e: e.matmul(self.p45[0:1, 0:512], self.onescol_sb[:],
                        self.sqs[:, 0:512], start=True, stop=True), inc=True)
                self.qw("vector", self.sP)
                self.ve(lambda e, hl=hl, blk=blk: e.tensor_reduce(
                    out=self.qb1[0:1, hl * 16 + blk:hl * 16 + blk + 1],
                    in_=self.p45[0:1, 0:512], axis=AX.X, op=ALU.max), inc=True)
            self.qw("vector", self.sV)
            self.ve(lambda e, hl=hl: e.tensor_scalar(
                out=self.r1[0:1, 0:16], in0=self.qb1[0:1, hl * 16:(hl + 1) * 16],
                scalar1=self.r2[0:1, 16:17], scalar2=float((ATT_SCALE * 1.002) ** 2),
                op0=ALU.mult, op1=ALU.mult), inc=True)
            self.qw("scalar", self.sV)
            self.ae(lambda e: e.activation(out=self.r1[0:1, 0:16], in_=self.r1[0:1, 0:16],
                                           func=AF.Ln), inc=True)
            self.qw("scalar", self.sA)
            self.ae(lambda e: e.activation(out=self.r1[0:1, 0:16], in_=self.r1[0:1, 0:16],
                    func=AF.Exp, scale=0.5), inc=True)
            self.qw("vector", self.sA)
            self.ve(lambda e: e.tensor_scalar(out=self.r1[0:1, 16:32], in0=self.r1[0:1, 0:16],
                    scalar1=-1.0, scalar2=None, op0=ALU.mult), inc=True)
            self.qw("tensor", self.sV)
            self.pe(lambda e: e.matmul(self.p45[:, 0:16], self.onesrow_sb[:],
                    self.r1[0:1, 16:32], start=True, stop=True), inc=True)
            self.qw("vector", self.sP)
            self.ve(lambda e, hl=hl: e.tensor_copy(out=self.cbias[:, hl * 16:(hl + 1) * 16],
                    in_=self.p45[:, 0:16]), inc=True)
        self.barrier()

        self.dma("sync", self.maskb[:], self.masks_t.ap())
        self.qw("sync", self.dS)
        for eng in ("tensor", "scalar", "vector"):
            self.qw(eng, self.dS)
        PP = [self.p45, self.p67]
        ep_sv = {0: 0, 1: 0}
        ep_sa = {0: 0, 1: 0}
        ystore = {0: 0, 1: 0}
        pending_ep = []

        pairs = []
        tbi = 0
        for hl in range(2):
            for tb in range(NF):
                npair = 2 * (tb + 1)
                for pr in range(npair):
                    pairs.append((hl, tb, pr, npair, tbi % 2))
                tbi += 1
        NP = len(pairs)
        exp_v = {}
        pre_v = {}
        cons_v = {}

        def emit_qk(g):
            hl, tb, pr, npair, pi = pairs[g]
            par = g % 2
            ps = self.p01 if par == 0 else self.p23
            qrhs0 = QT[:, hl * 8192 + tb * 512:hl * 8192 + tb * 512 + 512]
            if g >= 2:
                self.qw("tensor", self.sA, exp_v[g - 2])
            for k in range(2):
                sc = 2 * pr + k
                self.pe(lambda e, sc=sc, k=k, ps=ps, hl=hl, qrhs0=qrhs0: e.matmul(
                    ps[:, k * 512:(k + 1) * 512],
                    KT[:, hl * 8192 + sc * 128:hl * 8192 + sc * 128 + 128],
                    qrhs0, start=True, stop=True), inc=(k == 1))
            wS = self.sP.v
            self.qw("scalar", self.sP, wS)
            if g >= 2:
                self.qw("scalar", self.sP, cons_v[g - 2])
            ev = self.ae(lambda e, ps=ps, par=par, hl=hl, tb=tb: e.activation(
                out=self.ptb[:, par * 1024:(par + 1) * 1024], in_=ps[:], func=AF.Exp,
                scale=ATT_SCALE, bias=self.cbias[:, hl * 16 + tb:hl * 16 + tb + 1]),
                inc=True)
            exp_v[g] = ev
            wmask = None
            for k in range(2):
                sc = 2 * pr + k
                if sc >= 4 * tb:
                    kk = sc - 4 * tb
                    self.qw("vector", self.sA, ev)
                    wmask = self.ve(lambda e, k=k, par=par, kk=kk: e.tensor_tensor(
                        out=self.ptb[:, par * 1024 + k * 512:par * 1024 + (k + 1) * 512],
                        in0=self.ptb[:, par * 1024 + k * 512:par * 1024 + (k + 1) * 512],
                        in1=self.maskb[:, kk * 512:(kk + 1) * 512], op=ALU.mult),
                        inc=True)
            if wmask is None:
                self.qw("vector", self.sA, ev)
            if g >= 2:
                self.qw("vector", self.sP, cons_v[g - 2])
            pre_v[g] = self.ve(lambda e, par=par: e.tensor_tensor(
                out=self.sumbuf[:, par * 512:(par + 1) * 512],
                in0=self.ptb[:, par * 1024:par * 1024 + 512],
                in1=self.ptb[:, par * 1024 + 512:par * 1024 + 1024], op=ALU.add), inc=True)

        def emit_consume(g):
            hl, tb, pr, npair, pi = pairs[g]
            par = g % 2
            nsc = 2 * npair
            outT = PP[pi][:, 0:512]
            sums = PP[pi][0:1, 512:1024]
            self.qw("tensor", self.sV, pre_v[g])
            if pr == 0:
                self.qw("tensor", self.sV, ep_sv[pi])
                self.qw("tensor", self.sA, ep_sa[pi])
            st = (pr == 0)
            sp = (pr == npair - 1)
            self.pe(lambda e, par=par, st=st, sp=sp, sums=sums: e.matmul(
                sums, self.onescol_sb[:], self.sumbuf[:, par * 512:(par + 1) * 512],
                start=st, stop=sp))
            for k in range(2):
                sc = 2 * pr + k
                st2 = (sc == 0)
                sp2 = (sc == nsc - 1)
                self.pe(lambda e, k=k, par=par, sc=sc, hl=hl, st2=st2, sp2=sp2,
                        outT=outT: e.matmul(
                    outT, VT[:, sc * 256 + hl * 128:sc * 256 + hl * 128 + 128],
                    self.ptb[:, par * 1024 + k * 512:par * 1024 + (k + 1) * 512],
                    start=st2, stop=sp2), inc=(k == 1))
            cons_v[g] = self.sP.v
            if not sp:
                return
            self.qw("scalar", self.sP, cons_v[g])
            self.ae(lambda e, sums=sums: e.activation(out=self.r1[0:1, 0:512], in_=sums,
                                                      func=AF.Ln), inc=True)
            self.qw("scalar", self.sA)
            wrec = self.ae(lambda e: e.activation(out=self.r1[0:1, 0:512],
                           in_=self.r1[0:1, 0:512], func=AF.Exp, scale=-1.0), inc=True)
            ep_sa[pi] = wrec
            rbdst = PP[pi][:, 512:1024]

            def mk_ep(pi=pi, hl=hl, tb=tb, wrec=wrec, rbdst=rbdst, outT=outT):
                def ep():
                    self.qw("tensor", self.sA, wrec)
                    wbc = self.pe(lambda e: e.matmul(rbdst, self.onesrow_sb[:],
                                  self.r1[0:1, 0:512], start=True, stop=True), inc=True)
                    self.qw("vector", self.sP, wbc)
                    wrb = self.ve(lambda e: e.tensor_copy(out=self.rb, in_=rbdst),
                                  inc=True)
                    ypar = pi
                    self.qw("vector", self.sV, wrb)
                    self.qw("vector", self.ys[ypar], ystore[ypar])
                    wye = self.ve(lambda e, ypar=ypar, outT=outT: e.scalar_tensor_tensor(
                        out=self.ysb[:, ypar * 512:(ypar + 1) * 512], in0=outT, scalar=1.0,
                        in1=self.rb, op0=ALU.mult, op1=ALU.mult), inc=True)
                    ep_sv[pi] = wye
                    self.qw("sync", self.sV, wye)
                    self.dma("sync", self.y_in[hl * 128:(hl + 1) * 128,
                                               tb * 512:(tb + 1) * 512],
                             self.ysb[:, ypar * 512:(ypar + 1) * 512],
                             ctr=self.ys[ypar])
                    ystore[ypar] = self.ys[ypar].v
                return ep

            pending_ep.append(mk_ep())

        for g in range(NP):
            emit_qk(g)
            if pairs[g][2] == 1 and pending_ep:
                pending_ep.pop(0)()
            if g >= 1:
                emit_consume(g - 1)
        emit_consume(NP - 1)
        while pending_ep:
            pending_ep.pop(0)()
        self.barrier()
        self.phase_done("scores_" + pfx)

        hcc = self.cc.h
        self.cc.v += 1
        self.q("gpsimd", lambda e: e.collective_compute(
            "AllGather", ALU.bypass, replica_groups=rg,
            ins=[self.y_in.opt()], outs=[self.y_out.opt()]).then_inc(hcc, 1))
        wcc2 = self.cc.v
        if mid_hook is not None:
            mid_hook()
        self.qw("sync", self.cc, wcc2)
        if "yout" in self.dbg:
            self.dma("sync", self.dbg["yout"], self.y_out)
            self.qw("sync", self.dS)

        ysall = self.B[3]
        if not hasattr(self, "_cid"):
            self._cid = self.nc.partition_id(engines=[mybir.EngineType.SP])
        cid = self._cid
        for k in range(16):
            self.dma("sync", ysall[:, k * 1024:(k + 1) * 1024],
                     self.y_out[k * 128:(k + 1) * 128, bass.ts(cid, TOWN)])
        wY = self.dS.v
        self.qw("tensor", self.dS, wY)
        self.qw("gpsimd", self.dS, wY)
        self.qw("sync", self.dS, wY)
        bo = self.bcol["abo" if a else "bbo"]

        def wo_out(ct, ps, wmm):
            self.qw("sync", self.sV)
            self.dma("sync", self.io32[:, 0:1024], self.xstream[:, ct * 1024:(ct + 1) * 1024])
            wS = self.dS.v
            self.qw("vector", self.sP, wmm)
            self.qw("vector", self.dS, wS)
            self.ve(lambda e, ct=ct, ps=ps: e.scalar_tensor_tensor(
                out=self.io32[:, 1024:2048], in0=ps[:], scalar=bo[:, ct:ct + 1],
                in1=self.io32[:, 0:1024], op0=ALU.add, op1=ALU.add), inc=True)
            self.qw("sync", self.sV)
            self.dma("sync", self.xstream[:, ct * 1024:(ct + 1) * 1024],
                     self.io32[:, 1024:2048])

        self._colslab_proj(self.inp["aWo" if a else "bWo"], None, wo_out,
                           lambda k, hh: ysall[:, k * 1024 + hh * 512:k * 1024 + hh * 512 + 512])
        self.barrier()
        self.tap("stream", self.xstream)
        self.phase_done("attn_" + pfx)

    def build(self, phases):
        self.declare_dram()
        self.declare_inline()
        self.alloc_sbuf()
        self.make_sems()
        for ph in phases:
            if self.stopped:
                break
            ph()
        with self.es:
            block = self.es.enter_context(self.nc.Block())
            for eng in ("tensor", "scalar", "vector", "gpsimd", "sync"):
                getattr(block, eng)(lambda e, eng=eng: [fn(e) for fn in self.ops[eng]])
        return self.nc


def build_core(core_id, taps=(), upto=None, which="full"):
    b = Builder(core_id, taps=taps, upto=upto)
    prep_serial = [b.ph_prep_casts, b.ph_prep_mms, b.barrier]
    if which == "ffn_only":
        phases = [b.ph_consts, b.ph_input] + prep_serial + [b.ph_ffn, b.ph_output]
    elif which == "attn_a_only":
        phases = [b.ph_consts, b.ph_input] + prep_serial + [lambda: b.ph_attn("a"), b.ph_output]
    elif which == "attn_b_only":
        phases = [b.ph_consts, b.ph_input] + prep_serial + [lambda: b.ph_attn("b"), b.ph_output]
    else:
        phases = [b.ph_consts, b.ph_input, b.ph_prep_casts,
                  lambda: b.ph_attn("a", mid_hook=b.ph_prep_mms), b.ph_ffn,
                  lambda: b.ph_attn("b"), b.ph_ffn, b.ph_output]
    return b.build(phases)


def make_in_map(inputs, core_id):
    x = np.asarray(inputs["L0_attn_a_x"]).reshape(T, D)
    h0 = 2 * core_id
    aWq = np.asarray(inputs["L0_attn_a_Wq"])
    aWk = np.asarray(inputs["L0_attn_a_Wk"])
    aWv = np.asarray(inputs["L0_attn_a_Wv"])
    bWk = np.asarray(inputs["L0_attn_b_Wk"])
    bWv = np.asarray(inputs["L0_attn_b_Wv"])
    m = {
        "x_own": np.ascontiguousarray(x[core_id * TOWN:(core_id + 1) * TOWN]),
        "aWq_my": aWq[h0:h0 + 2], "aWk_my": aWk[h0:h0 + 2], "aWv_my": aWv[h0:h0 + 2],
        "abq_my": np.asarray(inputs["L0_attn_a_bq"])[h0:h0 + 2],
        "abk_my": np.asarray(inputs["L0_attn_a_bk"])[h0:h0 + 2],
        "abv_my": np.asarray(inputs["L0_attn_a_bv"])[h0:h0 + 2],
        "aWo": inputs["L0_attn_a_Wo"], "abo": inputs["L0_attn_a_bo"],
        "bWq": inputs["L0_attn_b_Wq"],
        "bWk_my": bWk[:, h0 * 128:(h0 + 2) * 128],
        "bWv_my": bWv[:, h0 * 128:(h0 + 2) * 128],
        "bWo": inputs["L0_attn_b_Wo"],
        "bbq": inputs["L0_attn_b_bq"],
        "bbk_my": np.asarray(inputs["L0_attn_b_bk"])[h0 * 128:(h0 + 2) * 128],
        "bbv_my": np.asarray(inputs["L0_attn_b_bv"])[h0 * 128:(h0 + 2) * 128],
        "bbo": inputs["L0_attn_b_bo"],
        "su_ug": inputs["L0_ffn_SU_ug"], "su_dp": inputs["L0_ffn_SU_dp"],
        "wsug": inputs["L0_ffn_Wscale_ug"], "wsdp": inputs["L0_ffn_Wscale_dp"],
        "fWupT": np.ascontiguousarray(np.asarray(inputs["L0_ffn_W_up"]).T),
        "fWgateT": np.ascontiguousarray(np.asarray(inputs["L0_ffn_W_gate"]).T),
        "fWdownT": np.ascontiguousarray(np.asarray(inputs["L0_ffn_W_down"]).T),
    }
    return {k: np.ascontiguousarray(np.asarray(v, dtype=np.float32)) for k, v in m.items()}


def kernel(**inputs):
    nc = build_core(0)
    in_maps = [make_in_map(inputs, c) for c in range(NCORE)]
    res = run_bass_kernel_spmd(nc, in_maps, core_ids=list(range(NCORE)))
    out = np.concatenate([np.asarray(res.results[c]["y_own"]) for c in range(NCORE)], axis=0)
    return out.reshape(1, T, D).astype(np.float32)
